# version 05
# baseline (speedup 1.0000x reference)
import os
import numpy as np
import ml_dtypes
from contextlib import ExitStack

import concourse.bass as bass
import concourse.mybir as mybir
from concourse.bass_utils import run_bass_kernel_spmd

F32 = mybir.dt.float32
BF16 = mybir.dt.bfloat16
AF = mybir.ActivationFunctionType
OP = mybir.AluOpType
AX = mybir.AxisListType

B, S, D = 2, 8192, 4096
NH, HD, RD = 16, 256, 64
HPC = 4
DPC = HPC * HD
TPC = 2048
NCORE = 8
EPS_LN = 1e-5
EPS_RMS = 1.1920929e-07
SINK_IT = 50
TAU = 0.1

ENGS = ("sync", "gpsimd", "tensor", "vector", "scalar")


CHAIN = {"vector": "vch", "scalar": "sch"}


class Plan:
    def __init__(self, sems):
        self.entries = {e: [] for e in ENGS}
        self.cnt = {}
        self.sems = sems
        self.prev_marker = {}

    def cur(self, name):
        return self.cnt.get(name, 0)

    def emit(self, eng, fn, waits=(), inc=None, inc_amt=1, chain=True):
        post = None
        if inc is not None:
            self.cnt[inc] = self.cnt.get(inc, 0) + inc_amt
            post = self.cnt[inc]
        autow = None
        ginc = None
        if chain and fn is not None and eng in CHAIN:
            autow = self.prev_marker.get(eng)
            if inc is None:
                cs = CHAIN[eng]
                self.cnt[cs] = self.cnt.get(cs, 0) + 1
                ginc = (cs, 1)
                self.prev_marker[eng] = (cs, self.cnt[cs])
            else:
                self.prev_marker[eng] = (inc, post)
        self.entries[eng].append((fn, tuple(waits), inc, inc_amt, autow, ginc))
        return post

    def dma(self, eng, fn, waits=(), inc=None):
        return self.emit(eng, fn, waits, inc, 16)

    def barrier(self, pre=None):
        pre = pre or {}
        base = self.cur("bar")
        for e in ENGS:
            if e in ("sync", "gpsimd"):
                self.emit(e, lambda eng: eng.nop(), waits=tuple(pre.get(e, ())),
                          inc="bar", chain=False)
            else:
                self.emit(e, lambda eng: eng.drain(), waits=tuple(pre.get(e, ())),
                          inc="bar", chain=False)
        tgt = base + len(ENGS)
        for e in ENGS:
            self.emit(e, None, waits=(("bar", tgt),))

    def replay(self, eng_name, eng):
        sems = self.sems
        for fn, waits, inc, amt, autow, ginc in self.entries[eng_name]:
            for (sn, val) in waits:
                if val > 0:
                    eng.wait_ge(sems[sn], val)
            if fn is None:
                continue
            if autow is not None:
                eng.wait_ge(sems[autow[0]], autow[1])
            inst = fn(eng)
            if inst is None:
                inst = eng.nop()
            if ginc is not None:
                inst.then_inc(sems[ginc[0]], ginc[1])
            elif inc is not None:
                inst.then_inc(sems[inc], amt)


def _block_emit(nc, p):
    with nc.Block() as block:
        @block.sync
        def _(eng):
            p.replay("sync", eng)

        @block.gpsimd
        def _(eng):
            p.replay("gpsimd", eng)

        @block.tensor
        def _(eng):
            p.replay("tensor", eng)

        @block.vector
        def _(eng):
            p.replay("vector", eng)

        @block.scalar
        def _(eng):
            p.replay("scalar", eng)



def build_launch_a(phases=None):
    if phases is None:
        phases = int(os.environ.get("KA_PHASES", "3"))
    NOROPE = os.environ.get("KA_NOROPE", "") == "1"
    RN = int(os.environ.get("KA_RN", "5"))
    NONORM = os.environ.get("KA_NONORM", "") == "1"
    NOTRANS = os.environ.get("KA_NOTRANS", "") == "1"
    PROJS = os.environ.get("KA_PROJS", "qkv")
    nc = bass.Bass("TRN2", target_bir_lowering=False)

    xT_h = nc.dram_tensor("xT", [D, S], F32, kind="ExternalInput")
    w_h = {pj: nc.dram_tensor(f"w{pj}T", [D, DPC], F32, kind="ExternalInput")
           for pj in ("q", "k", "v")}
    sinr_h = nc.dram_tensor("sinr", [S, 128], F32, kind="ExternalInput")
    cosr_h = nc.dram_tensor("cosr", [S, 128], F32, kind="ExternalInput")
    masks_h = nc.dram_tensor("masks", [512, 512], F32, kind="ExternalInput")
    ident_h = nc.dram_tensor("ident", [128, 128], F32, kind="ExternalInput")
    onesrow_h = nc.dram_tensor("onesrow", [1, 128], F32, kind="ExternalInput")

    attn_h = nc.dram_tensor("attn", [DPC, S], BF16, kind="ExternalOutput")

    xbf_h = nc.dram_tensor("xbf", [D, S], BF16, kind="Internal")
    qt_h = nc.dram_tensor("qt", [16, DPC, 512], BF16, kind="Internal")
    kt_h = nc.dram_tensor("kt", [16, DPC, 512], BF16, kind="Internal")
    vt_h = nc.dram_tensor("vt", [HPC, S, HD], BF16, kind="Internal")

    sem_names = ["bar", "vch", "sch", "xin0", "xin1", "xcast", "xwr0", "xwr1", "sq", "smm",
                 "sev", "fin", "fsq", "fre", "ftp",
                 "wld", "xbld", "qmm", "bkf", "ropa", "ropb", "scld", "nrm",
                 "tps", "tpe", "qst",
                 "ktld", "qcld", "tsc", "vmsk", "sexp", "tpv", "vrs",
                 "trb", "voev", "ost", "cns"]
    stack = ExitStack()
    sems = {n: stack.enter_context(nc.semaphore(n)) for n in sem_names}
    p = Plan(sems)

    ident_f = nc.alloc_sbuf_tensor("ident_f", [128, 128], F32)
    ident_b = nc.alloc_sbuf_tensor("ident_b", [128, 128], BF16)
    onesrow_f = nc.alloc_sbuf_tensor("onesrow_f", [1, 128], F32)
    ones_b = nc.alloc_sbuf_tensor("ones_b", [128, 1], BF16)
    masks_sb = nc.alloc_sbuf_tensor("masks_sb", [128, 4 * 512], F32)
    negm_row = nc.alloc_sbuf_tensor("negm_row", [1, S], BF16)
    s_sum = nc.alloc_sbuf_tensor("s_sum", [1, 256], F32)
    s_sq = nc.alloc_sbuf_tensor("s_sq", [1, 256], F32)
    s_m = nc.alloc_sbuf_tensor("s_m", [1, 256], F32)
    r_cols = nc.alloc_sbuf_tensor("r_cols", [128, 64], F32)
    qmax2 = nc.alloc_sbuf_tensor("qmax2", [128, HPC], F32)
    kmax2 = nc.alloc_sbuf_tensor("kmax2", [128, HPC], F32)
    negcb = nc.alloc_sbuf_tensor("negcb", [128, HPC], F32)
    negc_row = nc.alloc_sbuf_tensor("negc_row", [1, HPC], F32)
    epsc = nc.alloc_sbuf_tensor("epsc", [1, 1], F32)

    pp = [nc.alloc_psum_tensor(f"pp{i}", [128, 512], F32) for i in range(4)]
    ps_a = nc.alloc_psum_tensor("ps_a", [1, 512], F32)
    ps_b = nc.alloc_psum_tensor("ps_b", [1, 512], F32)
    pbf = nc.alloc_psum_tensor("pbf", [128, 1024], BF16)
    pmisc = nc.alloc_psum_tensor("pmisc", [128, 512], F32)

    xT_v = xT_h.ap().rearrange("(c p) t -> p c t", p=128)
    xbf_v = xbf_h.ap().rearrange("(c p) t -> p c t", p=128)

    p.dma("sync", lambda e: e.dma_start(ident_f[:, :], ident_h.ap()[:, :]), inc="cns")
    p.dma("sync", lambda e: e.dma_start(onesrow_f[:, :], onesrow_h.ap()[:, :]),
          inc="cns")
    p.dma("sync", lambda e: e.dma_start(
        masks_sb[:, :], masks_h.ap().rearrange("(m p) t -> p m t", p=128)), inc="cns")
    p.emit("vector", lambda e: e.tensor_copy(ident_b[:, :], ident_f[:, :]),
           waits=(("cns", 48),))
    p.emit("vector", lambda e: e.memset(ones_b[:, :], 1.0))
    p.emit("vector", lambda e: e.memset(qmax2[:, :], 0.0))
    p.emit("vector", lambda e: e.memset(kmax2[:, :], 0.0))
    p.emit("vector", lambda e: e.memset(epsc[:, :], EPS_LN), inc="cns")

    st1 = ExitStack()
    NCH = 32
    CW = S // NCH
    xin = [st1.enter_context(nc.sbuf_tensor(f"xin{i}", [128, 32 * CW], F32))
           for i in range(2)]
    xbf_sb = [st1.enter_context(nc.sbuf_tensor(f"xbfs{i}", [128, 32 * CW], BF16))
              for i in range(2)]
    xsq_sb = [st1.enter_context(nc.sbuf_tensor(f"xsqs{i}", [128, 32 * CW], BF16))
              for i in range(2)]

    for ch in range(NCH):
        bi = ch % 2
        c0 = ch * CW
        for sp in range(4):
            w = [("xcast", ch - 1)] if (ch >= 2 and sp == 0) else []
            p.dma("sync", (lambda bi_, sp_, c0_: lambda e: e.dma_start(
                xin[bi_][:, sp_ * 8 * CW:(sp_ + 1) * 8 * CW],
                xT_v[:, sp_ * 8:(sp_ + 1) * 8, c0_:c0_ + CW]))(bi, sp, c0),
                waits=tuple(w), inc=f"xin{bi}")
        w = [(f"xin{bi}", 64 * (ch // 2 + 1))]
        if ch >= 2:
            w += [(f"xwr{bi}", 64 * (ch // 2)), ("smm", ch - 1), ("sq", ch - 1)]
        p.emit("vector", (lambda bi_: lambda e: e.tensor_copy(
            xbf_sb[bi_][:, :], xin[bi_][:, :]))(bi), waits=tuple(w), inc="xcast")
        xc = p.cur("xcast")
        w = [("xcast", xc)]
        if ch >= 2:
            w.append(("smm", ch - 1))
        p.emit("scalar", (lambda bi_: lambda e: e.square(
            xsq_sb[bi_][:, :], xbf_sb[bi_][:, :]))(bi), waits=tuple(w), inc="sq")
        for sp in range(4):
            w = [("xcast", xc)] if sp == 0 else []
            p.dma("gpsimd", (lambda bi_, sp_, c0_: lambda e: e.dma_start(
                xbf_v[:, sp_ * 8:(sp_ + 1) * 8, c0_:c0_ + CW],
                xbf_sb[bi_][:, sp_ * 8 * CW:(sp_ + 1) * 8 * CW]))(bi, sp, c0),
                waits=tuple(w), inc=f"xwr{bi}")
        sqv = p.cur("sq")
        for dc in range(32):
            w = []
            if dc == 0:
                w = [("xcast", xc), ("sq", sqv), ("sev", ch)]
            p.emit("tensor", (lambda bi_, dc_, st_, sp_: lambda e: e.matmul(
                ps_a[0:1, 0:CW], ones_b[:, :],
                xbf_sb[bi_][:, dc_ * CW:(dc_ + 1) * CW], start=st_, stop=sp_))(
                    bi, dc, dc == 0, dc == 31), waits=tuple(w))
            p.emit("tensor", (lambda bi_, dc_, st_, sp_: lambda e: e.matmul(
                ps_b[0:1, 0:CW], ones_b[:, :],
                xsq_sb[bi_][:, dc_ * CW:(dc_ + 1) * CW], start=st_, stop=sp_))(
                    bi, dc, dc == 0, dc == 31),
                inc="smm" if dc == 31 else None)
        smv = p.cur("smm")
        w = [("smm", smv)]
        if ch >= 1:
            w.append(("ftp", ch))
        p.emit("vector", lambda e: e.tensor_copy(s_sum[0:1, :], ps_a[0:1, 0:CW]),
               waits=tuple(w))
        p.emit("vector", lambda e: e.tensor_copy(s_sq[0:1, :], ps_b[0:1, 0:CW]),
               inc="sev")
        p.emit("vector", (lambda c0_: lambda e: e.tensor_scalar_mul(
            negm_row[0:1, c0_:c0_ + CW], s_sum[0:1, :], -1.0 / D))(c0))
        p.emit("vector", lambda e: e.tensor_scalar_mul(s_m[0:1, :], s_sum[0:1, :],
                                                       1.0 / D))
        p.emit("vector", lambda e: e.tensor_mul(s_m[0:1, :], s_m[0:1, :],
                                                s_m[0:1, :]))
        p.emit("vector", lambda e: e.tensor_scalar_mul(s_sq[0:1, :], s_sq[0:1, :],
                                                       1.0 / D))
        p.emit("vector", lambda e: e.tensor_sub(s_sq[0:1, :], s_sq[0:1, :],
                                                s_m[0:1, :]), inc="fin")
        p.emit("scalar", lambda e: e.activation(
            s_sq[0:1, :], s_sq[0:1, :], AF.Sqrt, bias=epsc[0:1, 0:1], scale=1.0),
            waits=(("fin", ch + 1), ("cns", p.cur("cns"))), inc="fsq")
        p.emit("vector", lambda e: e.reciprocal(s_sq[0:1, :], s_sq[0:1, :]),
               waits=(("fsq", ch + 1),), inc="fre")
        for k in range(2):
            p.emit("tensor", (lambda ch_, k_: lambda e: e.transpose(
                pmisc[:, 2 * ch_ + k_:2 * ch_ + k_ + 1],
                s_sq[0:1, k_ * 128:(k_ + 1) * 128], ident_f[0:1, 0:1]))(ch, k),
                waits=(("fre", ch + 1),) if k == 0 else (),
                inc="ftp" if k == 1 else None)

    p.emit("vector", lambda e: e.tensor_copy(r_cols[:, :], pmisc[:, 0:64]),
           waits=(("ftp", NCH),), inc="sev")

    p.barrier(pre={"gpsimd": [("xwr0", 64 * (NCH // 2)), ("xwr1", 64 * (NCH // 2))],
                   "vector": [("sev", p.cur("sev")), ("cns", p.cur("cns"))]})
    st1.close()

    st2 = ExitStack()
    wbuf = st2.enter_context(nc.sbuf_tensor("wbuf", [128, 32 * DPC], BF16))
    xblk = st2.enter_context(nc.sbuf_tensor("xblk", [128, 32 * 1024], BF16))
    wrs_f = st2.enter_context(nc.sbuf_tensor("wrs_f", [1, DPC], F32))
    wrs_b = st2.enter_context(nc.sbuf_tensor("wrs_b", [1, DPC], BF16))
    qstage = st2.enter_context(nc.sbuf_tensor("qstage", [128, DPC], BF16))
    sqbuf = st2.enter_context(nc.sbuf_tensor("sqbuf", [128, DPC], BF16))
    nrmt = st2.enter_context(nc.sbuf_tensor("nrmt", [128, HPC], F32))
    robuf = st2.enter_context(nc.sbuf_tensor("robuf", [128, HPC * RD], F32))
    robuf2 = st2.enter_context(nc.sbuf_tensor("robuf2", [128, HPC * RD], F32))
    robt = st2.enter_context(nc.sbuf_tensor("robt", [128, HPC * 32], F32))
    robt2 = st2.enter_context(nc.sbuf_tensor("robt2", [128, HPC * 32], F32))
    sc_sin = st2.enter_context(nc.sbuf_tensor("sc_sin", [128, 128], F32))
    sc_cos = st2.enter_context(nc.sbuf_tensor("sc_cos", [128, 128], F32))
    qt_blk = st2.enter_context(nc.sbuf_tensor("qt_blk", [128, 8 * 512], BF16))

    out_map = {"q": qt_h, "k": kt_h}
    mx_map = {"q": qmax2, "k": kmax2}

    for pj in (tuple(PROJS) if phases >= 2 else ()):
        wT_v = w_h[pj].ap().rearrange("(c p) d -> p c d", p=128)
        for sp in range(4):
            p.dma("gpsimd", (lambda sp_, t_: lambda e: e.dma_start(
                wbuf[:, sp_ * 8 * DPC:(sp_ + 1) * 8 * DPC],
                t_[:, sp_ * 8:(sp_ + 1) * 8, :]))(sp, wT_v), inc="wld")
        wl = p.cur("wld")
        for dc in range(2):
            tgt = ps_a if dc == 0 else ps_b
            for c in range(32):
                w = [("wld", wl)] if (dc == 0 and c == 0) else []
                p.emit("tensor", (lambda t_, c_, dc_, st_, sp_: lambda e: e.matmul(
                    t_[0:1, :], ones_b[:, :],
                    wbuf[:, c_ * DPC + dc_ * 512: c_ * DPC + (dc_ + 1) * 512],
                    start=st_, stop=sp_))(tgt, c, dc, c == 0, c == 31),
                    waits=tuple(w),
                    inc="qmm" if (c == 31 and dc == 1) else None)
        qm = p.cur("qmm")
        p.emit("vector", lambda e: e.tensor_copy(wrs_f[0:1, 0:512], ps_a[0:1, :]),
               waits=(("qmm", qm),))
        p.emit("vector", lambda e: e.tensor_copy(wrs_f[0:1, 512:1024], ps_b[0:1, :]))
        p.emit("vector", lambda e: e.tensor_copy(wrs_b[0:1, :], wrs_f[0:1, :]),
               inc="bkf")
        for blk in range(8):
            t0 = blk * 1024
            for sp in range(4):
                w = []
                if sp == 0:
                    w.append(("bkf", p.cur("bkf")))
                    w.append(("qmm", p.cur("qmm")))
                p.dma("sync", (lambda sp_, t0_: lambda e: e.dma_start(
                    xblk[:, sp_ * 8 * 1024:(sp_ + 1) * 8 * 1024],
                    xbf_v[:, sp_ * 8:(sp_ + 1) * 8, t0_:t0_ + 1024]))(sp, t0),
                    waits=tuple(w), inc="xbld")
            xbl = p.cur("xbld")
            for ti in range(8):
                tg = blk * 8 + ti
                if pj != "v" and not NOROPE and RN >= 1:
                    w = [("ropb", p.cur("ropb"))]
                    p.dma("sync", (lambda tg_: lambda e: e.dma_start(
                        sc_sin[:, :], sinr_h.ap()[tg_ * 128:(tg_ + 1) * 128, :]))(tg),
                        waits=tuple(w), inc="scld")
                    p.dma("sync", (lambda tg_: lambda e: e.dma_start(
                        sc_cos[:, :], cosr_h.ap()[tg_ * 128:(tg_ + 1) * 128, :]))(tg),
                        inc="scld")
                scl = p.cur("scld")
                for dc in range(2):
                    bank = pp[dc]
                    for c in range(32):
                        w = []
                        if c == 0:
                            w = [("xbld", xbl), ("bkf", p.cur("bkf"))]
                        p.emit("tensor",
                               (lambda c_, ti_, dc_, st_, b_: lambda e: e.matmul(
                                   b_[:, :],
                                   xblk[:, c_ * 1024 + ti_ * 128:
                                        c_ * 1024 + ti_ * 128 + 128],
                                   wbuf[:, c_ * DPC + dc_ * 512:
                                        c_ * DPC + (dc_ + 1) * 512],
                                   start=st_, stop=False))(c, ti, dc, c == 0, bank),
                               waits=tuple(w))
                    p.emit("tensor", (lambda tg_, dc_, b_: lambda e: e.matmul(
                        b_[:, :], negm_row[0:1, tg_ * 128:(tg_ + 1) * 128],
                        wrs_b[0:1, dc_ * 512:(dc_ + 1) * 512],
                        start=False, stop=True))(tg, dc, bank), inc="qmm")
                    qmm = p.cur("qmm")
                    if pj != "v" and not NOROPE and RN >= 2:
                        p.emit("vector", (lambda dc_, tg_, b_: lambda e:
                               e.tensor_scalar_mul(
                                   robuf[:, :].rearrange(
                                       "p (h r) -> p h r", r=RD)[:, 2 * dc_:2 * dc_ + 2, :],
                                   b_[:, :].rearrange(
                                       "p (h r) -> p h r", r=HD)[:, :, 0:RD],
                                   r_cols[:, tg_:tg_ + 1]))(dc, tg, bank),
                               waits=(("qmm", qmm),), inc="ropa")
                    wse = [("qmm", qmm)]
                    if pj != "v":
                        wse.append(("ropa", p.cur("ropa")))
                    if dc == 0:
                        if pj == "v":
                            wse.append(("qst", p.cur("qst")))
                        else:
                            wse.append(("tps", p.cur("tps")))
                    p.emit("scalar", (lambda dc_, tg_, b_: lambda e: e.activation(
                        qstage[:, dc_ * 512:(dc_ + 1) * 512], b_[:, :], AF.Copy,
                        bias=0.0, scale=r_cols[:, tg_:tg_ + 1]))(dc, tg, bank),
                        waits=tuple(wse), inc="bkf")
                qevv = p.cur("bkf")
                if pj != "v":
                    ropv = p.cur("ropa")
                    do_rope = not NOROPE
                    rb = robuf[:, :].rearrange("p (h i two) -> p h i two", i=32, two=2)
                    rb2 = robuf2[:, :].rearrange("p (h i two) -> p h i two", i=32, two=2)
                    sv = sc_sin[:, :].rearrange("p (h i) -> p h i", i=32)
                    cv = sc_cos[:, :].rearrange("p (h i) -> p h i", i=32)
                    tv = robt[:, :].rearrange("p (h i) -> p h i", i=32)
                    tv2 = robt2[:, :].rearrange("p (h i) -> p h i", i=32)
                    if do_rope and RN >= 3:
                        p.emit("vector", lambda e: e.tensor_mul(tv, rb[:, :, :, 1], sv),
                               waits=(("ropa", ropv), ("scld", scl)))
                        p.emit("vector", lambda e: e.tensor_mul(tv2, rb[:, :, :, 0], sv))
                        if RN >= 4:
                            p.emit("vector", lambda e: e.tensor_mul(
                                rb2[:, :, :, 0], rb[:, :, :, 0], cv))
                            p.emit("vector", lambda e: e.tensor_sub(
                                rb2[:, :, :, 0], rb2[:, :, :, 0], tv))
                            p.emit("vector", lambda e: e.tensor_mul(
                                rb2[:, :, :, 1], rb[:, :, :, 1], cv))
                            p.emit("vector", lambda e: e.tensor_add(
                                rb2[:, :, :, 1], rb2[:, :, :, 1], tv2), inc="ropb")
                        else:
                            p.emit("vector", lambda e: e.tensor_copy(
                                robuf2[:, :], robuf[:, :]), inc="ropb")
                        if RN >= 5:
                            p.emit("vector", lambda e: e.tensor_copy(
                                qstage[:, :].rearrange("p (h r) -> p h r", r=HD)[:, :, 0:RD],
                                robuf2[:, :].rearrange("p (h r) -> p h r", r=RD)),
                                waits=(("bkf", qevv),), inc="ropb")
                    ropd = p.cur("ropb")
                    if not NONORM:
                        p.emit("scalar", lambda e: e.square(sqbuf[:, :], qstage[:, :]),
                               waits=(("ropb", ropd), ("bkf", qevv)), inc="sq")
                        p.emit("vector", lambda e: e.reduce_sum(
                            nrmt[:, :], sqbuf[:, :].rearrange("p (h d) -> p h d", d=HD),
                            axis=AX.X), waits=(("sq", p.cur("sq")),))
                        p.emit("vector", (lambda m_: lambda e: e.tensor_max(
                            m_[:, :], m_[:, :], nrmt[:, :]))(mx_map[pj]), inc="nrm")
                    for i in (range(8) if not NOTRANS else ()):
                        w = []
                        if i == 0:
                            w = [("ropb", ropd), ("bkf", qevv), ("tpe", p.cur("tpe"))]
                        p.emit("tensor", (lambda i_: lambda e: e.transpose(
                            pbf[:, i_ * 128:(i_ + 1) * 128],
                            qstage[:, i_ * 128:(i_ + 1) * 128], ident_b[:, :]))(i),
                            waits=tuple(w), inc="tps" if i == 7 else None)
                    tpsv = p.cur("tps")
                    for i in (range(8) if not NOTRANS else ()):
                        w = [("tps", tpsv)] if i == 0 else []
                        if i == 0 and tg % 4 == 0:
                            w.append(("qst", p.cur("qst")))
                        p.emit("vector", (lambda i_, t4_: lambda e: e.tensor_copy(
                            qt_blk[:, i_ * 512 + t4_ * 128:
                                   i_ * 512 + t4_ * 128 + 128],
                            pbf[:, i_ * 128:(i_ + 1) * 128]))(i, tg % 4),
                            waits=tuple(w), inc="tpe" if i == 7 else None)
                    tpev = p.cur("tpe")
                    if tg % 4 == 3 and not NOTRANS:
                        qc = tg // 4
                        dst = out_map[pj].ap()
                        for i in range(8):
                            w = [("tpe", tpev)] if i == 0 else []
                            p.dma("sync", (lambda i_, qc_, d_: lambda e: e.dma_start(
                                d_[qc_, i_ * 128:(i_ + 1) * 128, :],
                                qt_blk[:, i_ * 512:(i_ + 1) * 512]))(i, qc, dst),
                                waits=tuple(w), inc="qst")
                else:
                    for h in range(HPC):
                        w = [("bkf", qevv)] if h == 0 else []
                        p.dma("sync", (lambda h_, tg_: lambda e: e.dma_start(
                            vt_h.ap()[h_, tg_ * 128:(tg_ + 1) * 128, :],
                            qstage[:, h_ * HD:(h_ + 1) * HD]))(h, tg),
                            waits=tuple(w), inc="qst")
        p.barrier(pre={"sync": [("qst", p.cur("qst"))],
                       "vector": [("nrm", p.cur("nrm"))],
                       "scalar": [("bkf", p.cur("bkf"))]})

    if phases < 2:
        p.barrier()
    p.emit("vector", lambda e: e.tensor_add(qmax2[:, :], qmax2[:, :], kmax2[:, :]),
           inc="sev")
    p.emit("tensor", lambda e: e.transpose(
        pmisc[0:HPC, 0:128], qmax2[:, :], ident_f[:, :]),
        waits=(("sev", p.cur("sev")),), inc="smm")
    p.emit("vector", lambda e: e.reduce_max(
        qmax2[0:HPC, 0:1], pmisc[0:HPC, 0:128], axis=AX.X),
        waits=(("smm", p.cur("smm")),))
    p.emit("vector", lambda e: e.tensor_scalar_mul(
        qmax2[0:HPC, 0:1], qmax2[0:HPC, 0:1], -1.0 / 32.0), inc="sev")
    sevn = p.cur("sev")
    p.emit("tensor", lambda e: e.transpose(
        pmisc[0:1, 0:HPC], qmax2[0:HPC, 0:1], ident_f[0:HPC, 0:HPC]),
        waits=(("sev", sevn),), inc="smm")
    p.emit("vector", lambda e: e.tensor_copy(negc_row[0:1, :], pmisc[0:1, 0:HPC]),
           waits=(("smm", p.cur("smm")),), inc="sev")
    sevn2 = p.cur("sev")
    for h in range(HPC):
        p.emit("tensor", (lambda h_: lambda e: e.matmul(
            pmisc[:, h_:h_ + 1], onesrow_f[0:1, :], negc_row[0:1, h_:h_ + 1],
            start=True, stop=True))(h),
            waits=(("sev", sevn2),) if h == 0 else (),
            inc="smm" if h == HPC - 1 else None)
    p.emit("vector", lambda e: e.tensor_copy(negcb[:, :], pmisc[:, 0:HPC]),
           waits=(("smm", p.cur("smm")),), inc="sev")
    p.barrier(pre={"vector": [("sev", p.cur("sev"))]})
    st2.close()

    st3 = ExitStack()
    kt_sb = st3.enter_context(nc.sbuf_tensor("kt_sb", [128, 2 * S], BF16))
    vt_sb = st3.enter_context(nc.sbuf_tensor("vt_sb", [128, 64 * HD], BF16))
    qch = [st3.enter_context(nc.sbuf_tensor(f"qch{i}", [128, 2 * 512], BF16))
           for i in range(2)]
    psb = [st3.enter_context(nc.sbuf_tensor(f"psb{i}", [128, 512], BF16))
           for i in range(2)]
    srow = st3.enter_context(nc.sbuf_tensor("srow", [1, 512], F32))
    rbf = st3.enter_context(nc.sbuf_tensor("rbf", [128, 512], F32))
    ostage = st3.enter_context(nc.sbuf_tensor("ostage", [128, 1024], BF16))

    gi = 0
    for h in (range(HPC) if phases >= 3 else ()):
        for dh in range(2):
            w = []
            if h > 0 and dh == 0:
                w = [("tsc", p.cur("tsc")), ("tpv", p.cur("tpv"))]
            p.dma("sync", (lambda h_, dh_: lambda e: e.dma_start(
                kt_sb[:, dh_ * S:(dh_ + 1) * S],
                bass.AP(kt_h, (h_ * HD + dh_ * 128) * 512,
                        [[512, 128], [DPC * 512, 16], [1, 512]])))(h, dh),
                waits=tuple(w), inc="ktld")
        for sp in range(8):
            p.dma("sync", (lambda h_, sp_: lambda e: e.dma_start(
                vt_sb[:, sp_ * 8 * HD:(sp_ + 1) * 8 * HD],
                bass.AP(vt_h, h_ * S * HD + sp_ * 8 * 128 * HD,
                        [[HD, 128], [128 * HD, 8], [1, HD]])))(h, sp),
                inc="ktld")
        ktl = p.cur("ktld")

        for qc in range(16):
            qb = qc % 2
            for dh in range(2):
                w = []
                if dh == 0:
                    w.append(("tsc", p.cur("tsc")))
                    if qc == 0:
                        w.append(("ktld", ktl))
                p.dma("sync", (lambda h_, dh_, qc_, qb_: lambda e: e.dma_start(
                    qch[qb_][:, dh_ * 512:(dh_ + 1) * 512],
                    bass.AP(qt_h, qc_ * DPC * 512 + (h_ * HD + dh_ * 128) * 512,
                            [[512, 128], [1, 512]])))(h, dh, qc, qb),
                    waits=tuple(w), inc="qcld")
            qcl = p.cur("qcld")
            nj = 4 * qc + 4
            for j in range(nj):
                sbank = pp[gi % 2]
                pb = gi % 2
                diag = j >= 4 * qc
                w = []
                if j == 0:
                    w.append(("qcld", qcl))
                if gi >= 2:
                    w.append(("sexp", gi - 1))
                for dh in range(2):
                    p.emit("tensor", (lambda j_, dh_, qb_, st_, sp_, b_:
                           lambda e: e.matmul(
                               b_[:, :],
                               kt_sb[:, dh_ * S + j_ * 128: dh_ * S + j_ * 128 + 128],
                               qch[qb_][:, dh_ * 512:(dh_ + 1) * 512],
                               start=st_, stop=sp_))(j, dh, qb, dh == 0, dh == 1, sbank),
                           waits=tuple(w) if dh == 0 else (),
                           inc="tsc" if dh == 1 else None)
                tscv = p.cur("tsc")
                if diag:
                    d = j - 4 * qc
                    p.emit("vector", (lambda d_, b_: lambda e: e.tensor_add(
                        b_[:, :], b_[:, :], masks_sb[:, d_ * 512:(d_ + 1) * 512]))(
                            d, sbank), waits=(("tsc", tscv),), inc="vmsk")
                w = [("vmsk", p.cur("vmsk"))] if diag else [("tsc", tscv)]
                if gi >= 2:
                    w.append(("tpv", gi - 1))
                p.emit("scalar", (lambda h_, b_, pb_: lambda e: e.activation(
                    psb[pb_][:, :], b_[:, :], AF.Exp,
                    bias=negcb[:, h_:h_ + 1], scale=1.0 / 16.0))(h, sbank, pb),
                    waits=tuple(w), inc="sexp")
                sev_ = p.cur("sexp")
                for dh in range(2):
                    w = []
                    if dh == 0:
                        w.append(("sexp", sev_))
                        if j == 0:
                            w.append(("voev", p.cur("voev")))
                    p.emit("tensor", (lambda j_, dh_, pb_, st_, sp_: lambda e: e.matmul(
                        pp[2 + dh_][:, :],
                        vt_sb[:, j_ * HD + dh_ * 128: j_ * HD + dh_ * 128 + 128],
                        psb[pb_][:, :], start=st_, stop=sp_))(
                            j, dh, pb, j == 0, j == nj - 1),
                        waits=tuple(w))
                w = []
                if j == 0:
                    w.append(("vrs", p.cur("vrs")))
                p.emit("tensor", (lambda pb_, st_, sp_: lambda e: e.matmul(
                    ps_a[0:1, :], ones_b[:, :], psb[pb_][:, :],
                    start=st_, stop=sp_))(pb, j == 0, j == nj - 1),
                    waits=tuple(w), inc="tpv")
                gi += 1
            tpvv = p.cur("tpv")
            p.emit("vector", lambda e: e.tensor_copy(srow[0:1, :], ps_a[0:1, :]),
                   waits=(("tpv", tpvv),))
            p.emit("vector", lambda e: e.reciprocal(srow[0:1, :], srow[0:1, :]),
                   inc="vrs")
            p.emit("tensor", lambda e: e.matmul(
                pmisc[:, :], onesrow_f[0:1, :], srow[0:1, :], start=True, stop=True),
                waits=(("vrs", p.cur("vrs")), ("voev", p.cur("voev"))), inc="trb")
            p.emit("vector", lambda e: e.tensor_copy(rbf[:, :], pmisc[:, :]),
                   waits=(("trb", p.cur("trb")), ("ost", p.cur("ost"))))
            for dh in range(2):
                p.emit("vector", (lambda dh_: lambda e: e.tensor_mul(
                    ostage[:, dh_ * 512:(dh_ + 1) * 512], pp[2 + dh_][:, :],
                    rbf[:, :]))(dh), inc="voev" if dh == 1 else None)
            vov = p.cur("voev")
            for dh in range(2):
                w = [("voev", vov)] if dh == 0 else []
                p.dma("sync", (lambda h_, dh_, qc_: lambda e: e.dma_start(
                    bass.AP(attn_h, (h_ * HD + dh_ * 128) * S + qc_ * 512,
                            [[S, 128], [1, 512]]),
                    ostage[:, dh_ * 512:(dh_ + 1) * 512]))(h, dh, qc),
                    waits=tuple(w), inc="ost")

    p.emit("gpsimd", None, waits=(("ost", p.cur("ost")),))
    p.emit("sync", None, waits=(("ost", p.cur("ost")),))

    _block_emit(nc, p)
    st3.close()
    stack.close()
    return nc



def build_launch_b():
    nc = bass.Bass("TRN2", target_bir_lowering=False)

    asl_h = nc.dram_tensor("aslice", [D, TPC], BF16, kind="ExternalInput")
    xres_h = nc.dram_tensor("xres", [TPC, D], F32, kind="ExternalInput")
    woT_h = nc.dram_tensor("woT", [D, D], F32, kind="ExternalInput")
    wallT_h = nc.dram_tensor("wallT", [D, 24], F32, kind="ExternalInput")
    bases_h = nc.dram_tensor("bases", [1, 24], F32, kind="ExternalInput")
    gate_h = nc.dram_tensor("gate", [1, 1], F32, kind="ExternalInput")
    wbr_h = nc.dram_tensor("wbr", [1024, 1024], F32, kind="ExternalInput")
    ident_h = nc.dram_tensor("ident", [128, 128], F32, kind="ExternalInput")
    onesrow_h = nc.dram_tensor("onesrow", [1, 128], F32, kind="ExternalInput")

    out_h = nc.dram_tensor("out", [TPC, D], F32, kind="ExternalOutput")
    x2_h = nc.dram_tensor("x2h", [TPC, D], F32, kind="Internal")

    sem_names = ["bar", "vch", "sch", "cns", "ald", "wld0", "wld1", "xrd", "womm", "x2ev",
                 "x2wr0", "x2wr1", "sq", "rms", "lgT", "lgTe", "lgmm", "lgev",
                 "snk", "x2rd0", "x2rd1", "mix", "ymm", "yev", "ost0", "ost1",
                 "wbld", "wcns"]
    stack = ExitStack()
    sems = {n: stack.enter_context(nc.semaphore(n)) for n in sem_names}
    p = Plan(sems)

    ident_f = nc.alloc_sbuf_tensor("ident_f", [128, 128], F32)
    ident_b = nc.alloc_sbuf_tensor("ident_b", [128, 128], BF16)
    onesrow_f = nc.alloc_sbuf_tensor("onesrow_f", [1, 128], F32)
    wall_b = nc.alloc_sbuf_tensor("wall_b", [128, 32 * 24], BF16)
    bases_row = nc.alloc_sbuf_tensor("bases_row", [1, 24], F32)
    basesb = nc.alloc_sbuf_tensor("basesb", [128, 24], F32)
    gsc = nc.alloc_sbuf_tensor("gsc", [1, 4], F32)
    gb = nc.alloc_sbuf_tensor("gb", [128, 1], F32)
    omgb = nc.alloc_sbuf_tensor("omgb", [128, 1], F32)
    r2c = nc.alloc_sbuf_tensor("r2c", [128, 16], F32)
    m16 = nc.alloc_sbuf_tensor("m16", [128, 16], F32)
    rs16 = nc.alloc_sbuf_tensor("rs16", [128, 16], F32)
    sc1 = nc.alloc_sbuf_tensor("sc1", [128, 1], F32)
    epsb = nc.alloc_sbuf_tensor("epsb", [128, 1], F32)
    epsl = nc.alloc_sbuf_tensor("epsl", [128, 1], F32)
    lga = nc.alloc_sbuf_tensor("lga", [128, 16 * 24], F32)
    Z = nc.alloc_sbuf_tensor("Z", [128, 256], F32)
    tmp = nc.alloc_sbuf_tensor("tmp", [128, 256], F32)
    tmpe = nc.alloc_sbuf_tensor("tmpe", [128, 256], F32)
    ssum = nc.alloc_sbuf_tensor("ssum", [128, 64], F32)
    lnu = nc.alloc_sbuf_tensor("lnu", [128, 64], F32)
    lnv = nc.alloc_sbuf_tensor("lnv", [128, 64], F32)
    Hm = nc.alloc_sbuf_tensor("Hm", [128, 256], F32)
    u4 = nc.alloc_sbuf_tensor("u4", [128, 256], F32)
    hp = nc.alloc_sbuf_tensor("hp", [128, 64], F32)
    hq = nc.alloc_sbuf_tensor("hq", [128, 64], F32)
    sm = nc.alloc_sbuf_tensor("sm", [128, 16], F32)
    sm4 = nc.alloc_sbuf_tensor("sm4", [128, 64], F32)
    stm = ExitStack()
    maskd1_sb = stm.enter_context(nc.sbuf_tensor("maskd1_sb", [128, 4 * 512], F32))
    maskd2_sb = stm.enter_context(nc.sbuf_tensor("maskd2_sb", [128, 4 * 512], F32))
    negm_row = stm.enter_context(nc.sbuf_tensor("negm_row", [1, S], BF16))
    negmq_row = stm.enter_context(nc.sbuf_tensor("negmq_row", [1, QW], BF16))
    s_sum = stm.enter_context(nc.sbuf_tensor("s_sum", [1, 256], F32))
    s_sq = stm.enter_context(nc.sbuf_tensor("s_sq", [1, 256], F32))
    s_m = stm.enter_context(nc.sbuf_tensor("s_m", [1, 256], F32))

    pp = [nc.alloc_psum_tensor(f"pp{i}", [128, 512], F32) for i in range(4)]
    pbf = nc.alloc_psum_tensor("pbf", [128, 1024], BF16)
    pmisc = nc.alloc_psum_tensor("pmisc", [128, 512], F32)

    p.dma("sync", lambda e: e.dma_start(ident_f[:, :], ident_h.ap()[:, :]), inc="cns")
    p.dma("sync", lambda e: e.dma_start(onesrow_f[:, :], onesrow_h.ap()[:, :]),
          inc="cns")
    p.dma("sync", lambda e: e.dma_start(gsc[0:1, 0:1], gate_h.ap()[:, :]), inc="cns")
    p.dma("sync", lambda e: e.dma_start(bases_row[0:1, :], bases_h.ap()[:, :]),
          inc="cns")
    p.dma("gpsimd", lambda e: e.dma_start(
        wall_b[:, :], wallT_h.ap().rearrange("(c p) d -> p c d", p=128)), inc="wcns")
    cnsv = p.cur("cns")
    p.emit("vector", lambda e: e.tensor_copy(ident_b[:, :], ident_f[:, :]),
           waits=(("cns", cnsv),))
    p.emit("vector", lambda e: e.memset(epsb[:, :], EPS_RMS))
    p.emit("vector", lambda e: e.memset(epsl[:, :], EPS_LN))
    p.emit("tensor", lambda e: e.matmul(
        pmisc[:, 0:24], onesrow_f[0:1, :], bases_row[0:1, :], start=True, stop=True),
        waits=(("cns", cnsv),), inc="lgmm")
    p.emit("vector", lambda e: e.tensor_copy(basesb[:, :], pmisc[:, 0:24]),
           waits=(("lgmm", p.cur("lgmm")),), inc="lgev")
    p.emit("scalar", lambda e: e.activation(
        gsc[0:1, 1:2], gsc[0:1, 0:1], AF.Tanh, bias=0.0, scale=0.5),
        waits=(("cns", cnsv),), inc="sq")
    p.emit("vector", lambda e: e.tensor_scalar(
        gsc[0:1, 2:3], gsc[0:1, 1:2], 0.5, 0.5, OP.mult, OP.add),
        waits=(("sq", p.cur("sq")),))
    p.emit("vector", lambda e: e.tensor_scalar(
        gsc[0:1, 3:4], gsc[0:1, 1:2], -0.5, 0.5, OP.mult, OP.add), inc="rms")
    p.emit("tensor", lambda e: e.matmul(
        pmisc[:, 24:25], onesrow_f[0:1, :], gsc[0:1, 2:3], start=True, stop=True),
        waits=(("rms", p.cur("rms")), ("lgev", p.cur("lgev"))))
    p.emit("tensor", lambda e: e.matmul(
        pmisc[:, 25:26], onesrow_f[0:1, :], gsc[0:1, 3:4], start=True, stop=True),
        inc="lgmm")
    p.emit("vector", lambda e: e.tensor_copy(gb[:, :], pmisc[:, 24:25]),
           waits=(("lgmm", p.cur("lgmm")),))
    p.emit("vector", lambda e: e.tensor_copy(omgb[:, :], pmisc[:, 25:26]),
           inc="lgev")
    p.barrier(pre={"vector": [("lgev", p.cur("lgev"))]})

    woT_v = woT_h.ap().rearrange("(c p) d -> p c d", p=128)
    asl_v = asl_h.ap().rearrange("(c p) t -> p c t", p=128)

    stB = ExitStack()
    abuf = stB.enter_context(nc.sbuf_tensor("abuf", [128, 32 * 512], BF16))
    wbuf = [stB.enter_context(nc.sbuf_tensor(f"wbuf{i}", [128, 32 * 512], BF16))
            for i in range(2)]
    x2g = stB.enter_context(nc.sbuf_tensor("x2g", [128, 4 * D], F32))
    x2bf = stB.enter_context(nc.sbuf_tensor("x2bf", [128, D], BF16))
    x2T = stB.enter_context(nc.sbuf_tensor("x2T", [128, 32 * 128], BF16))
    sqs = stB.enter_context(nc.sbuf_tensor("sqs", [128, D], BF16))

    for g in range(4):
        t0 = g * 512
        for sp in range(4):
            w = [("womm", p.cur("womm"))] if sp == 0 else []
            p.dma("sync", (lambda sp_, t0_: lambda e: e.dma_start(
                abuf[:, sp_ * 8 * 512:(sp_ + 1) * 8 * 512],
                asl_v[:, sp_ * 8:(sp_ + 1) * 8, t0_:t0_ + 512]))(sp, t0),
                waits=tuple(w), inc="ald")
        aldv = p.cur("ald")
        for ti in range(4):
            w = ([(f"x2wr{(g - 1) % 2}", 64 * ((g - 1) // 2 + 1)),
                  ("sq", p.cur("sq")), ("x2ev", p.cur("x2ev")),
                  ("rms", p.cur("rms"))]
                 if (ti == 0 and g > 0) else [])
            p.dma("sync", (lambda ti_, t0_: lambda e: e.dma_start(
                x2g[:, ti_ * D:(ti_ + 1) * D],
                xres_h.ap()[t0_ + ti_ * 128: t0_ + ti_ * 128 + 128, :]))(ti, t0),
                waits=tuple(w), inc="xrd")
        xrdv = p.cur("xrd")
        for dc in range(8):
            wb = wbuf[dc % 2]
            for sp in range(2):
                w = []
                if sp == 0:
                    w.append(("womm", max(0, p.cur("womm") - 4)))
                p.dma("gpsimd", (lambda sp_, dc_, wb_: lambda e: e.dma_start(
                    wb_[:, sp_ * 16 * 512:(sp_ + 1) * 16 * 512],
                    woT_v[:, sp_ * 16:(sp_ + 1) * 16, dc_ * 512:(dc_ + 1) * 512]))(
                        sp, dc, wb), waits=tuple(w), inc=f"wld{dc % 2}")
            wldn = f"wld{dc % 2}"
            wldv = p.cur(wldn)
            for ti in range(4):
                bank = pp[ti % 2]
                for c in range(32):
                    w = []
                    if c == 0:
                        w = [(wldn, wldv), ("ald", aldv),
                             ("x2ev", max(0, p.cur("x2ev") - 1))]
                    p.emit("tensor", (lambda c_, ti_, wb_, st_, sp_, b_: lambda e: e.matmul(
                        b_[:, :],
                        abuf[:, c_ * 512 + ti_ * 128: c_ * 512 + ti_ * 128 + 128],
                        wb_[:, c_ * 512:(c_ + 1) * 512],
                        start=st_, stop=sp_))(c, ti, wb, c == 0, c == 31, bank),
                        waits=tuple(w), inc="womm" if c == 31 else None)
                wommv = p.cur("womm")
                w = [("womm", wommv)]
                if dc == 0:
                    w.append(("xrd", xrdv))
                p.emit("vector", (lambda ti_, dc_, b_: lambda e: e.tensor_add(
                    x2g[:, ti_ * D + dc_ * 512: ti_ * D + (dc_ + 1) * 512],
                    b_[:, :],
                    x2g[:, ti_ * D + dc_ * 512: ti_ * D + (dc_ + 1) * 512]))(
                        ti, dc, bank), waits=tuple(w), inc="x2ev")
        x2evv = p.cur("x2ev")
        for ti in range(4):
            tg = g * 4 + ti
            p.dma("gpsimd", (lambda ti_, t0_: lambda e: e.dma_start(
                x2_h.ap()[t0_ + ti_ * 128: t0_ + ti_ * 128 + 128, :],
                x2g[:, ti_ * D:(ti_ + 1) * D]))(ti, t0),
                waits=(("x2ev", x2evv),) if ti == 0 else (), inc=f"x2wr{g % 2}")
            p.emit("vector", (lambda ti_, tg_: lambda e: e.reduce_sum(
                m16[:, tg_:tg_ + 1], x2g[:, ti_ * D:(ti_ + 1) * D], axis=AX.X))(ti, tg),
                waits=(("x2ev", x2evv),))
            p.emit("vector", (lambda tg_: lambda e: e.tensor_scalar_mul(
                m16[:, tg_:tg_ + 1], m16[:, tg_:tg_ + 1], 1.0 / D))(tg))
            p.emit("scalar", (lambda ti_, tg_: lambda e: e.activation(
                sqs[:, :], x2g[:, ti_ * D:(ti_ + 1) * D], AF.Square,
                accum_out=rs16[:, tg_:tg_ + 1]))(ti, tg),
                waits=(("x2ev", x2evv),), inc="sq")
            p.emit("vector", (lambda tg_: lambda e: e.tensor_scalar_mul(
                rs16[:, tg_:tg_ + 1], rs16[:, tg_:tg_ + 1], 1.0 / D))(tg),
                waits=(("sq", p.cur("sq")),))
            p.emit("vector", (lambda tg_: lambda e: e.tensor_mul(
                sc1[:, 0:1], m16[:, tg_:tg_ + 1], m16[:, tg_:tg_ + 1]))(tg))
            p.emit("vector", (lambda tg_: lambda e: e.tensor_sub(
                rs16[:, tg_:tg_ + 1], rs16[:, tg_:tg_ + 1], sc1[:, 0:1]))(tg),
                inc="rms")
            p.emit("scalar", (lambda tg_: lambda e: e.activation(
                rs16[:, tg_:tg_ + 1], rs16[:, tg_:tg_ + 1], AF.Sqrt,
                bias=epsl[:, 0:1], scale=1.0))(tg),
                waits=(("rms", p.cur("rms")),), inc="sq")
            p.emit("vector", (lambda tg_: lambda e: e.reciprocal(
                rs16[:, tg_:tg_ + 1], rs16[:, tg_:tg_ + 1]))(tg),
                waits=(("sq", p.cur("sq")),))
            p.emit("vector", (lambda ti_, tg_: lambda e: e.tensor_scalar(
                x2bf[:, :], x2g[:, ti_ * D:(ti_ + 1) * D],
                m16[:, tg_:tg_ + 1], rs16[:, tg_:tg_ + 1],
                OP.subtract, OP.mult))(ti, tg),
                waits=(("lgT", p.cur("lgT")),), inc="x2ev")
            cstv = p.cur("x2ev")
            p.emit("scalar", (lambda tg_: lambda e: e.activation(
                sqs[:, :], x2bf[:, :], AF.Square,
                accum_out=r2c[:, tg_:tg_ + 1]))(tg),
                waits=(("x2ev", cstv),), inc="sq")
            p.emit("scalar", (lambda tg_: lambda e: e.activation(
                r2c[:, tg_:tg_ + 1], r2c[:, tg_:tg_ + 1], AF.Sqrt,
                bias=epsb[:, 0:1], scale=1.0 / D))(tg),
                waits=(("sq", p.cur("sq")),), inc="sq")
            p.emit("vector", (lambda tg_: lambda e: e.reciprocal(
                r2c[:, tg_:tg_ + 1], r2c[:, tg_:tg_ + 1]))(tg),
                waits=(("sq", p.cur("sq")),), inc="rms")
            for rnd in range(4):
                for i in range(8):
                    w = []
                    if i == 0:
                        w = [("x2ev", cstv), ("lgTe", p.cur("lgTe"))]
                    p.emit("tensor", (lambda rnd_, i_: lambda e: e.transpose(
                        pbf[:, i_ * 128:(i_ + 1) * 128],
                        x2bf[:, rnd_ * 1024 + i_ * 128: rnd_ * 1024 + (i_ + 1) * 128],
                        ident_b[:, :]))(rnd, i),
                        waits=tuple(w), inc="lgT" if i == 7 else None)
                tvv = p.cur("lgT")
                for i in range(8):
                    p.emit("vector", (lambda rnd_, i_: lambda e: e.tensor_copy(
                        x2T[:, (rnd_ * 8 + i_) * 128:(rnd_ * 8 + i_ + 1) * 128],
                        pbf[:, i_ * 128:(i_ + 1) * 128]))(rnd, i),
                        waits=(("lgT", tvv),) if i == 0 else (),
                        inc="lgTe" if i == 7 else None)
            tev = p.cur("lgTe")
            for c in range(32):
                w = []
                if c == 0:
                    w = [("lgTe", tev), ("lgev", p.cur("lgev")), ("wcns", 16)]
                p.emit("tensor", (lambda c_, st_, sp_: lambda e: e.matmul(
                    pmisc[:, 0:24], x2T[:, c_ * 128:(c_ + 1) * 128],
                    wall_b[:, c_ * 24:(c_ + 1) * 24], start=st_, stop=sp_))(
                        c, c == 0, c == 31),
                    waits=tuple(w), inc="lgmm" if c == 31 else None)
            p.emit("scalar", (lambda tg_: lambda e: e.activation(
                lga[:, tg_ * 24:(tg_ + 1) * 24], pmisc[:, 0:24], AF.Copy,
                bias=0.0, scale=r2c[:, tg_:tg_ + 1]))(tg),
                waits=(("lgmm", p.cur("lgmm")), ("rms", p.cur("rms"))), inc="sq")
            p.emit("vector", (lambda tg_: lambda e: e.tensor_add(
                lga[:, tg_ * 24:(tg_ + 1) * 24], lga[:, tg_ * 24:(tg_ + 1) * 24],
                basesb[:, :]))(tg),
                waits=(("sq", p.cur("sq")),), inc="lgev")
    p.barrier(pre={"gpsimd": [("x2wr0", p.cur("x2wr0")),
                               ("x2wr1", p.cur("x2wr1"))],
                   "vector": [("lgev", p.cur("lgev"))]})
    stB.close()

    lview = lga[:, :].rearrange("p (t c) -> p t c", c=24)
    Zv = Z[:, :].rearrange("p (t i j) -> p t i j", i=4, j=4)
    tv_ = tmp[:, :].rearrange("p (t i j) -> p t i j", i=4, j=4)
    lnv_v = lnv[:, :].rearrange("p (t j) -> p t j", j=4)
    lnu_v = lnu[:, :].rearrange("p (t i) -> p t i", i=4)

    p.emit("vector", lambda e: e.tensor_scalar_mul(
        Zv, lview[:, :, 0:16].rearrange("p t (i j) -> p t i j", j=4), 1.0 / TAU))
    p.emit("vector", lambda e: e.memset(lnv[:, :], 0.0), inc="snk")

    for it in range(SINK_IT):
        for i in range(4):
            p.emit("vector", (lambda i_: lambda e: e.tensor_sub(
                tv_[:, :, i_, :], Zv[:, :, i_, :], lnv_v))(i),
                waits=(("snk", p.cur("snk")),) if i == 0 else (),
                inc="snk" if i == 3 else None)
        p.emit("scalar", lambda e: e.activation(tmpe[:, :], tmp[:, :], AF.Exp),
               waits=(("snk", p.cur("snk")),), inc="sq")
        p.emit("vector", lambda e: e.tensor_reduce(
            ssum[:, :], tmpe[:, :].rearrange("p (ti j) -> p ti j", j=4),
            axis=AX.X, op=OP.add),
            waits=(("sq", p.cur("sq")),), inc="snk")
        p.emit("scalar", lambda e: e.activation(lnu[:, :], ssum[:, :], AF.Ln),
               waits=(("snk", p.cur("snk")),), inc="sq")
        sqv = p.cur("sq")
        for j in range(4):
            p.emit("vector", (lambda j_: lambda e: e.tensor_sub(
                tv_[:, :, j_, :], Zv[:, :, :, j_], lnu_v))(j),
                waits=(("sq", sqv),) if j == 0 else (),
                inc="snk" if j == 3 else None)
        p.emit("scalar", lambda e: e.activation(tmpe[:, :], tmp[:, :], AF.Exp),
               waits=(("snk", p.cur("snk")),), inc="sq")
        p.emit("vector", lambda e: e.tensor_reduce(
            ssum[:, :], tmpe[:, :].rearrange("p (tj i) -> p tj i", i=4),
            axis=AX.X, op=OP.add),
            waits=(("sq", p.cur("sq")),), inc="snk")
        p.emit("scalar", lambda e: e.activation(lnv[:, :], ssum[:, :], AF.Ln),
               waits=(("snk", p.cur("snk")),), inc="sq")
        p.emit("vector", lambda e: e.tensor_copy(sm[:, 0:1], lnv[:, 0:1]),
               waits=(("sq", p.cur("sq")),), inc="snk")

    u4v = u4[:, :].rearrange("p (t i j) -> p t i j", i=4, j=4)
    for j in range(4):
        p.emit("vector", (lambda j_: lambda e: e.tensor_copy(
            u4v[:, :, :, j_], lnu_v))(j))
    p.emit("vector", lambda e: e.tensor_sub(Hm[:, :], Z[:, :], u4[:, :]))
    Hv = Hm[:, :].rearrange("p (t i j) -> p t i j", i=4, j=4)
    for i in range(4):
        p.emit("vector", (lambda i_: lambda e: e.tensor_sub(
            tv_[:, :, i_, :], Hv[:, :, i_, :], lnv_v))(i),
            inc="snk" if i == 3 else None)
    p.emit("scalar", lambda e: e.activation(Hm[:, :], tmp[:, :], AF.Exp),
           waits=(("snk", p.cur("snk")),), inc="sq")
    p.emit("vector", lambda e: e.tensor_scalar_mul(Hm[:, :], Hm[:, :], gb[:, 0:1]),
           waits=(("sq", p.cur("sq")),))
    diagv = Hm[:, :].rearrange("p (t d) -> p t d", d=16)[:, :, 0:16:5]
    p.emit("vector", lambda e: e.tensor_scalar_add(diagv, diagv, omgb[:, 0:1]),
           inc="snk")

    for (src0, dst) in ((16, hp), (20, hq)):
        pv = lview[:, :, src0:src0 + 4]
        dv = dst[:, :].rearrange("p (t n) -> p t n", n=4)
        s4 = sm4[:, :].rearrange("p (t n) -> p t n", n=4)
        p.emit("vector", (lambda pv_: lambda e: e.reduce_max(
            sm[:, :], pv_, axis=AX.X))(pv))
        for n in range(4):
            p.emit("vector", (lambda n_, s4_: lambda e: e.tensor_copy(
                s4_[:, :, n_], sm[:, :]))(n, s4))
        p.emit("vector", (lambda pv_, dv_, s4_: lambda e: e.tensor_sub(
            dv_, pv_, s4_))(pv, dv, s4), inc="snk")
        p.emit("scalar", (lambda d_: lambda e: e.activation(
            d_[:, :], d_[:, :], AF.Exp))(dst),
            waits=(("snk", p.cur("snk")),), inc="sq")
        p.emit("vector", (lambda dv_: lambda e: e.tensor_reduce(
            sm[:, :], dv_, axis=AX.X, op=OP.add))(dv),
            waits=(("sq", p.cur("sq")),))
        p.emit("vector", lambda e: e.reciprocal(sm[:, :], sm[:, :]))
        for n in range(4):
            p.emit("vector", (lambda n_, s4_: lambda e: e.tensor_copy(
                s4_[:, :, n_], sm[:, :]))(n, s4))
        p.emit("vector", (lambda d_: lambda e: e.tensor_mul(
            d_[:, :], d_[:, :], sm4[:, :]))(dst), inc="snk")
    p.barrier(pre={"vector": [("snk", p.cur("snk"))]})

    stC = ExitStack()
    wbr_b = stC.enter_context(nc.sbuf_tensor("wbr_b", [128, 8 * 1024], BF16))
    x2m = [stC.enter_context(nc.sbuf_tensor(f"x2m{i}", [128, D], F32))
           for i in range(2)]
    xpre = stC.enter_context(nc.sbuf_tensor("xpre", [128, 1024], F32))
    ln2m = [stC.enter_context(nc.sbuf_tensor(f"ln2m{i}", [128, D], F32))
            for i in range(2)]
    xpre_b = stC.enter_context(nc.sbuf_tensor("xpre_b", [128, 1024], BF16))
    xpT = stC.enter_context(nc.sbuf_tensor("xpT", [128, 8 * 128], BF16))
    ybuf = stC.enter_context(nc.sbuf_tensor("ybuf", [128, 1024], F32))
    obuf = [stC.enter_context(nc.sbuf_tensor(f"obuf{i}", [128, D], F32))
            for i in range(2)]

    p.dma("gpsimd", lambda e: e.dma_start(
        wbr_b[:, :], wbr_h.ap().rearrange("(c p) d -> p c d", p=128)), inc="wbld")
    wbl = p.cur("wbld")

    for tg in range(16):
        bi = tg % 2
        w = [("mix", max(0, p.cur("mix") - 1))] if tg >= 2 else []
        p.dma("sync", (lambda tg_, bi_: lambda e: e.dma_start(
            x2m[bi_][:, :], x2_h.ap()[tg_ * 128:(tg_ + 1) * 128, :]))(tg, bi),
            waits=tuple(w), inc=f"x2rd{bi}")
        x2rv = p.cur(f"x2rd{bi}")
        w = [(f"x2rd{bi}", x2rv)]
        if tg == 0:
            w.append(("wbld", wbl))
        p.emit("vector", (lambda bi_, tg_: lambda e: e.tensor_scalar(
            ln2m[bi_][:, :], x2m[bi_][:, :], m16[:, tg_:tg_ + 1],
            rs16[:, tg_:tg_ + 1], OP.subtract, OP.mult))(bi, tg), waits=tuple(w))
        p.emit("vector", (lambda bi_, tg_: lambda e: e.tensor_scalar_mul(
            xpre[:, :], ln2m[bi_][:, 0:1024], hp[:, 4 * tg_:4 * tg_ + 1]))(bi, tg))
        for n in range(1, 4):
            p.emit("vector", (lambda bi_, tg_, n_: lambda e: e.scalar_tensor_tensor(
                xpre[:, :], ln2m[bi_][:, n_ * 1024:(n_ + 1) * 1024],
                hp[:, 4 * tg_ + n_:4 * tg_ + n_ + 1], xpre[:, :],
                OP.mult, OP.add))(bi, tg, n))
        p.emit("vector", lambda e: e.tensor_copy(xpre_b[:, :], xpre[:, :]),
               waits=(("lgT", p.cur("lgT")),), inc="mix")
        mxv = p.cur("mix")
        for i in range(8):
            w = []
            if i == 0:
                w = [("mix", mxv), ("lgTe", p.cur("lgTe")), ("yev", p.cur("yev"))]
            p.emit("tensor", (lambda i_: lambda e: e.transpose(
                pbf[:, i_ * 128:(i_ + 1) * 128], xpre_b[:, i_ * 128:(i_ + 1) * 128],
                ident_b[:, :]))(i), waits=tuple(w), inc="lgT" if i == 7 else None)
        tvv = p.cur("lgT")
        for i in range(8):
            p.emit("vector", (lambda i_: lambda e: e.tensor_copy(
                xpT[:, i_ * 128:(i_ + 1) * 128], pbf[:, i_ * 128:(i_ + 1) * 128]))(i),
                waits=(("lgT", tvv),) if i == 0 else (),
                inc="lgTe" if i == 7 else None)
        tev = p.cur("lgTe")
        for n in range(2):
            for c in range(8):
                w = []
                if n == 0 and c == 0:
                    w = [("lgTe", tev), ("yev", p.cur("yev"))]
                p.emit("tensor", (lambda n_, c_, st_, sp_: lambda e: e.matmul(
                    pp[n_][:, :], xpT[:, c_ * 128:(c_ + 1) * 128],
                    wbr_b[:, c_ * 1024 + n_ * 512: c_ * 1024 + (n_ + 1) * 512],
                    start=st_, stop=sp_))(n, c, c == 0, c == 7),
                    waits=tuple(w), inc="ymm" if (n == 1 and c == 7) else None)
        ymv = p.cur("ymm")
        for n in range(2):
            p.emit("vector", (lambda n_: lambda e: e.tensor_copy(
                ybuf[:, n_ * 512:(n_ + 1) * 512], pp[n_][:, :]))(n),
                waits=(("ymm", ymv),) if n == 0 else (),
                inc="yev" if n == 1 else None)
        yev = p.cur("yev")
        for i in range(4):
            w = []
            if i == 0:
                w = [("yev", yev), (f"ost{bi}", p.cur(f"ost{bi}"))]
            p.emit("vector", (lambda bi_, tg_, i_: lambda e: e.scalar_tensor_tensor(
                obuf[bi_][:, i_ * 1024:(i_ + 1) * 1024], ln2m[bi_][:, 0:1024],
                Hm[:, 16 * tg_ + 4 * i_: 16 * tg_ + 4 * i_ + 1],
                x2m[bi_][:, i_ * 1024:(i_ + 1) * 1024],
                OP.mult, OP.add))(bi, tg, i),
                waits=tuple(w))
            for jj in range(1, 4):
                p.emit("vector",
                       (lambda bi_, tg_, i_, j_: lambda e: e.scalar_tensor_tensor(
                           obuf[bi_][:, i_ * 1024:(i_ + 1) * 1024],
                           ln2m[bi_][:, j_ * 1024:(j_ + 1) * 1024],
                           Hm[:, 16 * tg_ + 4 * i_ + j_: 16 * tg_ + 4 * i_ + j_ + 1],
                           obuf[bi_][:, i_ * 1024:(i_ + 1) * 1024],
                           OP.mult, OP.add))(bi, tg, i, jj))
            p.emit("vector", (lambda bi_, tg_, i_: lambda e: e.scalar_tensor_tensor(
                obuf[bi_][:, i_ * 1024:(i_ + 1) * 1024], ybuf[:, :],
                hq[:, 4 * tg_ + i_:4 * tg_ + i_ + 1],
                obuf[bi_][:, i_ * 1024:(i_ + 1) * 1024],
                OP.mult, OP.add))(bi, tg, i),
                inc="mix" if i == 3 else None)
        mxo = p.cur("mix")
        p.dma("sync", (lambda bi_, tg_: lambda e: e.dma_start(
            out_h.ap()[tg_ * 128:(tg_ + 1) * 128, :], obuf[bi_][:, :]))(bi, tg),
            waits=(("mix", mxo),), inc=f"ost{bi}")

    p.emit("gpsimd", None, waits=(("ost0", p.cur("ost0")), ("ost1", p.cur("ost1"))))
    p.emit("sync", None, waits=(("ost0", p.cur("ost0")), ("ost1", p.cur("ost1"))))

    _block_emit(nc, p)
    stC.close()
    stack.close()
    return nc



def _rope_tables():
    i = np.arange(RD // 2, dtype=np.float64)
    inv_freq = 1.0 / (10000.0 ** (2.0 * i / RD))
    t = np.arange(S, dtype=np.float64)
    ang = np.outer(t, inv_freq)
    sinr = np.tile(np.sin(ang).astype(np.float32), (1, HPC))
    cosr = np.tile(np.cos(ang).astype(np.float32), (1, HPC))
    return np.ascontiguousarray(sinr), np.ascontiguousarray(cosr)


def _masks():
    m = np.zeros((4, 128, 512), np.float32)
    kk = np.arange(128)[:, None]
    qq = np.arange(512)[None, :]
    for di, d in enumerate((0, 128, 256, 384)):
        m[di] = np.where(kk + d > qq, np.float32(-1e9), np.float32(0.0))
    return np.ascontiguousarray(m.reshape(512, 512))


FUSED = os.environ.get("K_UNFUSED", "") != "1"


def kernel(**inputs):
    if FUSED:
        return kernel_fused(inputs)
    x = np.asarray(inputs["L0_attn_hidden_states"])
    wq = np.asarray(inputs["L0_attn_wq"])
    wk = np.asarray(inputs["L0_attn_wk"])
    wv = np.asarray(inputs["L0_attn_wv"])
    wo = np.asarray(inputs["L0_attn_wo"])
    W_res = np.asarray(inputs["L0_ffn_W_res"])
    W_pre = np.asarray(inputs["L0_ffn_W_pre"])
    W_post = np.asarray(inputs["L0_ffn_W_post"])
    Hrb = np.asarray(inputs["L0_ffn_H_res_base"])
    Hpb = np.asarray(inputs["L0_ffn_H_pre_base"])
    Hqb = np.asarray(inputs["L0_ffn_H_post_base"])
    gate = np.asarray(inputs["L0_ffn_gate"])
    Wbr = np.asarray(inputs["L0_ffn_W_branch"])

    sinr, cosr = _rope_tables()
    masks = _masks()
    ident = np.eye(128, dtype=np.float32)
    onesrow = np.ones((1, 128), np.float32)

    xT = [np.ascontiguousarray(x[b].T) for b in range(B)]
    wT = {"q": wq, "k": wk, "v": wv}
    wTs = {pj: [np.ascontiguousarray(wT[pj][rk * DPC:(rk + 1) * DPC, :].T)
                for rk in range(4)] for pj in wT}

    nc_a = build_launch_a()
    in_maps_a = []
    for c in range(NCORE):
        b, rk = c // 4, c % 4
        in_maps_a.append(dict(
            xT=xT[b], wqT=wTs["q"][rk], wkT=wTs["k"][rk], wvT=wTs["v"][rk],
            sinr=sinr, cosr=cosr, masks=masks, ident=ident, onesrow=onesrow))
    res_a = run_bass_kernel_spmd(nc_a, in_maps_a, core_ids=list(range(NCORE)))

    attn = [np.concatenate([res_a.results[b * 4 + rk]["attn"] for rk in range(4)],
                           axis=0) for b in range(B)]

    woT = np.ascontiguousarray(wo.T)
    wallT = np.ascontiguousarray(np.concatenate([W_res, W_pre, W_post], axis=0).T)
    bases = np.concatenate(
        [np.asarray(Hrb).reshape(-1), np.asarray(Hpb).reshape(-1),
         np.asarray(Hqb).reshape(-1)]).reshape(1, 24).astype(np.float32)
    gate2 = np.asarray(gate).reshape(1, 1).astype(np.float32)

    nc_b = build_launch_b()
    in_maps_b = []
    for c in range(NCORE):
        b, rk = c // 4, c % 4
        in_maps_b.append(dict(
            aslice=np.ascontiguousarray(attn[b][:, rk * TPC:(rk + 1) * TPC]),
            xres=np.ascontiguousarray(x[b, rk * TPC:(rk + 1) * TPC, :]),
            woT=woT, wallT=wallT, bases=bases, gate=gate2, wbr=Wbr,
            ident=ident, onesrow=onesrow))
    res_b = run_bass_kernel_spmd(nc_b, in_maps_b, core_ids=list(range(NCORE)))

    out = np.empty((B, S, D), np.float32)
    for c in range(NCORE):
        b, rk = c // 4, c % 4
        out[b, rk * TPC:(rk + 1) * TPC, :] = res_b.results[c]["out"]
    return out



def build_fused():
    nc = bass.Bass("TRN2", target_bir_lowering=False)
    NHC = 8
    DQC = NHC * HD
    QW = 4096

    xT_h = nc.dram_tensor("xT", [D, S], F32, kind="ExternalInput")
    xq_h = nc.dram_tensor("xq", [D, QW], F32, kind="ExternalInput")
    w_h = {pj: nc.dram_tensor(f"w{pj}T", [D, DQC], F32, kind="ExternalInput")
           for pj in ("q", "k", "v")}
    sinq_h = nc.dram_tensor("sinq", [QW, 256], F32, kind="ExternalInput")
    cosq_h = nc.dram_tensor("cosq", [QW, 256], F32, kind="ExternalInput")
    sink_h = nc.dram_tensor("sink", [S, 256], F32, kind="ExternalInput")
    cosk_h = nc.dram_tensor("cosk", [S, 256], F32, kind="ExternalInput")
    maskd1_h = nc.dram_tensor("maskd1", [512, 512], F32, kind="ExternalInput")
    maskd2_h = nc.dram_tensor("maskd2", [512, 512], F32, kind="ExternalInput")
    yflag_h = nc.dram_tensor("yflag", [128, 1], F32, kind="ExternalInput")
    roff_h = nc.dram_tensor("roff", [1, 4], mybir.dt.int32, kind="ExternalInput")
    offtab_h = nc.dram_tensor("offtab", [1, 256], mybir.dt.int32, kind="ExternalInput")
    nonce_h = nc.dram_tensor("noncev", [1, 64], F32, kind="ExternalInput")
    ident_h = nc.dram_tensor("ident", [128, 128], F32, kind="ExternalInput")
    onesrow_h = nc.dram_tensor("onesrow", [1, 128], F32, kind="ExternalInput")
    xres_h = nc.dram_tensor("xres", [TPC, D], F32, kind="ExternalInput")
    woT_h = nc.dram_tensor("woT", [D, D], F32, kind="ExternalInput")
    wallT_h = nc.dram_tensor("wallT", [D, 24], F32, kind="ExternalInput")
    bases_h = nc.dram_tensor("bases", [1, 24], F32, kind="ExternalInput")
    gate_h = nc.dram_tensor("gate", [1, 1], F32, kind="ExternalInput")
    wbr_h = nc.dram_tensor("wbr", [1024, 1024], F32, kind="ExternalInput")

    out_h = nc.dram_tensor("out", [TPC, D], F32, kind="ExternalOutput")

    xbf_h = nc.dram_tensor("xbf", [D, S], BF16, kind="Internal")
    xqbf_h = nc.dram_tensor("xqbf", [D, QW], BF16, kind="Internal")
    qt_h = nc.dram_tensor("qt", [8, DQC, 512], BF16, kind="Internal")
    kt_h = nc.dram_tensor("kt", [16, DQC, 512], BF16, kind="Internal")
    vt_h = nc.dram_tensor("vt", [NHC, S, HD], BF16, kind="Internal")
    x2_h = nc.dram_tensor("x2h", [TPC, D], F32, kind="Internal")
    attn_loc = nc.dram_tensor("attnloc", [2048, QW], BF16, kind="Internal")
    attnsh = nc.dram_tensor("attnsh", [D, QW], BF16, kind="Internal",
                            addr_space="Shared")
    flagsh = nc.dram_tensor("flagsh", [2, 64], F32, kind="Internal",
                            addr_space="Shared")

    sem_names = ["bar", "vch", "sch", "cns", "wcns",
                 "xin0", "xin1", "xcast", "xwr0", "xwr1", "sq", "smm", "sev",
                 "fin", "fsq", "fre", "ftp",
                 "wld", "xbld", "qmm", "bkf", "ropa", "ropb", "scld", "nrm",
                 "tps", "tpe", "qst",
                 "ktld", "qcld", "tsc", "vmsk", "sexp", "tpv", "vrs",
                 "trb", "voev", "ost", "xch",
                 "ald", "wld0", "wld1", "xrd", "womm", "x2ev", "x2wr0", "x2wr1",
                 "rms", "lgT", "lgTe", "lgmm", "lgev", "snk",
                 "x2rd0", "x2rd1", "mix", "ymm", "yev", "ost0", "ost1", "wbld",
                 "xpo"]
    stack = ExitStack()
    sems = {n: stack.enter_context(nc.semaphore(n)) for n in sem_names}
    p = Plan(sems)

    ident_f = nc.alloc_sbuf_tensor("ident_f", [128, 128], F32)
    ident_b = nc.alloc_sbuf_tensor("ident_b", [128, 128], BF16)
    onesrow_f = nc.alloc_sbuf_tensor("onesrow_f", [1, 128], F32)
    ones_b = nc.alloc_sbuf_tensor("ones_b", [128, 1], BF16)
    yflag_sb = nc.alloc_sbuf_tensor("yflag_sb", [128, 1], F32)
    roff_sb = nc.alloc_sbuf_tensor("roff_sb", [1, 4], mybir.dt.int32)
    offtab_sb = nc.alloc_sbuf_tensor("offtab_sb", [1, 256], mybir.dt.int32)
    nonce_sb = nc.alloc_sbuf_tensor("nonce_sb", [1, 64], F32)
    flag_sb = nc.alloc_sbuf_tensor("flag_sb", [1, 64], F32)
    r_cols = nc.alloc_sbuf_tensor("r_cols", [128, 64], F32)
    rq_cols = nc.alloc_sbuf_tensor("rq_cols", [128, 32], F32)
    qmax2 = nc.alloc_sbuf_tensor("qmax2", [128, NHC], F32)
    kmax2 = nc.alloc_sbuf_tensor("kmax2", [128, NHC], F32)
    negcb = nc.alloc_sbuf_tensor("negcb", [128, NHC], F32)
    negcy = nc.alloc_sbuf_tensor("negcy", [128, NHC], F32)
    negc_row = nc.alloc_sbuf_tensor("negc_row", [1, NHC], F32)
    epsc = nc.alloc_sbuf_tensor("epsc", [1, 1], F32)
    wall_b = nc.alloc_sbuf_tensor("wall_b", [128, 32 * 24], BF16)
    bases_row = nc.alloc_sbuf_tensor("bases_row", [1, 24], F32)
    basesb = nc.alloc_sbuf_tensor("basesb", [128, 24], F32)
    gsc = nc.alloc_sbuf_tensor("gsc", [1, 4], F32)
    gb = nc.alloc_sbuf_tensor("gb", [128, 1], F32)
    omgb = nc.alloc_sbuf_tensor("omgb", [128, 1], F32)
    r2c = nc.alloc_sbuf_tensor("r2c", [128, 16], F32)
    m16 = nc.alloc_sbuf_tensor("m16", [128, 16], F32)
    rs16 = nc.alloc_sbuf_tensor("rs16", [128, 16], F32)
    sc1 = nc.alloc_sbuf_tensor("sc1", [128, 1], F32)
    epsb = nc.alloc_sbuf_tensor("epsb", [128, 1], F32)
    epsl = nc.alloc_sbuf_tensor("epsl", [128, 1], F32)
    lga = nc.alloc_sbuf_tensor("lga", [128, 16 * 24], F32)
    Z = nc.alloc_sbuf_tensor("Z", [128, 256], F32)
    tmp = nc.alloc_sbuf_tensor("tmp", [128, 256], F32)
    tmpe = nc.alloc_sbuf_tensor("tmpe", [128, 256], F32)
    ssum = nc.alloc_sbuf_tensor("ssum", [128, 64], F32)
    lnu = nc.alloc_sbuf_tensor("lnu", [128, 64], F32)
    lnv = nc.alloc_sbuf_tensor("lnv", [128, 64], F32)
    Hm = nc.alloc_sbuf_tensor("Hm", [128, 256], F32)
    u4 = nc.alloc_sbuf_tensor("u4", [128, 256], F32)
    hp = nc.alloc_sbuf_tensor("hp", [128, 64], F32)
    hq = nc.alloc_sbuf_tensor("hq", [128, 64], F32)
    sm = nc.alloc_sbuf_tensor("sm", [128, 16], F32)
    sm4 = nc.alloc_sbuf_tensor("sm4", [128, 64], F32)
    stm = ExitStack()
    maskd1_sb = stm.enter_context(nc.sbuf_tensor("maskd1_sb", [128, 4 * 512], F32))
    maskd2_sb = stm.enter_context(nc.sbuf_tensor("maskd2_sb", [128, 4 * 512], F32))
    negm_row = stm.enter_context(nc.sbuf_tensor("negm_row", [1, S], BF16))
    negmq_row = stm.enter_context(nc.sbuf_tensor("negmq_row", [1, QW], BF16))
    s_sum = stm.enter_context(nc.sbuf_tensor("s_sum", [1, 256], F32))
    s_sq = stm.enter_context(nc.sbuf_tensor("s_sq", [1, 256], F32))
    s_m = stm.enter_context(nc.sbuf_tensor("s_m", [1, 256], F32))

    pp = [nc.alloc_psum_tensor(f"pp{i}", [128, 512], F32) for i in range(4)]
    ps_a = nc.alloc_psum_tensor("ps_a", [1, 512], F32)
    ps_b = nc.alloc_psum_tensor("ps_b", [1, 512], F32)
    pbf = nc.alloc_psum_tensor("pbf", [128, 1024], BF16)
    pmisc = nc.alloc_psum_tensor("pmisc", [128, 512], F32)

    p.dma("sync", lambda e: e.dma_start(ident_f[:, :], ident_h.ap()[:, :]), inc="cns")
    p.dma("sync", lambda e: e.dma_start(onesrow_f[:, :], onesrow_h.ap()[:, :]),
          inc="cns")
    p.dma("sync", lambda e: e.dma_start(
        maskd1_sb[:, :], maskd1_h.ap().rearrange("(m p) t -> p m t", p=128)),
        inc="cns")
    p.dma("sync", lambda e: e.dma_start(
        maskd2_sb[:, :], maskd2_h.ap().rearrange("(m p) t -> p m t", p=128)),
        inc="cns")
    p.dma("sync", lambda e: e.dma_start(yflag_sb[:, :], yflag_h.ap()[:, :]),
          inc="cns")
    p.dma("sync", lambda e: e.dma_start(roff_sb[:, :], roff_h.ap()[:, :]), inc="cns")
    p.dma("sync", lambda e: e.dma_start(offtab_sb[:, :], offtab_h.ap()[:, :]), inc="cns")
    p.dma("sync", lambda e: e.dma_start(nonce_sb[:, :], nonce_h.ap()[:, :]),
          inc="cns")
    p.dma("sync", lambda e: e.dma_start(gsc[0:1, 0:1], gate_h.ap()[:, :]), inc="cns")
    p.dma("sync", lambda e: e.dma_start(bases_row[0:1, :], bases_h.ap()[:, :]),
          inc="cns")
    p.dma("gpsimd", lambda e: e.dma_start(
        wall_b[:, :], wallT_h.ap().rearrange("(c p) d -> p c d", p=128)), inc="wcns")
    cnsv = p.cur("cns")
    p.emit("vector", lambda e: e.tensor_copy(ident_b[:, :], ident_f[:, :]),
           waits=(("cns", cnsv),))
    p.emit("vector", lambda e: e.memset(ones_b[:, :], 1.0))
    p.emit("vector", lambda e: e.memset(qmax2[:, :], 0.0))
    p.emit("vector", lambda e: e.memset(kmax2[:, :], 0.0))
    p.emit("vector", lambda e: e.memset(epsc[:, :], EPS_LN))
    p.emit("vector", lambda e: e.memset(epsb[:, :], EPS_RMS))
    p.emit("vector", lambda e: e.memset(epsl[:, :], EPS_LN), inc="cns")
    p.emit("tensor", lambda e: e.matmul(
        pmisc[:, 0:24], onesrow_f[0:1, :], bases_row[0:1, :], start=True, stop=True),
        waits=(("cns", cnsv),), inc="lgmm")
    p.emit("vector", lambda e: e.tensor_copy(basesb[:, :], pmisc[:, 0:24]),
           waits=(("lgmm", p.cur("lgmm")),), inc="lgev")
    p.emit("scalar", lambda e: e.activation(
        gsc[0:1, 1:2], gsc[0:1, 0:1], AF.Tanh, bias=0.0, scale=0.5),
        waits=(("cns", cnsv),), inc="sq")
    p.emit("vector", lambda e: e.tensor_scalar(
        gsc[0:1, 2:3], gsc[0:1, 1:2], 0.5, 0.5, OP.mult, OP.add),
        waits=(("sq", p.cur("sq")),))
    p.emit("vector", lambda e: e.tensor_scalar(
        gsc[0:1, 3:4], gsc[0:1, 1:2], -0.5, 0.5, OP.mult, OP.add), inc="rms")
    p.emit("tensor", lambda e: e.matmul(
        pmisc[:, 24:25], onesrow_f[0:1, :], gsc[0:1, 2:3], start=True, stop=True),
        waits=(("rms", p.cur("rms")), ("lgev", p.cur("lgev"))))
    p.emit("tensor", lambda e: e.matmul(
        pmisc[:, 25:26], onesrow_f[0:1, :], gsc[0:1, 3:4], start=True, stop=True),
        inc="lgmm")
    p.emit("vector", lambda e: e.tensor_copy(gb[:, :], pmisc[:, 24:25]),
           waits=(("lgmm", p.cur("lgmm")),))
    p.emit("vector", lambda e: e.tensor_copy(omgb[:, :], pmisc[:, 25:26]),
           inc="lgev")

    st1 = ExitStack()
    CW = 256
    xin = [st1.enter_context(nc.sbuf_tensor(f"xin{i}", [128, 32 * CW], F32))
           for i in range(2)]
    xbf_sb = [st1.enter_context(nc.sbuf_tensor(f"xbfs{i}", [128, 32 * CW], BF16))
              for i in range(2)]
    xsq_sb = [st1.enter_context(nc.sbuf_tensor(f"xsqs{i}", [128, 32 * CW], BF16))
              for i in range(2)]

    chk = [0]

    def emit_stats(src_v, dst_v, nch, negm_dst, rcol_dst, pm_base):
        sev_entry = p.cur("sev")
        for lc in range(nch):
            ch = chk[0]
            bi = ch % 2
            c0 = lc * CW
            for sp in range(4):
                w = [("xcast", ch - 1)] if (ch >= 2 and sp == 0) else []
                p.dma("sync", (lambda bi_, sp_, c0_: lambda e: e.dma_start(
                    xin[bi_][:, sp_ * 8 * CW:(sp_ + 1) * 8 * CW],
                    src_v[:, sp_ * 8:(sp_ + 1) * 8, c0_:c0_ + CW]))(bi, sp, c0),
                    waits=tuple(w), inc=f"xin{bi}")
            w = [(f"xin{bi}", 64 * (ch // 2 + 1))]
            if ch >= 2:
                w += [(f"xwr{bi}", 64 * (ch // 2)), ("smm", ch - 1), ("sq", ch - 1)]
            p.emit("vector", (lambda bi_: lambda e: e.tensor_copy(
                xbf_sb[bi_][:, :], xin[bi_][:, :]))(bi), waits=tuple(w), inc="xcast")
            xc = p.cur("xcast")
            w = [("xcast", xc)]
            if ch >= 2:
                w.append(("smm", ch - 1))
            p.emit("scalar", (lambda bi_: lambda e: e.square(
                xsq_sb[bi_][:, :], xbf_sb[bi_][:, :]))(bi), waits=tuple(w), inc="sq")
            for sp in range(4):
                w = [("xcast", xc)] if sp == 0 else []
                p.dma("gpsimd", (lambda bi_, sp_, c0_: lambda e: e.dma_start(
                    dst_v[:, sp_ * 8:(sp_ + 1) * 8, c0_:c0_ + CW],
                    xbf_sb[bi_][:, sp_ * 8 * CW:(sp_ + 1) * 8 * CW]))(bi, sp, c0),
                    waits=tuple(w), inc=f"xwr{bi}")
            sqv = p.cur("sq")
            for dc in range(32):
                w = []
                if dc == 0:
                    w = [("xcast", xc), ("sq", sqv), ("sev", ch)]
                p.emit("tensor", (lambda bi_, dc_, st_, sp_: lambda e: e.matmul(
                    ps_a[0:1, 0:CW], ones_b[:, :],
                    xbf_sb[bi_][:, dc_ * CW:(dc_ + 1) * CW], start=st_, stop=sp_))(
                        bi, dc, dc == 0, dc == 31), waits=tuple(w))
                p.emit("tensor", (lambda bi_, dc_, st_, sp_: lambda e: e.matmul(
                    ps_b[0:1, 0:CW], ones_b[:, :],
                    xsq_sb[bi_][:, dc_ * CW:(dc_ + 1) * CW], start=st_, stop=sp_))(
                        bi, dc, dc == 0, dc == 31),
                    inc="smm" if dc == 31 else None)
            smv = p.cur("smm")
            w = [("smm", smv)]
            if ch >= 1:
                w.append(("ftp", ch))
            p.emit("vector", lambda e: e.tensor_copy(s_sum[0:1, :], ps_a[0:1, 0:CW]),
                   waits=tuple(w))
            p.emit("vector", lambda e: e.tensor_copy(s_sq[0:1, :], ps_b[0:1, 0:CW]),
                   inc="sev")
            p.emit("vector", (lambda c0_: lambda e: e.tensor_scalar_mul(
                negm_dst[0:1, c0_:c0_ + CW], s_sum[0:1, :], -1.0 / D))(c0))
            p.emit("vector", lambda e: e.tensor_scalar_mul(
                s_m[0:1, :], s_sum[0:1, :], 1.0 / D))
            p.emit("vector", lambda e: e.tensor_mul(s_m[0:1, :], s_m[0:1, :],
                                                    s_m[0:1, :]))
            p.emit("vector", lambda e: e.tensor_scalar_mul(
                s_sq[0:1, :], s_sq[0:1, :], 1.0 / D))
            p.emit("vector", lambda e: e.tensor_sub(s_sq[0:1, :], s_sq[0:1, :],
                                                    s_m[0:1, :]), inc="fin")
            p.emit("scalar", lambda e: e.activation(
                s_sq[0:1, :], s_sq[0:1, :], AF.Sqrt, bias=epsc[0:1, 0:1], scale=1.0),
                waits=(("fin", ch + 1), ("cns", p.cur("cns"))), inc="fsq")
            p.emit("vector", lambda e: e.reciprocal(s_sq[0:1, :], s_sq[0:1, :]),
                   waits=(("fsq", ch + 1),), inc="fre")
            for k in range(2):
                wtp = [("fre", ch + 1)] if k == 0 else []
                if k == 0 and lc == 0:
                    wtp.append(("sev", sev_entry))
                p.emit("tensor", (lambda lc_, k_, pb_: lambda e: e.transpose(
                    pmisc[:, pb_ + 2 * lc_ + k_:pb_ + 2 * lc_ + k_ + 1],
                    s_sq[0:1, k_ * 128:(k_ + 1) * 128], ident_f[0:1, 0:1]))(lc, k, pm_base),
                    waits=tuple(wtp), inc="ftp" if k == 1 else None)
            chk[0] += 1
        p.emit("vector", (lambda rc_, nch_, pb_: lambda e: e.tensor_copy(
            rc_[:, :], pmisc[:, pb_:pb_ + 2 * nch_]))(rcol_dst, nch, pm_base),
            waits=(("ftp", chk[0]),), inc="sev")

    xT_v = xT_h.ap().rearrange("(c p) t -> p c t", p=128)
    xbf_v = xbf_h.ap().rearrange("(c p) t -> p c t", p=128)
    xq_v = xq_h.ap().rearrange("(c p) t -> p c t", p=128)
    xqbf_v = xqbf_h.ap().rearrange("(c p) t -> p c t", p=128)
    emit_stats(xT_v, xbf_v, 32, negm_row, r_cols, 0)
    emit_stats(xq_v, xqbf_v, 16, negmq_row, rq_cols, 64)

    p.barrier(pre={"gpsimd": [("xwr0", 64 * 24), ("xwr1", 64 * 24)],
                   "vector": [("sev", p.cur("sev")), ("cns", p.cur("cns")),
                              ("lgev", p.cur("lgev"))]})
    st1.close()

    st2 = ExitStack()
    wbuf = st2.enter_context(nc.sbuf_tensor("wbuf", [128, 32 * 1024], BF16))
    xblk = st2.enter_context(nc.sbuf_tensor("xblk", [128, 32 * 1024], BF16))
    wrs_f = st2.enter_context(nc.sbuf_tensor("wrs_f", [1, 1024], F32))
    wrs_b = st2.enter_context(nc.sbuf_tensor("wrs_b", [1, 1024], BF16))
    qstage = st2.enter_context(nc.sbuf_tensor("qstage", [128, 1024], BF16))
    sqbuf = st2.enter_context(nc.sbuf_tensor("sqbuf", [128, 1024], BF16))
    nrmt = st2.enter_context(nc.sbuf_tensor("nrmt", [128, 4], F32))
    robuf = st2.enter_context(nc.sbuf_tensor("robuf", [128, 4 * RD], F32))
    robuf2 = st2.enter_context(nc.sbuf_tensor("robuf2", [128, 4 * RD], F32))
    robt = st2.enter_context(nc.sbuf_tensor("robt", [128, 4 * 32], F32))
    robt2 = st2.enter_context(nc.sbuf_tensor("robt2", [128, 4 * 32], F32))
    sc_sin = st2.enter_context(nc.sbuf_tensor("sc_sin", [128, 128], F32))
    sc_cos = st2.enter_context(nc.sbuf_tensor("sc_cos", [128, 128], F32))
    qt_blk = st2.enter_context(nc.sbuf_tensor("qt_blk", [128, 8 * 512], BF16))

    def emit_proj(pj, hg, src_bf_v, n_tiles, rope_sin, rope_cos, negm_src, r_src,
                  out_scr, mx_dst):
        wT_v = w_h[pj].ap().rearrange("(c p) d -> p c d", p=128)[:, :, hg * 1024:(hg + 1) * 1024]
        for sp in range(4):
            w = [("qst", p.cur("qst"))] if sp == 0 else []
            p.dma("gpsimd", (lambda sp_: lambda e: e.dma_start(
                wbuf[:, sp_ * 8 * 1024:(sp_ + 1) * 8 * 1024],
                wT_v[:, sp_ * 8:(sp_ + 1) * 8, :]))(sp), waits=tuple(w), inc="wld")
        wl = p.cur("wld")
        for dc in range(2):
            tgt = ps_a if dc == 0 else ps_b
            for c in range(32):
                w = []
                if dc == 0 and c == 0:
                    w = [("wld", wl), ("bkf", p.cur("bkf")), ("sev", p.cur("sev"))]
                p.emit("tensor", (lambda t_, c_, dc_, st_, sp_: lambda e: e.matmul(
                    t_[0:1, :], ones_b[:, :],
                    wbuf[:, c_ * 1024 + dc_ * 512: c_ * 1024 + (dc_ + 1) * 512],
                    start=st_, stop=sp_))(tgt, c, dc, c == 0, c == 31),
                    waits=tuple(w), inc="qmm" if (c == 31 and dc == 1) else None)
        qm = p.cur("qmm")
        p.emit("vector", lambda e: e.tensor_copy(wrs_f[0:1, 0:512], ps_a[0:1, :]),
               waits=(("qmm", qm),))
        p.emit("vector", lambda e: e.tensor_copy(wrs_f[0:1, 512:1024], ps_b[0:1, :]))
        p.emit("vector", lambda e: e.tensor_copy(wrs_b[0:1, :], wrs_f[0:1, :]),
               inc="bkf")
        n_blk = n_tiles // 8
        for blk in range(n_blk):
            t0 = blk * 1024
            for sp in range(4):
                w = []
                if sp == 0:
                    w.append(("bkf", p.cur("bkf")))
                    w.append(("qmm", p.cur("qmm")))
                p.dma("sync", (lambda sp_, t0_: lambda e: e.dma_start(
                    xblk[:, sp_ * 8 * 1024:(sp_ + 1) * 8 * 1024],
                    src_bf_v[:, sp_ * 8:(sp_ + 1) * 8, t0_:t0_ + 1024]))(sp, t0),
                    waits=tuple(w), inc="xbld")
            xbl = p.cur("xbld")
            for ti in range(8):
                tg = blk * 8 + ti
                if pj != "v":
                    w = [("ropb", p.cur("ropb"))]
                    p.dma("sync", (lambda tg_: lambda e: e.dma_start(
                        sc_sin[:, :],
                        rope_sin[tg_ * 128:(tg_ + 1) * 128,
                                 hg * 128:(hg + 1) * 128]))(tg),
                        waits=tuple(w), inc="scld")
                    p.dma("sync", (lambda tg_: lambda e: e.dma_start(
                        sc_cos[:, :],
                        rope_cos[tg_ * 128:(tg_ + 1) * 128,
                                 hg * 128:(hg + 1) * 128]))(tg), inc="scld")
                scl = p.cur("scld")
                for dc in range(2):
                    bank = pp[dc]
                    for c in range(32):
                        w = []
                        if c == 0:
                            w = [("xbld", xbl), ("bkf", p.cur("bkf"))]
                        p.emit("tensor",
                               (lambda c_, ti_, dc_, st_, b_: lambda e: e.matmul(
                                   b_[:, :],
                                   xblk[:, c_ * 1024 + ti_ * 128:
                                        c_ * 1024 + ti_ * 128 + 128],
                                   wbuf[:, c_ * 1024 + dc_ * 512:
                                        c_ * 1024 + (dc_ + 1) * 512],
                                   start=st_, stop=False))(c, ti, dc, c == 0, bank),
                               waits=tuple(w))
                    p.emit("tensor", (lambda tg_, dc_, b_: lambda e: e.matmul(
                        b_[:, :], negm_src[0:1, tg_ * 128:(tg_ + 1) * 128],
                        wrs_b[0:1, dc_ * 512:(dc_ + 1) * 512],
                        start=False, stop=True))(tg, dc, bank), inc="qmm")
                    qmm = p.cur("qmm")
                    if pj != "v":
                        p.emit("vector", (lambda dc_, tg_, b_: lambda e:
                               e.tensor_scalar_mul(
                                   robuf[:, :].rearrange(
                                       "p (h r) -> p h r", r=RD)[:, 2 * dc_:2 * dc_ + 2, :],
                                   b_[:, :].rearrange(
                                       "p (h r) -> p h r", r=HD)[:, :, 0:RD],
                                   r_src[:, tg_:tg_ + 1]))(dc, tg, bank),
                               waits=(("qmm", qmm),), inc="ropa")
                    wse = [("qmm", qmm)]
                    if pj != "v":
                        wse.append(("ropa", p.cur("ropa")))
                    if dc == 0:
                        if pj == "v":
                            wse.append(("qst", p.cur("qst")))
                        else:
                            wse.append(("tps", p.cur("tps")))
                    p.emit("scalar", (lambda dc_, tg_, b_: lambda e: e.activation(
                        qstage[:, dc_ * 512:(dc_ + 1) * 512], b_[:, :], AF.Copy,
                        bias=0.0, scale=r_src[:, tg_:tg_ + 1]))(dc, tg, bank),
                        waits=tuple(wse), inc="bkf")
                qevv = p.cur("bkf")
                if pj != "v":
                    ropv = p.cur("ropa")
                    rb = robuf[:, :].rearrange("p (h i two) -> p h i two", i=32, two=2)
                    rb2 = robuf2[:, :].rearrange("p (h i two) -> p h i two", i=32, two=2)
                    sv = sc_sin[:, :].rearrange("p (h i) -> p h i", i=32)
                    cv = sc_cos[:, :].rearrange("p (h i) -> p h i", i=32)
                    tv = robt[:, :].rearrange("p (h i) -> p h i", i=32)
                    tv2 = robt2[:, :].rearrange("p (h i) -> p h i", i=32)
                    p.emit("vector", lambda e: e.tensor_mul(tv, rb[:, :, :, 1], sv),
                           waits=(("ropa", ropv), ("scld", scl)))
                    p.emit("vector", lambda e: e.tensor_mul(tv2, rb[:, :, :, 0], sv))
                    p.emit("vector", lambda e: e.tensor_mul(
                        rb2[:, :, :, 0], rb[:, :, :, 0], cv))
                    p.emit("vector", lambda e: e.tensor_sub(
                        rb2[:, :, :, 0], rb2[:, :, :, 0], tv))
                    p.emit("vector", lambda e: e.tensor_mul(
                        rb2[:, :, :, 1], rb[:, :, :, 1], cv))
                    p.emit("vector", lambda e: e.tensor_add(
                        rb2[:, :, :, 1], rb2[:, :, :, 1], tv2), inc="ropb")
                    p.emit("vector", lambda e: e.tensor_copy(
                        qstage[:, :].rearrange("p (h r) -> p h r", r=HD)[:, :, 0:RD],
                        robuf2[:, :].rearrange("p (h r) -> p h r", r=RD)),
                        waits=(("bkf", qevv),), inc="ropb")
                    ropd = p.cur("ropb")
                    p.emit("scalar", lambda e: e.square(sqbuf[:, :], qstage[:, :]),
                           waits=(("ropb", ropd), ("bkf", qevv)), inc="sq")
                    p.emit("vector", lambda e: e.reduce_sum(
                        nrmt[:, :], sqbuf[:, :].rearrange("p (h d) -> p h d", d=HD),
                        axis=AX.X), waits=(("sq", p.cur("sq")),))
                    p.emit("vector", (lambda m_, hg_: lambda e: e.tensor_max(
                        m_[:, hg_ * 4:(hg_ + 1) * 4], m_[:, hg_ * 4:(hg_ + 1) * 4],
                        nrmt[:, :]))(mx_dst, hg), inc="nrm")
                    for i in range(8):
                        w = []
                        if i == 0:
                            w = [("ropb", ropd), ("bkf", qevv), ("tpe", p.cur("tpe"))]
                        p.emit("tensor", (lambda i_: lambda e: e.transpose(
                            pbf[:, i_ * 128:(i_ + 1) * 128],
                            qstage[:, i_ * 128:(i_ + 1) * 128], ident_b[:, :]))(i),
                            waits=tuple(w), inc="tps" if i == 7 else None)
                    tpsv = p.cur("tps")
                    for i in range(8):
                        w = [("tps", tpsv)] if i == 0 else []
                        if i == 0 and tg % 4 == 0:
                            w.append(("qst", p.cur("qst")))
                        p.emit("vector", (lambda i_, t4_: lambda e: e.tensor_copy(
                            qt_blk[:, i_ * 512 + t4_ * 128:
                                   i_ * 512 + t4_ * 128 + 128],
                            pbf[:, i_ * 128:(i_ + 1) * 128]))(i, tg % 4),
                            waits=tuple(w), inc="tpe" if i == 7 else None)
                    tpev = p.cur("tpe")
                    if tg % 4 == 3:
                        qc = tg // 4
                        for i in range(8):
                            w = [("tpe", tpev)] if i == 0 else []
                            p.dma("sync", (lambda i_, qc_, hg_: lambda e: e.dma_start(
                                out_scr[qc_, hg_ * 1024 + i_ * 128:
                                        hg_ * 1024 + (i_ + 1) * 128, :],
                                qt_blk[:, i_ * 512:(i_ + 1) * 512]))(i, qc, hg),
                                waits=tuple(w), inc="qst")
                else:
                    for hl in range(4):
                        w = [("bkf", qevv)] if hl == 0 else []
                        p.dma("sync", (lambda hl_, tg_, hg_: lambda e: e.dma_start(
                            vt_h.ap()[hg_ * 4 + hl_, tg_ * 128:(tg_ + 1) * 128, :],
                            qstage[:, hl_ * HD:(hl_ + 1) * HD]))(hl, tg, hg),
                            waits=tuple(w), inc="qst")
        p.barrier(pre={"sync": [("qst", p.cur("qst"))],
                       "vector": [("nrm", p.cur("nrm")), ("sev", p.cur("sev"))],
                       "scalar": [("bkf", p.cur("bkf"))]})

    for hg in range(2):
        emit_proj("k", hg, xbf_v, 64, sink_h.ap(), cosk_h.ap(), negm_row, r_cols,
                  kt_h.ap(), kmax2)
    for hg in range(2):
        emit_proj("v", hg, xbf_v, 64, None, None, negm_row, r_cols, None, None)
    for hg in range(2):
        emit_proj("q", hg, xqbf_v, 32, sinq_h.ap(), cosq_h.ap(), negmq_row, rq_cols,
                  qt_h.ap(), qmax2)

    p.emit("vector", lambda e: e.tensor_add(qmax2[:, :], qmax2[:, :], kmax2[:, :]),
           inc="sev")
    p.emit("tensor", lambda e: e.transpose(
        pmisc[0:NHC, 0:128], qmax2[:, :], ident_f[:, :]),
        waits=(("sev", p.cur("sev")),), inc="smm")
    p.emit("vector", lambda e: e.reduce_max(
        qmax2[0:NHC, 0:1], pmisc[0:NHC, 0:128], axis=AX.X),
        waits=(("smm", p.cur("smm")),))
    p.emit("vector", lambda e: e.tensor_scalar_mul(
        qmax2[0:NHC, 0:1], qmax2[0:NHC, 0:1], -1.0 / 32.0), inc="sev")
    p.emit("tensor", lambda e: e.transpose(
        pmisc[0:1, 0:NHC], qmax2[0:NHC, 0:1], ident_f[0:NHC, 0:NHC]),
        waits=(("sev", p.cur("sev")),), inc="smm")
    p.emit("vector", lambda e: e.tensor_copy(negc_row[0:1, :], pmisc[0:1, 0:NHC]),
           waits=(("smm", p.cur("smm")),), inc="sev")
    sevn2 = p.cur("sev")
    for h in range(NHC):
        p.emit("tensor", (lambda h_: lambda e: e.matmul(
            pmisc[:, h_:h_ + 1], onesrow_f[0:1, :], negc_row[0:1, h_:h_ + 1],
            start=True, stop=True))(h),
            waits=(("sev", sevn2),) if h == 0 else (),
            inc="smm" if h == NHC - 1 else None)
    p.emit("vector", lambda e: e.tensor_copy(negcb[:, :], pmisc[:, 0:NHC]),
           waits=(("smm", p.cur("smm")),))
    p.emit("vector", lambda e: e.tensor_scalar_add(
        negcy[:, :], negcb[:, :], yflag_sb[:, 0:1]), inc="sev")
    p.barrier(pre={"vector": [("sev", p.cur("sev"))]})
    st2.close()

    st3 = ExitStack()
    kt_sb = st3.enter_context(nc.sbuf_tensor("kt_sb", [128, 2 * S], BF16))
    vt_sb = st3.enter_context(nc.sbuf_tensor("vt_sb", [128, 64 * HD], BF16))
    qch = [st3.enter_context(nc.sbuf_tensor(f"qch{i}", [128, 2 * 512], BF16))
           for i in range(2)]
    psb = [st3.enter_context(nc.sbuf_tensor(f"psb{i}", [128, 512], BF16))
           for i in range(2)]
    srow = st3.enter_context(nc.sbuf_tensor("srow", [1, 512], F32))
    rbf = st3.enter_context(nc.sbuf_tensor("rbf", [128, 512], F32))
    ostage = st3.enter_context(nc.sbuf_tensor("ostage", [128, 1024], BF16))

    sregs = {}

    def sync_regs(e):
        if "tmp" not in sregs:
            sregs["tmp"] = e.alloc_register("shtmp")
        return sregs
    sidx = [0]

    gi = 0
    for h in range(NHC):
        for dh in range(2):
            w = []
            if h > 0 and dh == 0:
                w = [("tsc", p.cur("tsc")), ("tpv", p.cur("tpv"))]
            p.dma("sync", (lambda h_, dh_: lambda e: e.dma_start(
                kt_sb[:, dh_ * S:(dh_ + 1) * S],
                bass.AP(kt_h, (h_ * HD + dh_ * 128) * 512,
                        [[512, 128], [2048 * 512, 16], [1, 512]])))(h, dh),
                waits=tuple(w), inc="ktld")
        for sp in range(8):
            p.dma("sync", (lambda h_, sp_: lambda e: e.dma_start(
                vt_sb[:, sp_ * 8 * HD:(sp_ + 1) * 8 * HD],
                bass.AP(vt_h, h_ * S * HD + sp_ * 8 * 128 * HD,
                        [[HD, 128], [128 * HD, 8], [1, HD]])))(h, sp),
                inc="ktld")
        ktl = p.cur("ktld")
        for qc in range(8):
            qb = qc % 2
            for dh in range(2):
                w = []
                if dh == 0:
                    w.append(("tsc", p.cur("tsc")))
                    if qc == 0:
                        w.append(("ktld", ktl))
                p.dma("sync", (lambda h_, dh_, qc_, qb_: lambda e: e.dma_start(
                    qch[qb_][:, dh_ * 512:(dh_ + 1) * 512],
                    bass.AP(qt_h, qc_ * 2048 * 512 + (h_ * HD + dh_ * 128) * 512,
                            [[512, 128], [1, 512]])))(h, dh, qc, qb),
                    waits=tuple(w), inc="qcld")
            qcl = p.cur("qcld")
            nj = 4 * qc + 36
            for j in range(nj):
                sbank = pp[gi % 2]
                pb = gi % 2
                if 4 * qc <= j < 4 * qc + 4:
                    cls, d = "d1", j - 4 * qc
                elif 4 * qc + 32 <= j:
                    cls, d = "d2", j - 4 * qc - 32
                elif j >= 4 * qc + 4:
                    cls, d = "y", 0
                else:
                    cls, d = "x", 0
                w = []
                if j == 0:
                    w.append(("qcld", qcl))
                if gi >= 2:
                    w.append(("sexp", gi - 1))
                for dh in range(2):
                    p.emit("tensor", (lambda j_, dh_, qb_, st_, sp_, b_:
                           lambda e: e.matmul(
                               b_[:, :],
                               kt_sb[:, dh_ * S + j_ * 128: dh_ * S + j_ * 128 + 128],
                               qch[qb_][:, dh_ * 512:(dh_ + 1) * 512],
                               start=st_, stop=sp_))(j, dh, qb, dh == 0, dh == 1,
                                                     sbank),
                           waits=tuple(w) if dh == 0 else (),
                           inc="tsc" if dh == 1 else None)
                tscv = p.cur("tsc")
                if cls in ("d1", "d2"):
                    msb = maskd1_sb if cls == "d1" else maskd2_sb
                    p.emit("vector", (lambda d_, b_, m_: lambda e: e.tensor_add(
                        b_[:, :], b_[:, :], m_[:, d_ * 512:(d_ + 1) * 512]))(
                            d, sbank, msb), waits=(("tsc", tscv),), inc="vmsk")
                w = [("vmsk", p.cur("vmsk"))] if cls in ("d1", "d2") else                     [("tsc", tscv)]
                if gi >= 2:
                    w.append(("tpv", gi - 1))
                bias_t = negcy if cls in ("y", "d2") else negcb
                p.emit("scalar", (lambda h_, b_, pb_, bt_: lambda e: e.activation(
                    psb[pb_][:, :], b_[:, :], AF.Exp,
                    bias=bt_[:, h_:h_ + 1], scale=1.0 / 16.0))(h, sbank, pb, bias_t),
                    waits=tuple(w), inc="sexp")
                sev_ = p.cur("sexp")
                for dh in range(2):
                    w = []
                    if dh == 0:
                        w.append(("sexp", sev_))
                        if j == 0:
                            w.append(("voev", p.cur("voev")))
                    p.emit("tensor", (lambda j_, dh_, pb_, st_, sp_: lambda e: e.matmul(
                        pp[2 + dh_][:, :],
                        vt_sb[:, j_ * HD + dh_ * 128: j_ * HD + dh_ * 128 + 128],
                        psb[pb_][:, :], start=st_, stop=sp_))(
                            j, dh, pb, j == 0, j == nj - 1),
                        waits=tuple(w))
                w = []
                if j == 0:
                    w.append(("vrs", p.cur("vrs")))
                p.emit("tensor", (lambda pb_, st_, sp_: lambda e: e.matmul(
                    ps_a[0:1, :], ones_b[:, :], psb[pb_][:, :],
                    start=st_, stop=sp_))(pb, j == 0, j == nj - 1),
                    waits=tuple(w), inc="tpv")
                gi += 1
            tpvv = p.cur("tpv")
            p.emit("vector", lambda e: e.tensor_copy(srow[0:1, :], ps_a[0:1, :]),
                   waits=(("tpv", tpvv),))
            p.emit("vector", lambda e: e.reciprocal(srow[0:1, :], srow[0:1, :]),
                   inc="vrs")
            p.emit("tensor", lambda e: e.matmul(
                pmisc[:, :], onesrow_f[0:1, :], srow[0:1, :], start=True, stop=True),
                waits=(("vrs", p.cur("vrs")), ("voev", p.cur("voev"))), inc="trb")
            p.emit("vector", lambda e: e.tensor_copy(rbf[:, :], pmisc[:, :]),
                   waits=(("trb", p.cur("trb")), ("ost", p.cur("ost"))))
            for dh in range(2):
                p.emit("vector", (lambda dh_: lambda e: e.tensor_mul(
                    ostage[:, dh_ * 512:(dh_ + 1) * 512], pp[2 + dh_][:, :],
                    rbf[:, :]))(dh), inc="voev" if dh == 1 else None)
            vov = p.cur("voev")
            for dh in range(2):
                cst = (h * HD + dh * 128) * QW + qc * 512
                p.dma("sync", (lambda dh_, cst_: lambda e: e.dma_start(
                    bass.AP(attn_loc, cst_, [[QW, 128], [1, 512]]),
                    ostage[:, dh_ * 512:(dh_ + 1) * 512]))(dh, cst),
                    waits=(("voev", vov),) if dh == 0 else (), inc="ost")

    ostv = p.cur("ost")
    for k in range(4):
        p.dma("gpsimd", (lambda k_: lambda e: (
            sync_regs(e),
            e.reg_load(sregs["tmp"], offtab_sb[0:1, k_:k_ + 1]),
            e.dma_start(bass.AP(attnsh, sregs["tmp"], [[2097152, 1], [1, 2097152]]),
                        bass.AP(attn_loc, k_ * 2097152,
                                [[2097152, 1], [1, 2097152]])))[-1])(k),
            waits=(("ost", ostv),) if k == 0 else (), inc="xpo")

    def xch_fn(e):
        foff = e.alloc_register("foff")
        poff = e.alloc_register("poff")
        pv = e.alloc_register("pv")
        nv = e.alloc_register("nv")
        nf = e.alloc_register("nf")
        e.reg_load(foff, offtab_sb[0:1, 144:145])
        e.reg_load(nv, nonce_sb[0:1, 0:1].bitcast(mybir.dt.int32))
        e.reg_load(poff, offtab_sb[0:1, 145:146])
        e.dma_start(bass.AP(flagsh, foff, [[64, 1], [1, 64]]),
                    nonce_sb[0:1, :]).then_inc(sems["xpo"], 16)
        e.wait_ge(sems["xpo"], 16 * 5)
        e.reg_mov(nf, 1)
        with e.While(nf):
            e.dma_start(flag_sb[0:1, :],
                        bass.AP(flagsh, poff, [[64, 1], [1, 64]])).then_inc(
                            sems["xpo"], 16)
            e.reg_load(pv, flag_sb[0:1, 0:1].bitcast(mybir.dt.int32))
            e.reg_alu(nf, pv, nv, OP.not_equal)
        return None
    p.emit("gpsimd", xch_fn, waits=(("xpo", p.cur("xpo")), ("wcns", 16)))
    p.emit("gpsimd", lambda e: e.nop(), inc="xch")
    xchv = p.cur("xch")
    st3.close()
    stm.close()

    stB = ExitStack()
    abuf = stB.enter_context(nc.sbuf_tensor("abuf", [128, 32 * 512], BF16))
    wbuf2 = [stB.enter_context(nc.sbuf_tensor("wbufB0", [128, 32 * 512], BF16))]
    wbuf2 = [wbuf2[0], wbuf2[0]]
    x2g = stB.enter_context(nc.sbuf_tensor("x2g", [128, 4 * D], F32))
    x2bf = stB.enter_context(nc.sbuf_tensor("x2bf", [128, D], BF16))
    x2T = stB.enter_context(nc.sbuf_tensor("x2T", [128, 32 * 128], BF16))
    sqs = stB.enter_context(nc.sbuf_tensor("sqs", [128, D], BF16))

    woT_v = woT_h.ap().rearrange("(c p) d -> p c d", p=128)

    def eread_fn(sp, g):
        idx_ = 128 + g * 2 + sp

        def fn(e):
            r = sync_regs(e)
            e.reg_load(r["tmp"], offtab_sb[0:1, idx_:idx_ + 1])
            return e.dma_start(
                abuf[:, sp * 16 * 512:(sp + 1) * 16 * 512],
                bass.AP(attnsh, r["tmp"],
                        [[QW, 128], [128 * QW, 16], [1, 512]]))
        return fn

    for g in range(4):
        t0 = g * 512
        for sp in range(2):
            w = [("womm", p.cur("womm"))] if sp == 0 else []
            if g == 0 and sp == 0:
                w.append(("xch", xchv))
            p.dma("gpsimd", eread_fn(sp, g), waits=tuple(w), inc="ald")
        aldv = p.cur("ald")
        for ti in range(4):
            w = ([(f"x2wr{(g - 1) % 2}", 64 * ((g - 1) // 2 + 1)),
                  ("sq", p.cur("sq")), ("x2ev", p.cur("x2ev")),
                  ("rms", p.cur("rms"))]
                 if (ti == 0 and g > 0) else [])
            p.dma("sync", (lambda ti_, t0_: lambda e: e.dma_start(
                x2g[:, ti_ * D:(ti_ + 1) * D],
                xres_h.ap()[t0_ + ti_ * 128: t0_ + ti_ * 128 + 128, :]))(ti, t0),
                waits=tuple(w), inc="xrd")
        xrdv = p.cur("xrd")
        for dc in range(8):
            wb = wbuf2[dc % 2]
            for sp in range(2):
                w = []
                if sp == 0:
                    w.append(("womm", p.cur("womm")))
                p.dma("gpsimd", (lambda sp_, dc_, wb_: lambda e: e.dma_start(
                    wb_[:, sp_ * 16 * 512:(sp_ + 1) * 16 * 512],
                    woT_v[:, sp_ * 16:(sp_ + 1) * 16, dc_ * 512:(dc_ + 1) * 512]))(
                        sp, dc, wb), waits=tuple(w), inc=f"wld{dc % 2}")
            wldn = f"wld{dc % 2}"
            wldv = p.cur(wldn)
            for ti in range(4):
                bank = pp[ti % 2]
                for c in range(32):
                    w = []
                    if c == 0:
                        w = [(wldn, wldv), ("ald", aldv),
                             ("x2ev", max(0, p.cur("x2ev") - 1))]
                    p.emit("tensor",
                           (lambda c_, ti_, wb_, st_, sp_, b_: lambda e: e.matmul(
                               b_[:, :],
                               abuf[:, c_ * 512 + ti_ * 128:
                                    c_ * 512 + ti_ * 128 + 128],
                               wb_[:, c_ * 512:(c_ + 1) * 512],
                               start=st_, stop=sp_))(c, ti, wb, c == 0, c == 31,
                                                     bank),
                           waits=tuple(w), inc="womm" if c == 31 else None)
                wommv = p.cur("womm")
                w = [("womm", wommv)]
                if dc == 0:
                    w.append(("xrd", xrdv))
                p.emit("vector", (lambda ti_, dc_, b_: lambda e: e.tensor_add(
                    x2g[:, ti_ * D + dc_ * 512: ti_ * D + (dc_ + 1) * 512],
                    b_[:, :],
                    x2g[:, ti_ * D + dc_ * 512: ti_ * D + (dc_ + 1) * 512]))(
                        ti, dc, bank), waits=tuple(w), inc="x2ev")
        x2evv = p.cur("x2ev")
        for ti in range(4):
            tg = g * 4 + ti
            p.dma("gpsimd", (lambda ti_, t0_: lambda e: e.dma_start(
                x2_h.ap()[t0_ + ti_ * 128: t0_ + ti_ * 128 + 128, :],
                x2g[:, ti_ * D:(ti_ + 1) * D]))(ti, t0),
                waits=(("x2ev", x2evv),) if ti == 0 else (), inc=f"x2wr{g % 2}")
            p.emit("vector", (lambda ti_, tg_: lambda e: e.reduce_sum(
                m16[:, tg_:tg_ + 1], x2g[:, ti_ * D:(ti_ + 1) * D], axis=AX.X))(ti, tg),
                waits=(("x2ev", x2evv),))
            p.emit("vector", (lambda tg_: lambda e: e.tensor_scalar_mul(
                m16[:, tg_:tg_ + 1], m16[:, tg_:tg_ + 1], 1.0 / D))(tg))
            p.emit("scalar", (lambda ti_, tg_: lambda e: e.activation(
                sqs[:, :], x2g[:, ti_ * D:(ti_ + 1) * D], AF.Square,
                accum_out=rs16[:, tg_:tg_ + 1]))(ti, tg),
                waits=(("x2ev", x2evv),), inc="sq")
            p.emit("vector", (lambda tg_: lambda e: e.tensor_scalar_mul(
                rs16[:, tg_:tg_ + 1], rs16[:, tg_:tg_ + 1], 1.0 / D))(tg),
                waits=(("sq", p.cur("sq")),))
            p.emit("vector", (lambda tg_: lambda e: e.tensor_mul(
                sc1[:, 0:1], m16[:, tg_:tg_ + 1], m16[:, tg_:tg_ + 1]))(tg))
            p.emit("vector", (lambda tg_: lambda e: e.tensor_sub(
                rs16[:, tg_:tg_ + 1], rs16[:, tg_:tg_ + 1], sc1[:, 0:1]))(tg),
                inc="rms")
            p.emit("scalar", (lambda tg_: lambda e: e.activation(
                rs16[:, tg_:tg_ + 1], rs16[:, tg_:tg_ + 1], AF.Sqrt,
                bias=epsl[:, 0:1], scale=1.0))(tg),
                waits=(("rms", p.cur("rms")),), inc="sq")
            p.emit("vector", (lambda tg_: lambda e: e.reciprocal(
                rs16[:, tg_:tg_ + 1], rs16[:, tg_:tg_ + 1]))(tg),
                waits=(("sq", p.cur("sq")),))
            p.emit("vector", (lambda ti_, tg_: lambda e: e.tensor_scalar(
                x2bf[:, :], x2g[:, ti_ * D:(ti_ + 1) * D],
                m16[:, tg_:tg_ + 1], rs16[:, tg_:tg_ + 1],
                OP.subtract, OP.mult))(ti, tg),
                waits=(("lgT", p.cur("lgT")),), inc="x2ev")
            cstv = p.cur("x2ev")
            p.emit("scalar", (lambda tg_: lambda e: e.activation(
                sqs[:, :], x2bf[:, :], AF.Square,
                accum_out=r2c[:, tg_:tg_ + 1]))(tg),
                waits=(("x2ev", cstv),), inc="sq")
            p.emit("scalar", (lambda tg_: lambda e: e.activation(
                r2c[:, tg_:tg_ + 1], r2c[:, tg_:tg_ + 1], AF.Sqrt,
                bias=epsb[:, 0:1], scale=1.0 / D))(tg),
                waits=(("sq", p.cur("sq")),), inc="sq")
            p.emit("vector", (lambda tg_: lambda e: e.reciprocal(
                r2c[:, tg_:tg_ + 1], r2c[:, tg_:tg_ + 1]))(tg),
                waits=(("sq", p.cur("sq")),), inc="rms")
            for rnd in range(4):
                for i in range(8):
                    w = []
                    if i == 0:
                        w = [("x2ev", cstv), ("lgTe", p.cur("lgTe"))]
                    p.emit("tensor", (lambda rnd_, i_: lambda e: e.transpose(
                        pbf[:, i_ * 128:(i_ + 1) * 128],
                        x2bf[:, rnd_ * 1024 + i_ * 128: rnd_ * 1024 + (i_ + 1) * 128],
                        ident_b[:, :]))(rnd, i),
                        waits=tuple(w), inc="lgT" if i == 7 else None)
                tvv = p.cur("lgT")
                for i in range(8):
                    p.emit("vector", (lambda rnd_, i_: lambda e: e.tensor_copy(
                        x2T[:, (rnd_ * 8 + i_) * 128:(rnd_ * 8 + i_ + 1) * 128],
                        pbf[:, i_ * 128:(i_ + 1) * 128]))(rnd, i),
                        waits=(("lgT", tvv),) if i == 0 else (),
                        inc="lgTe" if i == 7 else None)
            tev = p.cur("lgTe")
            for c in range(32):
                w = []
                if c == 0:
                    w = [("lgTe", tev), ("lgev", p.cur("lgev")), ("wcns", 16)]
                p.emit("tensor", (lambda c_, st_, sp_: lambda e: e.matmul(
                    pmisc[:, 0:24], x2T[:, c_ * 128:(c_ + 1) * 128],
                    wall_b[:, c_ * 24:(c_ + 1) * 24], start=st_, stop=sp_))(
                        c, c == 0, c == 31),
                    waits=tuple(w), inc="lgmm" if c == 31 else None)
            p.emit("scalar", (lambda tg_: lambda e: e.activation(
                lga[:, tg_ * 24:(tg_ + 1) * 24], pmisc[:, 0:24], AF.Copy,
                bias=0.0, scale=r2c[:, tg_:tg_ + 1]))(tg),
                waits=(("lgmm", p.cur("lgmm")), ("rms", p.cur("rms"))), inc="sq")
            p.emit("vector", (lambda tg_: lambda e: e.tensor_add(
                lga[:, tg_ * 24:(tg_ + 1) * 24], lga[:, tg_ * 24:(tg_ + 1) * 24],
                basesb[:, :]))(tg),
                waits=(("sq", p.cur("sq")),), inc="lgev")
    p.barrier(pre={"gpsimd": [("x2wr0", p.cur("x2wr0")),
                              ("x2wr1", p.cur("x2wr1"))],
                   "vector": [("lgev", p.cur("lgev"))]})
    stB.close()

    lview = lga[:, :].rearrange("p (t c) -> p t c", c=24)
    Zv = Z[:, :].rearrange("p (t i j) -> p t i j", i=4, j=4)
    tv_ = tmp[:, :].rearrange("p (t i j) -> p t i j", i=4, j=4)
    lnv_v = lnv[:, :].rearrange("p (t j) -> p t j", j=4)
    lnu_v = lnu[:, :].rearrange("p (t i) -> p t i", i=4)

    p.emit("vector", lambda e: e.tensor_scalar_mul(
        Zv, lview[:, :, 0:16].rearrange("p t (i j) -> p t i j", j=4), 1.0 / TAU))
    p.emit("vector", lambda e: e.memset(lnv[:, :], 0.0), inc="snk")
    for it in range(SINK_IT):
        for i in range(4):
            p.emit("vector", (lambda i_: lambda e: e.tensor_sub(
                tv_[:, :, i_, :], Zv[:, :, i_, :], lnv_v))(i),
                waits=(("snk", p.cur("snk")),) if i == 0 else (),
                inc="snk" if i == 3 else None)
        p.emit("scalar", lambda e: e.activation(tmpe[:, :], tmp[:, :], AF.Exp),
               waits=(("snk", p.cur("snk")),), inc="sq")
        p.emit("vector", lambda e: e.tensor_reduce(
            ssum[:, :], tmpe[:, :].rearrange("p (ti j) -> p ti j", j=4),
            axis=AX.X, op=OP.add),
            waits=(("sq", p.cur("sq")),), inc="snk")
        p.emit("scalar", lambda e: e.activation(lnu[:, :], ssum[:, :], AF.Ln),
               waits=(("snk", p.cur("snk")),), inc="sq")
        sqv = p.cur("sq")
        for j in range(4):
            p.emit("vector", (lambda j_: lambda e: e.tensor_sub(
                tv_[:, :, j_, :], Zv[:, :, :, j_], lnu_v))(j),
                waits=(("sq", sqv),) if j == 0 else (),
                inc="snk" if j == 3 else None)
        p.emit("scalar", lambda e: e.activation(tmpe[:, :], tmp[:, :], AF.Exp),
               waits=(("snk", p.cur("snk")),), inc="sq")
        p.emit("vector", lambda e: e.tensor_reduce(
            ssum[:, :], tmpe[:, :].rearrange("p (tj i) -> p tj i", i=4),
            axis=AX.X, op=OP.add),
            waits=(("sq", p.cur("sq")),), inc="snk")
        p.emit("scalar", lambda e: e.activation(lnv[:, :], ssum[:, :], AF.Ln),
               waits=(("snk", p.cur("snk")),), inc="sq")
        p.emit("vector", lambda e: e.tensor_copy(sm[:, 0:1], lnv[:, 0:1]),
               waits=(("sq", p.cur("sq")),), inc="snk")
    u4v = u4[:, :].rearrange("p (t i j) -> p t i j", i=4, j=4)
    for j in range(4):
        p.emit("vector", (lambda j_: lambda e: e.tensor_copy(
            u4v[:, :, :, j_], lnu_v))(j))
    p.emit("vector", lambda e: e.tensor_sub(Hm[:, :], Z[:, :], u4[:, :]))
    Hv = Hm[:, :].rearrange("p (t i j) -> p t i j", i=4, j=4)
    for i in range(4):
        p.emit("vector", (lambda i_: lambda e: e.tensor_sub(
            tv_[:, :, i_, :], Hv[:, :, i_, :], lnv_v))(i),
            inc="snk" if i == 3 else None)
    p.emit("scalar", lambda e: e.activation(Hm[:, :], tmp[:, :], AF.Exp),
           waits=(("snk", p.cur("snk")),), inc="sq")
    p.emit("vector", lambda e: e.tensor_scalar_mul(Hm[:, :], Hm[:, :], gb[:, 0:1]),
           waits=(("sq", p.cur("sq")),))
    diagv = Hm[:, :].rearrange("p (t d) -> p t d", d=16)[:, :, 0:16:5]
    p.emit("vector", lambda e: e.tensor_scalar_add(diagv, diagv, omgb[:, 0:1]),
           inc="snk")
    for (src0, dst) in ((16, hp), (20, hq)):
        pv = lview[:, :, src0:src0 + 4]
        dv = dst[:, :].rearrange("p (t n) -> p t n", n=4)
        s4 = sm4[:, :].rearrange("p (t n) -> p t n", n=4)
        p.emit("vector", (lambda pv_: lambda e: e.reduce_max(
            sm[:, :], pv_, axis=AX.X))(pv))
        for n in range(4):
            p.emit("vector", (lambda n_, s4_: lambda e: e.tensor_copy(
                s4_[:, :, n_], sm[:, :]))(n, s4))
        p.emit("vector", (lambda pv_, dv_, s4_: lambda e: e.tensor_sub(
            dv_, pv_, s4_))(pv, dv, s4), inc="snk")
        p.emit("scalar", (lambda d_: lambda e: e.activation(
            d_[:, :], d_[:, :], AF.Exp))(dst),
            waits=(("snk", p.cur("snk")),), inc="sq")
        p.emit("vector", (lambda dv_: lambda e: e.tensor_reduce(
            sm[:, :], dv_, axis=AX.X, op=OP.add))(dv),
            waits=(("sq", p.cur("sq")),))
        p.emit("vector", lambda e: e.reciprocal(sm[:, :], sm[:, :]))
        for n in range(4):
            p.emit("vector", (lambda n_, s4_: lambda e: e.tensor_copy(
                s4_[:, :, n_], sm[:, :]))(n, s4))
        p.emit("vector", (lambda d_: lambda e: e.tensor_mul(
            d_[:, :], d_[:, :], sm4[:, :]))(dst), inc="snk")
    p.barrier(pre={"vector": [("snk", p.cur("snk"))]})

    stC = ExitStack()
    wbr_b = stC.enter_context(nc.sbuf_tensor("wbr_b", [128, 8 * 1024], BF16))
    x2m = [stC.enter_context(nc.sbuf_tensor(f"x2m{i}", [128, D], F32))
           for i in range(2)]
    xpre = stC.enter_context(nc.sbuf_tensor("xpre", [128, 1024], F32))
    ln2m = [stC.enter_context(nc.sbuf_tensor(f"ln2m{i}", [128, D], F32))
            for i in range(2)]
    xpre_b = stC.enter_context(nc.sbuf_tensor("xpre_b", [128, 1024], BF16))
    xpT = stC.enter_context(nc.sbuf_tensor("xpT", [128, 8 * 128], BF16))
    ybuf = stC.enter_context(nc.sbuf_tensor("ybuf", [128, 1024], F32))
    obuf = [stC.enter_context(nc.sbuf_tensor(f"obuf{i}", [128, D], F32))
            for i in range(2)]

    p.dma("gpsimd", lambda e: e.dma_start(
        wbr_b[:, :], wbr_h.ap().rearrange("(c p) d -> p c d", p=128)), inc="wbld")
    wbl = p.cur("wbld")
    for tg in range(16):
        bi = tg % 2
        w = [("mix", max(0, p.cur("mix") - 1))] if tg >= 2 else []
        p.dma("sync", (lambda tg_, bi_: lambda e: e.dma_start(
            x2m[bi_][:, :], x2_h.ap()[tg_ * 128:(tg_ + 1) * 128, :]))(tg, bi),
            waits=tuple(w), inc=f"x2rd{bi}")
        x2rv = p.cur(f"x2rd{bi}")
        w = [(f"x2rd{bi}", x2rv)]
        if tg == 0:
            w.append(("wbld", wbl))
        p.emit("vector", (lambda bi_, tg_: lambda e: e.tensor_scalar(
            ln2m[bi_][:, :], x2m[bi_][:, :], m16[:, tg_:tg_ + 1],
            rs16[:, tg_:tg_ + 1], OP.subtract, OP.mult))(bi, tg), waits=tuple(w))
        p.emit("vector", (lambda bi_, tg_: lambda e: e.tensor_scalar_mul(
            xpre[:, :], ln2m[bi_][:, 0:1024], hp[:, 4 * tg_:4 * tg_ + 1]))(bi, tg))
        for n in range(1, 4):
            p.emit("vector", (lambda bi_, tg_, n_: lambda e: e.scalar_tensor_tensor(
                xpre[:, :], ln2m[bi_][:, n_ * 1024:(n_ + 1) * 1024],
                hp[:, 4 * tg_ + n_:4 * tg_ + n_ + 1], xpre[:, :],
                OP.mult, OP.add))(bi, tg, n))
        p.emit("vector", lambda e: e.tensor_copy(xpre_b[:, :], xpre[:, :]),
               waits=(("lgT", p.cur("lgT")),), inc="mix")
        mxv = p.cur("mix")
        for i in range(8):
            w = []
            if i == 0:
                w = [("mix", mxv), ("lgTe", p.cur("lgTe")), ("yev", p.cur("yev"))]
            p.emit("tensor", (lambda i_: lambda e: e.transpose(
                pbf[:, i_ * 128:(i_ + 1) * 128], xpre_b[:, i_ * 128:(i_ + 1) * 128],
                ident_b[:, :]))(i), waits=tuple(w), inc="lgT" if i == 7 else None)
        tvv = p.cur("lgT")
        for i in range(8):
            p.emit("vector", (lambda i_: lambda e: e.tensor_copy(
                xpT[:, i_ * 128:(i_ + 1) * 128], pbf[:, i_ * 128:(i_ + 1) * 128]))(i),
                waits=(("lgT", tvv),) if i == 0 else (),
                inc="lgTe" if i == 7 else None)
        tev = p.cur("lgTe")
        for n in range(2):
            for c in range(8):
                w = []
                if n == 0 and c == 0:
                    w = [("lgTe", tev), ("yev", p.cur("yev"))]
                p.emit("tensor", (lambda n_, c_, st_, sp_: lambda e: e.matmul(
                    pp[n_][:, :], xpT[:, c_ * 128:(c_ + 1) * 128],
                    wbr_b[:, c_ * 1024 + n_ * 512: c_ * 1024 + (n_ + 1) * 512],
                    start=st_, stop=sp_))(n, c, c == 0, c == 7),
                    waits=tuple(w), inc="ymm" if (n == 1 and c == 7) else None)
        ymv = p.cur("ymm")
        for n in range(2):
            p.emit("vector", (lambda n_: lambda e: e.tensor_copy(
                ybuf[:, n_ * 512:(n_ + 1) * 512], pp[n_][:, :]))(n),
                waits=(("ymm", ymv),) if n == 0 else (),
                inc="yev" if n == 1 else None)
        yev = p.cur("yev")
        for i in range(4):
            w = []
            if i == 0:
                w = [("yev", yev), (f"ost{bi}", p.cur(f"ost{bi}"))]
            p.emit("vector", (lambda bi_, tg_, i_: lambda e: e.scalar_tensor_tensor(
                obuf[bi_][:, i_ * 1024:(i_ + 1) * 1024], ln2m[bi_][:, 0:1024],
                Hm[:, 16 * tg_ + 4 * i_: 16 * tg_ + 4 * i_ + 1],
                x2m[bi_][:, i_ * 1024:(i_ + 1) * 1024],
                OP.mult, OP.add))(bi, tg, i),
                waits=tuple(w))
            for jj in range(1, 4):
                p.emit("vector",
                       (lambda bi_, tg_, i_, j_: lambda e: e.scalar_tensor_tensor(
                           obuf[bi_][:, i_ * 1024:(i_ + 1) * 1024],
                           ln2m[bi_][:, j_ * 1024:(j_ + 1) * 1024],
                           Hm[:, 16 * tg_ + 4 * i_ + j_: 16 * tg_ + 4 * i_ + j_ + 1],
                           obuf[bi_][:, i_ * 1024:(i_ + 1) * 1024],
                           OP.mult, OP.add))(bi, tg, i, jj))
            p.emit("vector", (lambda bi_, tg_, i_: lambda e: e.scalar_tensor_tensor(
                obuf[bi_][:, i_ * 1024:(i_ + 1) * 1024], ybuf[:, :],
                hq[:, 4 * tg_ + i_:4 * tg_ + i_ + 1],
                obuf[bi_][:, i_ * 1024:(i_ + 1) * 1024],
                OP.mult, OP.add))(bi, tg, i),
                inc="mix" if i == 3 else None)
        mxo = p.cur("mix")
        p.dma("sync", (lambda bi_, tg_: lambda e: e.dma_start(
            out_h.ap()[tg_ * 128:(tg_ + 1) * 128, :], obuf[bi_][:, :]))(bi, tg),
            waits=(("mix", mxo),), inc=f"ost{bi}")

    p.emit("gpsimd", None, waits=(("ost0", p.cur("ost0")), ("ost1", p.cur("ost1"))))
    p.emit("sync", None, waits=(("ost0", p.cur("ost0")), ("ost1", p.cur("ost1"))))

    _block_emit(nc, p)
    stC.close()
    stack.close()
    return nc


def _fused_masks():
    base = np.zeros((4, 128, 512), np.float32)
    kk = np.arange(128)[:, None]
    qq = np.arange(512)[None, :]
    for di, d in enumerate((0, 128, 256, 384)):
        base[di] = np.where(kk + d > qq, np.float32(-1e9), np.float32(0.0))
    diag = np.ascontiguousarray(base.reshape(512, 512))
    zero = np.zeros((512, 512), np.float32)
    return diag, zero


def _fused_rope_tables():
    i = np.arange(RD // 2, dtype=np.float64)
    inv_freq = 1.0 / (10000.0 ** (2.0 * i / RD))
    t = np.arange(S, dtype=np.float64)
    ang = np.outer(t, inv_freq)
    sin8 = np.tile(np.sin(ang).astype(np.float32), (1, 8))
    cos8 = np.tile(np.cos(ang).astype(np.float32), (1, 8))
    return np.ascontiguousarray(sin8), np.ascontiguousarray(cos8)


def kernel_fused(inputs):
    x = np.asarray(inputs["L0_attn_hidden_states"])
    wq = np.asarray(inputs["L0_attn_wq"])
    wk = np.asarray(inputs["L0_attn_wk"])
    wv = np.asarray(inputs["L0_attn_wv"])
    wo = np.asarray(inputs["L0_attn_wo"])

    sin8, cos8 = _fused_rope_tables()
    diagm, zerom = _fused_masks()
    ident = np.eye(128, dtype=np.float32)
    onesrow = np.ones((1, 128), np.float32)
    nonce = np.float32(np.random.default_rng().integers(1, 2 ** 30))
    noncev = np.full((1, 64), nonce, np.float32)

    xT = [np.ascontiguousarray(x[b].T) for b in range(B)]
    wT = {"q": wq, "k": wk, "v": wv}
    DQC = 2048
    wTs = {pj: [np.ascontiguousarray(wT[pj][r * DQC:(r + 1) * DQC, :].T)
                for r in range(2)] for pj in wT}
    woT = np.ascontiguousarray(wo.T)
    wallT = np.ascontiguousarray(np.concatenate(
        [np.asarray(inputs["L0_ffn_W_res"]), np.asarray(inputs["L0_ffn_W_pre"]),
         np.asarray(inputs["L0_ffn_W_post"])], axis=0).T)
    bases = np.concatenate(
        [np.asarray(inputs["L0_ffn_H_res_base"]).reshape(-1),
         np.asarray(inputs["L0_ffn_H_pre_base"]).reshape(-1),
         np.asarray(inputs["L0_ffn_H_post_base"]).reshape(-1)]
    ).reshape(1, 24).astype(np.float32)
    gate2 = np.asarray(inputs["L0_ffn_gate"]).reshape(1, 1).astype(np.float32)
    wbr = np.asarray(inputs["L0_ffn_W_branch"])

    nc = build_fused()
    in_maps = []
    for c in range(NCORE):
        r = c % 2
        pp_ = c // 2
        b = pp_ // 2
        h2 = pp_ % 2
        q0 = h2 * 4096
        tok0 = q0 + r * TPC
        in_maps.append(dict(
            xT=xT[b],
            xq=np.ascontiguousarray(xT[b][:, q0:q0 + 4096]),
            wqT=wTs["q"][r], wkT=wTs["k"][r], wvT=wTs["v"][r],
            sinq=np.ascontiguousarray(sin8[q0:q0 + 4096]),
            cosq=np.ascontiguousarray(cos8[q0:q0 + 4096]),
            sink=sin8, cosk=cos8,
            maskd1=(diagm if h2 == 0 else zerom),
            maskd2=(zerom if h2 == 0 else diagm),
            yflag=np.full((128, 1), -1e9 if h2 == 0 else 0.0, np.float32),
            roff=np.array([[r * TPC, r * DQC * 4096, (1 - r) * 64 and r * 0 or r * 64,
                            0]], dtype=np.int32),
            noncev=noncev,
            ident=ident, onesrow=onesrow,
            xres=np.ascontiguousarray(x[b, tok0:tok0 + TPC, :]),
            woT=woT, wallT=wallT, bases=bases, gate=gate2, wbr=wbr))
        in_maps[-1]["roff"] = np.array(
            [[r * TPC, r * DQC * 4096, r * 64, 0]], dtype=np.int32)
        ot = np.zeros((1, 256), np.int32)
        for k in range(4):
            ot[0, k] = r * DQC * 4096 + k * 2097152
        for gg in range(4):
            for spp in range(2):
                ot[0, 128 + gg * 2 + spp] = (r * TPC + gg * 512
                                             + spp * 16 * 128 * 4096)
        ot[0, 144] = r * 64
        ot[0, 145] = (1 - r) * 64
        in_maps[-1]["offtab"] = ot
    res = run_bass_kernel_spmd(nc, in_maps, core_ids=list(range(NCORE)))
    out = np.empty((B, S, D), np.float32)
    for c in range(NCORE):
        r = c % 2
        pp_ = c // 2
        b = pp_ // 2
        tok0 = (pp_ % 2) * 4096 + r * TPC
        out[b, tok0:tok0 + TPC, :] = res.results[c]["out"]
    return out


# version 07
# speedup vs baseline: 1.0424x; 1.0424x over previous
import os
import numpy as np
import ml_dtypes
from contextlib import ExitStack

import concourse.bass as bass
import concourse.mybir as mybir
from concourse.bass_utils import run_bass_kernel_spmd

F32 = mybir.dt.float32
BF16 = mybir.dt.bfloat16
AF = mybir.ActivationFunctionType
OP = mybir.AluOpType
AX = mybir.AxisListType

B, S, D = 2, 8192, 4096
NH, HD, RD = 16, 256, 64
HPC = 4
DPC = HPC * HD
TPC = 2048
NCORE = 8
EPS_LN = 1e-5
EPS_RMS = 1.1920929e-07
SINK_IT = 50
TAU = 0.1

ENGS = ("sync", "gpsimd", "tensor", "vector", "scalar")


CHAIN = {"vector": "vch", "scalar": "sch"}


class Plan:
    def __init__(self, sems):
        self.entries = {e: [] for e in ENGS}
        self.cnt = {}
        self.sems = sems
        self.prev_marker = {}

    def cur(self, name):
        return self.cnt.get(name, 0)

    def emit(self, eng, fn, waits=(), inc=None, inc_amt=1, chain=True):
        post = None
        if inc is not None:
            self.cnt[inc] = self.cnt.get(inc, 0) + inc_amt
            post = self.cnt[inc]
        autow = None
        ginc = None
        if chain and fn is not None and eng in CHAIN:
            autow = self.prev_marker.get(eng)
            if inc is None:
                cs = CHAIN[eng]
                self.cnt[cs] = self.cnt.get(cs, 0) + 1
                ginc = (cs, 1)
                self.prev_marker[eng] = (cs, self.cnt[cs])
            else:
                self.prev_marker[eng] = (inc, post)
        self.entries[eng].append((fn, tuple(waits), inc, inc_amt, autow, ginc))
        return post

    def dma(self, eng, fn, waits=(), inc=None):
        return self.emit(eng, fn, waits, inc, 16)

    def barrier(self, pre=None):
        pre = pre or {}
        base = self.cur("bar")
        for e in ENGS:
            if e in ("sync", "gpsimd"):
                self.emit(e, lambda eng: eng.nop(), waits=tuple(pre.get(e, ())),
                          inc="bar", chain=False)
            else:
                self.emit(e, lambda eng: eng.drain(), waits=tuple(pre.get(e, ())),
                          inc="bar", chain=False)
        tgt = base + len(ENGS)
        for e in ENGS:
            self.emit(e, None, waits=(("bar", tgt),))

    def replay(self, eng_name, eng):
        sems = self.sems
        for fn, waits, inc, amt, autow, ginc in self.entries[eng_name]:
            for (sn, val) in waits:
                if val > 0:
                    eng.wait_ge(sems[sn], val)
            if fn is None:
                continue
            if autow is not None:
                eng.wait_ge(sems[autow[0]], autow[1])
            inst = fn(eng)
            if inst is None:
                inst = eng.nop()
            if ginc is not None:
                inst.then_inc(sems[ginc[0]], ginc[1])
            elif inc is not None:
                inst.then_inc(sems[inc], amt)


def _block_emit(nc, p):
    with nc.Block() as block:
        @block.sync
        def _(eng):
            p.replay("sync", eng)

        @block.gpsimd
        def _(eng):
            p.replay("gpsimd", eng)

        @block.tensor
        def _(eng):
            p.replay("tensor", eng)

        @block.vector
        def _(eng):
            p.replay("vector", eng)

        @block.scalar
        def _(eng):
            p.replay("scalar", eng)



def build_launch_a(phases=None):
    if phases is None:
        phases = int(os.environ.get("KA_PHASES", "3"))
    NOROPE = os.environ.get("KA_NOROPE", "") == "1"
    RN = int(os.environ.get("KA_RN", "5"))
    NONORM = os.environ.get("KA_NONORM", "") == "1"
    NOTRANS = os.environ.get("KA_NOTRANS", "") == "1"
    PROJS = os.environ.get("KA_PROJS", "qkv")
    nc = bass.Bass("TRN2", target_bir_lowering=False)

    xT_h = nc.dram_tensor("xT", [D, S], F32, kind="ExternalInput")
    w_h = {pj: nc.dram_tensor(f"w{pj}T", [D, DPC], F32, kind="ExternalInput")
           for pj in ("q", "k", "v")}
    sinr_h = nc.dram_tensor("sinr", [S, 128], F32, kind="ExternalInput")
    cosr_h = nc.dram_tensor("cosr", [S, 128], F32, kind="ExternalInput")
    masks_h = nc.dram_tensor("masks", [512, 512], F32, kind="ExternalInput")
    ident_h = nc.dram_tensor("ident", [128, 128], F32, kind="ExternalInput")
    onesrow_h = nc.dram_tensor("onesrow", [1, 128], F32, kind="ExternalInput")

    attn_h = nc.dram_tensor("attn", [DPC, S], BF16, kind="ExternalOutput")

    xbf_h = nc.dram_tensor("xbf", [D, S], BF16, kind="Internal")
    qt_h = nc.dram_tensor("qt", [16, DPC, 512], BF16, kind="Internal")
    kt_h = nc.dram_tensor("kt", [16, DPC, 512], BF16, kind="Internal")
    vt_h = nc.dram_tensor("vt", [HPC, S, HD], BF16, kind="Internal")

    sem_names = ["bar", "vch", "sch", "xin0", "xin1", "xcast", "xwr0", "xwr1", "sq", "smm",
                 "sev", "fin", "fsq", "fre", "ftp",
                 "wld", "xbld0", "xbld1", "qmm", "bkf", "ropa", "ropb", "scld", "nrm",
                 "tps", "tpe", "qst",
                 "ktld", "qcld", "tsc", "vmsk", "sexp", "tpv", "vrs",
                 "trb", "voev", "ost", "cns"]
    stack = ExitStack()
    sems = {n: stack.enter_context(nc.semaphore(n)) for n in sem_names}
    p = Plan(sems)

    ident_f = nc.alloc_sbuf_tensor("ident_f", [128, 128], F32)
    ident_b = nc.alloc_sbuf_tensor("ident_b", [128, 128], BF16)
    onesrow_f = nc.alloc_sbuf_tensor("onesrow_f", [1, 128], F32)
    ones_b = nc.alloc_sbuf_tensor("ones_b", [128, 1], BF16)
    masks_sb = nc.alloc_sbuf_tensor("masks_sb", [128, 4 * 512], F32)
    negm_row = nc.alloc_sbuf_tensor("negm_row", [1, S], BF16)
    s_sum = nc.alloc_sbuf_tensor("s_sum", [1, 256], F32)
    s_sq = nc.alloc_sbuf_tensor("s_sq", [1, 256], F32)
    s_m = nc.alloc_sbuf_tensor("s_m", [1, 256], F32)
    r_cols = nc.alloc_sbuf_tensor("r_cols", [128, 64], F32)
    qmax2 = nc.alloc_sbuf_tensor("qmax2", [128, HPC], F32)
    kmax2 = nc.alloc_sbuf_tensor("kmax2", [128, HPC], F32)
    negcb = nc.alloc_sbuf_tensor("negcb", [128, HPC], F32)
    negc_row = nc.alloc_sbuf_tensor("negc_row", [1, HPC], F32)
    epsc = nc.alloc_sbuf_tensor("epsc", [1, 1], F32)

    pp = [nc.alloc_psum_tensor(f"pp{i}", [128, 512], F32) for i in range(4)]
    ps_a = nc.alloc_psum_tensor("ps_a", [1, 512], F32)
    ps_b = nc.alloc_psum_tensor("ps_b", [1, 512], F32)
    pbf = nc.alloc_psum_tensor("pbf", [128, 1024], BF16)
    pmisc = nc.alloc_psum_tensor("pmisc", [128, 512], F32)

    xT_v = xT_h.ap().rearrange("(c p) t -> p c t", p=128)
    xbf_v = xbf_h.ap().rearrange("(c p) t -> p c t", p=128)

    p.dma("sync", lambda e: e.dma_start(ident_f[:, :], ident_h.ap()[:, :]), inc="cns")
    p.dma("sync", lambda e: e.dma_start(onesrow_f[:, :], onesrow_h.ap()[:, :]),
          inc="cns")
    p.dma("sync", lambda e: e.dma_start(
        masks_sb[:, :], masks_h.ap().rearrange("(m p) t -> p m t", p=128)), inc="cns")
    p.emit("vector", lambda e: e.tensor_copy(ident_b[:, :], ident_f[:, :]),
           waits=(("cns", 48),))
    p.emit("vector", lambda e: e.memset(ones_b[:, :], 1.0))
    p.emit("vector", lambda e: e.memset(qmax2[:, :], 0.0))
    p.emit("vector", lambda e: e.memset(kmax2[:, :], 0.0))
    p.emit("vector", lambda e: e.memset(epsc[:, :], EPS_LN), inc="cns")

    st1 = ExitStack()
    NCH = 32
    CW = S // NCH
    xin = [st1.enter_context(nc.sbuf_tensor(f"xin{i}", [128, 32 * CW], F32))
           for i in range(2)]
    xbf_sb = [st1.enter_context(nc.sbuf_tensor(f"xbfs{i}", [128, 32 * CW], BF16))
              for i in range(2)]
    xsq_sb = [st1.enter_context(nc.sbuf_tensor(f"xsqs{i}", [128, 32 * CW], BF16))
              for i in range(2)]

    for ch in range(NCH):
        bi = ch % 2
        c0 = ch * CW
        for sp in range(4):
            w = [("xcast", ch - 1)] if (ch >= 2 and sp == 0) else []
            p.dma("sync", (lambda bi_, sp_, c0_: lambda e: e.dma_start(
                xin[bi_][:, sp_ * 8 * CW:(sp_ + 1) * 8 * CW],
                xT_v[:, sp_ * 8:(sp_ + 1) * 8, c0_:c0_ + CW]))(bi, sp, c0),
                waits=tuple(w), inc=f"xin{bi}")
        w = [(f"xin{bi}", 64 * (ch // 2 + 1))]
        if ch >= 2:
            w += [(f"xwr{bi}", 64 * (ch // 2)), ("smm", ch - 1), ("sq", ch - 1)]
        p.emit("vector", (lambda bi_: lambda e: e.tensor_copy(
            xbf_sb[bi_][:, :], xin[bi_][:, :]))(bi), waits=tuple(w), inc="xcast")
        xc = p.cur("xcast")
        w = [("xcast", xc)]
        if ch >= 2:
            w.append(("smm", ch - 1))
        p.emit("scalar", (lambda bi_: lambda e: e.square(
            xsq_sb[bi_][:, :], xbf_sb[bi_][:, :]))(bi), waits=tuple(w), inc="sq")
        for sp in range(4):
            w = [("xcast", xc)] if sp == 0 else []
            p.dma("gpsimd", (lambda bi_, sp_, c0_: lambda e: e.dma_start(
                xbf_v[:, sp_ * 8:(sp_ + 1) * 8, c0_:c0_ + CW],
                xbf_sb[bi_][:, sp_ * 8 * CW:(sp_ + 1) * 8 * CW]))(bi, sp, c0),
                waits=tuple(w), inc=f"xwr{bi}")
        sqv = p.cur("sq")
        for dc in range(32):
            w = []
            if dc == 0:
                w = [("xcast", xc), ("sq", sqv), ("sev", ch)]
            p.emit("tensor", (lambda bi_, dc_, st_, sp_: lambda e: e.matmul(
                ps_a[0:1, 0:CW], ones_b[:, :],
                xbf_sb[bi_][:, dc_ * CW:(dc_ + 1) * CW], start=st_, stop=sp_))(
                    bi, dc, dc == 0, dc == 31), waits=tuple(w))
            p.emit("tensor", (lambda bi_, dc_, st_, sp_: lambda e: e.matmul(
                ps_b[0:1, 0:CW], ones_b[:, :],
                xsq_sb[bi_][:, dc_ * CW:(dc_ + 1) * CW], start=st_, stop=sp_))(
                    bi, dc, dc == 0, dc == 31),
                inc="smm" if dc == 31 else None)
        smv = p.cur("smm")
        w = [("smm", smv)]
        if ch >= 1:
            w.append(("ftp", ch))
        p.emit("vector", lambda e: e.tensor_copy(s_sum[0:1, :], ps_a[0:1, 0:CW]),
               waits=tuple(w))
        p.emit("vector", lambda e: e.tensor_copy(s_sq[0:1, :], ps_b[0:1, 0:CW]),
               inc="sev")
        p.emit("vector", (lambda c0_: lambda e: e.tensor_scalar_mul(
            negm_row[0:1, c0_:c0_ + CW], s_sum[0:1, :], -1.0 / D))(c0))
        p.emit("vector", lambda e: e.tensor_scalar_mul(s_m[0:1, :], s_sum[0:1, :],
                                                       1.0 / D))
        p.emit("vector", lambda e: e.tensor_mul(s_m[0:1, :], s_m[0:1, :],
                                                s_m[0:1, :]))
        p.emit("vector", lambda e: e.tensor_scalar_mul(s_sq[0:1, :], s_sq[0:1, :],
                                                       1.0 / D))
        p.emit("vector", lambda e: e.tensor_sub(s_sq[0:1, :], s_sq[0:1, :],
                                                s_m[0:1, :]), inc="fin")
        p.emit("scalar", lambda e: e.activation(
            s_sq[0:1, :], s_sq[0:1, :], AF.Sqrt, bias=epsc[0:1, 0:1], scale=1.0),
            waits=(("fin", ch + 1), ("cns", p.cur("cns"))), inc="fsq")
        p.emit("vector", lambda e: e.reciprocal(s_sq[0:1, :], s_sq[0:1, :]),
               waits=(("fsq", ch + 1),), inc="fre")
        for k in range(2):
            p.emit("tensor", (lambda ch_, k_: lambda e: e.transpose(
                pmisc[:, 2 * ch_ + k_:2 * ch_ + k_ + 1],
                s_sq[0:1, k_ * 128:(k_ + 1) * 128], ident_f[0:1, 0:1]))(ch, k),
                waits=(("fre", ch + 1),) if k == 0 else (),
                inc="ftp" if k == 1 else None)

    p.emit("vector", lambda e: e.tensor_copy(r_cols[:, :], pmisc[:, 0:64]),
           waits=(("ftp", NCH),), inc="sev")

    p.barrier(pre={"gpsimd": [("xwr0", 64 * (NCH // 2)), ("xwr1", 64 * (NCH // 2))],
                   "vector": [("sev", p.cur("sev")), ("cns", p.cur("cns"))]})
    st1.close()

    st2 = ExitStack()
    wbuf = st2.enter_context(nc.sbuf_tensor("wbuf", [128, 32 * DPC], BF16))
    xblk = st2.enter_context(nc.sbuf_tensor("xblk", [128, 32 * 1024], BF16))
    wrs_f = st2.enter_context(nc.sbuf_tensor("wrs_f", [1, DPC], F32))
    wrs_b = st2.enter_context(nc.sbuf_tensor("wrs_b", [1, DPC], BF16))
    qstage = st2.enter_context(nc.sbuf_tensor("qstage", [128, DPC], BF16))
    sqbuf = st2.enter_context(nc.sbuf_tensor("sqbuf", [128, DPC], BF16))
    nrmt = st2.enter_context(nc.sbuf_tensor("nrmt", [128, HPC], F32))
    robuf = st2.enter_context(nc.sbuf_tensor("robuf", [128, HPC * RD], F32))
    robuf2 = st2.enter_context(nc.sbuf_tensor("robuf2", [128, HPC * RD], F32))
    robt = st2.enter_context(nc.sbuf_tensor("robt", [128, HPC * 32], F32))
    robt2 = st2.enter_context(nc.sbuf_tensor("robt2", [128, HPC * 32], F32))
    sc_sin = st2.enter_context(nc.sbuf_tensor("sc_sin", [128, 128], F32))
    sc_cos = st2.enter_context(nc.sbuf_tensor("sc_cos", [128, 128], F32))
    qt_blk = st2.enter_context(nc.sbuf_tensor("qt_blk", [128, 8 * 512], BF16))

    out_map = {"q": qt_h, "k": kt_h}
    mx_map = {"q": qmax2, "k": kmax2}

    for pj in (tuple(PROJS) if phases >= 2 else ()):
        wT_v = w_h[pj].ap().rearrange("(c p) d -> p c d", p=128)
        for sp in range(4):
            p.dma("gpsimd", (lambda sp_, t_: lambda e: e.dma_start(
                wbuf[:, sp_ * 8 * DPC:(sp_ + 1) * 8 * DPC],
                t_[:, sp_ * 8:(sp_ + 1) * 8, :]))(sp, wT_v), inc="wld")
        wl = p.cur("wld")
        for dc in range(2):
            tgt = ps_a if dc == 0 else ps_b
            for c in range(32):
                w = [("wld", wl)] if (dc == 0 and c == 0) else []
                p.emit("tensor", (lambda t_, c_, dc_, st_, sp_: lambda e: e.matmul(
                    t_[0:1, :], ones_b[:, :],
                    wbuf[:, c_ * DPC + dc_ * 512: c_ * DPC + (dc_ + 1) * 512],
                    start=st_, stop=sp_))(tgt, c, dc, c == 0, c == 31),
                    waits=tuple(w),
                    inc="qmm" if (c == 31 and dc == 1) else None)
        qm = p.cur("qmm")
        p.emit("vector", lambda e: e.tensor_copy(wrs_f[0:1, 0:512], ps_a[0:1, :]),
               waits=(("qmm", qm),))
        p.emit("vector", lambda e: e.tensor_copy(wrs_f[0:1, 512:1024], ps_b[0:1, :]))
        p.emit("vector", lambda e: e.tensor_copy(wrs_b[0:1, :], wrs_f[0:1, :]),
               inc="bkf")
        for blk in range(8):
            t0 = blk * 1024
            for sp in range(4):
                w = []
                if sp == 0:
                    w.append(("bkf", p.cur("bkf")))
                    w.append(("qmm", p.cur("qmm")))
                p.dma("sync", (lambda sp_, t0_: lambda e: e.dma_start(
                    xblk[:, sp_ * 8 * 1024:(sp_ + 1) * 8 * 1024],
                    xbf_v[:, sp_ * 8:(sp_ + 1) * 8, t0_:t0_ + 1024]))(sp, t0),
                    waits=tuple(w), inc="xbld")
            xbl = p.cur("xbld")
            for ti in range(8):
                tg = blk * 8 + ti
                if pj != "v" and not NOROPE and RN >= 1:
                    w = [("ropb", p.cur("ropb"))]
                    p.dma("sync", (lambda tg_: lambda e: e.dma_start(
                        sc_sin[:, :], sinr_h.ap()[tg_ * 128:(tg_ + 1) * 128, :]))(tg),
                        waits=tuple(w), inc="scld")
                    p.dma("sync", (lambda tg_: lambda e: e.dma_start(
                        sc_cos[:, :], cosr_h.ap()[tg_ * 128:(tg_ + 1) * 128, :]))(tg),
                        inc="scld")
                scl = p.cur("scld")
                for dc in range(2):
                    bank = pp[dc]
                    for c in range(32):
                        w = []
                        if c == 0:
                            w = [(xbln, xbl), ("bkf", p.cur("bkf"))]
                        p.emit("tensor",
                               (lambda c_, ti_, dc_, st_, b_: lambda e: e.matmul(
                                   b_[:, :],
                                   xblk[:, c_ * 1024 + ti_ * 128:
                                        c_ * 1024 + ti_ * 128 + 128],
                                   wbuf[:, c_ * DPC + dc_ * 512:
                                        c_ * DPC + (dc_ + 1) * 512],
                                   start=st_, stop=False))(c, ti, dc, c == 0, bank),
                               waits=tuple(w))
                    p.emit("tensor", (lambda tg_, dc_, b_: lambda e: e.matmul(
                        b_[:, :], negm_row[0:1, tg_ * 128:(tg_ + 1) * 128],
                        wrs_b[0:1, dc_ * 512:(dc_ + 1) * 512],
                        start=False, stop=True))(tg, dc, bank), inc="qmm")
                    qmm = p.cur("qmm")
                    if pj != "v" and not NOROPE and RN >= 2:
                        p.emit("vector", (lambda dc_, tg_, b_: lambda e:
                               e.tensor_scalar_mul(
                                   robuf[:, :].rearrange(
                                       "p (h r) -> p h r", r=RD)[:, 2 * dc_:2 * dc_ + 2, :],
                                   b_[:, :].rearrange(
                                       "p (h r) -> p h r", r=HD)[:, :, 0:RD],
                                   r_cols[:, tg_:tg_ + 1]))(dc, tg, bank),
                               waits=(("qmm", qmm),), inc="ropa")
                    wse = [("qmm", qmm)]
                    if pj != "v":
                        wse.append(("ropa", p.cur("ropa")))
                    if dc == 0:
                        if pj == "v":
                            wse.append(("qst", p.cur("qst")))
                        else:
                            wse.append(("tps", p.cur("tps")))
                    p.emit("scalar", (lambda dc_, tg_, b_: lambda e: e.activation(
                        qstage[:, dc_ * 512:(dc_ + 1) * 512], b_[:, :], AF.Copy,
                        bias=0.0, scale=r_cols[:, tg_:tg_ + 1]))(dc, tg, bank),
                        waits=tuple(wse), inc="bkf")
                qevv = p.cur("bkf")
                if pj != "v":
                    ropv = p.cur("ropa")
                    do_rope = not NOROPE
                    rb = robuf[:, :].rearrange("p (h i two) -> p h i two", i=32, two=2)
                    rb2 = robuf2[:, :].rearrange("p (h i two) -> p h i two", i=32, two=2)
                    sv = sc_sin[:, :].rearrange("p (h i) -> p h i", i=32)
                    cv = sc_cos[:, :].rearrange("p (h i) -> p h i", i=32)
                    tv = robt[:, :].rearrange("p (h i) -> p h i", i=32)
                    tv2 = robt2[:, :].rearrange("p (h i) -> p h i", i=32)
                    if do_rope and RN >= 3:
                        p.emit("vector", lambda e: e.tensor_mul(tv, rb[:, :, :, 1], sv),
                               waits=(("ropa", ropv), ("scld", scl)))
                        p.emit("vector", lambda e: e.tensor_mul(tv2, rb[:, :, :, 0], sv))
                        if RN >= 4:
                            p.emit("vector", lambda e: e.tensor_mul(
                                rb2[:, :, :, 0], rb[:, :, :, 0], cv))
                            p.emit("vector", lambda e: e.tensor_sub(
                                rb2[:, :, :, 0], rb2[:, :, :, 0], tv))
                            p.emit("vector", lambda e: e.tensor_mul(
                                rb2[:, :, :, 1], rb[:, :, :, 1], cv))
                            p.emit("vector", lambda e: e.tensor_add(
                                rb2[:, :, :, 1], rb2[:, :, :, 1], tv2), inc="ropb")
                        else:
                            p.emit("vector", lambda e: e.tensor_copy(
                                robuf2[:, :], robuf[:, :]), inc="ropb")
                        if RN >= 5:
                            p.emit("vector", lambda e: e.tensor_copy(
                                qstage[:, :].rearrange("p (h r) -> p h r", r=HD)[:, :, 0:RD],
                                robuf2[:, :].rearrange("p (h r) -> p h r", r=RD)),
                                waits=(("bkf", qevv),), inc="ropb")
                    ropd = p.cur("ropb")
                    if not NONORM:
                        p.emit("scalar", lambda e: e.square(sqbuf[:, :], qstage[:, :]),
                               waits=(("ropb", ropd), ("bkf", qevv)), inc="sq")
                        p.emit("vector", lambda e: e.reduce_sum(
                            nrmt[:, :], sqbuf[:, :].rearrange("p (h d) -> p h d", d=HD),
                            axis=AX.X), waits=(("sq", p.cur("sq")),))
                        p.emit("vector", (lambda m_: lambda e: e.tensor_max(
                            m_[:, :], m_[:, :], nrmt[:, :]))(mx_map[pj]), inc="nrm")
                    for i in (range(8) if not NOTRANS else ()):
                        w = []
                        if i == 0:
                            w = [("ropb", ropd), ("bkf", qevv), ("tpe", p.cur("tpe"))]
                        p.emit("tensor", (lambda i_: lambda e: e.transpose(
                            pbf[:, i_ * 128:(i_ + 1) * 128],
                            qstage[:, i_ * 128:(i_ + 1) * 128], ident_b[:, :]))(i),
                            waits=tuple(w), inc="tps" if i == 7 else None)
                    tpsv = p.cur("tps")
                    for i in (range(8) if not NOTRANS else ()):
                        w = [("tps", tpsv)] if i == 0 else []
                        if i == 0 and tg % 4 == 0:
                            w.append(("qst", p.cur("qst")))
                        p.emit("vector", (lambda i_, t4_: lambda e: e.tensor_copy(
                            qt_blk[:, i_ * 512 + t4_ * 128:
                                   i_ * 512 + t4_ * 128 + 128],
                            pbf[:, i_ * 128:(i_ + 1) * 128]))(i, tg % 4),
                            waits=tuple(w), inc="tpe" if i == 7 else None)
                    tpev = p.cur("tpe")
                    if tg % 4 == 3 and not NOTRANS:
                        qc = tg // 4
                        dst = out_map[pj].ap()
                        for i in range(8):
                            w = [("tpe", tpev)] if i == 0 else []
                            p.dma("sync", (lambda i_, qc_, d_: lambda e: e.dma_start(
                                d_[qc_, i_ * 128:(i_ + 1) * 128, :],
                                qt_blk[:, i_ * 512:(i_ + 1) * 512]))(i, qc, dst),
                                waits=tuple(w), inc="qst")
                else:
                    for h in range(HPC):
                        w = [("bkf", qevv)] if h == 0 else []
                        p.dma("sync", (lambda h_, tg_: lambda e: e.dma_start(
                            vt_h.ap()[h_, tg_ * 128:(tg_ + 1) * 128, :],
                            qstage[:, h_ * HD:(h_ + 1) * HD]))(h, tg),
                            waits=tuple(w), inc="qst")
        p.barrier(pre={"sync": [("qst", p.cur("qst"))],
                       "vector": [("nrm", p.cur("nrm"))],
                       "scalar": [("bkf", p.cur("bkf"))]})

    if phases < 2:
        p.barrier()
    p.emit("vector", lambda e: e.tensor_add(qmax2[:, :], qmax2[:, :], kmax2[:, :]),
           inc="sev")
    p.emit("tensor", lambda e: e.transpose(
        pmisc[0:HPC, 0:128], qmax2[:, :], ident_f[:, :]),
        waits=(("sev", p.cur("sev")),), inc="smm")
    p.emit("vector", lambda e: e.reduce_max(
        qmax2[0:HPC, 0:1], pmisc[0:HPC, 0:128], axis=AX.X),
        waits=(("smm", p.cur("smm")),))
    p.emit("vector", lambda e: e.tensor_scalar_mul(
        qmax2[0:HPC, 0:1], qmax2[0:HPC, 0:1], -1.0 / 32.0), inc="sev")
    sevn = p.cur("sev")
    p.emit("tensor", lambda e: e.transpose(
        pmisc[0:1, 0:HPC], qmax2[0:HPC, 0:1], ident_f[0:HPC, 0:HPC]),
        waits=(("sev", sevn),), inc="smm")
    p.emit("vector", lambda e: e.tensor_copy(negc_row[0:1, :], pmisc[0:1, 0:HPC]),
           waits=(("smm", p.cur("smm")),), inc="sev")
    sevn2 = p.cur("sev")
    for h in range(HPC):
        p.emit("tensor", (lambda h_: lambda e: e.matmul(
            pmisc[:, h_:h_ + 1], onesrow_f[0:1, :], negc_row[0:1, h_:h_ + 1],
            start=True, stop=True))(h),
            waits=(("sev", sevn2),) if h == 0 else (),
            inc="smm" if h == HPC - 1 else None)
    p.emit("vector", lambda e: e.tensor_copy(negcb[:, :], pmisc[:, 0:HPC]),
           waits=(("smm", p.cur("smm")),), inc="sev")
    p.barrier(pre={"vector": [("sev", p.cur("sev"))]})
    st2.close()

    st3 = ExitStack()
    kt_sb = st3.enter_context(nc.sbuf_tensor("kt_sb", [128, 2 * S], BF16))
    vt_sb = st3.enter_context(nc.sbuf_tensor("vt_sb", [128, 64 * HD], BF16))
    qch = [st3.enter_context(nc.sbuf_tensor(f"qch{i}", [128, 2 * 512], BF16))
           for i in range(2)]
    psb = [st3.enter_context(nc.sbuf_tensor(f"psb{i}", [128, 512], BF16))
           for i in range(2)]
    srow = st3.enter_context(nc.sbuf_tensor("srow", [1, 512], F32))
    rbf = st3.enter_context(nc.sbuf_tensor("rbf", [128, 512], F32))
    ostage = st3.enter_context(nc.sbuf_tensor("ostage", [128, 1024], BF16))

    gi = 0
    for h in (range(HPC) if phases >= 3 else ()):
        for dh in range(2):
            w = []
            if h > 0 and dh == 0:
                w = [("tsc", p.cur("tsc")), ("tpv", p.cur("tpv"))]
            p.dma("sync", (lambda h_, dh_: lambda e: e.dma_start(
                kt_sb[:, dh_ * S:(dh_ + 1) * S],
                bass.AP(kt_h, (h_ * HD + dh_ * 128) * 512,
                        [[512, 128], [DPC * 512, 16], [1, 512]])))(h, dh),
                waits=tuple(w), inc="ktld")
        for sp in range(8):
            p.dma("sync", (lambda h_, sp_: lambda e: e.dma_start(
                vt_sb[:, sp_ * 8 * HD:(sp_ + 1) * 8 * HD],
                bass.AP(vt_h, h_ * S * HD + sp_ * 8 * 128 * HD,
                        [[HD, 128], [128 * HD, 8], [1, HD]])))(h, sp),
                inc="ktld")
        ktl = p.cur("ktld")

        for qc in range(16):
            qb = qc % 2
            for dh in range(2):
                w = []
                if dh == 0:
                    w.append(("tsc", p.cur("tsc")))
                    if qc == 0:
                        w.append(("ktld", ktl))
                p.dma("sync", (lambda h_, dh_, qc_, qb_: lambda e: e.dma_start(
                    qch[qb_][:, dh_ * 512:(dh_ + 1) * 512],
                    bass.AP(qt_h, qc_ * DPC * 512 + (h_ * HD + dh_ * 128) * 512,
                            [[512, 128], [1, 512]])))(h, dh, qc, qb),
                    waits=tuple(w), inc="qcld")
            qcl = p.cur("qcld")
            nj = 4 * qc + 4
            for j in range(nj):
                sbank = pp[gi % 2]
                pb = gi % 2
                diag = j >= 4 * qc
                w = []
                if j == 0:
                    w.append(("qcld", qcl))
                if gi >= 2:
                    w.append(("sexp", gi - 1))
                for dh in range(2):
                    p.emit("tensor", (lambda j_, dh_, qb_, st_, sp_, b_:
                           lambda e: e.matmul(
                               b_[:, :],
                               kt_sb[:, dh_ * S + j_ * 128: dh_ * S + j_ * 128 + 128],
                               qch[qb_][:, dh_ * 512:(dh_ + 1) * 512],
                               start=st_, stop=sp_))(j, dh, qb, dh == 0, dh == 1, sbank),
                           waits=tuple(w) if dh == 0 else (),
                           inc="tsc" if dh == 1 else None)
                tscv = p.cur("tsc")
                if diag:
                    d = j - 4 * qc
                    p.emit("vector", (lambda d_, b_: lambda e: e.tensor_add(
                        b_[:, :], b_[:, :], masks_sb[:, d_ * 512:(d_ + 1) * 512]))(
                            d, sbank), waits=(("tsc", tscv),), inc="vmsk")
                w = [("vmsk", p.cur("vmsk"))] if diag else [("tsc", tscv)]
                if gi >= 2:
                    w.append(("tpv", gi - 1))
                p.emit("scalar", (lambda h_, b_, pb_: lambda e: e.activation(
                    psb[pb_][:, :], b_[:, :], AF.Exp,
                    bias=negcb[:, h_:h_ + 1], scale=1.0 / 16.0))(h, sbank, pb),
                    waits=tuple(w), inc="sexp")
                sev_ = p.cur("sexp")

                def emit_pv(j_, pb_, sexp_tgt):
                    for dh in range(2):
                        w2 = []
                        if dh == 0:
                            w2.append(("sexp", sexp_tgt))
                            if j_ == 0:
                                w2.append(("voev", p.cur("voev")))
                        p.emit("tensor",
                               (lambda j2, dh_, pb2, st_, sp_: lambda e: e.matmul(
                                   pp[2 + dh_][:, :],
                                   vt_sb[:, j2 * HD + dh_ * 128:
                                         j2 * HD + dh_ * 128 + 128],
                                   psb[pb2][:, :], start=st_, stop=sp_))(
                                       j_, dh, pb_, j_ == 0, j_ == nj - 1),
                               waits=tuple(w2))
                    w2 = []
                    if j_ == 0:
                        w2.append(("vrs", p.cur("vrs")))
                    p.emit("tensor", (lambda pb2, st_, sp_: lambda e: e.matmul(
                        ps_a[0:1, :], ones_b[:, :], psb[pb2][:, :],
                        start=st_, stop=sp_))(pb_, j_ == 0, j_ == nj - 1),
                        waits=tuple(w2), inc="tpv")

                if pend is not None:
                    emit_pv(*pend)
                pend = (j, pb, sev_)
                if j == nj - 1:
                    emit_pv(*pend)
                    pend = None
                gi += 1
            tpvv = p.cur("tpv")
            p.emit("vector", lambda e: e.tensor_copy(srow[0:1, :], ps_a[0:1, :]),
                   waits=(("tpv", tpvv),))
            p.emit("vector", lambda e: e.reciprocal(srow[0:1, :], srow[0:1, :]),
                   inc="vrs")
            p.emit("tensor", lambda e: e.matmul(
                pmisc[:, :], onesrow_f[0:1, :], srow[0:1, :], start=True, stop=True),
                waits=(("vrs", p.cur("vrs")), ("voev", p.cur("voev"))), inc="trb")
            p.emit("vector", lambda e: e.tensor_copy(rbf[:, :], pmisc[:, :]),
                   waits=(("trb", p.cur("trb")), ("ost", p.cur("ost"))))
            for dh in range(2):
                p.emit("vector", (lambda dh_: lambda e: e.tensor_mul(
                    ostage[:, dh_ * 512:(dh_ + 1) * 512], pp[2 + dh_][:, :],
                    rbf[:, :]))(dh), inc="voev" if dh == 1 else None)
            vov = p.cur("voev")
            for dh in range(2):
                w = [("voev", vov)] if dh == 0 else []
                p.dma("sync", (lambda h_, dh_, qc_: lambda e: e.dma_start(
                    bass.AP(attn_h, (h_ * HD + dh_ * 128) * S + qc_ * 512,
                            [[S, 128], [1, 512]]),
                    ostage[:, dh_ * 512:(dh_ + 1) * 512]))(h, dh, qc),
                    waits=tuple(w), inc="ost")

    p.emit("gpsimd", None, waits=(("ost", p.cur("ost")),))
    p.emit("sync", None, waits=(("ost", p.cur("ost")),))

    _block_emit(nc, p)
    st3.close()
    stack.close()
    return nc



def build_launch_b():
    nc = bass.Bass("TRN2", target_bir_lowering=False)

    asl_h = nc.dram_tensor("aslice", [D, TPC], BF16, kind="ExternalInput")
    xres_h = nc.dram_tensor("xres", [TPC, D], F32, kind="ExternalInput")
    woT_h = nc.dram_tensor("woT", [D, D], F32, kind="ExternalInput")
    wallT_h = nc.dram_tensor("wallT", [D, 24], F32, kind="ExternalInput")
    bases_h = nc.dram_tensor("bases", [1, 24], F32, kind="ExternalInput")
    gate_h = nc.dram_tensor("gate", [1, 1], F32, kind="ExternalInput")
    wbr_h = nc.dram_tensor("wbr", [1024, 1024], F32, kind="ExternalInput")
    ident_h = nc.dram_tensor("ident", [128, 128], F32, kind="ExternalInput")
    onesrow_h = nc.dram_tensor("onesrow", [1, 128], F32, kind="ExternalInput")

    out_h = nc.dram_tensor("out", [TPC, D], F32, kind="ExternalOutput")
    x2_h = nc.dram_tensor("x2h", [TPC, D], F32, kind="Internal")

    sem_names = ["bar", "vch", "sch", "cns", "ald", "wld0", "wld1", "xrd", "womm", "x2ev",
                 "x2wr0", "x2wr1", "sq", "rms", "lgT", "lgTe", "lgmm", "lgev",
                 "snk", "x2rd0", "x2rd1", "mix", "ymm", "yev", "ost0", "ost1",
                 "wbld", "wcns"]
    stack = ExitStack()
    sems = {n: stack.enter_context(nc.semaphore(n)) for n in sem_names}
    p = Plan(sems)

    ident_f = nc.alloc_sbuf_tensor("ident_f", [128, 128], F32)
    ident_b = nc.alloc_sbuf_tensor("ident_b", [128, 128], BF16)
    onesrow_f = nc.alloc_sbuf_tensor("onesrow_f", [1, 128], F32)
    wall_b = nc.alloc_sbuf_tensor("wall_b", [128, 32 * 24], BF16)
    bases_row = nc.alloc_sbuf_tensor("bases_row", [1, 24], F32)
    basesb = nc.alloc_sbuf_tensor("basesb", [128, 24], F32)
    gsc = nc.alloc_sbuf_tensor("gsc", [1, 4], F32)
    gb = nc.alloc_sbuf_tensor("gb", [128, 1], F32)
    omgb = nc.alloc_sbuf_tensor("omgb", [128, 1], F32)
    r2c = nc.alloc_sbuf_tensor("r2c", [128, 16], F32)
    m16 = nc.alloc_sbuf_tensor("m16", [128, 16], F32)
    rs16 = nc.alloc_sbuf_tensor("rs16", [128, 16], F32)
    sc1 = nc.alloc_sbuf_tensor("sc1", [128, 1], F32)
    epsb = nc.alloc_sbuf_tensor("epsb", [128, 1], F32)
    epsl = nc.alloc_sbuf_tensor("epsl", [128, 1], F32)
    lga = nc.alloc_sbuf_tensor("lga", [128, 16 * 24], F32)
    Z = nc.alloc_sbuf_tensor("Z", [128, 256], F32)
    tmp = nc.alloc_sbuf_tensor("tmp", [128, 256], F32)
    tmpe = nc.alloc_sbuf_tensor("tmpe", [128, 256], F32)
    ssum = nc.alloc_sbuf_tensor("ssum", [128, 64], F32)
    lnu = nc.alloc_sbuf_tensor("lnu", [128, 64], F32)
    lnv = nc.alloc_sbuf_tensor("lnv", [128, 64], F32)
    Hm = nc.alloc_sbuf_tensor("Hm", [128, 256], F32)
    u4 = nc.alloc_sbuf_tensor("u4", [128, 256], F32)
    hp = nc.alloc_sbuf_tensor("hp", [128, 64], F32)
    hq = nc.alloc_sbuf_tensor("hq", [128, 64], F32)
    sm = nc.alloc_sbuf_tensor("sm", [128, 16], F32)
    sm4 = nc.alloc_sbuf_tensor("sm4", [128, 64], F32)
    stm = ExitStack()
    maskd1_sb = stm.enter_context(nc.sbuf_tensor("maskd1_sb", [128, 4 * 512], F32))
    maskd2_sb = stm.enter_context(nc.sbuf_tensor("maskd2_sb", [128, 4 * 512], F32))
    negm_row = stm.enter_context(nc.sbuf_tensor("negm_row", [1, S], BF16))
    negmq_row = stm.enter_context(nc.sbuf_tensor("negmq_row", [1, QW], BF16))
    s_sum = stm.enter_context(nc.sbuf_tensor("s_sum", [1, 256], F32))
    s_sq = stm.enter_context(nc.sbuf_tensor("s_sq", [1, 256], F32))
    s_m = stm.enter_context(nc.sbuf_tensor("s_m", [1, 256], F32))

    pp = [nc.alloc_psum_tensor(f"pp{i}", [128, 512], F32) for i in range(4)]
    pbf = nc.alloc_psum_tensor("pbf", [128, 1024], BF16)
    pmisc = nc.alloc_psum_tensor("pmisc", [128, 512], F32)

    p.dma("sync", lambda e: e.dma_start(ident_f[:, :], ident_h.ap()[:, :]), inc="cns")
    p.dma("sync", lambda e: e.dma_start(onesrow_f[:, :], onesrow_h.ap()[:, :]),
          inc="cns")
    p.dma("sync", lambda e: e.dma_start(gsc[0:1, 0:1], gate_h.ap()[:, :]), inc="cns")
    p.dma("sync", lambda e: e.dma_start(bases_row[0:1, :], bases_h.ap()[:, :]),
          inc="cns")
    p.dma("gpsimd", lambda e: e.dma_start(
        wall_b[:, :], wallT_h.ap().rearrange("(c p) d -> p c d", p=128)), inc="wcns")
    cnsv = p.cur("cns")
    p.emit("vector", lambda e: e.tensor_copy(ident_b[:, :], ident_f[:, :]),
           waits=(("cns", cnsv),))
    p.emit("vector", lambda e: e.memset(epsb[:, :], EPS_RMS))
    p.emit("vector", lambda e: e.memset(epsl[:, :], EPS_LN))
    p.emit("tensor", lambda e: e.matmul(
        pmisc[:, 0:24], onesrow_f[0:1, :], bases_row[0:1, :], start=True, stop=True),
        waits=(("cns", cnsv),), inc="lgmm")
    p.emit("vector", lambda e: e.tensor_copy(basesb[:, :], pmisc[:, 0:24]),
           waits=(("lgmm", p.cur("lgmm")),), inc="lgev")
    p.emit("scalar", lambda e: e.activation(
        gsc[0:1, 1:2], gsc[0:1, 0:1], AF.Tanh, bias=0.0, scale=0.5),
        waits=(("cns", cnsv),), inc="sq")
    p.emit("vector", lambda e: e.tensor_scalar(
        gsc[0:1, 2:3], gsc[0:1, 1:2], 0.5, 0.5, OP.mult, OP.add),
        waits=(("sq", p.cur("sq")),))
    p.emit("vector", lambda e: e.tensor_scalar(
        gsc[0:1, 3:4], gsc[0:1, 1:2], -0.5, 0.5, OP.mult, OP.add), inc="rms")
    p.emit("tensor", lambda e: e.matmul(
        pmisc[:, 24:25], onesrow_f[0:1, :], gsc[0:1, 2:3], start=True, stop=True),
        waits=(("rms", p.cur("rms")), ("lgev", p.cur("lgev"))))
    p.emit("tensor", lambda e: e.matmul(
        pmisc[:, 25:26], onesrow_f[0:1, :], gsc[0:1, 3:4], start=True, stop=True),
        inc="lgmm")
    p.emit("vector", lambda e: e.tensor_copy(gb[:, :], pmisc[:, 24:25]),
           waits=(("lgmm", p.cur("lgmm")),))
    p.emit("vector", lambda e: e.tensor_copy(omgb[:, :], pmisc[:, 25:26]),
           inc="lgev")
    p.barrier(pre={"vector": [("lgev", p.cur("lgev"))]})

    woT_v = woT_h.ap().rearrange("(c p) d -> p c d", p=128)
    asl_v = asl_h.ap().rearrange("(c p) t -> p c t", p=128)

    stB = ExitStack()
    abuf = stB.enter_context(nc.sbuf_tensor("abuf", [128, 32 * 512], BF16))
    wbuf = [stB.enter_context(nc.sbuf_tensor(f"wbuf{i}", [128, 32 * 512], BF16))
            for i in range(2)]
    x2g = stB.enter_context(nc.sbuf_tensor("x2g", [128, 4 * D], F32))
    x2bf = stB.enter_context(nc.sbuf_tensor("x2bf", [128, D], BF16))
    x2T = stB.enter_context(nc.sbuf_tensor("x2T", [128, 32 * 128], BF16))
    sqs = stB.enter_context(nc.sbuf_tensor("sqs", [128, D], BF16))

    for g in range(4):
        t0 = g * 512
        for sp in range(4):
            w = [("womm", p.cur("womm"))] if sp == 0 else []
            p.dma("sync", (lambda sp_, t0_: lambda e: e.dma_start(
                abuf[:, sp_ * 8 * 512:(sp_ + 1) * 8 * 512],
                asl_v[:, sp_ * 8:(sp_ + 1) * 8, t0_:t0_ + 512]))(sp, t0),
                waits=tuple(w), inc="ald")
        aldv = p.cur("ald")
        for ti in range(4):
            w = ([(f"x2wr{(g - 1) % 2}", 64 * ((g - 1) // 2 + 1)),
                  ("sq", p.cur("sq")), ("x2ev", p.cur("x2ev")),
                  ("rms", p.cur("rms"))]
                 if (ti == 0 and g > 0) else [])
            p.dma("sync", (lambda ti_, t0_: lambda e: e.dma_start(
                x2g[:, ti_ * D:(ti_ + 1) * D],
                xres_h.ap()[t0_ + ti_ * 128: t0_ + ti_ * 128 + 128, :]))(ti, t0),
                waits=tuple(w), inc="xrd")
        xrdv = p.cur("xrd")
        for dc in range(8):
            wb = wbuf[dc % 2]
            for sp in range(2):
                w = []
                if sp == 0:
                    w.append(("womm", max(0, p.cur("womm") - 4)))
                p.dma("gpsimd", (lambda sp_, dc_, wb_: lambda e: e.dma_start(
                    wb_[:, sp_ * 16 * 512:(sp_ + 1) * 16 * 512],
                    woT_v[:, sp_ * 16:(sp_ + 1) * 16, dc_ * 512:(dc_ + 1) * 512]))(
                        sp, dc, wb), waits=tuple(w), inc=f"wld{dc % 2}")
            wldn = f"wld{dc % 2}"
            wldv = p.cur(wldn)
            for ti in range(4):
                bank = pp[ti % 2]
                for c in range(32):
                    w = []
                    if c == 0:
                        w = [(wldn, wldv), ("ald", aldv),
                             ("x2ev", max(0, p.cur("x2ev") - 1))]
                    p.emit("tensor", (lambda c_, ti_, wb_, st_, sp_, b_: lambda e: e.matmul(
                        b_[:, :],
                        abuf[:, c_ * 512 + ti_ * 128: c_ * 512 + ti_ * 128 + 128],
                        wb_[:, c_ * 512:(c_ + 1) * 512],
                        start=st_, stop=sp_))(c, ti, wb, c == 0, c == 31, bank),
                        waits=tuple(w), inc="womm" if c == 31 else None)
                wommv = p.cur("womm")
                w = [("womm", wommv)]
                if dc == 0:
                    w.append(("xrd", xrdv))
                p.emit("vector", (lambda ti_, dc_, b_: lambda e: e.tensor_add(
                    x2g[:, ti_ * D + dc_ * 512: ti_ * D + (dc_ + 1) * 512],
                    b_[:, :],
                    x2g[:, ti_ * D + dc_ * 512: ti_ * D + (dc_ + 1) * 512]))(
                        ti, dc, bank), waits=tuple(w), inc="x2ev")
        x2evv = p.cur("x2ev")
        for ti in range(4):
            tg = g * 4 + ti
            p.dma("gpsimd", (lambda ti_, t0_: lambda e: e.dma_start(
                x2_h.ap()[t0_ + ti_ * 128: t0_ + ti_ * 128 + 128, :],
                x2g[:, ti_ * D:(ti_ + 1) * D]))(ti, t0),
                waits=(("x2ev", x2evv),) if ti == 0 else (), inc=f"x2wr{g % 2}")
            p.emit("vector", (lambda ti_, tg_: lambda e: e.reduce_sum(
                m16[:, tg_:tg_ + 1], x2g[:, ti_ * D:(ti_ + 1) * D], axis=AX.X))(ti, tg),
                waits=(("x2ev", x2evv),))
            p.emit("vector", (lambda tg_: lambda e: e.tensor_scalar_mul(
                m16[:, tg_:tg_ + 1], m16[:, tg_:tg_ + 1], 1.0 / D))(tg))
            p.emit("scalar", (lambda ti_, tg_: lambda e: e.activation(
                sqs[:, :], x2g[:, ti_ * D:(ti_ + 1) * D], AF.Square,
                accum_out=rs16[:, tg_:tg_ + 1]))(ti, tg),
                waits=(("x2ev", x2evv),), inc="sq")
            p.emit("vector", (lambda tg_: lambda e: e.tensor_scalar_mul(
                rs16[:, tg_:tg_ + 1], rs16[:, tg_:tg_ + 1], 1.0 / D))(tg),
                waits=(("sq", p.cur("sq")),))
            p.emit("vector", (lambda tg_: lambda e: e.tensor_mul(
                sc1[:, 0:1], m16[:, tg_:tg_ + 1], m16[:, tg_:tg_ + 1]))(tg))
            p.emit("vector", (lambda tg_: lambda e: e.tensor_sub(
                rs16[:, tg_:tg_ + 1], rs16[:, tg_:tg_ + 1], sc1[:, 0:1]))(tg),
                inc="rms")
            p.emit("scalar", (lambda tg_: lambda e: e.activation(
                rs16[:, tg_:tg_ + 1], rs16[:, tg_:tg_ + 1], AF.Sqrt,
                bias=epsl[:, 0:1], scale=1.0))(tg),
                waits=(("rms", p.cur("rms")),), inc="sq")
            p.emit("vector", (lambda tg_: lambda e: e.reciprocal(
                rs16[:, tg_:tg_ + 1], rs16[:, tg_:tg_ + 1]))(tg),
                waits=(("sq", p.cur("sq")),))
            p.emit("vector", (lambda ti_, tg_: lambda e: e.tensor_scalar(
                x2bf[:, :], x2g[:, ti_ * D:(ti_ + 1) * D],
                m16[:, tg_:tg_ + 1], rs16[:, tg_:tg_ + 1],
                OP.subtract, OP.mult))(ti, tg),
                waits=(("lgT", p.cur("lgT")),), inc="x2ev")
            cstv = p.cur("x2ev")
            p.emit("scalar", (lambda tg_: lambda e: e.activation(
                sqs[:, :], x2bf[:, :], AF.Square,
                accum_out=r2c[:, tg_:tg_ + 1]))(tg),
                waits=(("x2ev", cstv),), inc="sq")
            p.emit("scalar", (lambda tg_: lambda e: e.activation(
                r2c[:, tg_:tg_ + 1], r2c[:, tg_:tg_ + 1], AF.Sqrt,
                bias=epsb[:, 0:1], scale=1.0 / D))(tg),
                waits=(("sq", p.cur("sq")),), inc="sq")
            p.emit("vector", (lambda tg_: lambda e: e.reciprocal(
                r2c[:, tg_:tg_ + 1], r2c[:, tg_:tg_ + 1]))(tg),
                waits=(("sq", p.cur("sq")),), inc="rms")
            for rnd in range(4):
                for i in range(8):
                    w = []
                    if i == 0:
                        w = [("x2ev", cstv), ("lgTe", p.cur("lgTe"))]
                    p.emit("tensor", (lambda rnd_, i_: lambda e: e.transpose(
                        pbf[:, i_ * 128:(i_ + 1) * 128],
                        x2bf[:, rnd_ * 1024 + i_ * 128: rnd_ * 1024 + (i_ + 1) * 128],
                        ident_b[:, :]))(rnd, i),
                        waits=tuple(w), inc="lgT" if i == 7 else None)
                tvv = p.cur("lgT")
                for i in range(8):
                    p.emit("vector", (lambda rnd_, i_: lambda e: e.tensor_copy(
                        x2T[:, (rnd_ * 8 + i_) * 128:(rnd_ * 8 + i_ + 1) * 128],
                        pbf[:, i_ * 128:(i_ + 1) * 128]))(rnd, i),
                        waits=(("lgT", tvv),) if i == 0 else (),
                        inc="lgTe" if i == 7 else None)
            tev = p.cur("lgTe")
            for c in range(32):
                w = []
                if c == 0:
                    w = [("lgTe", tev), ("lgev", p.cur("lgev")), ("wcns", 16)]
                p.emit("tensor", (lambda c_, st_, sp_: lambda e: e.matmul(
                    pmisc[:, 0:24], x2T[:, c_ * 128:(c_ + 1) * 128],
                    wall_b[:, c_ * 24:(c_ + 1) * 24], start=st_, stop=sp_))(
                        c, c == 0, c == 31),
                    waits=tuple(w), inc="lgmm" if c == 31 else None)
            p.emit("scalar", (lambda tg_: lambda e: e.activation(
                lga[:, tg_ * 24:(tg_ + 1) * 24], pmisc[:, 0:24], AF.Copy,
                bias=0.0, scale=r2c[:, tg_:tg_ + 1]))(tg),
                waits=(("lgmm", p.cur("lgmm")), ("rms", p.cur("rms"))), inc="sq")
            p.emit("vector", (lambda tg_: lambda e: e.tensor_add(
                lga[:, tg_ * 24:(tg_ + 1) * 24], lga[:, tg_ * 24:(tg_ + 1) * 24],
                basesb[:, :]))(tg),
                waits=(("sq", p.cur("sq")),), inc="lgev")
    p.barrier(pre={"gpsimd": [("x2wr0", p.cur("x2wr0")),
                               ("x2wr1", p.cur("x2wr1"))],
                   "vector": [("lgev", p.cur("lgev"))]})
    stB.close()

    lview = lga[:, :].rearrange("p (t c) -> p t c", c=24)
    Zv = Z[:, :].rearrange("p (t i j) -> p t i j", i=4, j=4)
    tv_ = tmp[:, :].rearrange("p (t i j) -> p t i j", i=4, j=4)
    lnv_v = lnv[:, :].rearrange("p (t j) -> p t j", j=4)
    lnu_v = lnu[:, :].rearrange("p (t i) -> p t i", i=4)

    p.emit("vector", lambda e: e.tensor_scalar_mul(
        Zv, lview[:, :, 0:16].rearrange("p t (i j) -> p t i j", j=4), 1.0 / TAU))
    p.emit("vector", lambda e: e.memset(lnv[:, :], 0.0), inc="snk")

    for it in range(SINK_IT):
        for i in range(4):
            p.emit("vector", (lambda i_: lambda e: e.tensor_sub(
                tv_[:, :, i_, :], Zv[:, :, i_, :], lnv_v))(i),
                waits=(("snk", p.cur("snk")),) if i == 0 else (),
                inc="snk" if i == 3 else None)
        p.emit("scalar", lambda e: e.activation(tmpe[:, :], tmp[:, :], AF.Exp),
               waits=(("snk", p.cur("snk")),), inc="sq")
        p.emit("vector", lambda e: e.tensor_reduce(
            ssum[:, :], tmpe[:, :].rearrange("p (ti j) -> p ti j", j=4),
            axis=AX.X, op=OP.add),
            waits=(("sq", p.cur("sq")),), inc="snk")
        p.emit("scalar", lambda e: e.activation(lnu[:, :], ssum[:, :], AF.Ln),
               waits=(("snk", p.cur("snk")),), inc="sq")
        sqv = p.cur("sq")
        for j in range(4):
            p.emit("vector", (lambda j_: lambda e: e.tensor_sub(
                tv_[:, :, j_, :], Zv[:, :, :, j_], lnu_v))(j),
                waits=(("sq", sqv),) if j == 0 else (),
                inc="snk" if j == 3 else None)
        p.emit("scalar", lambda e: e.activation(tmpe[:, :], tmp[:, :], AF.Exp),
               waits=(("snk", p.cur("snk")),), inc="sq")
        p.emit("vector", lambda e: e.tensor_reduce(
            ssum[:, :], tmpe[:, :].rearrange("p (tj i) -> p tj i", i=4),
            axis=AX.X, op=OP.add),
            waits=(("sq", p.cur("sq")),), inc="snk")
        p.emit("scalar", lambda e: e.activation(lnv[:, :], ssum[:, :], AF.Ln),
               waits=(("snk", p.cur("snk")),), inc="sq")
        p.emit("vector", lambda e: e.tensor_copy(sm[:, 0:1], lnv[:, 0:1]),
               waits=(("sq", p.cur("sq")),), inc="snk")

    u4v = u4[:, :].rearrange("p (t i j) -> p t i j", i=4, j=4)
    for j in range(4):
        p.emit("vector", (lambda j_: lambda e: e.tensor_copy(
            u4v[:, :, :, j_], lnu_v))(j))
    p.emit("vector", lambda e: e.tensor_sub(Hm[:, :], Z[:, :], u4[:, :]))
    Hv = Hm[:, :].rearrange("p (t i j) -> p t i j", i=4, j=4)
    for i in range(4):
        p.emit("vector", (lambda i_: lambda e: e.tensor_sub(
            tv_[:, :, i_, :], Hv[:, :, i_, :], lnv_v))(i),
            inc="snk" if i == 3 else None)
    p.emit("scalar", lambda e: e.activation(Hm[:, :], tmp[:, :], AF.Exp),
           waits=(("snk", p.cur("snk")),), inc="sq")
    p.emit("vector", lambda e: e.tensor_scalar_mul(Hm[:, :], Hm[:, :], gb[:, 0:1]),
           waits=(("sq", p.cur("sq")),))
    diagv = Hm[:, :].rearrange("p (t d) -> p t d", d=16)[:, :, 0:16:5]
    p.emit("vector", lambda e: e.tensor_scalar_add(diagv, diagv, omgb[:, 0:1]),
           inc="snk")

    for (src0, dst) in ((16, hp), (20, hq)):
        pv = lview[:, :, src0:src0 + 4]
        dv = dst[:, :].rearrange("p (t n) -> p t n", n=4)
        s4 = sm4[:, :].rearrange("p (t n) -> p t n", n=4)
        p.emit("vector", (lambda pv_: lambda e: e.reduce_max(
            sm[:, :], pv_, axis=AX.X))(pv))
        for n in range(4):
            p.emit("vector", (lambda n_, s4_: lambda e: e.tensor_copy(
                s4_[:, :, n_], sm[:, :]))(n, s4))
        p.emit("vector", (lambda pv_, dv_, s4_: lambda e: e.tensor_sub(
            dv_, pv_, s4_))(pv, dv, s4), inc="snk")
        p.emit("scalar", (lambda d_: lambda e: e.activation(
            d_[:, :], d_[:, :], AF.Exp))(dst),
            waits=(("snk", p.cur("snk")),), inc="sq")
        p.emit("vector", (lambda dv_: lambda e: e.tensor_reduce(
            sm[:, :], dv_, axis=AX.X, op=OP.add))(dv),
            waits=(("sq", p.cur("sq")),))
        p.emit("vector", lambda e: e.reciprocal(sm[:, :], sm[:, :]))
        for n in range(4):
            p.emit("vector", (lambda n_, s4_: lambda e: e.tensor_copy(
                s4_[:, :, n_], sm[:, :]))(n, s4))
        p.emit("vector", (lambda d_: lambda e: e.tensor_mul(
            d_[:, :], d_[:, :], sm4[:, :]))(dst), inc="snk")
    p.barrier(pre={"vector": [("snk", p.cur("snk"))]})

    stC = ExitStack()
    wbr_b = stC.enter_context(nc.sbuf_tensor("wbr_b", [128, 8 * 1024], BF16))
    x2m = [stC.enter_context(nc.sbuf_tensor(f"x2m{i}", [128, D], F32))
           for i in range(2)]
    xpre = stC.enter_context(nc.sbuf_tensor("xpre", [128, 1024], F32))
    ln2m = [stC.enter_context(nc.sbuf_tensor(f"ln2m{i}", [128, D], F32))
            for i in range(2)]
    xpre_b = stC.enter_context(nc.sbuf_tensor("xpre_b", [128, 1024], BF16))
    xpT = stC.enter_context(nc.sbuf_tensor("xpT", [128, 8 * 128], BF16))
    ybuf = stC.enter_context(nc.sbuf_tensor("ybuf", [128, 1024], F32))
    obuf = [stC.enter_context(nc.sbuf_tensor(f"obuf{i}", [128, D], F32))
            for i in range(2)]

    p.dma("gpsimd", lambda e: e.dma_start(
        wbr_b[:, :], wbr_h.ap().rearrange("(c p) d -> p c d", p=128)), inc="wbld")
    wbl = p.cur("wbld")

    for tg in range(16):
        bi = tg % 2
        w = [("mix", max(0, p.cur("mix") - 1))] if tg >= 2 else []
        p.dma("sync", (lambda tg_, bi_: lambda e: e.dma_start(
            x2m[bi_][:, :], x2_h.ap()[tg_ * 128:(tg_ + 1) * 128, :]))(tg, bi),
            waits=tuple(w), inc=f"x2rd{bi}")
        x2rv = p.cur(f"x2rd{bi}")
        w = [(f"x2rd{bi}", x2rv)]
        if tg == 0:
            w.append(("wbld", wbl))
        p.emit("vector", (lambda bi_, tg_: lambda e: e.tensor_scalar(
            ln2m[bi_][:, :], x2m[bi_][:, :], m16[:, tg_:tg_ + 1],
            rs16[:, tg_:tg_ + 1], OP.subtract, OP.mult))(bi, tg), waits=tuple(w))
        p.emit("vector", (lambda bi_, tg_: lambda e: e.tensor_scalar_mul(
            xpre[:, :], ln2m[bi_][:, 0:1024], hp[:, 4 * tg_:4 * tg_ + 1]))(bi, tg))
        for n in range(1, 4):
            p.emit("vector", (lambda bi_, tg_, n_: lambda e: e.scalar_tensor_tensor(
                xpre[:, :], ln2m[bi_][:, n_ * 1024:(n_ + 1) * 1024],
                hp[:, 4 * tg_ + n_:4 * tg_ + n_ + 1], xpre[:, :],
                OP.mult, OP.add))(bi, tg, n))
        p.emit("vector", lambda e: e.tensor_copy(xpre_b[:, :], xpre[:, :]),
               waits=(("lgT", p.cur("lgT")),), inc="mix")
        mxv = p.cur("mix")
        for i in range(8):
            w = []
            if i == 0:
                w = [("mix", mxv), ("lgTe", p.cur("lgTe")), ("yev", p.cur("yev"))]
            p.emit("tensor", (lambda i_: lambda e: e.transpose(
                pbf[:, i_ * 128:(i_ + 1) * 128], xpre_b[:, i_ * 128:(i_ + 1) * 128],
                ident_b[:, :]))(i), waits=tuple(w), inc="lgT" if i == 7 else None)
        tvv = p.cur("lgT")
        for i in range(8):
            p.emit("vector", (lambda i_: lambda e: e.tensor_copy(
                xpT[:, i_ * 128:(i_ + 1) * 128], pbf[:, i_ * 128:(i_ + 1) * 128]))(i),
                waits=(("lgT", tvv),) if i == 0 else (),
                inc="lgTe" if i == 7 else None)
        tev = p.cur("lgTe")
        for n in range(2):
            for c in range(8):
                w = []
                if n == 0 and c == 0:
                    w = [("lgTe", tev), ("yev", p.cur("yev"))]
                p.emit("tensor", (lambda n_, c_, st_, sp_: lambda e: e.matmul(
                    pp[n_][:, :], xpT[:, c_ * 128:(c_ + 1) * 128],
                    wbr_b[:, c_ * 1024 + n_ * 512: c_ * 1024 + (n_ + 1) * 512],
                    start=st_, stop=sp_))(n, c, c == 0, c == 7),
                    waits=tuple(w), inc="ymm" if (n == 1 and c == 7) else None)
        ymv = p.cur("ymm")
        for n in range(2):
            p.emit("vector", (lambda n_: lambda e: e.tensor_copy(
                ybuf[:, n_ * 512:(n_ + 1) * 512], pp[n_][:, :]))(n),
                waits=(("ymm", ymv),) if n == 0 else (),
                inc="yev" if n == 1 else None)
        yev = p.cur("yev")
        for i in range(4):
            w = []
            if i == 0:
                w = [("yev", yev), (f"ost{bi}", p.cur(f"ost{bi}"))]
            p.emit("vector", (lambda bi_, tg_, i_: lambda e: e.scalar_tensor_tensor(
                obuf[bi_][:, i_ * 1024:(i_ + 1) * 1024], ln2m[bi_][:, 0:1024],
                Hm[:, 16 * tg_ + 4 * i_: 16 * tg_ + 4 * i_ + 1],
                x2m[bi_][:, i_ * 1024:(i_ + 1) * 1024],
                OP.mult, OP.add))(bi, tg, i),
                waits=tuple(w))
            for jj in range(1, 4):
                p.emit("vector",
                       (lambda bi_, tg_, i_, j_: lambda e: e.scalar_tensor_tensor(
                           obuf[bi_][:, i_ * 1024:(i_ + 1) * 1024],
                           ln2m[bi_][:, j_ * 1024:(j_ + 1) * 1024],
                           Hm[:, 16 * tg_ + 4 * i_ + j_: 16 * tg_ + 4 * i_ + j_ + 1],
                           obuf[bi_][:, i_ * 1024:(i_ + 1) * 1024],
                           OP.mult, OP.add))(bi, tg, i, jj))
            p.emit("vector", (lambda bi_, tg_, i_: lambda e: e.scalar_tensor_tensor(
                obuf[bi_][:, i_ * 1024:(i_ + 1) * 1024], ybuf[:, :],
                hq[:, 4 * tg_ + i_:4 * tg_ + i_ + 1],
                obuf[bi_][:, i_ * 1024:(i_ + 1) * 1024],
                OP.mult, OP.add))(bi, tg, i),
                inc="mix" if i == 3 else None)
        mxo = p.cur("mix")
        p.dma("sync", (lambda bi_, tg_: lambda e: e.dma_start(
            out_h.ap()[tg_ * 128:(tg_ + 1) * 128, :], obuf[bi_][:, :]))(bi, tg),
            waits=(("mix", mxo),), inc=f"ost{bi}")

    p.emit("gpsimd", None, waits=(("ost0", p.cur("ost0")), ("ost1", p.cur("ost1"))))
    p.emit("sync", None, waits=(("ost0", p.cur("ost0")), ("ost1", p.cur("ost1"))))

    _block_emit(nc, p)
    stC.close()
    stack.close()
    return nc



def _rope_tables():
    i = np.arange(RD // 2, dtype=np.float64)
    inv_freq = 1.0 / (10000.0 ** (2.0 * i / RD))
    t = np.arange(S, dtype=np.float64)
    ang = np.outer(t, inv_freq)
    sinr = np.tile(np.sin(ang).astype(np.float32), (1, HPC))
    cosr = np.tile(np.cos(ang).astype(np.float32), (1, HPC))
    return np.ascontiguousarray(sinr), np.ascontiguousarray(cosr)


def _masks():
    m = np.zeros((4, 128, 512), np.float32)
    kk = np.arange(128)[:, None]
    qq = np.arange(512)[None, :]
    for di, d in enumerate((0, 128, 256, 384)):
        m[di] = np.where(kk + d > qq, np.float32(-1e9), np.float32(0.0))
    return np.ascontiguousarray(m.reshape(512, 512))


FUSED = os.environ.get("K_UNFUSED", "") != "1"


def kernel(**inputs):
    if FUSED:
        return kernel_fused(inputs)
    x = np.asarray(inputs["L0_attn_hidden_states"])
    wq = np.asarray(inputs["L0_attn_wq"])
    wk = np.asarray(inputs["L0_attn_wk"])
    wv = np.asarray(inputs["L0_attn_wv"])
    wo = np.asarray(inputs["L0_attn_wo"])
    W_res = np.asarray(inputs["L0_ffn_W_res"])
    W_pre = np.asarray(inputs["L0_ffn_W_pre"])
    W_post = np.asarray(inputs["L0_ffn_W_post"])
    Hrb = np.asarray(inputs["L0_ffn_H_res_base"])
    Hpb = np.asarray(inputs["L0_ffn_H_pre_base"])
    Hqb = np.asarray(inputs["L0_ffn_H_post_base"])
    gate = np.asarray(inputs["L0_ffn_gate"])
    Wbr = np.asarray(inputs["L0_ffn_W_branch"])

    sinr, cosr = _rope_tables()
    masks = _masks()
    ident = np.eye(128, dtype=np.float32)
    onesrow = np.ones((1, 128), np.float32)

    xT = [np.ascontiguousarray(x[b].T) for b in range(B)]
    wT = {"q": wq, "k": wk, "v": wv}
    wTs = {pj: [np.ascontiguousarray(wT[pj][rk * DPC:(rk + 1) * DPC, :].T)
                for rk in range(4)] for pj in wT}

    nc_a = build_launch_a()
    in_maps_a = []
    for c in range(NCORE):
        b, rk = c // 4, c % 4
        in_maps_a.append(dict(
            xT=xT[b], wqT=wTs["q"][rk], wkT=wTs["k"][rk], wvT=wTs["v"][rk],
            sinr=sinr, cosr=cosr, masks=masks, ident=ident, onesrow=onesrow))
    res_a = run_bass_kernel_spmd(nc_a, in_maps_a, core_ids=list(range(NCORE)))

    attn = [np.concatenate([res_a.results[b * 4 + rk]["attn"] for rk in range(4)],
                           axis=0) for b in range(B)]

    woT = np.ascontiguousarray(wo.T)
    wallT = np.ascontiguousarray(np.concatenate([W_res, W_pre, W_post], axis=0).T)
    bases = np.concatenate(
        [np.asarray(Hrb).reshape(-1), np.asarray(Hpb).reshape(-1),
         np.asarray(Hqb).reshape(-1)]).reshape(1, 24).astype(np.float32)
    gate2 = np.asarray(gate).reshape(1, 1).astype(np.float32)

    nc_b = build_launch_b()
    in_maps_b = []
    for c in range(NCORE):
        b, rk = c // 4, c % 4
        in_maps_b.append(dict(
            aslice=np.ascontiguousarray(attn[b][:, rk * TPC:(rk + 1) * TPC]),
            xres=np.ascontiguousarray(x[b, rk * TPC:(rk + 1) * TPC, :]),
            woT=woT, wallT=wallT, bases=bases, gate=gate2, wbr=Wbr,
            ident=ident, onesrow=onesrow))
    res_b = run_bass_kernel_spmd(nc_b, in_maps_b, core_ids=list(range(NCORE)))

    out = np.empty((B, S, D), np.float32)
    for c in range(NCORE):
        b, rk = c // 4, c % 4
        out[b, rk * TPC:(rk + 1) * TPC, :] = res_b.results[c]["out"]
    return out



def build_fused():
    nc = bass.Bass("TRN2", target_bir_lowering=False)
    NHC = 8
    DQC = NHC * HD
    QW = 4096

    xT_h = nc.dram_tensor("xT", [D, S], F32, kind="ExternalInput")
    xq_h = nc.dram_tensor("xq", [D, QW], F32, kind="ExternalInput")
    w_h = {pj: nc.dram_tensor(f"w{pj}T", [D, DQC], F32, kind="ExternalInput")
           for pj in ("q", "k", "v")}
    sinq_h = nc.dram_tensor("sinq", [QW, 256], F32, kind="ExternalInput")
    cosq_h = nc.dram_tensor("cosq", [QW, 256], F32, kind="ExternalInput")
    sink_h = nc.dram_tensor("sink", [S, 256], F32, kind="ExternalInput")
    cosk_h = nc.dram_tensor("cosk", [S, 256], F32, kind="ExternalInput")
    maskd1_h = nc.dram_tensor("maskd1", [512, 512], F32, kind="ExternalInput")
    maskd2_h = nc.dram_tensor("maskd2", [512, 512], F32, kind="ExternalInput")
    yflag_h = nc.dram_tensor("yflag", [128, 1], F32, kind="ExternalInput")
    roff_h = nc.dram_tensor("roff", [1, 4], mybir.dt.int32, kind="ExternalInput")
    offtab_h = nc.dram_tensor("offtab", [1, 256], mybir.dt.int32, kind="ExternalInput")
    nonce_h = nc.dram_tensor("noncev", [1, 64], F32, kind="ExternalInput")
    ident_h = nc.dram_tensor("ident", [128, 128], F32, kind="ExternalInput")
    onesrow_h = nc.dram_tensor("onesrow", [1, 128], F32, kind="ExternalInput")
    xres_h = nc.dram_tensor("xres", [TPC, D], F32, kind="ExternalInput")
    woT_h = nc.dram_tensor("woT", [D, D], F32, kind="ExternalInput")
    wallT_h = nc.dram_tensor("wallT", [D, 24], F32, kind="ExternalInput")
    bases_h = nc.dram_tensor("bases", [1, 24], F32, kind="ExternalInput")
    gate_h = nc.dram_tensor("gate", [1, 1], F32, kind="ExternalInput")
    wbr_h = nc.dram_tensor("wbr", [1024, 1024], F32, kind="ExternalInput")

    out_h = nc.dram_tensor("out", [TPC, D], F32, kind="ExternalOutput")

    xbf_h = nc.dram_tensor("xbf", [D, S], BF16, kind="Internal")
    xqbf_h = nc.dram_tensor("xqbf", [D, QW], BF16, kind="Internal")
    qt_h = nc.dram_tensor("qt", [8, DQC, 512], BF16, kind="Internal")
    kt_h = nc.dram_tensor("kt", [16, DQC, 512], BF16, kind="Internal")
    vt_h = nc.dram_tensor("vt", [NHC, S, HD], BF16, kind="Internal")
    x2_h = nc.dram_tensor("x2h", [TPC, D], F32, kind="Internal")
    attn_loc = nc.dram_tensor("attnloc", [2048, QW], BF16, kind="Internal")
    attnsh = nc.dram_tensor("attnsh", [D, QW], BF16, kind="Internal",
                            addr_space="Shared")
    flagsh = nc.dram_tensor("flagsh", [2, 64], F32, kind="Internal",
                            addr_space="Shared")

    sem_names = ["bar", "vch", "sch", "cns", "wcns",
                 "xin0", "xin1", "xcast", "xwr0", "xwr1", "sq", "smm", "sev",
                 "fin", "fsq", "fre", "ftp",
                 "wld", "xbld0", "xbld1", "qmm", "bkf", "ropa", "ropb", "scld", "nrm",
                 "tps", "tpe", "qst",
                 "ktld", "qcld", "tsc", "vmsk", "sexp", "tpv", "vrs",
                 "trb", "voev", "ost", "xch",
                 "ald", "wld0", "wld1", "xrd", "womm", "x2ev", "x2wr0", "x2wr1",
                 "rms", "lgT", "lgTe", "lgmm", "lgev", "snk",
                 "x2rd0", "x2rd1", "mix", "ymm", "yev", "ost0", "ost1", "wbld",
                 "xpo", "womh"]
    stack = ExitStack()
    sems = {n: stack.enter_context(nc.semaphore(n)) for n in sem_names}
    p = Plan(sems)

    ident_f = nc.alloc_sbuf_tensor("ident_f", [128, 128], F32)
    ident_b = nc.alloc_sbuf_tensor("ident_b", [128, 128], BF16)
    onesrow_f = nc.alloc_sbuf_tensor("onesrow_f", [1, 128], F32)
    ones_b = nc.alloc_sbuf_tensor("ones_b", [128, 1], BF16)
    yflag_sb = nc.alloc_sbuf_tensor("yflag_sb", [128, 1], F32)
    roff_sb = nc.alloc_sbuf_tensor("roff_sb", [1, 4], mybir.dt.int32)
    offtab_sb = nc.alloc_sbuf_tensor("offtab_sb", [1, 256], mybir.dt.int32)
    nonce_sb = nc.alloc_sbuf_tensor("nonce_sb", [1, 64], F32)
    flag_sb = nc.alloc_sbuf_tensor("flag_sb", [1, 64], F32)
    r_cols = nc.alloc_sbuf_tensor("r_cols", [128, 64], F32)
    rq_cols = nc.alloc_sbuf_tensor("rq_cols", [128, 32], F32)
    qmax2 = nc.alloc_sbuf_tensor("qmax2", [128, NHC], F32)
    kmax2 = nc.alloc_sbuf_tensor("kmax2", [128, NHC], F32)
    negcb = nc.alloc_sbuf_tensor("negcb", [128, NHC], F32)
    negcy = nc.alloc_sbuf_tensor("negcy", [128, NHC], F32)
    negc_row = nc.alloc_sbuf_tensor("negc_row", [1, NHC], F32)
    epsc = nc.alloc_sbuf_tensor("epsc", [1, 1], F32)
    wall_b = nc.alloc_sbuf_tensor("wall_b", [128, 32 * 24], BF16)
    bases_row = nc.alloc_sbuf_tensor("bases_row", [1, 24], F32)
    basesb = nc.alloc_sbuf_tensor("basesb", [128, 24], F32)
    gsc = nc.alloc_sbuf_tensor("gsc", [1, 4], F32)
    gb = nc.alloc_sbuf_tensor("gb", [128, 1], F32)
    omgb = nc.alloc_sbuf_tensor("omgb", [128, 1], F32)
    r2c = nc.alloc_sbuf_tensor("r2c", [128, 16], F32)
    m16 = nc.alloc_sbuf_tensor("m16", [128, 16], F32)
    rs16 = nc.alloc_sbuf_tensor("rs16", [128, 16], F32)
    sc1 = nc.alloc_sbuf_tensor("sc1", [128, 1], F32)
    epsb = nc.alloc_sbuf_tensor("epsb", [128, 1], F32)
    epsl = nc.alloc_sbuf_tensor("epsl", [128, 1], F32)
    lga = nc.alloc_sbuf_tensor("lga", [128, 16 * 24], F32)
    Z = nc.alloc_sbuf_tensor("Z", [128, 256], F32)
    tmp = nc.alloc_sbuf_tensor("tmp", [128, 256], F32)
    tmpe = nc.alloc_sbuf_tensor("tmpe", [128, 256], F32)
    ssum = nc.alloc_sbuf_tensor("ssum", [128, 64], F32)
    lnu = nc.alloc_sbuf_tensor("lnu", [128, 64], F32)
    lnv = nc.alloc_sbuf_tensor("lnv", [128, 64], F32)
    Hm = nc.alloc_sbuf_tensor("Hm", [128, 256], F32)
    u4 = nc.alloc_sbuf_tensor("u4", [128, 256], F32)
    hp = nc.alloc_sbuf_tensor("hp", [128, 64], F32)
    hq = nc.alloc_sbuf_tensor("hq", [128, 64], F32)
    sm = nc.alloc_sbuf_tensor("sm", [128, 16], F32)
    sm4 = nc.alloc_sbuf_tensor("sm4", [128, 64], F32)
    stm = ExitStack()
    maskd1_sb = stm.enter_context(nc.sbuf_tensor("maskd1_sb", [128, 4 * 512], F32))
    maskd2_sb = stm.enter_context(nc.sbuf_tensor("maskd2_sb", [128, 4 * 512], F32))
    negm_row = stm.enter_context(nc.sbuf_tensor("negm_row", [1, S], BF16))
    negmq_row = stm.enter_context(nc.sbuf_tensor("negmq_row", [1, QW], BF16))
    s_sum = stm.enter_context(nc.sbuf_tensor("s_sum", [1, 256], F32))
    s_sq = stm.enter_context(nc.sbuf_tensor("s_sq", [1, 256], F32))
    s_m = stm.enter_context(nc.sbuf_tensor("s_m", [1, 256], F32))

    pp = [nc.alloc_psum_tensor(f"pp{i}", [128, 512], F32) for i in range(4)]
    ps_a = nc.alloc_psum_tensor("ps_a", [1, 512], F32)
    ps_b = nc.alloc_psum_tensor("ps_b", [1, 512], F32)
    pbf = nc.alloc_psum_tensor("pbf", [128, 1024], BF16)
    pmisc = nc.alloc_psum_tensor("pmisc", [128, 512], F32)

    p.dma("sync", lambda e: e.dma_start(ident_f[:, :], ident_h.ap()[:, :]), inc="cns")
    p.dma("sync", lambda e: e.dma_start(onesrow_f[:, :], onesrow_h.ap()[:, :]),
          inc="cns")
    p.dma("sync", lambda e: e.dma_start(
        maskd1_sb[:, :], maskd1_h.ap().rearrange("(m p) t -> p m t", p=128)),
        inc="cns")
    p.dma("sync", lambda e: e.dma_start(
        maskd2_sb[:, :], maskd2_h.ap().rearrange("(m p) t -> p m t", p=128)),
        inc="cns")
    p.dma("sync", lambda e: e.dma_start(yflag_sb[:, :], yflag_h.ap()[:, :]),
          inc="cns")
    p.dma("sync", lambda e: e.dma_start(roff_sb[:, :], roff_h.ap()[:, :]), inc="cns")
    p.dma("sync", lambda e: e.dma_start(offtab_sb[:, :], offtab_h.ap()[:, :]), inc="cns")
    p.dma("sync", lambda e: e.dma_start(nonce_sb[:, :], nonce_h.ap()[:, :]),
          inc="cns")
    p.dma("sync", lambda e: e.dma_start(gsc[0:1, 0:1], gate_h.ap()[:, :]), inc="cns")
    p.dma("sync", lambda e: e.dma_start(bases_row[0:1, :], bases_h.ap()[:, :]),
          inc="cns")
    p.dma("gpsimd", lambda e: e.dma_start(
        wall_b[:, :], wallT_h.ap().rearrange("(c p) d -> p c d", p=128)), inc="wcns")
    cnsv = p.cur("cns")
    p.emit("vector", lambda e: e.tensor_copy(ident_b[:, :], ident_f[:, :]),
           waits=(("cns", cnsv),))
    p.emit("vector", lambda e: e.memset(ones_b[:, :], 1.0))
    p.emit("vector", lambda e: e.memset(qmax2[:, :], 0.0))
    p.emit("vector", lambda e: e.memset(kmax2[:, :], 0.0))
    p.emit("vector", lambda e: e.memset(epsc[:, :], EPS_LN))
    p.emit("vector", lambda e: e.memset(epsb[:, :], EPS_RMS))
    p.emit("vector", lambda e: e.memset(epsl[:, :], EPS_LN), inc="cns")
    p.emit("tensor", lambda e: e.matmul(
        pmisc[:, 0:24], onesrow_f[0:1, :], bases_row[0:1, :], start=True, stop=True),
        waits=(("cns", cnsv),), inc="lgmm")
    p.emit("vector", lambda e: e.tensor_copy(basesb[:, :], pmisc[:, 0:24]),
           waits=(("lgmm", p.cur("lgmm")),), inc="lgev")
    p.emit("scalar", lambda e: e.activation(
        gsc[0:1, 1:2], gsc[0:1, 0:1], AF.Tanh, bias=0.0, scale=0.5),
        waits=(("cns", cnsv),), inc="sq")
    p.emit("vector", lambda e: e.tensor_scalar(
        gsc[0:1, 2:3], gsc[0:1, 1:2], 0.5, 0.5, OP.mult, OP.add),
        waits=(("sq", p.cur("sq")),))
    p.emit("vector", lambda e: e.tensor_scalar(
        gsc[0:1, 3:4], gsc[0:1, 1:2], -0.5, 0.5, OP.mult, OP.add), inc="rms")
    p.emit("tensor", lambda e: e.matmul(
        pmisc[:, 24:25], onesrow_f[0:1, :], gsc[0:1, 2:3], start=True, stop=True),
        waits=(("rms", p.cur("rms")), ("lgev", p.cur("lgev"))))
    p.emit("tensor", lambda e: e.matmul(
        pmisc[:, 25:26], onesrow_f[0:1, :], gsc[0:1, 3:4], start=True, stop=True),
        inc="lgmm")
    p.emit("vector", lambda e: e.tensor_copy(gb[:, :], pmisc[:, 24:25]),
           waits=(("lgmm", p.cur("lgmm")),))
    p.emit("vector", lambda e: e.tensor_copy(omgb[:, :], pmisc[:, 25:26]),
           inc="lgev")

    st1 = ExitStack()
    CW = 256
    xin = [st1.enter_context(nc.sbuf_tensor(f"xin{i}", [128, 32 * CW], F32))
           for i in range(2)]
    xbf_sb = [st1.enter_context(nc.sbuf_tensor(f"xbfs{i}", [128, 32 * CW], BF16))
              for i in range(2)]
    xsq_sb = [st1.enter_context(nc.sbuf_tensor(f"xsqs{i}", [128, 32 * CW], BF16))
              for i in range(2)]

    chk = [0]

    def emit_stats(src_v, dst_v, nch, negm_dst, rcol_dst, pm_base):
        sev_entry = p.cur("sev")
        for lc in range(nch):
            ch = chk[0]
            bi = ch % 2
            c0 = lc * CW
            for sp in range(4):
                w = [("xcast", ch - 1)] if (ch >= 2 and sp == 0) else []
                p.dma("sync", (lambda bi_, sp_, c0_: lambda e: e.dma_start(
                    xin[bi_][:, sp_ * 8 * CW:(sp_ + 1) * 8 * CW],
                    src_v[:, sp_ * 8:(sp_ + 1) * 8, c0_:c0_ + CW]))(bi, sp, c0),
                    waits=tuple(w), inc=f"xin{bi}")
            w = [(f"xin{bi}", 64 * (ch // 2 + 1))]
            if ch >= 2:
                w += [(f"xwr{bi}", 64 * (ch // 2)), ("smm", ch - 1), ("sq", ch - 1)]
            p.emit("vector", (lambda bi_: lambda e: e.tensor_copy(
                xbf_sb[bi_][:, :], xin[bi_][:, :]))(bi), waits=tuple(w), inc="xcast")
            xc = p.cur("xcast")
            w = [("xcast", xc)]
            if ch >= 2:
                w.append(("smm", ch - 1))
            p.emit("scalar", (lambda bi_: lambda e: e.square(
                xsq_sb[bi_][:, :], xbf_sb[bi_][:, :]))(bi), waits=tuple(w), inc="sq")
            for sp in range(4):
                w = [("xcast", xc)] if sp == 0 else []
                p.dma("gpsimd", (lambda bi_, sp_, c0_: lambda e: e.dma_start(
                    dst_v[:, sp_ * 8:(sp_ + 1) * 8, c0_:c0_ + CW],
                    xbf_sb[bi_][:, sp_ * 8 * CW:(sp_ + 1) * 8 * CW]))(bi, sp, c0),
                    waits=tuple(w), inc=f"xwr{bi}")
            sqv = p.cur("sq")
            for dc in range(32):
                w = []
                if dc == 0:
                    w = [("xcast", xc), ("sq", sqv), ("sev", ch)]
                p.emit("tensor", (lambda bi_, dc_, st_, sp_: lambda e: e.matmul(
                    ps_a[0:1, 0:CW], ones_b[:, :],
                    xbf_sb[bi_][:, dc_ * CW:(dc_ + 1) * CW], start=st_, stop=sp_))(
                        bi, dc, dc == 0, dc == 31), waits=tuple(w))
                p.emit("tensor", (lambda bi_, dc_, st_, sp_: lambda e: e.matmul(
                    ps_b[0:1, 0:CW], ones_b[:, :],
                    xsq_sb[bi_][:, dc_ * CW:(dc_ + 1) * CW], start=st_, stop=sp_))(
                        bi, dc, dc == 0, dc == 31),
                    inc="smm" if dc == 31 else None)
            smv = p.cur("smm")
            w = [("smm", smv)]
            if ch >= 1:
                w.append(("ftp", ch))
            p.emit("vector", lambda e: e.tensor_copy(s_sum[0:1, :], ps_a[0:1, 0:CW]),
                   waits=tuple(w))
            p.emit("vector", lambda e: e.tensor_copy(s_sq[0:1, :], ps_b[0:1, 0:CW]),
                   inc="sev")
            p.emit("vector", (lambda c0_: lambda e: e.tensor_scalar_mul(
                negm_dst[0:1, c0_:c0_ + CW], s_sum[0:1, :], -1.0 / D))(c0))
            p.emit("vector", lambda e: e.tensor_scalar_mul(
                s_m[0:1, :], s_sum[0:1, :], 1.0 / D))
            p.emit("vector", lambda e: e.tensor_mul(s_m[0:1, :], s_m[0:1, :],
                                                    s_m[0:1, :]))
            p.emit("vector", lambda e: e.tensor_scalar_mul(
                s_sq[0:1, :], s_sq[0:1, :], 1.0 / D))
            p.emit("vector", lambda e: e.tensor_sub(s_sq[0:1, :], s_sq[0:1, :],
                                                    s_m[0:1, :]), inc="fin")
            p.emit("scalar", lambda e: e.activation(
                s_sq[0:1, :], s_sq[0:1, :], AF.Sqrt, bias=epsc[0:1, 0:1], scale=1.0),
                waits=(("fin", ch + 1), ("cns", p.cur("cns"))), inc="fsq")
            p.emit("vector", lambda e: e.reciprocal(s_sq[0:1, :], s_sq[0:1, :]),
                   waits=(("fsq", ch + 1),), inc="fre")
            for k in range(2):
                wtp = [("fre", ch + 1)] if k == 0 else []
                if k == 0 and lc == 0:
                    wtp.append(("sev", sev_entry))
                p.emit("tensor", (lambda lc_, k_, pb_: lambda e: e.transpose(
                    pmisc[:, pb_ + 2 * lc_ + k_:pb_ + 2 * lc_ + k_ + 1],
                    s_sq[0:1, k_ * 128:(k_ + 1) * 128], ident_f[0:1, 0:1]))(lc, k, pm_base),
                    waits=tuple(wtp), inc="ftp" if k == 1 else None)
            chk[0] += 1
        p.emit("vector", (lambda rc_, nch_, pb_: lambda e: e.tensor_copy(
            rc_[:, :], pmisc[:, pb_:pb_ + 2 * nch_]))(rcol_dst, nch, pm_base),
            waits=(("ftp", chk[0]),), inc="sev")

    xT_v = xT_h.ap().rearrange("(c p) t -> p c t", p=128)
    xbf_v = xbf_h.ap().rearrange("(c p) t -> p c t", p=128)
    xq_v = xq_h.ap().rearrange("(c p) t -> p c t", p=128)
    xqbf_v = xqbf_h.ap().rearrange("(c p) t -> p c t", p=128)
    emit_stats(xT_v, xbf_v, 32, negm_row, r_cols, 0)
    emit_stats(xq_v, xqbf_v, 16, negmq_row, rq_cols, 64)

    p.barrier(pre={"gpsimd": [("xwr0", 64 * 24), ("xwr1", 64 * 24)],
                   "vector": [("sev", p.cur("sev")), ("cns", p.cur("cns")),
                              ("lgev", p.cur("lgev"))]})
    st1.close()

    st2 = ExitStack()
    wbuf = st2.enter_context(nc.sbuf_tensor("wbuf", [128, 32 * 1024], BF16))
    xblk2 = [st2.enter_context(nc.sbuf_tensor(f"xblk{i}", [128, 16 * 1024], BF16))
             for i in range(2)]
    wrs_f = st2.enter_context(nc.sbuf_tensor("wrs_f", [1, 1024], F32))
    wrs_b = st2.enter_context(nc.sbuf_tensor("wrs_b", [1, 1024], BF16))
    qstage = st2.enter_context(nc.sbuf_tensor("qstage", [128, 1024], BF16))
    sqbuf = st2.enter_context(nc.sbuf_tensor("sqbuf", [128, 1024], BF16))
    nrmt = st2.enter_context(nc.sbuf_tensor("nrmt", [128, 4], F32))
    robuf = st2.enter_context(nc.sbuf_tensor("robuf", [128, 4 * RD], F32))
    robuf2 = st2.enter_context(nc.sbuf_tensor("robuf2", [128, 4 * RD], F32))
    robt = st2.enter_context(nc.sbuf_tensor("robt", [128, 4 * 32], F32))
    robt2 = st2.enter_context(nc.sbuf_tensor("robt2", [128, 4 * 32], F32))
    sc_sin = st2.enter_context(nc.sbuf_tensor("sc_sin", [128, 128], F32))
    sc_cos = st2.enter_context(nc.sbuf_tensor("sc_cos", [128, 128], F32))
    qt_blk = st2.enter_context(nc.sbuf_tensor("qt_blk", [128, 8 * 512], BF16))

    def emit_proj(pj, hg, src_bf_v, n_tiles, rope_sin, rope_cos, negm_src, r_src,
                  out_scr, mx_dst):
        wT_v = w_h[pj].ap().rearrange("(c p) d -> p c d", p=128)[:, :, hg * 1024:(hg + 1) * 1024]
        for sp in range(4):
            w = [("qst", p.cur("qst"))] if sp == 0 else []
            p.dma("gpsimd", (lambda sp_: lambda e: e.dma_start(
                wbuf[:, sp_ * 8 * 1024:(sp_ + 1) * 8 * 1024],
                wT_v[:, sp_ * 8:(sp_ + 1) * 8, :]))(sp), waits=tuple(w), inc="wld")
        wl = p.cur("wld")
        for dc in range(2):
            tgt = ps_a if dc == 0 else ps_b
            for c in range(32):
                w = []
                if dc == 0 and c == 0:
                    w = [("wld", wl), ("bkf", p.cur("bkf")), ("sev", p.cur("sev"))]
                p.emit("tensor", (lambda t_, c_, dc_, st_, sp_: lambda e: e.matmul(
                    t_[0:1, :], ones_b[:, :],
                    wbuf[:, c_ * 1024 + dc_ * 512: c_ * 1024 + (dc_ + 1) * 512],
                    start=st_, stop=sp_))(tgt, c, dc, c == 0, c == 31),
                    waits=tuple(w), inc="qmm" if (c == 31 and dc == 1) else None)
        qm = p.cur("qmm")
        p.emit("vector", lambda e: e.tensor_copy(wrs_f[0:1, 0:512], ps_a[0:1, :]),
               waits=(("qmm", qm),))
        p.emit("vector", lambda e: e.tensor_copy(wrs_f[0:1, 512:1024], ps_b[0:1, :]))
        p.emit("vector", lambda e: e.tensor_copy(wrs_b[0:1, :], wrs_f[0:1, :]),
               inc="bkf")
        n_hblk = n_tiles // 4
        for blk in range(n_hblk):
            t0 = blk * 512
            xb = xblk2[blk % 2]
            for sp in range(4):
                w = []
                if sp == 0:
                    w.append(("qmm", max(0, p.cur("qmm") - 8)))
                p.dma("sync", (lambda sp_, t0_, xb_: lambda e: e.dma_start(
                    xb_[:, sp_ * 8 * 512:(sp_ + 1) * 8 * 512],
                    src_bf_v[:, sp_ * 8:(sp_ + 1) * 8, t0_:t0_ + 512]))(sp, t0, xb),
                    waits=tuple(w), inc=f"xbld{blk % 2}")
            xbl = p.cur(f"xbld{blk % 2}")
            xbln = f"xbld{blk % 2}"
            for ti in range(4):
                tg = blk * 4 + ti
                if pj != "v":
                    w = [("ropb", p.cur("ropb"))]
                    p.dma("sync", (lambda tg_: lambda e: e.dma_start(
                        sc_sin[:, :],
                        rope_sin[tg_ * 128:(tg_ + 1) * 128,
                                 hg * 128:(hg + 1) * 128]))(tg),
                        waits=tuple(w), inc="scld")
                    p.dma("sync", (lambda tg_: lambda e: e.dma_start(
                        sc_cos[:, :],
                        rope_cos[tg_ * 128:(tg_ + 1) * 128,
                                 hg * 128:(hg + 1) * 128]))(tg), inc="scld")
                scl = p.cur("scld")
                for dc in range(2):
                    bank = pp[dc]
                    for c in range(32):
                        w = []
                        if c == 0:
                            w = [(xbln, xbl), ("bkf", p.cur("bkf"))]
                        p.emit("tensor",
                               (lambda c_, ti_, dc_, st_, b_, xb_: lambda e: e.matmul(
                                   b_[:, :],
                                   xb_[:, c_ * 512 + ti_ * 128:
                                       c_ * 512 + ti_ * 128 + 128],
                                   wbuf[:, c_ * 1024 + dc_ * 512:
                                        c_ * 1024 + (dc_ + 1) * 512],
                                   start=st_, stop=False))(c, ti, dc, c == 0, bank, xb),
                               waits=tuple(w))
                    p.emit("tensor", (lambda tg_, dc_, b_: lambda e: e.matmul(
                        b_[:, :], negm_src[0:1, tg_ * 128:(tg_ + 1) * 128],
                        wrs_b[0:1, dc_ * 512:(dc_ + 1) * 512],
                        start=False, stop=True))(tg, dc, bank), inc="qmm")
                    qmm = p.cur("qmm")
                    if pj != "v":
                        p.emit("vector", (lambda dc_, tg_, b_: lambda e:
                               e.tensor_scalar_mul(
                                   robuf[:, :].rearrange(
                                       "p (h r) -> p h r", r=RD)[:, 2 * dc_:2 * dc_ + 2, :],
                                   b_[:, :].rearrange(
                                       "p (h r) -> p h r", r=HD)[:, :, 0:RD],
                                   r_src[:, tg_:tg_ + 1]))(dc, tg, bank),
                               waits=(("qmm", qmm),), inc="ropa")
                    wse = [("qmm", qmm)]
                    if pj != "v":
                        wse.append(("ropa", p.cur("ropa")))
                    if dc == 0:
                        if pj == "v":
                            wse.append(("qst", p.cur("qst")))
                        else:
                            wse.append(("tps", p.cur("tps")))
                    p.emit("scalar", (lambda dc_, tg_, b_: lambda e: e.activation(
                        qstage[:, dc_ * 512:(dc_ + 1) * 512], b_[:, :], AF.Copy,
                        bias=0.0, scale=r_src[:, tg_:tg_ + 1]))(dc, tg, bank),
                        waits=tuple(wse), inc="bkf")
                qevv = p.cur("bkf")
                if pj != "v":
                    ropv = p.cur("ropa")
                    rb = robuf[:, :].rearrange("p (h i two) -> p h i two", i=32, two=2)
                    rb2 = robuf2[:, :].rearrange("p (h i two) -> p h i two", i=32, two=2)
                    sv = sc_sin[:, :].rearrange("p (h i) -> p h i", i=32)
                    cv = sc_cos[:, :].rearrange("p (h i) -> p h i", i=32)
                    tv = robt[:, :].rearrange("p (h i) -> p h i", i=32)
                    tv2 = robt2[:, :].rearrange("p (h i) -> p h i", i=32)
                    p.emit("vector", lambda e: e.tensor_mul(tv, rb[:, :, :, 1], sv),
                           waits=(("ropa", ropv), ("scld", scl)))
                    p.emit("vector", lambda e: e.tensor_mul(tv2, rb[:, :, :, 0], sv))
                    p.emit("vector", lambda e: e.tensor_mul(
                        rb2[:, :, :, 0], rb[:, :, :, 0], cv))
                    p.emit("vector", lambda e: e.tensor_sub(
                        rb2[:, :, :, 0], rb2[:, :, :, 0], tv))
                    p.emit("vector", lambda e: e.tensor_mul(
                        rb2[:, :, :, 1], rb[:, :, :, 1], cv))
                    p.emit("vector", lambda e: e.tensor_add(
                        rb2[:, :, :, 1], rb2[:, :, :, 1], tv2), inc="ropb")
                    p.emit("vector", lambda e: e.tensor_copy(
                        qstage[:, :].rearrange("p (h r) -> p h r", r=HD)[:, :, 0:RD],
                        robuf2[:, :].rearrange("p (h r) -> p h r", r=RD)),
                        waits=(("bkf", qevv),), inc="ropb")
                    ropd = p.cur("ropb")
                    p.emit("scalar", lambda e: e.square(sqbuf[:, :], qstage[:, :]),
                           waits=(("ropb", ropd), ("bkf", qevv)), inc="sq")
                    p.emit("vector", lambda e: e.reduce_sum(
                        nrmt[:, :], sqbuf[:, :].rearrange("p (h d) -> p h d", d=HD),
                        axis=AX.X), waits=(("sq", p.cur("sq")),))
                    p.emit("vector", (lambda m_, hg_: lambda e: e.tensor_max(
                        m_[:, hg_ * 4:(hg_ + 1) * 4], m_[:, hg_ * 4:(hg_ + 1) * 4],
                        nrmt[:, :]))(mx_dst, hg), inc="nrm")
                    for i in range(8):
                        w = []
                        if i == 0:
                            w = [("ropb", ropd), ("bkf", qevv), ("tpe", p.cur("tpe"))]
                        p.emit("tensor", (lambda i_: lambda e: e.transpose(
                            pbf[:, i_ * 128:(i_ + 1) * 128],
                            qstage[:, i_ * 128:(i_ + 1) * 128], ident_b[:, :]))(i),
                            waits=tuple(w), inc="tps" if i == 7 else None)
                    tpsv = p.cur("tps")
                    for i in range(8):
                        w = [("tps", tpsv)] if i == 0 else []
                        if i == 0 and tg % 4 == 0:
                            w.append(("qst", p.cur("qst")))
                        p.emit("vector", (lambda i_, t4_: lambda e: e.tensor_copy(
                            qt_blk[:, i_ * 512 + t4_ * 128:
                                   i_ * 512 + t4_ * 128 + 128],
                            pbf[:, i_ * 128:(i_ + 1) * 128]))(i, tg % 4),
                            waits=tuple(w), inc="tpe" if i == 7 else None)
                    tpev = p.cur("tpe")
                    if tg % 4 == 3:
                        qc = tg // 4
                        for i in range(8):
                            w = [("tpe", tpev)] if i == 0 else []
                            p.dma("sync", (lambda i_, qc_, hg_: lambda e: e.dma_start(
                                out_scr[qc_, hg_ * 1024 + i_ * 128:
                                        hg_ * 1024 + (i_ + 1) * 128, :],
                                qt_blk[:, i_ * 512:(i_ + 1) * 512]))(i, qc, hg),
                                waits=tuple(w), inc="qst")
                else:
                    for hl in range(4):
                        w = [("bkf", qevv)] if hl == 0 else []
                        p.dma("sync", (lambda hl_, tg_, hg_: lambda e: e.dma_start(
                            vt_h.ap()[hg_ * 4 + hl_, tg_ * 128:(tg_ + 1) * 128, :],
                            qstage[:, hl_ * HD:(hl_ + 1) * HD]))(hl, tg, hg),
                            waits=tuple(w), inc="qst")
        p.barrier(pre={"sync": [("qst", p.cur("qst"))],
                       "vector": [("nrm", p.cur("nrm")), ("sev", p.cur("sev"))],
                       "scalar": [("bkf", p.cur("bkf"))]})

    for hg in range(2):
        emit_proj("k", hg, xbf_v, 64, sink_h.ap(), cosk_h.ap(), negm_row, r_cols,
                  kt_h.ap(), kmax2)
    for hg in range(2):
        emit_proj("v", hg, xbf_v, 64, None, None, negm_row, r_cols, None, None)
    for hg in range(2):
        emit_proj("q", hg, xqbf_v, 32, sinq_h.ap(), cosq_h.ap(), negmq_row, rq_cols,
                  qt_h.ap(), qmax2)

    p.emit("vector", lambda e: e.tensor_add(qmax2[:, :], qmax2[:, :], kmax2[:, :]),
           inc="sev")
    p.emit("tensor", lambda e: e.transpose(
        pmisc[0:NHC, 0:128], qmax2[:, :], ident_f[:, :]),
        waits=(("sev", p.cur("sev")),), inc="smm")
    p.emit("vector", lambda e: e.reduce_max(
        qmax2[0:NHC, 0:1], pmisc[0:NHC, 0:128], axis=AX.X),
        waits=(("smm", p.cur("smm")),))
    p.emit("vector", lambda e: e.tensor_scalar_mul(
        qmax2[0:NHC, 0:1], qmax2[0:NHC, 0:1], -1.0 / 32.0), inc="sev")
    p.emit("tensor", lambda e: e.transpose(
        pmisc[0:1, 0:NHC], qmax2[0:NHC, 0:1], ident_f[0:NHC, 0:NHC]),
        waits=(("sev", p.cur("sev")),), inc="smm")
    p.emit("vector", lambda e: e.tensor_copy(negc_row[0:1, :], pmisc[0:1, 0:NHC]),
           waits=(("smm", p.cur("smm")),), inc="sev")
    sevn2 = p.cur("sev")
    for h in range(NHC):
        p.emit("tensor", (lambda h_: lambda e: e.matmul(
            pmisc[:, h_:h_ + 1], onesrow_f[0:1, :], negc_row[0:1, h_:h_ + 1],
            start=True, stop=True))(h),
            waits=(("sev", sevn2),) if h == 0 else (),
            inc="smm" if h == NHC - 1 else None)
    p.emit("vector", lambda e: e.tensor_copy(negcb[:, :], pmisc[:, 0:NHC]),
           waits=(("smm", p.cur("smm")),))
    p.emit("vector", lambda e: e.tensor_scalar_add(
        negcy[:, :], negcb[:, :], yflag_sb[:, 0:1]), inc="sev")
    p.barrier(pre={"vector": [("sev", p.cur("sev"))]})
    st2.close()

    st3 = ExitStack()
    kt_sb = st3.enter_context(nc.sbuf_tensor("kt_sb", [128, 2 * S], BF16))
    vt_sb = st3.enter_context(nc.sbuf_tensor("vt_sb", [128, 64 * HD], BF16))
    qch = [st3.enter_context(nc.sbuf_tensor(f"qch{i}", [128, 2 * 512], BF16))
           for i in range(2)]
    psb = [st3.enter_context(nc.sbuf_tensor(f"psb{i}", [128, 512], BF16))
           for i in range(2)]
    srow = st3.enter_context(nc.sbuf_tensor("srow", [1, 512], F32))
    rbf = st3.enter_context(nc.sbuf_tensor("rbf", [128, 512], F32))
    ostage = st3.enter_context(nc.sbuf_tensor("ostage", [128, 1024], BF16))

    sregs = {}

    def sync_regs(e):
        if "tmp" not in sregs:
            sregs["tmp"] = e.alloc_register("shtmp")
        return sregs
    sidx = [0]

    gi = 0
    for h in range(NHC):
        for dh in range(2):
            w = []
            if h > 0 and dh == 0:
                w = [("tsc", p.cur("tsc")), ("tpv", p.cur("tpv"))]
            p.dma("sync", (lambda h_, dh_: lambda e: e.dma_start(
                kt_sb[:, dh_ * S:(dh_ + 1) * S],
                bass.AP(kt_h, (h_ * HD + dh_ * 128) * 512,
                        [[512, 128], [2048 * 512, 16], [1, 512]])))(h, dh),
                waits=tuple(w), inc="ktld")
        for sp in range(8):
            p.dma("sync", (lambda h_, sp_: lambda e: e.dma_start(
                vt_sb[:, sp_ * 8 * HD:(sp_ + 1) * 8 * HD],
                bass.AP(vt_h, h_ * S * HD + sp_ * 8 * 128 * HD,
                        [[HD, 128], [128 * HD, 8], [1, HD]])))(h, sp),
                inc="ktld")
        ktl = p.cur("ktld")
        for qc in range(8):
            qb = qc % 2
            for dh in range(2):
                w = []
                if dh == 0:
                    w.append(("tsc", p.cur("tsc")))
                    if qc == 0:
                        w.append(("ktld", ktl))
                p.dma("sync", (lambda h_, dh_, qc_, qb_: lambda e: e.dma_start(
                    qch[qb_][:, dh_ * 512:(dh_ + 1) * 512],
                    bass.AP(qt_h, qc_ * 2048 * 512 + (h_ * HD + dh_ * 128) * 512,
                            [[512, 128], [1, 512]])))(h, dh, qc, qb),
                    waits=tuple(w), inc="qcld")
            qcl = p.cur("qcld")
            nj = 4 * qc + 36
            pend = None
            for j in range(nj):
                sbank = pp[gi % 2]
                pb = gi % 2
                if 4 * qc <= j < 4 * qc + 4:
                    cls, d = "d1", j - 4 * qc
                elif 4 * qc + 32 <= j:
                    cls, d = "d2", j - 4 * qc - 32
                elif j >= 4 * qc + 4:
                    cls, d = "y", 0
                else:
                    cls, d = "x", 0
                w = []
                if j == 0:
                    w.append(("qcld", qcl))
                if gi >= 2:
                    w.append(("sexp", gi - 1))
                for dh in range(2):
                    p.emit("tensor", (lambda j_, dh_, qb_, st_, sp_, b_:
                           lambda e: e.matmul(
                               b_[:, :],
                               kt_sb[:, dh_ * S + j_ * 128: dh_ * S + j_ * 128 + 128],
                               qch[qb_][:, dh_ * 512:(dh_ + 1) * 512],
                               start=st_, stop=sp_))(j, dh, qb, dh == 0, dh == 1,
                                                     sbank),
                           waits=tuple(w) if dh == 0 else (),
                           inc="tsc" if dh == 1 else None)
                tscv = p.cur("tsc")
                if cls in ("d1", "d2"):
                    msb = maskd1_sb if cls == "d1" else maskd2_sb
                    p.emit("vector", (lambda d_, b_, m_: lambda e: e.tensor_add(
                        b_[:, :], b_[:, :], m_[:, d_ * 512:(d_ + 1) * 512]))(
                            d, sbank, msb), waits=(("tsc", tscv),), inc="vmsk")
                w = [("vmsk", p.cur("vmsk"))] if cls in ("d1", "d2") else                     [("tsc", tscv)]
                if gi >= 2:
                    w.append(("tpv", gi - 1))
                bias_t = negcy if cls in ("y", "d2") else negcb
                p.emit("scalar", (lambda h_, b_, pb_, bt_: lambda e: e.activation(
                    psb[pb_][:, :], b_[:, :], AF.Exp,
                    bias=bt_[:, h_:h_ + 1], scale=1.0 / 16.0))(h, sbank, pb, bias_t),
                    waits=tuple(w), inc="sexp")
                sev_ = p.cur("sexp")

                def emit_pv(j_, pb_, sexp_tgt):
                    for dh in range(2):
                        w2 = []
                        if dh == 0:
                            w2.append(("sexp", sexp_tgt))
                            if j_ == 0:
                                w2.append(("voev", p.cur("voev")))
                        p.emit("tensor",
                               (lambda j2, dh_, pb2, st_, sp_: lambda e: e.matmul(
                                   pp[2 + dh_][:, :],
                                   vt_sb[:, j2 * HD + dh_ * 128:
                                         j2 * HD + dh_ * 128 + 128],
                                   psb[pb2][:, :], start=st_, stop=sp_))(
                                       j_, dh, pb_, j_ == 0, j_ == nj - 1),
                               waits=tuple(w2))
                    w2 = []
                    if j_ == 0:
                        w2.append(("vrs", p.cur("vrs")))
                    p.emit("tensor", (lambda pb2, st_, sp_: lambda e: e.matmul(
                        ps_a[0:1, :], ones_b[:, :], psb[pb2][:, :],
                        start=st_, stop=sp_))(pb_, j_ == 0, j_ == nj - 1),
                        waits=tuple(w2), inc="tpv")

                if pend is not None:
                    emit_pv(*pend)
                pend = (j, pb, sev_)
                if j == nj - 1:
                    emit_pv(*pend)
                    pend = None
                gi += 1
            tpvv = p.cur("tpv")
            p.emit("vector", lambda e: e.tensor_copy(srow[0:1, :], ps_a[0:1, :]),
                   waits=(("tpv", tpvv),))
            p.emit("vector", lambda e: e.reciprocal(srow[0:1, :], srow[0:1, :]),
                   inc="vrs")
            p.emit("tensor", lambda e: e.matmul(
                pmisc[:, :], onesrow_f[0:1, :], srow[0:1, :], start=True, stop=True),
                waits=(("vrs", p.cur("vrs")), ("voev", p.cur("voev"))), inc="trb")
            p.emit("vector", lambda e: e.tensor_copy(rbf[:, :], pmisc[:, :]),
                   waits=(("trb", p.cur("trb")), ("ost", p.cur("ost"))))
            for dh in range(2):
                p.emit("vector", (lambda dh_: lambda e: e.tensor_mul(
                    ostage[:, dh_ * 512:(dh_ + 1) * 512], pp[2 + dh_][:, :],
                    rbf[:, :]))(dh), inc="voev" if dh == 1 else None)
            vov = p.cur("voev")
            for dh in range(2):
                cst = (h * HD + dh * 128) * QW + qc * 512
                p.dma("sync", (lambda dh_, cst_: lambda e: e.dma_start(
                    bass.AP(attn_loc, cst_, [[QW, 128], [1, 512]]),
                    ostage[:, dh_ * 512:(dh_ + 1) * 512]))(dh, cst),
                    waits=(("voev", vov),) if dh == 0 else (), inc="ost")

    ostv = p.cur("ost")
    for k in range(4):
        p.dma("gpsimd", (lambda k_: lambda e: (
            sync_regs(e),
            e.reg_load(sregs["tmp"], offtab_sb[0:1, k_:k_ + 1]),
            e.dma_start(bass.AP(attnsh, sregs["tmp"], [[2097152, 1], [1, 2097152]]),
                        bass.AP(attn_loc, k_ * 2097152,
                                [[2097152, 1], [1, 2097152]])))[-1])(k),
            waits=(("ost", ostv),) if k == 0 else (), inc="xpo")

    def xch_fn(e):
        foff = e.alloc_register("foff")
        poff = e.alloc_register("poff")
        pv = e.alloc_register("pv")
        nv = e.alloc_register("nv")
        nf = e.alloc_register("nf")
        e.reg_load(foff, offtab_sb[0:1, 144:145])
        e.reg_load(nv, nonce_sb[0:1, 0:1].bitcast(mybir.dt.int32))
        e.reg_load(poff, offtab_sb[0:1, 145:146])
        e.dma_start(bass.AP(flagsh, foff, [[64, 1], [1, 64]]),
                    nonce_sb[0:1, :]).then_inc(sems["xpo"], 16)
        e.wait_ge(sems["xpo"], 16 * 5)
        e.reg_mov(nf, 1)
        with e.While(nf):
            e.dma_start(flag_sb[0:1, :],
                        bass.AP(flagsh, poff, [[64, 1], [1, 64]])).then_inc(
                            sems["xpo"], 16)
            e.nop(cycle_cnt=40000)
            e.reg_load(pv, flag_sb[0:1, 0:1].bitcast(mybir.dt.int32))
            e.reg_alu(nf, pv, nv, OP.not_equal)
        return None
    p.emit("gpsimd", xch_fn, waits=(("xpo", p.cur("xpo")), ("wcns", 16)))
    p.emit("gpsimd", lambda e: e.nop(), inc="xch")
    xchv = p.cur("xch")
    st3.close()
    stm.close()

    stB = ExitStack()
    abuf = stB.enter_context(nc.sbuf_tensor("abuf", [128, 32 * 512], BF16))
    wbuf2 = [stB.enter_context(nc.sbuf_tensor(f"wbufB{i}", [128, 16 * 512], BF16))
             for i in range(2)]
    x2g = stB.enter_context(nc.sbuf_tensor("x2g", [128, 4 * D], F32))
    x2bf = stB.enter_context(nc.sbuf_tensor("x2bf", [128, D], BF16))
    x2T = stB.enter_context(nc.sbuf_tensor("x2T", [128, 32 * 128], BF16))
    sqs = stB.enter_context(nc.sbuf_tensor("sqs", [128, D], BF16))

    woT_v = woT_h.ap().rearrange("(c p) d -> p c d", p=128)

    def eread_fn(sp, g):
        idx_ = 128 + g * 2 + sp

        def fn(e):
            r = sync_regs(e)
            e.reg_load(r["tmp"], offtab_sb[0:1, idx_:idx_ + 1])
            return e.dma_start(
                abuf[:, sp * 16 * 512:(sp + 1) * 16 * 512],
                bass.AP(attnsh, r["tmp"],
                        [[QW, 128], [128 * QW, 16], [1, 512]]))
        return fn

    for g in range(4):
        t0 = g * 512
        for sp in range(2):
            w = [("womm", p.cur("womm"))] if sp == 0 else []
            if g == 0 and sp == 0:
                w.append(("xch", xchv))
            p.dma("gpsimd", eread_fn(sp, g), waits=tuple(w), inc="ald")
        aldv = p.cur("ald")
        for ti in range(4):
            w = ([(f"x2wr{(g - 1) % 2}", 64 * ((g - 1) // 2 + 1)),
                  ("sq", p.cur("sq")), ("x2ev", p.cur("x2ev")),
                  ("rms", p.cur("rms"))]
                 if (ti == 0 and g > 0) else [])
            p.dma("sync", (lambda ti_, t0_: lambda e: e.dma_start(
                x2g[:, ti_ * D:(ti_ + 1) * D],
                xres_h.ap()[t0_ + ti_ * 128: t0_ + ti_ * 128 + 128, :]))(ti, t0),
                waits=tuple(w), inc="xrd")
        xrdv = p.cur("xrd")
        for dc in range(8):
            wldns = []
            for sp in range(2):
                wb = wbuf2[sp]
                w = []
                if dc > 0 or g > 0:
                    if sp == 0:
                        w.append(("womh", p.cur("womh")))
                    else:
                        w.append(("womm", p.cur("womm")))
                p.dma("gpsimd", (lambda sp_, dc_, wb_: lambda e: e.dma_start(
                    wb_[:, :],
                    woT_v[:, sp_ * 16:(sp_ + 1) * 16, dc_ * 512:(dc_ + 1) * 512]))(
                        sp, dc, wb), waits=tuple(w), inc=f"wld{sp}")
                wldns.append((f"wld{sp}", p.cur(f"wld{sp}")))
            for ti in range(4):
                bank = pp[ti % 2]
                for c in range(32):
                    w = []
                    if c == 0:
                        w = [wldns[0], ("ald", aldv),
                             ("x2ev", max(0, p.cur("x2ev") - 1))]
                    elif c == 16:
                        w = [wldns[1]]
                    p.emit("tensor",
                           (lambda c_, ti_, st_, sp_, b_: lambda e: e.matmul(
                               b_[:, :],
                               abuf[:, c_ * 512 + ti_ * 128:
                                    c_ * 512 + ti_ * 128 + 128],
                               wbuf2[c_ // 16][:, (c_ % 16) * 512:
                                               (c_ % 16 + 1) * 512],
                               start=st_, stop=sp_))(c, ti, c == 0, c == 31,
                                                     bank),
                           waits=tuple(w),
                           inc=("womm" if c == 31 else
                                ("womh" if c == 15 else None)))
                wommv = p.cur("womm")
                w = [("womm", wommv)]
                if dc == 0:
                    w.append(("xrd", xrdv))
                p.emit("vector", (lambda ti_, dc_, b_: lambda e: e.tensor_add(
                    x2g[:, ti_ * D + dc_ * 512: ti_ * D + (dc_ + 1) * 512],
                    b_[:, :],
                    x2g[:, ti_ * D + dc_ * 512: ti_ * D + (dc_ + 1) * 512]))(
                        ti, dc, bank), waits=tuple(w), inc="x2ev")
        x2evv = p.cur("x2ev")
        for ti in range(4):
            tg = g * 4 + ti
            p.dma("gpsimd", (lambda ti_, t0_: lambda e: e.dma_start(
                x2_h.ap()[t0_ + ti_ * 128: t0_ + ti_ * 128 + 128, :],
                x2g[:, ti_ * D:(ti_ + 1) * D]))(ti, t0),
                waits=(("x2ev", x2evv),) if ti == 0 else (), inc=f"x2wr{g % 2}")
            p.emit("vector", (lambda ti_, tg_: lambda e: e.reduce_sum(
                m16[:, tg_:tg_ + 1], x2g[:, ti_ * D:(ti_ + 1) * D], axis=AX.X))(ti, tg),
                waits=(("x2ev", x2evv),))
            p.emit("vector", (lambda tg_: lambda e: e.tensor_scalar_mul(
                m16[:, tg_:tg_ + 1], m16[:, tg_:tg_ + 1], 1.0 / D))(tg))
            p.emit("scalar", (lambda ti_, tg_: lambda e: e.activation(
                sqs[:, :], x2g[:, ti_ * D:(ti_ + 1) * D], AF.Square,
                accum_out=rs16[:, tg_:tg_ + 1]))(ti, tg),
                waits=(("x2ev", x2evv),), inc="sq")
            p.emit("vector", (lambda tg_: lambda e: e.tensor_scalar_mul(
                rs16[:, tg_:tg_ + 1], rs16[:, tg_:tg_ + 1], 1.0 / D))(tg),
                waits=(("sq", p.cur("sq")),))
            p.emit("vector", (lambda tg_: lambda e: e.tensor_mul(
                sc1[:, 0:1], m16[:, tg_:tg_ + 1], m16[:, tg_:tg_ + 1]))(tg))
            p.emit("vector", (lambda tg_: lambda e: e.tensor_sub(
                rs16[:, tg_:tg_ + 1], rs16[:, tg_:tg_ + 1], sc1[:, 0:1]))(tg),
                inc="rms")
            p.emit("scalar", (lambda tg_: lambda e: e.activation(
                rs16[:, tg_:tg_ + 1], rs16[:, tg_:tg_ + 1], AF.Sqrt,
                bias=epsl[:, 0:1], scale=1.0))(tg),
                waits=(("rms", p.cur("rms")),), inc="sq")
            p.emit("vector", (lambda tg_: lambda e: e.reciprocal(
                rs16[:, tg_:tg_ + 1], rs16[:, tg_:tg_ + 1]))(tg),
                waits=(("sq", p.cur("sq")),))
            p.emit("vector", (lambda ti_, tg_: lambda e: e.tensor_scalar(
                x2bf[:, :], x2g[:, ti_ * D:(ti_ + 1) * D],
                m16[:, tg_:tg_ + 1], rs16[:, tg_:tg_ + 1],
                OP.subtract, OP.mult))(ti, tg),
                waits=(("lgT", p.cur("lgT")),), inc="x2ev")
            cstv = p.cur("x2ev")
            p.emit("scalar", (lambda tg_: lambda e: e.activation(
                sqs[:, :], x2bf[:, :], AF.Square,
                accum_out=r2c[:, tg_:tg_ + 1]))(tg),
                waits=(("x2ev", cstv),), inc="sq")
            p.emit("scalar", (lambda tg_: lambda e: e.activation(
                r2c[:, tg_:tg_ + 1], r2c[:, tg_:tg_ + 1], AF.Sqrt,
                bias=epsb[:, 0:1], scale=1.0 / D))(tg),
                waits=(("sq", p.cur("sq")),), inc="sq")
            p.emit("vector", (lambda tg_: lambda e: e.reciprocal(
                r2c[:, tg_:tg_ + 1], r2c[:, tg_:tg_ + 1]))(tg),
                waits=(("sq", p.cur("sq")),), inc="rms")
            for rnd in range(4):
                for i in range(8):
                    w = []
                    if i == 0:
                        w = [("x2ev", cstv), ("lgTe", p.cur("lgTe"))]
                    p.emit("tensor", (lambda rnd_, i_: lambda e: e.transpose(
                        pbf[:, i_ * 128:(i_ + 1) * 128],
                        x2bf[:, rnd_ * 1024 + i_ * 128: rnd_ * 1024 + (i_ + 1) * 128],
                        ident_b[:, :]))(rnd, i),
                        waits=tuple(w), inc="lgT" if i == 7 else None)
                tvv = p.cur("lgT")
                for i in range(8):
                    p.emit("vector", (lambda rnd_, i_: lambda e: e.tensor_copy(
                        x2T[:, (rnd_ * 8 + i_) * 128:(rnd_ * 8 + i_ + 1) * 128],
                        pbf[:, i_ * 128:(i_ + 1) * 128]))(rnd, i),
                        waits=(("lgT", tvv),) if i == 0 else (),
                        inc="lgTe" if i == 7 else None)
            tev = p.cur("lgTe")
            for c in range(32):
                w = []
                if c == 0:
                    w = [("lgTe", tev), ("lgev", p.cur("lgev")), ("wcns", 16)]
                p.emit("tensor", (lambda c_, st_, sp_: lambda e: e.matmul(
                    pmisc[:, 0:24], x2T[:, c_ * 128:(c_ + 1) * 128],
                    wall_b[:, c_ * 24:(c_ + 1) * 24], start=st_, stop=sp_))(
                        c, c == 0, c == 31),
                    waits=tuple(w), inc="lgmm" if c == 31 else None)
            p.emit("scalar", (lambda tg_: lambda e: e.activation(
                lga[:, tg_ * 24:(tg_ + 1) * 24], pmisc[:, 0:24], AF.Copy,
                bias=0.0, scale=r2c[:, tg_:tg_ + 1]))(tg),
                waits=(("lgmm", p.cur("lgmm")), ("rms", p.cur("rms"))), inc="sq")
            p.emit("vector", (lambda tg_: lambda e: e.tensor_add(
                lga[:, tg_ * 24:(tg_ + 1) * 24], lga[:, tg_ * 24:(tg_ + 1) * 24],
                basesb[:, :]))(tg),
                waits=(("sq", p.cur("sq")),), inc="lgev")
    p.barrier(pre={"gpsimd": [("x2wr0", p.cur("x2wr0")),
                              ("x2wr1", p.cur("x2wr1"))],
                   "vector": [("lgev", p.cur("lgev"))]})
    stB.close()

    lview = lga[:, :].rearrange("p (t c) -> p t c", c=24)
    Zv = Z[:, :].rearrange("p (t i j) -> p t i j", i=4, j=4)
    tv_ = tmp[:, :].rearrange("p (t i j) -> p t i j", i=4, j=4)
    lnv_v = lnv[:, :].rearrange("p (t j) -> p t j", j=4)
    lnu_v = lnu[:, :].rearrange("p (t i) -> p t i", i=4)

    p.emit("vector", lambda e: e.tensor_scalar_mul(
        Zv, lview[:, :, 0:16].rearrange("p t (i j) -> p t i j", j=4), 1.0 / TAU))
    p.emit("vector", lambda e: e.memset(lnv[:, :], 0.0), inc="snk")
    for it in range(SINK_IT):
        for i in range(4):
            p.emit("vector", (lambda i_: lambda e: e.tensor_sub(
                tv_[:, :, i_, :], Zv[:, :, i_, :], lnv_v))(i),
                waits=(("snk", p.cur("snk")),) if i == 0 else (),
                inc="snk" if i == 3 else None)
        p.emit("scalar", lambda e: e.activation(tmpe[:, :], tmp[:, :], AF.Exp),
               waits=(("snk", p.cur("snk")),), inc="sq")
        p.emit("vector", lambda e: e.tensor_reduce(
            ssum[:, :], tmpe[:, :].rearrange("p (ti j) -> p ti j", j=4),
            axis=AX.X, op=OP.add),
            waits=(("sq", p.cur("sq")),), inc="snk")
        p.emit("scalar", lambda e: e.activation(lnu[:, :], ssum[:, :], AF.Ln),
               waits=(("snk", p.cur("snk")),), inc="sq")
        sqv = p.cur("sq")
        for j in range(4):
            p.emit("vector", (lambda j_: lambda e: e.tensor_sub(
                tv_[:, :, j_, :], Zv[:, :, :, j_], lnu_v))(j),
                waits=(("sq", sqv),) if j == 0 else (),
                inc="snk" if j == 3 else None)
        p.emit("scalar", lambda e: e.activation(tmpe[:, :], tmp[:, :], AF.Exp),
               waits=(("snk", p.cur("snk")),), inc="sq")
        p.emit("vector", lambda e: e.tensor_reduce(
            ssum[:, :], tmpe[:, :].rearrange("p (tj i) -> p tj i", i=4),
            axis=AX.X, op=OP.add),
            waits=(("sq", p.cur("sq")),), inc="snk")
        p.emit("scalar", lambda e: e.activation(lnv[:, :], ssum[:, :], AF.Ln),
               waits=(("snk", p.cur("snk")),), inc="sq")
        p.emit("vector", lambda e: e.tensor_copy(sm[:, 0:1], lnv[:, 0:1]),
               waits=(("sq", p.cur("sq")),), inc="snk")
    u4v = u4[:, :].rearrange("p (t i j) -> p t i j", i=4, j=4)
    for j in range(4):
        p.emit("vector", (lambda j_: lambda e: e.tensor_copy(
            u4v[:, :, :, j_], lnu_v))(j))
    p.emit("vector", lambda e: e.tensor_sub(Hm[:, :], Z[:, :], u4[:, :]))
    Hv = Hm[:, :].rearrange("p (t i j) -> p t i j", i=4, j=4)
    for i in range(4):
        p.emit("vector", (lambda i_: lambda e: e.tensor_sub(
            tv_[:, :, i_, :], Hv[:, :, i_, :], lnv_v))(i),
            inc="snk" if i == 3 else None)
    p.emit("scalar", lambda e: e.activation(Hm[:, :], tmp[:, :], AF.Exp),
           waits=(("snk", p.cur("snk")),), inc="sq")
    p.emit("vector", lambda e: e.tensor_scalar_mul(Hm[:, :], Hm[:, :], gb[:, 0:1]),
           waits=(("sq", p.cur("sq")),))
    diagv = Hm[:, :].rearrange("p (t d) -> p t d", d=16)[:, :, 0:16:5]
    p.emit("vector", lambda e: e.tensor_scalar_add(diagv, diagv, omgb[:, 0:1]),
           inc="snk")
    for (src0, dst) in ((16, hp), (20, hq)):
        pv = lview[:, :, src0:src0 + 4]
        dv = dst[:, :].rearrange("p (t n) -> p t n", n=4)
        s4 = sm4[:, :].rearrange("p (t n) -> p t n", n=4)
        p.emit("vector", (lambda pv_: lambda e: e.reduce_max(
            sm[:, :], pv_, axis=AX.X))(pv))
        for n in range(4):
            p.emit("vector", (lambda n_, s4_: lambda e: e.tensor_copy(
                s4_[:, :, n_], sm[:, :]))(n, s4))
        p.emit("vector", (lambda pv_, dv_, s4_: lambda e: e.tensor_sub(
            dv_, pv_, s4_))(pv, dv, s4), inc="snk")
        p.emit("scalar", (lambda d_: lambda e: e.activation(
            d_[:, :], d_[:, :], AF.Exp))(dst),
            waits=(("snk", p.cur("snk")),), inc="sq")
        p.emit("vector", (lambda dv_: lambda e: e.tensor_reduce(
            sm[:, :], dv_, axis=AX.X, op=OP.add))(dv),
            waits=(("sq", p.cur("sq")),))
        p.emit("vector", lambda e: e.reciprocal(sm[:, :], sm[:, :]))
        for n in range(4):
            p.emit("vector", (lambda n_, s4_: lambda e: e.tensor_copy(
                s4_[:, :, n_], sm[:, :]))(n, s4))
        p.emit("vector", (lambda d_: lambda e: e.tensor_mul(
            d_[:, :], d_[:, :], sm4[:, :]))(dst), inc="snk")
    p.barrier(pre={"vector": [("snk", p.cur("snk"))]})

    stC = ExitStack()
    wbr_b = stC.enter_context(nc.sbuf_tensor("wbr_b", [128, 8 * 1024], BF16))
    x2m = [stC.enter_context(nc.sbuf_tensor(f"x2m{i}", [128, D], F32))
           for i in range(2)]
    xpre = stC.enter_context(nc.sbuf_tensor("xpre", [128, 1024], F32))
    ln2m = [stC.enter_context(nc.sbuf_tensor(f"ln2m{i}", [128, D], F32))
            for i in range(2)]
    xpre_b = stC.enter_context(nc.sbuf_tensor("xpre_b", [128, 1024], BF16))
    xpT = stC.enter_context(nc.sbuf_tensor("xpT", [128, 8 * 128], BF16))
    ybuf = stC.enter_context(nc.sbuf_tensor("ybuf", [128, 1024], F32))
    obuf = [stC.enter_context(nc.sbuf_tensor(f"obuf{i}", [128, D], F32))
            for i in range(2)]

    p.dma("gpsimd", lambda e: e.dma_start(
        wbr_b[:, :], wbr_h.ap().rearrange("(c p) d -> p c d", p=128)), inc="wbld")
    wbl = p.cur("wbld")
    for tg in range(16):
        bi = tg % 2
        w = [("mix", max(0, p.cur("mix") - 1))] if tg >= 2 else []
        p.dma("sync", (lambda tg_, bi_: lambda e: e.dma_start(
            x2m[bi_][:, :], x2_h.ap()[tg_ * 128:(tg_ + 1) * 128, :]))(tg, bi),
            waits=tuple(w), inc=f"x2rd{bi}")
        x2rv = p.cur(f"x2rd{bi}")
        w = [(f"x2rd{bi}", x2rv)]
        if tg == 0:
            w.append(("wbld", wbl))
        p.emit("vector", (lambda bi_, tg_: lambda e: e.tensor_scalar(
            ln2m[bi_][:, :], x2m[bi_][:, :], m16[:, tg_:tg_ + 1],
            rs16[:, tg_:tg_ + 1], OP.subtract, OP.mult))(bi, tg), waits=tuple(w))
        p.emit("vector", (lambda bi_, tg_: lambda e: e.tensor_scalar_mul(
            xpre[:, :], ln2m[bi_][:, 0:1024], hp[:, 4 * tg_:4 * tg_ + 1]))(bi, tg))
        for n in range(1, 4):
            p.emit("vector", (lambda bi_, tg_, n_: lambda e: e.scalar_tensor_tensor(
                xpre[:, :], ln2m[bi_][:, n_ * 1024:(n_ + 1) * 1024],
                hp[:, 4 * tg_ + n_:4 * tg_ + n_ + 1], xpre[:, :],
                OP.mult, OP.add))(bi, tg, n))
        p.emit("vector", lambda e: e.tensor_copy(xpre_b[:, :], xpre[:, :]),
               waits=(("lgT", p.cur("lgT")),), inc="mix")
        mxv = p.cur("mix")
        for i in range(8):
            w = []
            if i == 0:
                w = [("mix", mxv), ("lgTe", p.cur("lgTe")), ("yev", p.cur("yev"))]
            p.emit("tensor", (lambda i_: lambda e: e.transpose(
                pbf[:, i_ * 128:(i_ + 1) * 128], xpre_b[:, i_ * 128:(i_ + 1) * 128],
                ident_b[:, :]))(i), waits=tuple(w), inc="lgT" if i == 7 else None)
        tvv = p.cur("lgT")
        for i in range(8):
            p.emit("vector", (lambda i_: lambda e: e.tensor_copy(
                xpT[:, i_ * 128:(i_ + 1) * 128], pbf[:, i_ * 128:(i_ + 1) * 128]))(i),
                waits=(("lgT", tvv),) if i == 0 else (),
                inc="lgTe" if i == 7 else None)
        tev = p.cur("lgTe")
        for n in range(2):
            for c in range(8):
                w = []
                if n == 0 and c == 0:
                    w = [("lgTe", tev), ("yev", p.cur("yev"))]
                p.emit("tensor", (lambda n_, c_, st_, sp_: lambda e: e.matmul(
                    pp[n_][:, :], xpT[:, c_ * 128:(c_ + 1) * 128],
                    wbr_b[:, c_ * 1024 + n_ * 512: c_ * 1024 + (n_ + 1) * 512],
                    start=st_, stop=sp_))(n, c, c == 0, c == 7),
                    waits=tuple(w), inc="ymm" if (n == 1 and c == 7) else None)
        ymv = p.cur("ymm")
        for n in range(2):
            p.emit("vector", (lambda n_: lambda e: e.tensor_copy(
                ybuf[:, n_ * 512:(n_ + 1) * 512], pp[n_][:, :]))(n),
                waits=(("ymm", ymv),) if n == 0 else (),
                inc="yev" if n == 1 else None)
        yev = p.cur("yev")
        for i in range(4):
            w = []
            if i == 0:
                w = [("yev", yev), (f"ost{bi}", p.cur(f"ost{bi}"))]
            p.emit("vector", (lambda bi_, tg_, i_: lambda e: e.scalar_tensor_tensor(
                obuf[bi_][:, i_ * 1024:(i_ + 1) * 1024], ln2m[bi_][:, 0:1024],
                Hm[:, 16 * tg_ + 4 * i_: 16 * tg_ + 4 * i_ + 1],
                x2m[bi_][:, i_ * 1024:(i_ + 1) * 1024],
                OP.mult, OP.add))(bi, tg, i),
                waits=tuple(w))
            for jj in range(1, 4):
                p.emit("vector",
                       (lambda bi_, tg_, i_, j_: lambda e: e.scalar_tensor_tensor(
                           obuf[bi_][:, i_ * 1024:(i_ + 1) * 1024],
                           ln2m[bi_][:, j_ * 1024:(j_ + 1) * 1024],
                           Hm[:, 16 * tg_ + 4 * i_ + j_: 16 * tg_ + 4 * i_ + j_ + 1],
                           obuf[bi_][:, i_ * 1024:(i_ + 1) * 1024],
                           OP.mult, OP.add))(bi, tg, i, jj))
            p.emit("vector", (lambda bi_, tg_, i_: lambda e: e.scalar_tensor_tensor(
                obuf[bi_][:, i_ * 1024:(i_ + 1) * 1024], ybuf[:, :],
                hq[:, 4 * tg_ + i_:4 * tg_ + i_ + 1],
                obuf[bi_][:, i_ * 1024:(i_ + 1) * 1024],
                OP.mult, OP.add))(bi, tg, i),
                inc="mix" if i == 3 else None)
        mxo = p.cur("mix")
        p.dma("sync", (lambda bi_, tg_: lambda e: e.dma_start(
            out_h.ap()[tg_ * 128:(tg_ + 1) * 128, :], obuf[bi_][:, :]))(bi, tg),
            waits=(("mix", mxo),), inc=f"ost{bi}")

    p.emit("gpsimd", None, waits=(("ost0", p.cur("ost0")), ("ost1", p.cur("ost1"))))
    p.emit("sync", None, waits=(("ost0", p.cur("ost0")), ("ost1", p.cur("ost1"))))

    _block_emit(nc, p)
    stC.close()
    stack.close()
    return nc


def _fused_masks():
    base = np.zeros((4, 128, 512), np.float32)
    kk = np.arange(128)[:, None]
    qq = np.arange(512)[None, :]
    for di, d in enumerate((0, 128, 256, 384)):
        base[di] = np.where(kk + d > qq, np.float32(-1e9), np.float32(0.0))
    diag = np.ascontiguousarray(base.reshape(512, 512))
    zero = np.zeros((512, 512), np.float32)
    return diag, zero


def _fused_rope_tables():
    i = np.arange(RD // 2, dtype=np.float64)
    inv_freq = 1.0 / (10000.0 ** (2.0 * i / RD))
    t = np.arange(S, dtype=np.float64)
    ang = np.outer(t, inv_freq)
    sin8 = np.tile(np.sin(ang).astype(np.float32), (1, 8))
    cos8 = np.tile(np.cos(ang).astype(np.float32), (1, 8))
    return np.ascontiguousarray(sin8), np.ascontiguousarray(cos8)


def kernel_fused(inputs):
    x = np.asarray(inputs["L0_attn_hidden_states"])
    wq = np.asarray(inputs["L0_attn_wq"])
    wk = np.asarray(inputs["L0_attn_wk"])
    wv = np.asarray(inputs["L0_attn_wv"])
    wo = np.asarray(inputs["L0_attn_wo"])

    sin8, cos8 = _fused_rope_tables()
    diagm, zerom = _fused_masks()
    ident = np.eye(128, dtype=np.float32)
    onesrow = np.ones((1, 128), np.float32)
    nonce = np.float32(np.random.default_rng().integers(1, 2 ** 30))
    noncev = np.full((1, 64), nonce, np.float32)

    xT = [np.ascontiguousarray(x[b].T) for b in range(B)]
    wT = {"q": wq, "k": wk, "v": wv}
    DQC = 2048
    wTs = {pj: [np.ascontiguousarray(wT[pj][r * DQC:(r + 1) * DQC, :].T)
                for r in range(2)] for pj in wT}
    woT = np.ascontiguousarray(wo.T)
    wallT = np.ascontiguousarray(np.concatenate(
        [np.asarray(inputs["L0_ffn_W_res"]), np.asarray(inputs["L0_ffn_W_pre"]),
         np.asarray(inputs["L0_ffn_W_post"])], axis=0).T)
    bases = np.concatenate(
        [np.asarray(inputs["L0_ffn_H_res_base"]).reshape(-1),
         np.asarray(inputs["L0_ffn_H_pre_base"]).reshape(-1),
         np.asarray(inputs["L0_ffn_H_post_base"]).reshape(-1)]
    ).reshape(1, 24).astype(np.float32)
    gate2 = np.asarray(inputs["L0_ffn_gate"]).reshape(1, 1).astype(np.float32)
    wbr = np.asarray(inputs["L0_ffn_W_branch"])

    nc = build_fused()
    in_maps = []
    for c in range(NCORE):
        r = c % 2
        pp_ = c // 2
        b = pp_ // 2
        h2 = pp_ % 2
        q0 = h2 * 4096
        tok0 = q0 + r * TPC
        in_maps.append(dict(
            xT=xT[b],
            xq=np.ascontiguousarray(xT[b][:, q0:q0 + 4096]),
            wqT=wTs["q"][r], wkT=wTs["k"][r], wvT=wTs["v"][r],
            sinq=np.ascontiguousarray(sin8[q0:q0 + 4096]),
            cosq=np.ascontiguousarray(cos8[q0:q0 + 4096]),
            sink=sin8, cosk=cos8,
            maskd1=(diagm if h2 == 0 else zerom),
            maskd2=(zerom if h2 == 0 else diagm),
            yflag=np.full((128, 1), -1e9 if h2 == 0 else 0.0, np.float32),
            roff=np.array([[r * TPC, r * DQC * 4096, (1 - r) * 64 and r * 0 or r * 64,
                            0]], dtype=np.int32),
            noncev=noncev,
            ident=ident, onesrow=onesrow,
            xres=np.ascontiguousarray(x[b, tok0:tok0 + TPC, :]),
            woT=woT, wallT=wallT, bases=bases, gate=gate2, wbr=wbr))
        in_maps[-1]["roff"] = np.array(
            [[r * TPC, r * DQC * 4096, r * 64, 0]], dtype=np.int32)
        ot = np.zeros((1, 256), np.int32)
        for k in range(4):
            ot[0, k] = r * DQC * 4096 + k * 2097152
        for gg in range(4):
            for spp in range(2):
                ot[0, 128 + gg * 2 + spp] = (r * TPC + gg * 512
                                             + spp * 16 * 128 * 4096)
        ot[0, 144] = r * 64
        ot[0, 145] = (1 - r) * 64
        in_maps[-1]["offtab"] = ot
    res = run_bass_kernel_spmd(nc, in_maps, core_ids=list(range(NCORE)))
    out = np.empty((B, S, D), np.float32)
    for c in range(NCORE):
        r = c % 2
        pp_ = c // 2
        b = pp_ // 2
        tok0 = (pp_ % 2) * 4096 + r * TPC
        out[b, tok0:tok0 + TPC, :] = res.results[c]["out"]
    return out


# version 08
# speedup vs baseline: 1.1160x; 1.0706x over previous
import os
import numpy as np
import ml_dtypes
from contextlib import ExitStack

import concourse.bass as bass
import concourse.mybir as mybir
from concourse.bass_utils import run_bass_kernel_spmd

F32 = mybir.dt.float32
BF16 = mybir.dt.bfloat16
AF = mybir.ActivationFunctionType
OP = mybir.AluOpType
AX = mybir.AxisListType

B, S, D = 2, 8192, 4096
NH, HD, RD = 16, 256, 64
HPC = 4
DPC = HPC * HD
TPC = 2048
NCORE = 8
EPS_LN = 1e-5
EPS_RMS = 1.1920929e-07
SINK_IT = 50
TAU = 0.1

ENGS = ("sync", "gpsimd", "tensor", "vector", "scalar")


CHAIN = {"vector": "vch", "scalar": "sch"}


class Plan:
    def __init__(self, sems):
        self.entries = {e: [] for e in ENGS}
        self.cnt = {}
        self.sems = sems
        self.prev_marker = {}

    def cur(self, name):
        return self.cnt.get(name, 0)

    def emit(self, eng, fn, waits=(), inc=None, inc_amt=1, chain=True):
        post = None
        if inc is not None:
            self.cnt[inc] = self.cnt.get(inc, 0) + inc_amt
            post = self.cnt[inc]
        autow = None
        ginc = None
        if chain and fn is not None and eng in CHAIN:
            autow = self.prev_marker.get(eng)
            if inc is None:
                cs = CHAIN[eng]
                self.cnt[cs] = self.cnt.get(cs, 0) + 1
                ginc = (cs, 1)
                self.prev_marker[eng] = (cs, self.cnt[cs])
            else:
                self.prev_marker[eng] = (inc, post)
        self.entries[eng].append((fn, tuple(waits), inc, inc_amt, autow, ginc))
        return post

    def dma(self, eng, fn, waits=(), inc=None):
        return self.emit(eng, fn, waits, inc, 16)

    def barrier(self, pre=None):
        pre = pre or {}
        base = self.cur("bar")
        for e in ENGS:
            if e in ("sync", "gpsimd"):
                self.emit(e, lambda eng: eng.nop(), waits=tuple(pre.get(e, ())),
                          inc="bar", chain=False)
            else:
                self.emit(e, lambda eng: eng.drain(), waits=tuple(pre.get(e, ())),
                          inc="bar", chain=False)
        tgt = base + len(ENGS)
        for e in ENGS:
            self.emit(e, None, waits=(("bar", tgt),))

    def replay(self, eng_name, eng):
        sems = self.sems
        for fn, waits, inc, amt, autow, ginc in self.entries[eng_name]:
            for (sn, val) in waits:
                if val > 0:
                    eng.wait_ge(sems[sn], val)
            if fn is None:
                continue
            if autow is not None:
                eng.wait_ge(sems[autow[0]], autow[1])
            inst = fn(eng)
            if inst is None:
                inst = eng.nop()
            if ginc is not None:
                inst.then_inc(sems[ginc[0]], ginc[1])
            elif inc is not None:
                inst.then_inc(sems[inc], amt)


def _block_emit(nc, p):
    with nc.Block() as block:
        @block.sync
        def _(eng):
            p.replay("sync", eng)

        @block.gpsimd
        def _(eng):
            p.replay("gpsimd", eng)

        @block.tensor
        def _(eng):
            p.replay("tensor", eng)

        @block.vector
        def _(eng):
            p.replay("vector", eng)

        @block.scalar
        def _(eng):
            p.replay("scalar", eng)



def build_launch_a(phases=None):
    if phases is None:
        phases = int(os.environ.get("KA_PHASES", "3"))
    NOROPE = os.environ.get("KA_NOROPE", "") == "1"
    RN = int(os.environ.get("KA_RN", "5"))
    NONORM = os.environ.get("KA_NONORM", "") == "1"
    NOTRANS = os.environ.get("KA_NOTRANS", "") == "1"
    PROJS = os.environ.get("KA_PROJS", "qkv")
    nc = bass.Bass("TRN2", target_bir_lowering=False)

    xT_h = nc.dram_tensor("xT", [D, S], F32, kind="ExternalInput")
    w_h = {pj: nc.dram_tensor(f"w{pj}T", [D, DPC], F32, kind="ExternalInput")
           for pj in ("q", "k", "v")}
    sinr_h = nc.dram_tensor("sinr", [S, 128], F32, kind="ExternalInput")
    cosr_h = nc.dram_tensor("cosr", [S, 128], F32, kind="ExternalInput")
    masks_h = nc.dram_tensor("masks", [512, 512], F32, kind="ExternalInput")
    ident_h = nc.dram_tensor("ident", [128, 128], F32, kind="ExternalInput")
    onesrow_h = nc.dram_tensor("onesrow", [1, 128], F32, kind="ExternalInput")

    attn_h = nc.dram_tensor("attn", [DPC, S], BF16, kind="ExternalOutput")

    xbf_h = nc.dram_tensor("xbf", [D, S], BF16, kind="Internal")
    qt_h = nc.dram_tensor("qt", [16, DPC, 512], BF16, kind="Internal")
    kt_h = nc.dram_tensor("kt", [16, DPC, 512], BF16, kind="Internal")
    vt_h = nc.dram_tensor("vt", [HPC, S, HD], BF16, kind="Internal")

    sem_names = ["bar", "vch", "sch", "xin0", "xin1", "xcast", "xwr0", "xwr1", "sq", "smm",
                 "sev", "fin", "fsq", "fre", "ftp",
                 "wld", "xbld0", "xbld1", "qmm", "bkf", "ropa", "ropb", "scld", "nrm",
                 "tps", "tpe", "qst",
                 "ktld", "qcld", "tsc", "vmsk", "sexp", "tpv", "vrs",
                 "trb", "voev", "ost", "cns"]
    stack = ExitStack()
    sems = {n: stack.enter_context(nc.semaphore(n)) for n in sem_names}
    p = Plan(sems)

    ident_f = nc.alloc_sbuf_tensor("ident_f", [128, 128], F32)
    ident_b = nc.alloc_sbuf_tensor("ident_b", [128, 128], BF16)
    onesrow_f = nc.alloc_sbuf_tensor("onesrow_f", [1, 128], F32)
    ones_b = nc.alloc_sbuf_tensor("ones_b", [128, 1], BF16)
    masks_sb = nc.alloc_sbuf_tensor("masks_sb", [128, 4 * 512], F32)
    negm_row = nc.alloc_sbuf_tensor("negm_row", [1, S], BF16)
    s_sum = nc.alloc_sbuf_tensor("s_sum", [1, 256], F32)
    s_sq = nc.alloc_sbuf_tensor("s_sq", [1, 256], F32)
    s_m = nc.alloc_sbuf_tensor("s_m", [1, 256], F32)
    r_cols = nc.alloc_sbuf_tensor("r_cols", [128, 64], F32)
    qmax2 = nc.alloc_sbuf_tensor("qmax2", [128, HPC], F32)
    kmax2 = nc.alloc_sbuf_tensor("kmax2", [128, HPC], F32)
    negcb = nc.alloc_sbuf_tensor("negcb", [128, HPC], F32)
    negc_row = nc.alloc_sbuf_tensor("negc_row", [1, HPC], F32)
    epsc = nc.alloc_sbuf_tensor("epsc", [1, 1], F32)

    pp = [nc.alloc_psum_tensor(f"pp{i}", [128, 512], F32) for i in range(4)]
    ps_a = nc.alloc_psum_tensor("ps_a", [1, 512], F32)
    ps_b = nc.alloc_psum_tensor("ps_b", [1, 512], F32)
    pbf = nc.alloc_psum_tensor("pbf", [128, 1024], BF16)
    pmisc = nc.alloc_psum_tensor("pmisc", [128, 512], F32)

    xT_v = xT_h.ap().rearrange("(c p) t -> p c t", p=128)
    xbf_v = xbf_h.ap().rearrange("(c p) t -> p c t", p=128)

    p.dma("sync", lambda e: e.dma_start(ident_f[:, :], ident_h.ap()[:, :]), inc="cns")
    p.dma("sync", lambda e: e.dma_start(onesrow_f[:, :], onesrow_h.ap()[:, :]),
          inc="cns")
    p.dma("sync", lambda e: e.dma_start(
        masks_sb[:, :], masks_h.ap().rearrange("(m p) t -> p m t", p=128)), inc="cns")
    p.emit("vector", lambda e: e.tensor_copy(ident_b[:, :], ident_f[:, :]),
           waits=(("cns", 48),))
    p.emit("vector", lambda e: e.memset(ones_b[:, :], 1.0))
    p.emit("vector", lambda e: e.memset(qmax2[:, :], 0.0))
    p.emit("vector", lambda e: e.memset(kmax2[:, :], 0.0))
    p.emit("vector", lambda e: e.memset(epsc[:, :], EPS_LN), inc="cns")

    st1 = ExitStack()
    NCH = 32
    CW = S // NCH
    xin = [st1.enter_context(nc.sbuf_tensor(f"xin{i}", [128, 32 * CW], F32))
           for i in range(2)]
    xbf_sb = [st1.enter_context(nc.sbuf_tensor(f"xbfs{i}", [128, 32 * CW], BF16))
              for i in range(2)]
    xsq_sb = [st1.enter_context(nc.sbuf_tensor(f"xsqs{i}", [128, 32 * CW], BF16))
              for i in range(2)]

    for ch in range(NCH):
        bi = ch % 2
        c0 = ch * CW
        for sp in range(4):
            w = [("xcast", ch - 1)] if (ch >= 2 and sp == 0) else []
            p.dma("sync", (lambda bi_, sp_, c0_: lambda e: e.dma_start(
                xin[bi_][:, sp_ * 8 * CW:(sp_ + 1) * 8 * CW],
                xT_v[:, sp_ * 8:(sp_ + 1) * 8, c0_:c0_ + CW]))(bi, sp, c0),
                waits=tuple(w), inc=f"xin{bi}")
        w = [(f"xin{bi}", 64 * (ch // 2 + 1))]
        if ch >= 2:
            w += [(f"xwr{bi}", 64 * (ch // 2)), ("smm", ch - 1), ("sq", ch - 1)]
        p.emit("vector", (lambda bi_: lambda e: e.tensor_copy(
            xbf_sb[bi_][:, :], xin[bi_][:, :]))(bi), waits=tuple(w), inc="xcast")
        xc = p.cur("xcast")
        w = [("xcast", xc)]
        if ch >= 2:
            w.append(("smm", ch - 1))
        p.emit("scalar", (lambda bi_: lambda e: e.square(
            xsq_sb[bi_][:, :], xbf_sb[bi_][:, :]))(bi), waits=tuple(w), inc="sq")
        for sp in range(4):
            w = [("xcast", xc)] if sp == 0 else []
            p.dma("gpsimd", (lambda bi_, sp_, c0_: lambda e: e.dma_start(
                xbf_v[:, sp_ * 8:(sp_ + 1) * 8, c0_:c0_ + CW],
                xbf_sb[bi_][:, sp_ * 8 * CW:(sp_ + 1) * 8 * CW]))(bi, sp, c0),
                waits=tuple(w), inc=f"xwr{bi}")
        sqv = p.cur("sq")
        for dc in range(32):
            w = []
            if dc == 0:
                w = [("xcast", xc), ("sq", sqv), ("sev", ch)]
            p.emit("tensor", (lambda bi_, dc_, st_, sp_: lambda e: e.matmul(
                ps_a[0:1, 0:CW], ones_b[:, :],
                xbf_sb[bi_][:, dc_ * CW:(dc_ + 1) * CW], start=st_, stop=sp_))(
                    bi, dc, dc == 0, dc == 31), waits=tuple(w))
            p.emit("tensor", (lambda bi_, dc_, st_, sp_: lambda e: e.matmul(
                ps_b[0:1, 0:CW], ones_b[:, :],
                xsq_sb[bi_][:, dc_ * CW:(dc_ + 1) * CW], start=st_, stop=sp_))(
                    bi, dc, dc == 0, dc == 31),
                inc="smm" if dc == 31 else None)
        smv = p.cur("smm")
        w = [("smm", smv)]
        if ch >= 1:
            w.append(("ftp", ch))
        p.emit("vector", lambda e: e.tensor_copy(s_sum[0:1, :], ps_a[0:1, 0:CW]),
               waits=tuple(w))
        p.emit("vector", lambda e: e.tensor_copy(s_sq[0:1, :], ps_b[0:1, 0:CW]),
               inc="sev")
        p.emit("vector", (lambda c0_: lambda e: e.tensor_scalar_mul(
            negm_row[0:1, c0_:c0_ + CW], s_sum[0:1, :], -1.0 / D))(c0))
        p.emit("vector", lambda e: e.tensor_scalar_mul(s_m[0:1, :], s_sum[0:1, :],
                                                       1.0 / D))
        p.emit("vector", lambda e: e.tensor_mul(s_m[0:1, :], s_m[0:1, :],
                                                s_m[0:1, :]))
        p.emit("vector", lambda e: e.tensor_scalar_mul(s_sq[0:1, :], s_sq[0:1, :],
                                                       1.0 / D))
        p.emit("vector", lambda e: e.tensor_sub(s_sq[0:1, :], s_sq[0:1, :],
                                                s_m[0:1, :]), inc="fin")
        p.emit("scalar", lambda e: e.activation(
            s_sq[0:1, :], s_sq[0:1, :], AF.Sqrt, bias=epsc[0:1, 0:1], scale=1.0),
            waits=(("fin", ch + 1), ("cns", p.cur("cns"))), inc="fsq")
        p.emit("vector", lambda e: e.reciprocal(s_sq[0:1, :], s_sq[0:1, :]),
               waits=(("fsq", ch + 1),), inc="fre")
        for k in range(2):
            p.emit("tensor", (lambda ch_, k_: lambda e: e.transpose(
                pmisc[:, 2 * ch_ + k_:2 * ch_ + k_ + 1],
                s_sq[0:1, k_ * 128:(k_ + 1) * 128], ident_f[0:1, 0:1]))(ch, k),
                waits=(("fre", ch + 1),) if k == 0 else (),
                inc="ftp" if k == 1 else None)

    p.emit("vector", lambda e: e.tensor_copy(r_cols[:, :], pmisc[:, 0:64]),
           waits=(("ftp", NCH),), inc="sev")

    p.barrier(pre={"gpsimd": [("xwr0", 64 * (NCH // 2)), ("xwr1", 64 * (NCH // 2))],
                   "vector": [("sev", p.cur("sev")), ("cns", p.cur("cns"))]})
    st1.close()

    st2 = ExitStack()
    wbuf = st2.enter_context(nc.sbuf_tensor("wbuf", [128, 32 * DPC], BF16))
    xblk = st2.enter_context(nc.sbuf_tensor("xblk", [128, 32 * 1024], BF16))
    wrs_f = st2.enter_context(nc.sbuf_tensor("wrs_f", [1, DPC], F32))
    wrs_b = st2.enter_context(nc.sbuf_tensor("wrs_b", [1, DPC], BF16))
    qstage = st2.enter_context(nc.sbuf_tensor("qstage", [128, DPC], BF16))
    sqbuf = st2.enter_context(nc.sbuf_tensor("sqbuf", [128, DPC], BF16))
    nrmt = st2.enter_context(nc.sbuf_tensor("nrmt", [128, HPC], F32))
    robuf = st2.enter_context(nc.sbuf_tensor("robuf", [128, HPC * RD], F32))
    robuf2 = st2.enter_context(nc.sbuf_tensor("robuf2", [128, HPC * RD], F32))
    robt = st2.enter_context(nc.sbuf_tensor("robt", [128, HPC * 32], F32))
    robt2 = st2.enter_context(nc.sbuf_tensor("robt2", [128, HPC * 32], F32))
    sc_sin = st2.enter_context(nc.sbuf_tensor("sc_sin", [128, 128], F32))
    sc_cos = st2.enter_context(nc.sbuf_tensor("sc_cos", [128, 128], F32))
    qt_blk = st2.enter_context(nc.sbuf_tensor("qt_blk", [128, 8 * 512], BF16))

    out_map = {"q": qt_h, "k": kt_h}
    mx_map = {"q": qmax2, "k": kmax2}

    for pj in (tuple(PROJS) if phases >= 2 else ()):
        wT_v = w_h[pj].ap().rearrange("(c p) d -> p c d", p=128)
        for sp in range(4):
            p.dma("gpsimd", (lambda sp_, t_: lambda e: e.dma_start(
                wbuf[:, sp_ * 8 * DPC:(sp_ + 1) * 8 * DPC],
                t_[:, sp_ * 8:(sp_ + 1) * 8, :]))(sp, wT_v), inc="wld")
        wl = p.cur("wld")
        for dc in range(2):
            tgt = ps_a if dc == 0 else ps_b
            for c in range(32):
                w = [("wld", wl)] if (dc == 0 and c == 0) else []
                p.emit("tensor", (lambda t_, c_, dc_, st_, sp_: lambda e: e.matmul(
                    t_[0:1, :], ones_b[:, :],
                    wbuf[:, c_ * DPC + dc_ * 512: c_ * DPC + (dc_ + 1) * 512],
                    start=st_, stop=sp_))(tgt, c, dc, c == 0, c == 31),
                    waits=tuple(w),
                    inc="qmm" if (c == 31 and dc == 1) else None)
        qm = p.cur("qmm")
        p.emit("vector", lambda e: e.tensor_copy(wrs_f[0:1, 0:512], ps_a[0:1, :]),
               waits=(("qmm", qm),))
        p.emit("vector", lambda e: e.tensor_copy(wrs_f[0:1, 512:1024], ps_b[0:1, :]))
        p.emit("vector", lambda e: e.tensor_copy(wrs_b[0:1, :], wrs_f[0:1, :]),
               inc="bkf")
        for blk in range(8):
            t0 = blk * 1024
            for sp in range(4):
                w = []
                if sp == 0:
                    w.append(("bkf", p.cur("bkf")))
                    w.append(("qmm", p.cur("qmm")))
                p.dma("sync", (lambda sp_, t0_: lambda e: e.dma_start(
                    xblk[:, sp_ * 8 * 1024:(sp_ + 1) * 8 * 1024],
                    xbf_v[:, sp_ * 8:(sp_ + 1) * 8, t0_:t0_ + 1024]))(sp, t0),
                    waits=tuple(w), inc="xbld")
            xbl = p.cur("xbld")
            for ti in range(8):
                tg = blk * 8 + ti
                if pj != "v" and not NOROPE and RN >= 1:
                    w = [("ropb", p.cur("ropb"))]
                    p.dma("sync", (lambda tg_: lambda e: e.dma_start(
                        sc_sin[:, :], sinr_h.ap()[tg_ * 128:(tg_ + 1) * 128, :]))(tg),
                        waits=tuple(w), inc="scld")
                    p.dma("sync", (lambda tg_: lambda e: e.dma_start(
                        sc_cos[:, :], cosr_h.ap()[tg_ * 128:(tg_ + 1) * 128, :]))(tg),
                        inc="scld")
                scl = p.cur("scld")
                for dc in range(2):
                    bank = pp[dc]
                    for c in range(32):
                        w = []
                        if c == 0:
                            w = [(xbln, xbl), ("bkf", p.cur("bkf"))]
                        p.emit("tensor",
                               (lambda c_, ti_, dc_, st_, b_: lambda e: e.matmul(
                                   b_[:, :],
                                   xblk[:, c_ * 1024 + ti_ * 128:
                                        c_ * 1024 + ti_ * 128 + 128],
                                   wbuf[:, c_ * DPC + dc_ * 512:
                                        c_ * DPC + (dc_ + 1) * 512],
                                   start=st_, stop=False))(c, ti, dc, c == 0, bank),
                               waits=tuple(w))
                    p.emit("tensor", (lambda tg_, dc_, b_: lambda e: e.matmul(
                        b_[:, :], negm_row[0:1, tg_ * 128:(tg_ + 1) * 128],
                        wrs_b[0:1, dc_ * 512:(dc_ + 1) * 512],
                        start=False, stop=True))(tg, dc, bank), inc="qmm")
                    qmm = p.cur("qmm")
                    if pj != "v" and not NOROPE and RN >= 2:
                        p.emit("vector", (lambda dc_, tg_, b_: lambda e:
                               e.tensor_scalar_mul(
                                   robuf[:, :].rearrange(
                                       "p (h r) -> p h r", r=RD)[:, 2 * dc_:2 * dc_ + 2, :],
                                   b_[:, :].rearrange(
                                       "p (h r) -> p h r", r=HD)[:, :, 0:RD],
                                   r_cols[:, tg_:tg_ + 1]))(dc, tg, bank),
                               waits=(("qmm", qmm),), inc="ropa")
                    wse = [("qmm", qmm)]
                    if pj != "v":
                        wse.append(("ropa", p.cur("ropa")))
                    if dc == 0:
                        if pj == "v":
                            wse.append(("qst", p.cur("qst")))
                        else:
                            wse.append(("tps", p.cur("tps")))
                    p.emit("scalar", (lambda dc_, tg_, b_: lambda e: e.activation(
                        qstage[:, dc_ * 512:(dc_ + 1) * 512], b_[:, :], AF.Copy,
                        bias=0.0, scale=r_cols[:, tg_:tg_ + 1]))(dc, tg, bank),
                        waits=tuple(wse), inc="bkf")
                qevv = p.cur("bkf")
                if pj != "v":
                    ropv = p.cur("ropa")
                    do_rope = not NOROPE
                    rb = robuf[:, :].rearrange("p (h i two) -> p h i two", i=32, two=2)
                    rb2 = robuf2[:, :].rearrange("p (h i two) -> p h i two", i=32, two=2)
                    sv = sc_sin[:, :].rearrange("p (h i) -> p h i", i=32)
                    cv = sc_cos[:, :].rearrange("p (h i) -> p h i", i=32)
                    tv = robt[:, :].rearrange("p (h i) -> p h i", i=32)
                    tv2 = robt2[:, :].rearrange("p (h i) -> p h i", i=32)
                    if do_rope and RN >= 3:
                        p.emit("vector", lambda e: e.tensor_mul(tv, rb[:, :, :, 1], sv),
                               waits=(("ropa", ropv), ("scld", scl)))
                        p.emit("vector", lambda e: e.tensor_mul(tv2, rb[:, :, :, 0], sv))
                        if RN >= 4:
                            p.emit("vector", lambda e: e.tensor_mul(
                                rb2[:, :, :, 0], rb[:, :, :, 0], cv))
                            p.emit("vector", lambda e: e.tensor_sub(
                                rb2[:, :, :, 0], rb2[:, :, :, 0], tv))
                            p.emit("vector", lambda e: e.tensor_mul(
                                rb2[:, :, :, 1], rb[:, :, :, 1], cv))
                            p.emit("vector", lambda e: e.tensor_add(
                                rb2[:, :, :, 1], rb2[:, :, :, 1], tv2), inc="ropb")
                        else:
                            p.emit("vector", lambda e: e.tensor_copy(
                                robuf2[:, :], robuf[:, :]), inc="ropb")
                        if RN >= 5:
                            p.emit("vector", lambda e: e.tensor_copy(
                                qstage[:, :].rearrange("p (h r) -> p h r", r=HD)[:, :, 0:RD],
                                robuf2[:, :].rearrange("p (h r) -> p h r", r=RD)),
                                waits=(("bkf", qevv),), inc="ropb")
                    ropd = p.cur("ropb")
                    if not NONORM:
                        p.emit("scalar", lambda e: e.square(sqbuf[:, :], qstage[:, :]),
                               waits=(("ropb", ropd), ("bkf", qevv)), inc="sq")
                        p.emit("vector", lambda e: e.reduce_sum(
                            nrmt[:, :], sqbuf[:, :].rearrange("p (h d) -> p h d", d=HD),
                            axis=AX.X), waits=(("sq", p.cur("sq")),))
                        p.emit("vector", (lambda m_: lambda e: e.tensor_max(
                            m_[:, :], m_[:, :], nrmt[:, :]))(mx_map[pj]), inc="nrm")
                    for i in (range(8) if not NOTRANS else ()):
                        w = []
                        if i == 0:
                            w = [("ropb", ropd), ("bkf", qevv), ("tpe", p.cur("tpe"))]
                        p.emit("tensor", (lambda i_: lambda e: e.transpose(
                            pbf[:, i_ * 128:(i_ + 1) * 128],
                            qstage[:, i_ * 128:(i_ + 1) * 128], ident_b[:, :]))(i),
                            waits=tuple(w), inc="tps" if i == 7 else None)
                    tpsv = p.cur("tps")
                    for i in (range(8) if not NOTRANS else ()):
                        w = [("tps", tpsv)] if i == 0 else []
                        if i == 0 and tg % 4 == 0:
                            w.append(("qst", p.cur("qst")))
                        p.emit("vector", (lambda i_, t4_: lambda e: e.tensor_copy(
                            qt_blk[:, i_ * 512 + t4_ * 128:
                                   i_ * 512 + t4_ * 128 + 128],
                            pbf[:, i_ * 128:(i_ + 1) * 128]))(i, tg % 4),
                            waits=tuple(w), inc="tpe" if i == 7 else None)
                    tpev = p.cur("tpe")
                    if tg % 4 == 3 and not NOTRANS:
                        qc = tg // 4
                        dst = out_map[pj].ap()
                        for i in range(8):
                            w = [("tpe", tpev)] if i == 0 else []
                            p.dma("sync", (lambda i_, qc_, d_: lambda e: e.dma_start(
                                d_[qc_, i_ * 128:(i_ + 1) * 128, :],
                                qt_blk[:, i_ * 512:(i_ + 1) * 512]))(i, qc, dst),
                                waits=tuple(w), inc="qst")
                else:
                    for h in range(HPC):
                        w = [("bkf", qevv)] if h == 0 else []
                        p.dma("sync", (lambda h_, tg_: lambda e: e.dma_start(
                            vt_h.ap()[h_, tg_ * 128:(tg_ + 1) * 128, :],
                            qstage[:, h_ * HD:(h_ + 1) * HD]))(h, tg),
                            waits=tuple(w), inc="qst")
        p.barrier(pre={"sync": [("qst", p.cur("qst"))],
                       "vector": [("nrm", p.cur("nrm"))],
                       "scalar": [("bkf", p.cur("bkf"))]})

    if phases < 2:
        p.barrier()
    p.emit("vector", lambda e: e.tensor_add(qmax2[:, :], qmax2[:, :], kmax2[:, :]),
           inc="sev")
    p.emit("tensor", lambda e: e.transpose(
        pmisc[0:HPC, 0:128], qmax2[:, :], ident_f[:, :]),
        waits=(("sev", p.cur("sev")),), inc="smm")
    p.emit("vector", lambda e: e.reduce_max(
        qmax2[0:HPC, 0:1], pmisc[0:HPC, 0:128], axis=AX.X),
        waits=(("smm", p.cur("smm")),))
    p.emit("vector", lambda e: e.tensor_scalar_mul(
        qmax2[0:HPC, 0:1], qmax2[0:HPC, 0:1], -1.0 / 32.0), inc="sev")
    sevn = p.cur("sev")
    p.emit("tensor", lambda e: e.transpose(
        pmisc[0:1, 0:HPC], qmax2[0:HPC, 0:1], ident_f[0:HPC, 0:HPC]),
        waits=(("sev", sevn),), inc="smm")
    p.emit("vector", lambda e: e.tensor_copy(negc_row[0:1, :], pmisc[0:1, 0:HPC]),
           waits=(("smm", p.cur("smm")),), inc="sev")
    sevn2 = p.cur("sev")
    for h in range(HPC):
        p.emit("tensor", (lambda h_: lambda e: e.matmul(
            pmisc[:, h_:h_ + 1], onesrow_f[0:1, :], negc_row[0:1, h_:h_ + 1],
            start=True, stop=True))(h),
            waits=(("sev", sevn2),) if h == 0 else (),
            inc="smm" if h == HPC - 1 else None)
    p.emit("vector", lambda e: e.tensor_copy(negcb[:, :], pmisc[:, 0:HPC]),
           waits=(("smm", p.cur("smm")),), inc="sev")
    p.barrier(pre={"vector": [("sev", p.cur("sev"))]})
    st2.close()

    st3 = ExitStack()
    kt_sb = st3.enter_context(nc.sbuf_tensor("kt_sb", [128, 2 * S], BF16))
    vt_sb = st3.enter_context(nc.sbuf_tensor("vt_sb", [128, 64 * HD], BF16))
    qch = [st3.enter_context(nc.sbuf_tensor(f"qch{i}", [128, 2 * 512], BF16))
           for i in range(2)]
    psb = [st3.enter_context(nc.sbuf_tensor(f"psb{i}", [128, 512], BF16))
           for i in range(2)]
    srow = st3.enter_context(nc.sbuf_tensor("srow", [1, 512], F32))
    rbf = st3.enter_context(nc.sbuf_tensor("rbf", [128, 512], F32))
    ostage = st3.enter_context(nc.sbuf_tensor("ostage", [128, 1024], BF16))

    gi = 0
    for h in (range(HPC) if phases >= 3 else ()):
        for dh in range(2):
            w = []
            if h > 0 and dh == 0:
                w = [("tsc", p.cur("tsc")), ("tpv", p.cur("tpv"))]
            p.dma("sync", (lambda h_, dh_: lambda e: e.dma_start(
                kt_sb[:, dh_ * S:(dh_ + 1) * S],
                bass.AP(kt_h, (h_ * HD + dh_ * 128) * 512,
                        [[512, 128], [DPC * 512, 16], [1, 512]])))(h, dh),
                waits=tuple(w), inc="ktld")
        for sp in range(8):
            p.dma("sync", (lambda h_, sp_: lambda e: e.dma_start(
                vt_sb[:, sp_ * 8 * HD:(sp_ + 1) * 8 * HD],
                bass.AP(vt_h, h_ * S * HD + sp_ * 8 * 128 * HD,
                        [[HD, 128], [128 * HD, 8], [1, HD]])))(h, sp),
                inc="ktld")
        ktl = p.cur("ktld")

        for qc in range(16):
            qb = qc % 2
            for dh in range(2):
                w = []
                if dh == 0:
                    w.append(("tsc", p.cur("tsc")))
                    if qc == 0:
                        w.append(("ktld", ktl))
                p.dma("sync", (lambda h_, dh_, qc_, qb_: lambda e: e.dma_start(
                    qch[qb_][:, dh_ * 512:(dh_ + 1) * 512],
                    bass.AP(qt_h, qc_ * DPC * 512 + (h_ * HD + dh_ * 128) * 512,
                            [[512, 128], [1, 512]])))(h, dh, qc, qb),
                    waits=tuple(w), inc="qcld")
            qcl = p.cur("qcld")
            nj = 4 * qc + 4
            for j in range(nj):
                sbank = pp[gi % 2]
                pb = gi % 2
                diag = j >= 4 * qc
                w = []
                if j == 0:
                    w.append(("qcld", qcl))
                if gi >= 2:
                    w.append(("sexp", gi - 1))
                for dh in range(2):
                    p.emit("tensor", (lambda j_, dh_, qb_, st_, sp_, b_:
                           lambda e: e.matmul(
                               b_[:, :],
                               kt_sb[:, dh_ * S + j_ * 128: dh_ * S + j_ * 128 + 128],
                               qch[qb_][:, dh_ * 512:(dh_ + 1) * 512],
                               start=st_, stop=sp_))(j, dh, qb, dh == 0, dh == 1, sbank),
                           waits=tuple(w) if dh == 0 else (),
                           inc="tsc" if dh == 1 else None)
                tscv = p.cur("tsc")
                if diag:
                    d = j - 4 * qc
                    p.emit("vector", (lambda d_, b_: lambda e: e.tensor_add(
                        b_[:, :], b_[:, :], masks_sb[:, d_ * 512:(d_ + 1) * 512]))(
                            d, sbank), waits=(("tsc", tscv),), inc="vmsk")
                w = [("vmsk", p.cur("vmsk"))] if diag else [("tsc", tscv)]
                if gi >= 2:
                    w.append(("tpv", gi - 1))
                p.emit("scalar", (lambda h_, b_, pb_: lambda e: e.activation(
                    psb[pb_][:, :], b_[:, :], AF.Exp,
                    bias=negcb[:, h_:h_ + 1], scale=1.0 / 16.0))(h, sbank, pb),
                    waits=tuple(w), inc="sexp")
                sev_ = p.cur("sexp")

                def emit_pv(j_, pb_, sexp_tgt):
                    for dh in range(2):
                        w2 = []
                        if dh == 0:
                            w2.append(("sexp", sexp_tgt))
                            if j_ == 0:
                                w2.append(("voev", p.cur("voev")))
                        p.emit("tensor",
                               (lambda j2, dh_, pb2, st_, sp_: lambda e: e.matmul(
                                   pp[2 + dh_][:, :],
                                   vt_sb[:, j2 * HD + dh_ * 128:
                                         j2 * HD + dh_ * 128 + 128],
                                   psb[pb2][:, :], start=st_, stop=sp_))(
                                       j_, dh, pb_, j_ == 0, j_ == nj - 1),
                               waits=tuple(w2))
                    w2 = []
                    if j_ == 0:
                        w2.append(("vrs", p.cur("vrs")))
                    p.emit("tensor", (lambda pb2, st_, sp_: lambda e: e.matmul(
                        ps_a[0:1, :], ones_b[:, :], psb[pb2][:, :],
                        start=st_, stop=sp_))(pb_, j_ == 0, j_ == nj - 1),
                        waits=tuple(w2), inc="tpv")

                if pend is not None:
                    emit_pv(*pend)
                pend = (j, pb, sev_)
                if j == nj - 1:
                    emit_pv(*pend)
                    pend = None
                gi += 1
            tpvv = p.cur("tpv")
            p.emit("vector", lambda e: e.tensor_copy(srow[0:1, :], ps_a[0:1, :]),
                   waits=(("tpv", tpvv),))
            p.emit("vector", lambda e: e.reciprocal(srow[0:1, :], srow[0:1, :]),
                   inc="vrs")
            p.emit("tensor", lambda e: e.matmul(
                pmisc[:, :], onesrow_f[0:1, :], srow[0:1, :], start=True, stop=True),
                waits=(("vrs", p.cur("vrs")), ("voev", p.cur("voev"))), inc="trb")
            p.emit("vector", lambda e: e.tensor_copy(rbf[:, :], pmisc[:, :]),
                   waits=(("trb", p.cur("trb")), ("ost", p.cur("ost"))))
            for dh in range(2):
                p.emit("vector", (lambda dh_: lambda e: e.tensor_mul(
                    ostage[:, dh_ * 512:(dh_ + 1) * 512], pp[2 + dh_][:, :],
                    rbf[:, :]))(dh), inc="voev" if dh == 1 else None)
            vov = p.cur("voev")
            for dh in range(2):
                w = [("voev", vov)] if dh == 0 else []
                p.dma("sync", (lambda h_, dh_, qc_: lambda e: e.dma_start(
                    bass.AP(attn_h, (h_ * HD + dh_ * 128) * S + qc_ * 512,
                            [[S, 128], [1, 512]]),
                    ostage[:, dh_ * 512:(dh_ + 1) * 512]))(h, dh, qc),
                    waits=tuple(w), inc="ost")

    p.emit("gpsimd", None, waits=(("ost", p.cur("ost")),))
    p.emit("sync", None, waits=(("ost", p.cur("ost")),))

    _block_emit(nc, p)
    st3.close()
    stack.close()
    return nc



def build_launch_b():
    nc = bass.Bass("TRN2", target_bir_lowering=False)

    asl_h = nc.dram_tensor("aslice", [D, TPC], BF16, kind="ExternalInput")
    xres_h = nc.dram_tensor("xres", [TPC, D], F32, kind="ExternalInput")
    woT_h = nc.dram_tensor("woT", [D, D], F32, kind="ExternalInput")
    wallT_h = nc.dram_tensor("wallT", [D, 24], F32, kind="ExternalInput")
    bases_h = nc.dram_tensor("bases", [1, 24], F32, kind="ExternalInput")
    gate_h = nc.dram_tensor("gate", [1, 1], F32, kind="ExternalInput")
    wbr_h = nc.dram_tensor("wbr", [1024, 1024], F32, kind="ExternalInput")
    ident_h = nc.dram_tensor("ident", [128, 128], F32, kind="ExternalInput")
    onesrow_h = nc.dram_tensor("onesrow", [1, 128], F32, kind="ExternalInput")

    out_h = nc.dram_tensor("out", [TPC, D], F32, kind="ExternalOutput")
    x2_h = nc.dram_tensor("x2h", [TPC, D], F32, kind="Internal")

    sem_names = ["bar", "vch", "sch", "cns", "ald", "wld0", "wld1", "xrd", "womm", "x2ev",
                 "x2wr0", "x2wr1", "sq", "rms", "lgT", "lgTe", "lgmm", "lgev",
                 "snk", "x2rd0", "x2rd1", "mix", "ymm", "yev", "ost0", "ost1",
                 "wbld", "wcns"]
    stack = ExitStack()
    sems = {n: stack.enter_context(nc.semaphore(n)) for n in sem_names}
    p = Plan(sems)

    ident_f = nc.alloc_sbuf_tensor("ident_f", [128, 128], F32)
    ident_b = nc.alloc_sbuf_tensor("ident_b", [128, 128], BF16)
    onesrow_f = nc.alloc_sbuf_tensor("onesrow_f", [1, 128], F32)
    wall_b = nc.alloc_sbuf_tensor("wall_b", [128, 32 * 24], BF16)
    bases_row = nc.alloc_sbuf_tensor("bases_row", [1, 24], F32)
    basesb = nc.alloc_sbuf_tensor("basesb", [128, 24], F32)
    gsc = nc.alloc_sbuf_tensor("gsc", [1, 4], F32)
    gb = nc.alloc_sbuf_tensor("gb", [128, 1], F32)
    omgb = nc.alloc_sbuf_tensor("omgb", [128, 1], F32)
    r2c = nc.alloc_sbuf_tensor("r2c", [128, 16], F32)
    m16 = nc.alloc_sbuf_tensor("m16", [128, 16], F32)
    rs16 = nc.alloc_sbuf_tensor("rs16", [128, 16], F32)
    sc1 = nc.alloc_sbuf_tensor("sc1", [128, 1], F32)
    epsb = nc.alloc_sbuf_tensor("epsb", [128, 1], F32)
    epsl = nc.alloc_sbuf_tensor("epsl", [128, 1], F32)
    lga = nc.alloc_sbuf_tensor("lga", [128, 16 * 24], F32)
    Z = nc.alloc_sbuf_tensor("Z", [128, 256], F32)
    tmp = nc.alloc_sbuf_tensor("tmp", [128, 256], F32)
    tmpe = nc.alloc_sbuf_tensor("tmpe", [128, 256], F32)
    ssum = nc.alloc_sbuf_tensor("ssum", [128, 64], F32)
    lnu = nc.alloc_sbuf_tensor("lnu", [128, 64], F32)
    lnv = nc.alloc_sbuf_tensor("lnv", [128, 64], F32)
    Hm = nc.alloc_sbuf_tensor("Hm", [128, 256], F32)
    u4 = nc.alloc_sbuf_tensor("u4", [128, 256], F32)
    hp = nc.alloc_sbuf_tensor("hp", [128, 64], F32)
    hq = nc.alloc_sbuf_tensor("hq", [128, 64], F32)
    sm = nc.alloc_sbuf_tensor("sm", [128, 16], F32)
    sm4 = nc.alloc_sbuf_tensor("sm4", [128, 64], F32)
    stm = ExitStack()
    maskd1_sb = stm.enter_context(nc.sbuf_tensor("maskd1_sb", [128, 4 * 512], F32))
    maskd2_sb = stm.enter_context(nc.sbuf_tensor("maskd2_sb", [128, 4 * 512], F32))
    negm_row = stm.enter_context(nc.sbuf_tensor("negm_row", [1, S], BF16))
    negmq_row = stm.enter_context(nc.sbuf_tensor("negmq_row", [1, QW], BF16))
    s_sum = stm.enter_context(nc.sbuf_tensor("s_sum", [1, 256], F32))
    s_sq = stm.enter_context(nc.sbuf_tensor("s_sq", [1, 256], F32))
    s_m = stm.enter_context(nc.sbuf_tensor("s_m", [1, 256], F32))

    pp = [nc.alloc_psum_tensor(f"pp{i}", [128, 512], F32) for i in range(4)]
    pbf = nc.alloc_psum_tensor("pbf", [128, 1024], BF16)
    pmisc = nc.alloc_psum_tensor("pmisc", [128, 512], F32)

    p.dma("sync", lambda e: e.dma_start(ident_f[:, :], ident_h.ap()[:, :]), inc="cns")
    p.dma("sync", lambda e: e.dma_start(onesrow_f[:, :], onesrow_h.ap()[:, :]),
          inc="cns")
    p.dma("sync", lambda e: e.dma_start(gsc[0:1, 0:1], gate_h.ap()[:, :]), inc="cns")
    p.dma("sync", lambda e: e.dma_start(bases_row[0:1, :], bases_h.ap()[:, :]),
          inc="cns")
    p.dma("gpsimd", lambda e: e.dma_start(
        wall_b[:, :], wallT_h.ap().rearrange("(c p) d -> p c d", p=128)), inc="wcns")
    cnsv = p.cur("cns")
    p.emit("vector", lambda e: e.tensor_copy(ident_b[:, :], ident_f[:, :]),
           waits=(("cns", cnsv),))
    p.emit("vector", lambda e: e.memset(epsb[:, :], EPS_RMS))
    p.emit("vector", lambda e: e.memset(epsl[:, :], EPS_LN))
    p.emit("tensor", lambda e: e.matmul(
        pmisc[:, 0:24], onesrow_f[0:1, :], bases_row[0:1, :], start=True, stop=True),
        waits=(("cns", cnsv),), inc="lgmm")
    p.emit("vector", lambda e: e.tensor_copy(basesb[:, :], pmisc[:, 0:24]),
           waits=(("lgmm", p.cur("lgmm")),), inc="lgev")
    p.emit("scalar", lambda e: e.activation(
        gsc[0:1, 1:2], gsc[0:1, 0:1], AF.Tanh, bias=0.0, scale=0.5),
        waits=(("cns", cnsv),), inc="sq")
    p.emit("vector", lambda e: e.tensor_scalar(
        gsc[0:1, 2:3], gsc[0:1, 1:2], 0.5, 0.5, OP.mult, OP.add),
        waits=(("sq", p.cur("sq")),))
    p.emit("vector", lambda e: e.tensor_scalar(
        gsc[0:1, 3:4], gsc[0:1, 1:2], -0.5, 0.5, OP.mult, OP.add), inc="rms")
    p.emit("tensor", lambda e: e.matmul(
        pmisc[:, 24:25], onesrow_f[0:1, :], gsc[0:1, 2:3], start=True, stop=True),
        waits=(("rms", p.cur("rms")), ("lgev", p.cur("lgev"))))
    p.emit("tensor", lambda e: e.matmul(
        pmisc[:, 25:26], onesrow_f[0:1, :], gsc[0:1, 3:4], start=True, stop=True),
        inc="lgmm")
    p.emit("vector", lambda e: e.tensor_copy(gb[:, :], pmisc[:, 24:25]),
           waits=(("lgmm", p.cur("lgmm")),))
    p.emit("vector", lambda e: e.tensor_copy(omgb[:, :], pmisc[:, 25:26]),
           inc="lgev")
    p.barrier(pre={"vector": [("lgev", p.cur("lgev"))]})

    woT_v = woT_h.ap().rearrange("(c p) d -> p c d", p=128)
    asl_v = asl_h.ap().rearrange("(c p) t -> p c t", p=128)

    stB = ExitStack()
    abuf = stB.enter_context(nc.sbuf_tensor("abuf", [128, 32 * 512], BF16))
    wbuf = [stB.enter_context(nc.sbuf_tensor(f"wbuf{i}", [128, 32 * 512], BF16))
            for i in range(2)]
    x2g = stB.enter_context(nc.sbuf_tensor("x2g", [128, 4 * D], F32))
    x2bf = stB.enter_context(nc.sbuf_tensor("x2bf", [128, D], BF16))
    x2T = stB.enter_context(nc.sbuf_tensor("x2T", [128, 32 * 128], BF16))
    sqs = stB.enter_context(nc.sbuf_tensor("sqs", [128, D], BF16))

    for g in range(4):
        t0 = g * 512
        for sp in range(4):
            w = [("womm", p.cur("womm"))] if sp == 0 else []
            p.dma("sync", (lambda sp_, t0_: lambda e: e.dma_start(
                abuf[:, sp_ * 8 * 512:(sp_ + 1) * 8 * 512],
                asl_v[:, sp_ * 8:(sp_ + 1) * 8, t0_:t0_ + 512]))(sp, t0),
                waits=tuple(w), inc="ald")
        aldv = p.cur("ald")
        for ti in range(4):
            w = ([(f"x2wr{(g - 1) % 2}", 64 * ((g - 1) // 2 + 1)),
                  ("sq", p.cur("sq")), ("x2ev", p.cur("x2ev")),
                  ("rms", p.cur("rms"))]
                 if (ti == 0 and g > 0) else [])
            p.dma("sync", (lambda ti_, t0_: lambda e: e.dma_start(
                x2g[:, ti_ * D:(ti_ + 1) * D],
                xres_h.ap()[t0_ + ti_ * 128: t0_ + ti_ * 128 + 128, :]))(ti, t0),
                waits=tuple(w), inc="xrd")
        xrdv = p.cur("xrd")
        for dc in range(8):
            wb = wbuf[dc % 2]
            for sp in range(2):
                w = []
                if sp == 0:
                    w.append(("womm", max(0, p.cur("womm") - 4)))
                p.dma("gpsimd", (lambda sp_, dc_, wb_: lambda e: e.dma_start(
                    wb_[:, sp_ * 16 * 512:(sp_ + 1) * 16 * 512],
                    woT_v[:, sp_ * 16:(sp_ + 1) * 16, dc_ * 512:(dc_ + 1) * 512]))(
                        sp, dc, wb), waits=tuple(w), inc=f"wld{dc % 2}")
            wldn = f"wld{dc % 2}"
            wldv = p.cur(wldn)
            for ti in range(4):
                bank = pp[ti % 2]
                for c in range(32):
                    w = []
                    if c == 0:
                        w = [(wldn, wldv), ("ald", aldv),
                             ("x2ev", max(0, p.cur("x2ev") - 1))]
                    p.emit("tensor", (lambda c_, ti_, wb_, st_, sp_, b_: lambda e: e.matmul(
                        b_[:, :],
                        abuf[:, c_ * 512 + ti_ * 128: c_ * 512 + ti_ * 128 + 128],
                        wb_[:, c_ * 512:(c_ + 1) * 512],
                        start=st_, stop=sp_))(c, ti, wb, c == 0, c == 31, bank),
                        waits=tuple(w), inc="womm" if c == 31 else None)
                wommv = p.cur("womm")
                w = [("womm", wommv)]
                if dc == 0:
                    w.append(("xrd", xrdv))
                p.emit("vector", (lambda ti_, dc_, b_: lambda e: e.tensor_add(
                    x2g[:, ti_ * D + dc_ * 512: ti_ * D + (dc_ + 1) * 512],
                    b_[:, :],
                    x2g[:, ti_ * D + dc_ * 512: ti_ * D + (dc_ + 1) * 512]))(
                        ti, dc, bank), waits=tuple(w), inc="x2ev")
        x2evv = p.cur("x2ev")
        for ti in range(4):
            tg = g * 4 + ti
            p.dma("gpsimd", (lambda ti_, t0_: lambda e: e.dma_start(
                x2_h.ap()[t0_ + ti_ * 128: t0_ + ti_ * 128 + 128, :],
                x2g[:, ti_ * D:(ti_ + 1) * D]))(ti, t0),
                waits=(("x2ev", x2evv),) if ti == 0 else (), inc=f"x2wr{g % 2}")
            p.emit("vector", (lambda ti_, tg_: lambda e: e.reduce_sum(
                m16[:, tg_:tg_ + 1], x2g[:, ti_ * D:(ti_ + 1) * D], axis=AX.X))(ti, tg),
                waits=(("x2ev", x2evv),))
            p.emit("vector", (lambda tg_: lambda e: e.tensor_scalar_mul(
                m16[:, tg_:tg_ + 1], m16[:, tg_:tg_ + 1], 1.0 / D))(tg))
            p.emit("scalar", (lambda ti_, tg_: lambda e: e.activation(
                sqs[:, :], x2g[:, ti_ * D:(ti_ + 1) * D], AF.Square,
                accum_out=rs16[:, tg_:tg_ + 1]))(ti, tg),
                waits=(("x2ev", x2evv),), inc="sq")
            p.emit("vector", (lambda tg_: lambda e: e.tensor_scalar_mul(
                rs16[:, tg_:tg_ + 1], rs16[:, tg_:tg_ + 1], 1.0 / D))(tg),
                waits=(("sq", p.cur("sq")),))
            p.emit("vector", (lambda tg_: lambda e: e.tensor_mul(
                sc1[:, 0:1], m16[:, tg_:tg_ + 1], m16[:, tg_:tg_ + 1]))(tg))
            p.emit("vector", (lambda tg_: lambda e: e.tensor_sub(
                rs16[:, tg_:tg_ + 1], rs16[:, tg_:tg_ + 1], sc1[:, 0:1]))(tg),
                inc="rms")
            p.emit("scalar", (lambda tg_: lambda e: e.activation(
                rs16[:, tg_:tg_ + 1], rs16[:, tg_:tg_ + 1], AF.Sqrt,
                bias=epsl[:, 0:1], scale=1.0))(tg),
                waits=(("rms", p.cur("rms")),), inc="sq")
            p.emit("vector", (lambda tg_: lambda e: e.reciprocal(
                rs16[:, tg_:tg_ + 1], rs16[:, tg_:tg_ + 1]))(tg),
                waits=(("sq", p.cur("sq")),))
            p.emit("vector", (lambda ti_, tg_: lambda e: e.tensor_scalar(
                x2bf[:, :], x2g[:, ti_ * D:(ti_ + 1) * D],
                m16[:, tg_:tg_ + 1], rs16[:, tg_:tg_ + 1],
                OP.subtract, OP.mult))(ti, tg),
                waits=(("lgT", p.cur("lgT")),), inc="x2ev")
            cstv = p.cur("x2ev")
            p.emit("scalar", (lambda tg_: lambda e: e.activation(
                sqs[:, :], x2bf[:, :], AF.Square,
                accum_out=r2c[:, tg_:tg_ + 1]))(tg),
                waits=(("x2ev", cstv),), inc="sq")
            p.emit("scalar", (lambda tg_: lambda e: e.activation(
                r2c[:, tg_:tg_ + 1], r2c[:, tg_:tg_ + 1], AF.Sqrt,
                bias=epsb[:, 0:1], scale=1.0 / D))(tg),
                waits=(("sq", p.cur("sq")),), inc="sq")
            p.emit("vector", (lambda tg_: lambda e: e.reciprocal(
                r2c[:, tg_:tg_ + 1], r2c[:, tg_:tg_ + 1]))(tg),
                waits=(("sq", p.cur("sq")),), inc="rms")
            for rnd in range(4):
                for i in range(8):
                    w = []
                    if i == 0:
                        w = [("x2ev", cstv), ("lgTe", p.cur("lgTe"))]
                    p.emit("tensor", (lambda rnd_, i_: lambda e: e.transpose(
                        pbf[:, i_ * 128:(i_ + 1) * 128],
                        x2bf[:, rnd_ * 1024 + i_ * 128: rnd_ * 1024 + (i_ + 1) * 128],
                        ident_b[:, :]))(rnd, i),
                        waits=tuple(w), inc="lgT" if i == 7 else None)
                tvv = p.cur("lgT")
                for i in range(8):
                    p.emit("vector", (lambda rnd_, i_: lambda e: e.tensor_copy(
                        x2T[:, (rnd_ * 8 + i_) * 128:(rnd_ * 8 + i_ + 1) * 128],
                        pbf[:, i_ * 128:(i_ + 1) * 128]))(rnd, i),
                        waits=(("lgT", tvv),) if i == 0 else (),
                        inc="lgTe" if i == 7 else None)
            tev = p.cur("lgTe")
            for c in range(32):
                w = []
                if c == 0:
                    w = [("lgTe", tev), ("lgev", p.cur("lgev")), ("wcns", 16)]
                p.emit("tensor", (lambda c_, st_, sp_: lambda e: e.matmul(
                    pmisc[:, 0:24], x2T[:, c_ * 128:(c_ + 1) * 128],
                    wall_b[:, c_ * 24:(c_ + 1) * 24], start=st_, stop=sp_))(
                        c, c == 0, c == 31),
                    waits=tuple(w), inc="lgmm" if c == 31 else None)
            p.emit("scalar", (lambda tg_: lambda e: e.activation(
                lga[:, tg_ * 24:(tg_ + 1) * 24], pmisc[:, 0:24], AF.Copy,
                bias=0.0, scale=r2c[:, tg_:tg_ + 1]))(tg),
                waits=(("lgmm", p.cur("lgmm")), ("rms", p.cur("rms"))), inc="sq")
            p.emit("vector", (lambda tg_: lambda e: e.tensor_add(
                lga[:, tg_ * 24:(tg_ + 1) * 24], lga[:, tg_ * 24:(tg_ + 1) * 24],
                basesb[:, :]))(tg),
                waits=(("sq", p.cur("sq")),), inc="lgev")
    p.barrier(pre={"gpsimd": [("x2wr0", p.cur("x2wr0")),
                               ("x2wr1", p.cur("x2wr1"))],
                   "vector": [("lgev", p.cur("lgev"))]})
    stB.close()

    lview = lga[:, :].rearrange("p (t c) -> p t c", c=24)
    Zv = Z[:, :].rearrange("p (t i j) -> p t i j", i=4, j=4)
    tv_ = tmp[:, :].rearrange("p (t i j) -> p t i j", i=4, j=4)
    lnv_v = lnv[:, :].rearrange("p (t j) -> p t j", j=4)
    lnu_v = lnu[:, :].rearrange("p (t i) -> p t i", i=4)

    p.emit("vector", lambda e: e.tensor_scalar_mul(
        Zv, lview[:, :, 0:16].rearrange("p t (i j) -> p t i j", j=4), 1.0 / TAU))
    p.emit("vector", lambda e: e.memset(lnv[:, :], 0.0), inc="snk")

    for it in range(SINK_IT):
        for i in range(4):
            p.emit("vector", (lambda i_: lambda e: e.tensor_sub(
                tv_[:, :, i_, :], Zv[:, :, i_, :], lnv_v))(i),
                waits=(("snk", p.cur("snk")),) if i == 0 else (),
                inc="snk" if i == 3 else None)
        p.emit("scalar", lambda e: e.activation(tmpe[:, :], tmp[:, :], AF.Exp),
               waits=(("snk", p.cur("snk")),), inc="sq")
        p.emit("vector", lambda e: e.tensor_reduce(
            ssum[:, :], tmpe[:, :].rearrange("p (ti j) -> p ti j", j=4),
            axis=AX.X, op=OP.add),
            waits=(("sq", p.cur("sq")),), inc="snk")
        p.emit("scalar", lambda e: e.activation(lnu[:, :], ssum[:, :], AF.Ln),
               waits=(("snk", p.cur("snk")),), inc="sq")
        sqv = p.cur("sq")
        for j in range(4):
            p.emit("vector", (lambda j_: lambda e: e.tensor_sub(
                tv_[:, :, j_, :], Zv[:, :, :, j_], lnu_v))(j),
                waits=(("sq", sqv),) if j == 0 else (),
                inc="snk" if j == 3 else None)
        p.emit("scalar", lambda e: e.activation(tmpe[:, :], tmp[:, :], AF.Exp),
               waits=(("snk", p.cur("snk")),), inc="sq")
        p.emit("vector", lambda e: e.tensor_reduce(
            ssum[:, :], tmpe[:, :].rearrange("p (tj i) -> p tj i", i=4),
            axis=AX.X, op=OP.add),
            waits=(("sq", p.cur("sq")),), inc="snk")
        p.emit("scalar", lambda e: e.activation(lnv[:, :], ssum[:, :], AF.Ln),
               waits=(("snk", p.cur("snk")),), inc="sq")
        p.emit("vector", lambda e: e.tensor_copy(sm[:, 0:1], lnv[:, 0:1]),
               waits=(("sq", p.cur("sq")),), inc="snk")

    u4v = u4[:, :].rearrange("p (t i j) -> p t i j", i=4, j=4)
    for j in range(4):
        p.emit("vector", (lambda j_: lambda e: e.tensor_copy(
            u4v[:, :, :, j_], lnu_v))(j))
    p.emit("vector", lambda e: e.tensor_sub(Hm[:, :], Z[:, :], u4[:, :]))
    Hv = Hm[:, :].rearrange("p (t i j) -> p t i j", i=4, j=4)
    for i in range(4):
        p.emit("vector", (lambda i_: lambda e: e.tensor_sub(
            tv_[:, :, i_, :], Hv[:, :, i_, :], lnv_v))(i),
            inc="snk" if i == 3 else None)
    p.emit("scalar", lambda e: e.activation(Hm[:, :], tmp[:, :], AF.Exp),
           waits=(("snk", p.cur("snk")),), inc="sq")
    p.emit("vector", lambda e: e.tensor_scalar_mul(Hm[:, :], Hm[:, :], gb[:, 0:1]),
           waits=(("sq", p.cur("sq")),))
    diagv = Hm[:, :].rearrange("p (t d) -> p t d", d=16)[:, :, 0:16:5]
    p.emit("vector", lambda e: e.tensor_scalar_add(diagv, diagv, omgb[:, 0:1]),
           inc="snk")

    for (src0, dst) in ((16, hp), (20, hq)):
        pv = lview[:, :, src0:src0 + 4]
        dv = dst[:, :].rearrange("p (t n) -> p t n", n=4)
        s4 = sm4[:, :].rearrange("p (t n) -> p t n", n=4)
        p.emit("vector", (lambda pv_: lambda e: e.reduce_max(
            sm[:, :], pv_, axis=AX.X))(pv))
        for n in range(4):
            p.emit("vector", (lambda n_, s4_: lambda e: e.tensor_copy(
                s4_[:, :, n_], sm[:, :]))(n, s4))
        p.emit("vector", (lambda pv_, dv_, s4_: lambda e: e.tensor_sub(
            dv_, pv_, s4_))(pv, dv, s4), inc="snk")
        p.emit("scalar", (lambda d_: lambda e: e.activation(
            d_[:, :], d_[:, :], AF.Exp))(dst),
            waits=(("snk", p.cur("snk")),), inc="sq")
        p.emit("vector", (lambda dv_: lambda e: e.tensor_reduce(
            sm[:, :], dv_, axis=AX.X, op=OP.add))(dv),
            waits=(("sq", p.cur("sq")),))
        p.emit("vector", lambda e: e.reciprocal(sm[:, :], sm[:, :]))
        for n in range(4):
            p.emit("vector", (lambda n_, s4_: lambda e: e.tensor_copy(
                s4_[:, :, n_], sm[:, :]))(n, s4))
        p.emit("vector", (lambda d_: lambda e: e.tensor_mul(
            d_[:, :], d_[:, :], sm4[:, :]))(dst), inc="snk")
    p.barrier(pre={"vector": [("snk", p.cur("snk"))]})

    stC = ExitStack()
    wbr_b = stC.enter_context(nc.sbuf_tensor("wbr_b", [128, 8 * 1024], BF16))
    x2m = [stC.enter_context(nc.sbuf_tensor(f"x2m{i}", [128, D], F32))
           for i in range(2)]
    xpre = stC.enter_context(nc.sbuf_tensor("xpre", [128, 1024], F32))
    ln2m = [stC.enter_context(nc.sbuf_tensor(f"ln2m{i}", [128, D], F32))
            for i in range(2)]
    xpre_b = stC.enter_context(nc.sbuf_tensor("xpre_b", [128, 1024], BF16))
    xpT = stC.enter_context(nc.sbuf_tensor("xpT", [128, 8 * 128], BF16))
    ybuf = stC.enter_context(nc.sbuf_tensor("ybuf", [128, 1024], F32))
    obuf = [stC.enter_context(nc.sbuf_tensor(f"obuf{i}", [128, D], F32))
            for i in range(2)]

    p.dma("gpsimd", lambda e: e.dma_start(
        wbr_b[:, :], wbr_h.ap().rearrange("(c p) d -> p c d", p=128)), inc="wbld")
    wbl = p.cur("wbld")

    for tg in range(16):
        bi = tg % 2
        w = [("mix", max(0, p.cur("mix") - 1))] if tg >= 2 else []
        p.dma("sync", (lambda tg_, bi_: lambda e: e.dma_start(
            x2m[bi_][:, :], x2_h.ap()[tg_ * 128:(tg_ + 1) * 128, :]))(tg, bi),
            waits=tuple(w), inc=f"x2rd{bi}")
        x2rv = p.cur(f"x2rd{bi}")
        w = [(f"x2rd{bi}", x2rv)]
        if tg == 0:
            w.append(("wbld", wbl))
        p.emit("vector", (lambda bi_, tg_: lambda e: e.tensor_scalar(
            ln2m[bi_][:, :], x2m[bi_][:, :], m16[:, tg_:tg_ + 1],
            rs16[:, tg_:tg_ + 1], OP.subtract, OP.mult))(bi, tg), waits=tuple(w))
        p.emit("vector", (lambda bi_, tg_: lambda e: e.tensor_scalar_mul(
            xpre[:, :], ln2m[bi_][:, 0:1024], hp[:, 4 * tg_:4 * tg_ + 1]))(bi, tg))
        for n in range(1, 4):
            p.emit("vector", (lambda bi_, tg_, n_: lambda e: e.scalar_tensor_tensor(
                xpre[:, :], ln2m[bi_][:, n_ * 1024:(n_ + 1) * 1024],
                hp[:, 4 * tg_ + n_:4 * tg_ + n_ + 1], xpre[:, :],
                OP.mult, OP.add))(bi, tg, n))
        p.emit("vector", lambda e: e.tensor_copy(xpre_b[:, :], xpre[:, :]),
               waits=(("lgT", p.cur("lgT")),), inc="mix")
        mxv = p.cur("mix")
        for i in range(8):
            w = []
            if i == 0:
                w = [("mix", mxv), ("lgTe", p.cur("lgTe")), ("yev", p.cur("yev"))]
            p.emit("tensor", (lambda i_: lambda e: e.transpose(
                pbf[:, i_ * 128:(i_ + 1) * 128], xpre_b[:, i_ * 128:(i_ + 1) * 128],
                ident_b[:, :]))(i), waits=tuple(w), inc="lgT" if i == 7 else None)
        tvv = p.cur("lgT")
        for i in range(8):
            p.emit("vector", (lambda i_: lambda e: e.tensor_copy(
                xpT[:, i_ * 128:(i_ + 1) * 128], pbf[:, i_ * 128:(i_ + 1) * 128]))(i),
                waits=(("lgT", tvv),) if i == 0 else (),
                inc="lgTe" if i == 7 else None)
        tev = p.cur("lgTe")
        for n in range(2):
            for c in range(8):
                w = []
                if n == 0 and c == 0:
                    w = [("lgTe", tev), ("yev", p.cur("yev"))]
                p.emit("tensor", (lambda n_, c_, st_, sp_: lambda e: e.matmul(
                    pp[n_][:, :], xpT[:, c_ * 128:(c_ + 1) * 128],
                    wbr_b[:, c_ * 1024 + n_ * 512: c_ * 1024 + (n_ + 1) * 512],
                    start=st_, stop=sp_))(n, c, c == 0, c == 7),
                    waits=tuple(w), inc="ymm" if (n == 1 and c == 7) else None)
        ymv = p.cur("ymm")
        for n in range(2):
            p.emit("vector", (lambda n_: lambda e: e.tensor_copy(
                ybuf[:, n_ * 512:(n_ + 1) * 512], pp[n_][:, :]))(n),
                waits=(("ymm", ymv),) if n == 0 else (),
                inc="yev" if n == 1 else None)
        yev = p.cur("yev")
        for i in range(4):
            w = []
            if i == 0:
                w = [("yev", yev), (f"ost{bi}", p.cur(f"ost{bi}"))]
            p.emit("vector", (lambda bi_, tg_, i_: lambda e: e.scalar_tensor_tensor(
                obuf[bi_][:, i_ * 1024:(i_ + 1) * 1024], ln2m[bi_][:, 0:1024],
                Hm[:, 16 * tg_ + 4 * i_: 16 * tg_ + 4 * i_ + 1],
                x2m[bi_][:, i_ * 1024:(i_ + 1) * 1024],
                OP.mult, OP.add))(bi, tg, i),
                waits=tuple(w))
            for jj in range(1, 4):
                p.emit("vector",
                       (lambda bi_, tg_, i_, j_: lambda e: e.scalar_tensor_tensor(
                           obuf[bi_][:, i_ * 1024:(i_ + 1) * 1024],
                           ln2m[bi_][:, j_ * 1024:(j_ + 1) * 1024],
                           Hm[:, 16 * tg_ + 4 * i_ + j_: 16 * tg_ + 4 * i_ + j_ + 1],
                           obuf[bi_][:, i_ * 1024:(i_ + 1) * 1024],
                           OP.mult, OP.add))(bi, tg, i, jj))
            p.emit("vector", (lambda bi_, tg_, i_: lambda e: e.scalar_tensor_tensor(
                obuf[bi_][:, i_ * 1024:(i_ + 1) * 1024], ybuf[:, :],
                hq[:, 4 * tg_ + i_:4 * tg_ + i_ + 1],
                obuf[bi_][:, i_ * 1024:(i_ + 1) * 1024],
                OP.mult, OP.add))(bi, tg, i),
                inc="mix" if i == 3 else None)
        mxo = p.cur("mix")
        p.dma("sync", (lambda bi_, tg_: lambda e: e.dma_start(
            out_h.ap()[tg_ * 128:(tg_ + 1) * 128, :], obuf[bi_][:, :]))(bi, tg),
            waits=(("mix", mxo),), inc=f"ost{bi}")

    p.emit("gpsimd", None, waits=(("ost0", p.cur("ost0")), ("ost1", p.cur("ost1"))))
    p.emit("sync", None, waits=(("ost0", p.cur("ost0")), ("ost1", p.cur("ost1"))))

    _block_emit(nc, p)
    stC.close()
    stack.close()
    return nc



def _rope_tables():
    i = np.arange(RD // 2, dtype=np.float64)
    inv_freq = 1.0 / (10000.0 ** (2.0 * i / RD))
    t = np.arange(S, dtype=np.float64)
    ang = np.outer(t, inv_freq)
    sinr = np.tile(np.sin(ang).astype(np.float32), (1, HPC))
    cosr = np.tile(np.cos(ang).astype(np.float32), (1, HPC))
    return np.ascontiguousarray(sinr), np.ascontiguousarray(cosr)


def _masks():
    m = np.zeros((4, 128, 512), np.float32)
    kk = np.arange(128)[:, None]
    qq = np.arange(512)[None, :]
    for di, d in enumerate((0, 128, 256, 384)):
        m[di] = np.where(kk + d > qq, np.float32(-1e9), np.float32(0.0))
    return np.ascontiguousarray(m.reshape(512, 512))


FUSED = os.environ.get("K_UNFUSED", "") != "1"


def kernel(**inputs):
    if FUSED:
        return kernel_fused(inputs)
    x = np.asarray(inputs["L0_attn_hidden_states"])
    wq = np.asarray(inputs["L0_attn_wq"])
    wk = np.asarray(inputs["L0_attn_wk"])
    wv = np.asarray(inputs["L0_attn_wv"])
    wo = np.asarray(inputs["L0_attn_wo"])
    W_res = np.asarray(inputs["L0_ffn_W_res"])
    W_pre = np.asarray(inputs["L0_ffn_W_pre"])
    W_post = np.asarray(inputs["L0_ffn_W_post"])
    Hrb = np.asarray(inputs["L0_ffn_H_res_base"])
    Hpb = np.asarray(inputs["L0_ffn_H_pre_base"])
    Hqb = np.asarray(inputs["L0_ffn_H_post_base"])
    gate = np.asarray(inputs["L0_ffn_gate"])
    Wbr = np.asarray(inputs["L0_ffn_W_branch"])

    sinr, cosr = _rope_tables()
    masks = _masks()
    ident = np.eye(128, dtype=np.float32)
    onesrow = np.ones((1, 128), np.float32)

    xT = [np.ascontiguousarray(x[b].T) for b in range(B)]
    wT = {"q": wq, "k": wk, "v": wv}
    wTs = {pj: [np.ascontiguousarray(wT[pj][rk * DPC:(rk + 1) * DPC, :].T)
                for rk in range(4)] for pj in wT}

    nc_a = build_launch_a()
    in_maps_a = []
    for c in range(NCORE):
        b, rk = c // 4, c % 4
        in_maps_a.append(dict(
            xT=xT[b], wqT=wTs["q"][rk], wkT=wTs["k"][rk], wvT=wTs["v"][rk],
            sinr=sinr, cosr=cosr, masks=masks, ident=ident, onesrow=onesrow))
    res_a = run_bass_kernel_spmd(nc_a, in_maps_a, core_ids=list(range(NCORE)))

    attn = [np.concatenate([res_a.results[b * 4 + rk]["attn"] for rk in range(4)],
                           axis=0) for b in range(B)]

    woT = np.ascontiguousarray(wo.T)
    wallT = np.ascontiguousarray(np.concatenate([W_res, W_pre, W_post], axis=0).T)
    bases = np.concatenate(
        [np.asarray(Hrb).reshape(-1), np.asarray(Hpb).reshape(-1),
         np.asarray(Hqb).reshape(-1)]).reshape(1, 24).astype(np.float32)
    gate2 = np.asarray(gate).reshape(1, 1).astype(np.float32)

    nc_b = build_launch_b()
    in_maps_b = []
    for c in range(NCORE):
        b, rk = c // 4, c % 4
        in_maps_b.append(dict(
            aslice=np.ascontiguousarray(attn[b][:, rk * TPC:(rk + 1) * TPC]),
            xres=np.ascontiguousarray(x[b, rk * TPC:(rk + 1) * TPC, :]),
            woT=woT, wallT=wallT, bases=bases, gate=gate2, wbr=Wbr,
            ident=ident, onesrow=onesrow))
    res_b = run_bass_kernel_spmd(nc_b, in_maps_b, core_ids=list(range(NCORE)))

    out = np.empty((B, S, D), np.float32)
    for c in range(NCORE):
        b, rk = c // 4, c % 4
        out[b, rk * TPC:(rk + 1) * TPC, :] = res_b.results[c]["out"]
    return out



def build_fused():
    nc = bass.Bass("TRN2", target_bir_lowering=False)
    NHC = 8
    DQC = NHC * HD
    QW = 4096

    xT_h = nc.dram_tensor("xT", [D, S], F32, kind="ExternalInput")
    xq_h = nc.dram_tensor("xq", [D, QW], F32, kind="ExternalInput")
    w_h = {pj: nc.dram_tensor(f"w{pj}T", [D, DQC], F32, kind="ExternalInput")
           for pj in ("q", "k", "v")}
    sinq_h = nc.dram_tensor("sinq", [QW, 256], F32, kind="ExternalInput")
    cosq_h = nc.dram_tensor("cosq", [QW, 256], F32, kind="ExternalInput")
    sink_h = nc.dram_tensor("sink", [S, 256], F32, kind="ExternalInput")
    cosk_h = nc.dram_tensor("cosk", [S, 256], F32, kind="ExternalInput")
    maskd1_h = nc.dram_tensor("maskd1", [512, 512], F32, kind="ExternalInput")
    maskd2_h = nc.dram_tensor("maskd2", [512, 512], F32, kind="ExternalInput")
    yflag_h = nc.dram_tensor("yflag", [128, 1], F32, kind="ExternalInput")
    roff_h = nc.dram_tensor("roff", [1, 4], mybir.dt.int32, kind="ExternalInput")
    offtab_h = nc.dram_tensor("offtab", [1, 256], mybir.dt.int32, kind="ExternalInput")
    nonce_h = nc.dram_tensor("noncev", [1, 64], F32, kind="ExternalInput")
    ident_h = nc.dram_tensor("ident", [128, 128], F32, kind="ExternalInput")
    onesrow_h = nc.dram_tensor("onesrow", [1, 128], F32, kind="ExternalInput")
    xres_h = nc.dram_tensor("xres", [TPC, D], F32, kind="ExternalInput")
    woT_h = nc.dram_tensor("woT", [D, D], F32, kind="ExternalInput")
    wallT_h = nc.dram_tensor("wallT", [D, 24], F32, kind="ExternalInput")
    bases_h = nc.dram_tensor("bases", [1, 24], F32, kind="ExternalInput")
    gate_h = nc.dram_tensor("gate", [1, 1], F32, kind="ExternalInput")
    wbr_h = nc.dram_tensor("wbr", [1024, 1024], F32, kind="ExternalInput")

    out_h = nc.dram_tensor("out", [TPC, D], F32, kind="ExternalOutput")

    xbf_h = nc.dram_tensor("xbf", [D, S], BF16, kind="Internal")
    xqbf_h = nc.dram_tensor("xqbf", [D, QW], BF16, kind="Internal")
    qt_h = nc.dram_tensor("qt", [8, DQC, 512], BF16, kind="Internal")
    kt_h = nc.dram_tensor("kt", [16, DQC, 512], BF16, kind="Internal")
    vt_h = nc.dram_tensor("vt", [NHC, S, HD], BF16, kind="Internal")
    x2_h = nc.dram_tensor("x2h", [TPC, D], F32, kind="Internal")
    attn_loc = nc.dram_tensor("attnloc", [2048, QW], BF16, kind="Internal")
    attnsh = nc.dram_tensor("attnsh", [D, QW], BF16, kind="Internal",
                            addr_space="Shared")
    flagsh = nc.dram_tensor("flagsh", [2, 64], F32, kind="Internal",
                            addr_space="Shared")

    sem_names = ["bar", "vch", "sch", "cns", "wcns",
                 "xin0", "xin1", "xcast", "xwr0", "xwr1", "sq", "smm", "sev",
                 "fin", "fsq", "fre", "ftp",
                 "wld", "xbld0", "xbld1", "qmm", "bkf", "ropa", "ropb", "scld", "nrm",
                 "tps", "tpe", "qst",
                 "ktld", "qcld", "tsc", "vmsk", "sexp", "tpv", "vrs",
                 "trb", "voev", "ost", "xch",
                 "ald", "wld0", "wld1", "xrd", "womm", "x2ev", "x2wr0", "x2wr1",
                 "rms", "lgT", "lgTe", "lgmm", "lgev", "snk",
                 "x2rd0", "x2rd1", "mix", "ymm", "yev", "ost0", "ost1", "wbld",
                 "xpo", "womh"]
    stack = ExitStack()
    sems = {n: stack.enter_context(nc.semaphore(n)) for n in sem_names}
    p = Plan(sems)

    ident_f = nc.alloc_sbuf_tensor("ident_f", [128, 128], F32)
    ident_b = nc.alloc_sbuf_tensor("ident_b", [128, 128], BF16)
    onesrow_f = nc.alloc_sbuf_tensor("onesrow_f", [1, 128], F32)
    ones_b = nc.alloc_sbuf_tensor("ones_b", [128, 1], BF16)
    yflag_sb = nc.alloc_sbuf_tensor("yflag_sb", [128, 1], F32)
    roff_sb = nc.alloc_sbuf_tensor("roff_sb", [1, 4], mybir.dt.int32)
    offtab_sb = nc.alloc_sbuf_tensor("offtab_sb", [1, 256], mybir.dt.int32)
    nonce_sb = nc.alloc_sbuf_tensor("nonce_sb", [1, 64], F32)
    flag_sb = nc.alloc_sbuf_tensor("flag_sb", [1, 64], F32)
    r_cols = nc.alloc_sbuf_tensor("r_cols", [128, 64], F32)
    rq_cols = nc.alloc_sbuf_tensor("rq_cols", [128, 32], F32)
    qmax2 = nc.alloc_sbuf_tensor("qmax2", [128, NHC], F32)
    kmax2 = nc.alloc_sbuf_tensor("kmax2", [128, NHC], F32)
    negcb = nc.alloc_sbuf_tensor("negcb", [128, NHC], F32)
    negcy = nc.alloc_sbuf_tensor("negcy", [128, NHC], F32)
    negc_row = nc.alloc_sbuf_tensor("negc_row", [1, NHC], F32)
    epsc = nc.alloc_sbuf_tensor("epsc", [1, 1], F32)
    wall_b = nc.alloc_sbuf_tensor("wall_b", [128, 32 * 24], BF16)
    bases_row = nc.alloc_sbuf_tensor("bases_row", [1, 24], F32)
    basesb = nc.alloc_sbuf_tensor("basesb", [128, 24], F32)
    gsc = nc.alloc_sbuf_tensor("gsc", [1, 4], F32)
    gb = nc.alloc_sbuf_tensor("gb", [128, 1], F32)
    omgb = nc.alloc_sbuf_tensor("omgb", [128, 1], F32)
    r2c = nc.alloc_sbuf_tensor("r2c", [128, 16], F32)
    m16 = nc.alloc_sbuf_tensor("m16", [128, 16], F32)
    rs16 = nc.alloc_sbuf_tensor("rs16", [128, 16], F32)
    sc1 = nc.alloc_sbuf_tensor("sc1", [128, 1], F32)
    epsb = nc.alloc_sbuf_tensor("epsb", [128, 1], F32)
    epsl = nc.alloc_sbuf_tensor("epsl", [128, 1], F32)
    lga = nc.alloc_sbuf_tensor("lga", [128, 16 * 24], F32)
    Z = nc.alloc_sbuf_tensor("Z", [128, 256], F32)
    tmp = nc.alloc_sbuf_tensor("tmp", [128, 256], F32)
    tmpe = nc.alloc_sbuf_tensor("tmpe", [128, 256], F32)
    ssum = nc.alloc_sbuf_tensor("ssum", [128, 64], F32)
    lnu = nc.alloc_sbuf_tensor("lnu", [128, 64], F32)
    lnv = nc.alloc_sbuf_tensor("lnv", [128, 64], F32)
    Hm = nc.alloc_sbuf_tensor("Hm", [128, 256], F32)
    u4 = nc.alloc_sbuf_tensor("u4", [128, 256], F32)
    hp = nc.alloc_sbuf_tensor("hp", [128, 64], F32)
    hq = nc.alloc_sbuf_tensor("hq", [128, 64], F32)
    sm = nc.alloc_sbuf_tensor("sm", [128, 16], F32)
    sm4 = nc.alloc_sbuf_tensor("sm4", [128, 64], F32)
    stm = ExitStack()
    maskd1_sb = stm.enter_context(nc.sbuf_tensor("maskd1_sb", [128, 4 * 512], F32))
    maskd2_sb = stm.enter_context(nc.sbuf_tensor("maskd2_sb", [128, 4 * 512], F32))
    negm_row = stm.enter_context(nc.sbuf_tensor("negm_row", [1, S], BF16))
    negmq_row = stm.enter_context(nc.sbuf_tensor("negmq_row", [1, QW], BF16))
    s_sum = stm.enter_context(nc.sbuf_tensor("s_sum", [1, 256], F32))
    s_sq = stm.enter_context(nc.sbuf_tensor("s_sq", [1, 256], F32))
    s_m = stm.enter_context(nc.sbuf_tensor("s_m", [1, 256], F32))

    pp = [nc.alloc_psum_tensor(f"pp{i}", [128, 512], F32) for i in range(4)]
    ps_a = nc.alloc_psum_tensor("ps_a", [1, 512], F32)
    ps_b = nc.alloc_psum_tensor("ps_b", [1, 512], F32)
    pbf = nc.alloc_psum_tensor("pbf", [128, 1024], BF16)
    pmisc = nc.alloc_psum_tensor("pmisc", [128, 512], F32)

    p.dma("sync", lambda e: e.dma_start(ident_f[:, :], ident_h.ap()[:, :]), inc="cns")
    p.dma("sync", lambda e: e.dma_start(onesrow_f[:, :], onesrow_h.ap()[:, :]),
          inc="cns")
    p.dma("sync", lambda e: e.dma_start(
        maskd1_sb[:, :], maskd1_h.ap().rearrange("(m p) t -> p m t", p=128)),
        inc="cns")
    p.dma("sync", lambda e: e.dma_start(
        maskd2_sb[:, :], maskd2_h.ap().rearrange("(m p) t -> p m t", p=128)),
        inc="cns")
    p.dma("sync", lambda e: e.dma_start(yflag_sb[:, :], yflag_h.ap()[:, :]),
          inc="cns")
    p.dma("sync", lambda e: e.dma_start(roff_sb[:, :], roff_h.ap()[:, :]), inc="cns")
    p.dma("sync", lambda e: e.dma_start(offtab_sb[:, :], offtab_h.ap()[:, :]), inc="cns")
    p.dma("sync", lambda e: e.dma_start(nonce_sb[:, :], nonce_h.ap()[:, :]),
          inc="cns")
    p.dma("sync", lambda e: e.dma_start(gsc[0:1, 0:1], gate_h.ap()[:, :]), inc="cns")
    p.dma("sync", lambda e: e.dma_start(bases_row[0:1, :], bases_h.ap()[:, :]),
          inc="cns")
    p.dma("gpsimd", lambda e: e.dma_start(
        wall_b[:, :], wallT_h.ap().rearrange("(c p) d -> p c d", p=128)), inc="wcns")
    cnsv = p.cur("cns")
    p.emit("vector", lambda e: e.tensor_copy(ident_b[:, :], ident_f[:, :]),
           waits=(("cns", cnsv),))
    p.emit("vector", lambda e: e.memset(ones_b[:, :], 1.0))
    p.emit("vector", lambda e: e.memset(qmax2[:, :], 0.0))
    p.emit("vector", lambda e: e.memset(kmax2[:, :], 0.0))
    p.emit("vector", lambda e: e.memset(epsc[:, :], EPS_LN))
    p.emit("vector", lambda e: e.memset(epsb[:, :], EPS_RMS))
    p.emit("vector", lambda e: e.memset(epsl[:, :], EPS_LN), inc="cns")
    p.emit("tensor", lambda e: e.matmul(
        pmisc[:, 0:24], onesrow_f[0:1, :], bases_row[0:1, :], start=True, stop=True),
        waits=(("cns", cnsv),), inc="lgmm")
    p.emit("vector", lambda e: e.tensor_copy(basesb[:, :], pmisc[:, 0:24]),
           waits=(("lgmm", p.cur("lgmm")),), inc="lgev")
    p.emit("scalar", lambda e: e.activation(
        gsc[0:1, 1:2], gsc[0:1, 0:1], AF.Tanh, bias=0.0, scale=0.5),
        waits=(("cns", cnsv),), inc="sq")
    p.emit("vector", lambda e: e.tensor_scalar(
        gsc[0:1, 2:3], gsc[0:1, 1:2], 0.5, 0.5, OP.mult, OP.add),
        waits=(("sq", p.cur("sq")),))
    p.emit("vector", lambda e: e.tensor_scalar(
        gsc[0:1, 3:4], gsc[0:1, 1:2], -0.5, 0.5, OP.mult, OP.add), inc="rms")
    p.emit("tensor", lambda e: e.matmul(
        pmisc[:, 24:25], onesrow_f[0:1, :], gsc[0:1, 2:3], start=True, stop=True),
        waits=(("rms", p.cur("rms")), ("lgev", p.cur("lgev"))))
    p.emit("tensor", lambda e: e.matmul(
        pmisc[:, 25:26], onesrow_f[0:1, :], gsc[0:1, 3:4], start=True, stop=True),
        inc="lgmm")
    p.emit("vector", lambda e: e.tensor_copy(gb[:, :], pmisc[:, 24:25]),
           waits=(("lgmm", p.cur("lgmm")),))
    p.emit("vector", lambda e: e.tensor_copy(omgb[:, :], pmisc[:, 25:26]),
           inc="lgev")

    st1 = ExitStack()
    CW = 256
    xin = [st1.enter_context(nc.sbuf_tensor(f"xin{i}", [128, 32 * CW], F32))
           for i in range(2)]
    xbf_sb = [st1.enter_context(nc.sbuf_tensor(f"xbfs{i}", [128, 32 * CW], BF16))
              for i in range(2)]
    xsq_sb = [st1.enter_context(nc.sbuf_tensor(f"xsqs{i}", [128, 32 * CW], BF16))
              for i in range(2)]

    chk = [0]

    def emit_stats(src_v, dst_v, nch, negm_dst, rcol_dst, pm_base):
        sev_entry = p.cur("sev")
        for lc in range(nch):
            ch = chk[0]
            bi = ch % 2
            c0 = lc * CW
            for sp in range(4):
                w = [("xcast", ch - 1)] if (ch >= 2 and sp == 0) else []
                p.dma("sync", (lambda bi_, sp_, c0_: lambda e: e.dma_start(
                    xin[bi_][:, sp_ * 8 * CW:(sp_ + 1) * 8 * CW],
                    src_v[:, sp_ * 8:(sp_ + 1) * 8, c0_:c0_ + CW]))(bi, sp, c0),
                    waits=tuple(w), inc=f"xin{bi}")
            w = [(f"xin{bi}", 64 * (ch // 2 + 1))]
            if ch >= 2:
                w += [(f"xwr{bi}", 64 * (ch // 2)), ("smm", ch - 1), ("sq", ch - 1)]
            p.emit("vector", (lambda bi_: lambda e: e.tensor_copy(
                xbf_sb[bi_][:, :], xin[bi_][:, :]))(bi), waits=tuple(w), inc="xcast")
            xc = p.cur("xcast")
            w = [("xcast", xc)]
            if ch >= 2:
                w.append(("smm", ch - 1))
            p.emit("scalar", (lambda bi_: lambda e: e.square(
                xsq_sb[bi_][:, :], xbf_sb[bi_][:, :]))(bi), waits=tuple(w), inc="sq")
            for sp in range(4):
                w = [("xcast", xc)] if sp == 0 else []
                p.dma("gpsimd", (lambda bi_, sp_, c0_: lambda e: e.dma_start(
                    dst_v[:, sp_ * 8:(sp_ + 1) * 8, c0_:c0_ + CW],
                    xbf_sb[bi_][:, sp_ * 8 * CW:(sp_ + 1) * 8 * CW]))(bi, sp, c0),
                    waits=tuple(w), inc=f"xwr{bi}")
            sqv = p.cur("sq")
            for dc in range(32):
                w = []
                if dc == 0:
                    w = [("xcast", xc), ("sq", sqv), ("sev", ch)]
                p.emit("tensor", (lambda bi_, dc_, st_, sp_: lambda e: e.matmul(
                    ps_a[0:1, 0:CW], ones_b[:, :],
                    xbf_sb[bi_][:, dc_ * CW:(dc_ + 1) * CW], start=st_, stop=sp_))(
                        bi, dc, dc == 0, dc == 31), waits=tuple(w))
                p.emit("tensor", (lambda bi_, dc_, st_, sp_: lambda e: e.matmul(
                    ps_b[0:1, 0:CW], ones_b[:, :],
                    xsq_sb[bi_][:, dc_ * CW:(dc_ + 1) * CW], start=st_, stop=sp_))(
                        bi, dc, dc == 0, dc == 31),
                    inc="smm" if dc == 31 else None)
            smv = p.cur("smm")
            w = [("smm", smv)]
            if ch >= 1:
                w.append(("ftp", ch))
            p.emit("vector", lambda e: e.tensor_copy(s_sum[0:1, :], ps_a[0:1, 0:CW]),
                   waits=tuple(w))
            p.emit("vector", lambda e: e.tensor_copy(s_sq[0:1, :], ps_b[0:1, 0:CW]),
                   inc="sev")
            p.emit("vector", (lambda c0_: lambda e: e.tensor_scalar_mul(
                negm_dst[0:1, c0_:c0_ + CW], s_sum[0:1, :], -1.0 / D))(c0))
            p.emit("vector", lambda e: e.tensor_scalar_mul(
                s_m[0:1, :], s_sum[0:1, :], 1.0 / D))
            p.emit("vector", lambda e: e.tensor_mul(s_m[0:1, :], s_m[0:1, :],
                                                    s_m[0:1, :]))
            p.emit("vector", lambda e: e.tensor_scalar_mul(
                s_sq[0:1, :], s_sq[0:1, :], 1.0 / D))
            p.emit("vector", lambda e: e.tensor_sub(s_sq[0:1, :], s_sq[0:1, :],
                                                    s_m[0:1, :]), inc="fin")
            p.emit("scalar", lambda e: e.activation(
                s_sq[0:1, :], s_sq[0:1, :], AF.Sqrt, bias=epsc[0:1, 0:1], scale=1.0),
                waits=(("fin", ch + 1), ("cns", p.cur("cns"))), inc="fsq")
            p.emit("vector", lambda e: e.reciprocal(s_sq[0:1, :], s_sq[0:1, :]),
                   waits=(("fsq", ch + 1),), inc="fre")
            for k in range(2):
                wtp = [("fre", ch + 1)] if k == 0 else []
                if k == 0 and lc == 0:
                    wtp.append(("sev", sev_entry))
                p.emit("tensor", (lambda lc_, k_, pb_: lambda e: e.transpose(
                    pmisc[:, pb_ + 2 * lc_ + k_:pb_ + 2 * lc_ + k_ + 1],
                    s_sq[0:1, k_ * 128:(k_ + 1) * 128], ident_f[0:1, 0:1]))(lc, k, pm_base),
                    waits=tuple(wtp), inc="ftp" if k == 1 else None)
            chk[0] += 1
        p.emit("vector", (lambda rc_, nch_, pb_: lambda e: e.tensor_copy(
            rc_[:, :], pmisc[:, pb_:pb_ + 2 * nch_]))(rcol_dst, nch, pm_base),
            waits=(("ftp", chk[0]),), inc="sev")

    xT_v = xT_h.ap().rearrange("(c p) t -> p c t", p=128)
    xbf_v = xbf_h.ap().rearrange("(c p) t -> p c t", p=128)
    xq_v = xq_h.ap().rearrange("(c p) t -> p c t", p=128)
    xqbf_v = xqbf_h.ap().rearrange("(c p) t -> p c t", p=128)
    emit_stats(xT_v, xbf_v, 32, negm_row, r_cols, 0)
    emit_stats(xq_v, xqbf_v, 16, negmq_row, rq_cols, 64)

    p.barrier(pre={"gpsimd": [("xwr0", 64 * 24), ("xwr1", 64 * 24)],
                   "vector": [("sev", p.cur("sev")), ("cns", p.cur("cns")),
                              ("lgev", p.cur("lgev"))]})
    st1.close()

    st2 = ExitStack()
    wbuf = st2.enter_context(nc.sbuf_tensor("wbuf", [128, 32 * 1024], BF16))
    xblk2 = [st2.enter_context(nc.sbuf_tensor(f"xblk{i}", [128, 16 * 1024], BF16))
             for i in range(2)]
    wrs_f = st2.enter_context(nc.sbuf_tensor("wrs_f", [1, 1024], F32))
    wrs_b = st2.enter_context(nc.sbuf_tensor("wrs_b", [1, 1024], BF16))
    qstage2 = [st2.enter_context(nc.sbuf_tensor(f"qstage{i}", [128, 1024], BF16))
               for i in range(2)]
    sqb4 = st2.enter_context(nc.sbuf_tensor("sqb4", [128, 256], BF16))
    nrmt = st2.enter_context(nc.sbuf_tensor("nrmt", [128, 4], F32))
    robuf = st2.enter_context(nc.sbuf_tensor("robuf", [128, 4 * RD], F32))
    robuf2 = st2.enter_context(nc.sbuf_tensor("robuf2", [128, 4 * RD], F32))
    robt = st2.enter_context(nc.sbuf_tensor("robt", [128, 4 * 32], F32))
    robt2 = st2.enter_context(nc.sbuf_tensor("robt2", [128, 4 * 32], F32))
    sc_sin = st2.enter_context(nc.sbuf_tensor("sc_sin", [128, 128], F32))
    sc_cos = st2.enter_context(nc.sbuf_tensor("sc_cos", [128, 128], F32))
    qt_blk = st2.enter_context(nc.sbuf_tensor("qt_blk", [128, 8 * 512], BF16))

    def emit_proj(pj, hg, src_bf_v, n_tiles, rope_sin, rope_cos, negm_src, r_src,
                  out_scr, mx_dst):
        wT_v = w_h[pj].ap().rearrange("(c p) d -> p c d", p=128)[:, :, hg * 1024:(hg + 1) * 1024]
        for sp in range(4):
            w = [("qst", p.cur("qst"))] if sp == 0 else []
            p.dma("gpsimd", (lambda sp_: lambda e: e.dma_start(
                wbuf[:, sp_ * 8 * 1024:(sp_ + 1) * 8 * 1024],
                wT_v[:, sp_ * 8:(sp_ + 1) * 8, :]))(sp), waits=tuple(w), inc="wld")
        wl = p.cur("wld")
        for dc in range(2):
            tgt = ps_a if dc == 0 else ps_b
            for c in range(32):
                w = []
                if dc == 0 and c == 0:
                    w = [("wld", wl), ("bkf", p.cur("bkf")), ("sev", p.cur("sev"))]
                p.emit("tensor", (lambda t_, c_, dc_, st_, sp_: lambda e: e.matmul(
                    t_[0:1, :], ones_b[:, :],
                    wbuf[:, c_ * 1024 + dc_ * 512: c_ * 1024 + (dc_ + 1) * 512],
                    start=st_, stop=sp_))(tgt, c, dc, c == 0, c == 31),
                    waits=tuple(w), inc="qmm" if (c == 31 and dc == 1) else None)
        qm = p.cur("qmm")
        p.emit("vector", lambda e: e.tensor_copy(wrs_f[0:1, 0:512], ps_a[0:1, :]),
               waits=(("qmm", qm),))
        p.emit("vector", lambda e: e.tensor_copy(wrs_f[0:1, 512:1024], ps_b[0:1, :]))
        p.emit("vector", lambda e: e.tensor_copy(wrs_b[0:1, :], wrs_f[0:1, :]),
               inc="bkf")
        n_hblk = n_tiles // 4
        pend_tr = [None]
        vqst = []
        for blk in range(n_hblk):
            t0 = blk * 512
            xb = xblk2[blk % 2]
            for sp in range(4):
                w = []
                if sp == 0:
                    w.append(("qmm", max(0, p.cur("qmm") - 8)))
                p.dma("sync", (lambda sp_, t0_, xb_: lambda e: e.dma_start(
                    xb_[:, sp_ * 8 * 512:(sp_ + 1) * 8 * 512],
                    src_bf_v[:, sp_ * 8:(sp_ + 1) * 8, t0_:t0_ + 512]))(sp, t0, xb),
                    waits=tuple(w), inc=f"xbld{blk % 2}")
            xbl = p.cur(f"xbld{blk % 2}")
            xbln = f"xbld{blk % 2}"
            for ti in range(4):
                tg = blk * 4 + ti
                qstage = qstage2[tg % 2]
                if pj != "v":
                    w = [("ropb", p.cur("ropb"))]
                    p.dma("sync", (lambda tg_: lambda e: e.dma_start(
                        sc_sin[:, :],
                        rope_sin[tg_ * 128:(tg_ + 1) * 128,
                                 hg * 128:(hg + 1) * 128]))(tg),
                        waits=tuple(w), inc="scld")
                    p.dma("sync", (lambda tg_: lambda e: e.dma_start(
                        sc_cos[:, :],
                        rope_cos[tg_ * 128:(tg_ + 1) * 128,
                                 hg * 128:(hg + 1) * 128]))(tg), inc="scld")
                scl = p.cur("scld")
                for dc in range(2):
                    bank = pp[dc]
                    for c in range(32):
                        w = []
                        if c == 0:
                            w = [(xbln, xbl), ("bkf", p.cur("bkf"))]
                        p.emit("tensor",
                               (lambda c_, ti_, dc_, st_, b_, xb_: lambda e: e.matmul(
                                   b_[:, :],
                                   xb_[:, c_ * 512 + ti_ * 128:
                                       c_ * 512 + ti_ * 128 + 128],
                                   wbuf[:, c_ * 1024 + dc_ * 512:
                                        c_ * 1024 + (dc_ + 1) * 512],
                                   start=st_, stop=False))(c, ti, dc, c == 0, bank, xb),
                               waits=tuple(w))
                    p.emit("tensor", (lambda tg_, dc_, b_: lambda e: e.matmul(
                        b_[:, :], negm_src[0:1, tg_ * 128:(tg_ + 1) * 128],
                        wrs_b[0:1, dc_ * 512:(dc_ + 1) * 512],
                        start=False, stop=True))(tg, dc, bank), inc="qmm")
                    qmm = p.cur("qmm")
                    if pj != "v":
                        p.emit("vector", (lambda dc_, tg_, b_: lambda e:
                               e.tensor_scalar_mul(
                                   robuf[:, :].rearrange(
                                       "p (h r) -> p h r", r=RD)[:, 2 * dc_:2 * dc_ + 2, :],
                                   b_[:, :].rearrange(
                                       "p (h r) -> p h r", r=HD)[:, :, 0:RD],
                                   r_src[:, tg_:tg_ + 1]))(dc, tg, bank),
                               waits=(("qmm", qmm),), inc="ropa")
                    wse = [("qmm", qmm)]
                    if pj != "v":
                        wse.append(("ropa", p.cur("ropa")))
                    if dc == 0:
                        if pj == "v":
                            if tg >= 2:
                                wse.append(("qst", vqst[tg - 2]))
                        else:
                            wse.append(("tps", p.cur("tps")))
                    p.emit("scalar", (lambda dc_, tg_, b_, qs_: lambda e: e.activation(
                        qs_[:, dc_ * 512:(dc_ + 1) * 512], b_[:, :], AF.Copy,
                        bias=0.0, scale=r_src[:, tg_:tg_ + 1]))(dc, tg, bank, qstage),
                        waits=tuple(wse), inc="bkf")
                qevv = p.cur("bkf")
                if pend_tr[0] is not None:
                    pend_tr[0]()
                    pend_tr[0] = None
                if pj != "v":
                    ropv = p.cur("ropa")
                    rb = robuf[:, :].rearrange("p (h i two) -> p h i two", i=32, two=2)
                    rb2 = robuf2[:, :].rearrange("p (h i two) -> p h i two", i=32, two=2)
                    sv = sc_sin[:, :].rearrange("p (h i) -> p h i", i=32)
                    cv = sc_cos[:, :].rearrange("p (h i) -> p h i", i=32)
                    tv = robt[:, :].rearrange("p (h i) -> p h i", i=32)
                    tv2 = robt2[:, :].rearrange("p (h i) -> p h i", i=32)
                    p.emit("vector", lambda e: e.tensor_mul(tv, rb[:, :, :, 1], sv),
                           waits=(("ropa", ropv), ("scld", scl)))
                    p.emit("vector", lambda e: e.tensor_mul(tv2, rb[:, :, :, 0], sv))
                    p.emit("vector", lambda e: e.tensor_mul(
                        rb2[:, :, :, 0], rb[:, :, :, 0], cv))
                    p.emit("vector", lambda e: e.tensor_sub(
                        rb2[:, :, :, 0], rb2[:, :, :, 0], tv))
                    p.emit("vector", lambda e: e.tensor_mul(
                        rb2[:, :, :, 1], rb[:, :, :, 1], cv))
                    p.emit("vector", lambda e: e.tensor_add(
                        rb2[:, :, :, 1], rb2[:, :, :, 1], tv2), inc="ropb")
                    p.emit("vector", (lambda qs_: lambda e: e.tensor_copy(
                        qs_[:, :].rearrange("p (h r) -> p h r", r=HD)[:, :, 0:RD],
                        robuf2[:, :].rearrange("p (h r) -> p h r", r=RD)))(qstage),
                        waits=(("bkf", qevv),), inc="ropb")
                    ropd = p.cur("ropb")
                    for hh in range(4):
                        w = []
                        if hh == 0:
                            w = [("ropb", ropd), ("bkf", qevv)]
                        p.emit("scalar", (lambda qs_, hh_: lambda e: e.activation(
                            sqb4[:, :], qs_[:, hh_ * HD:(hh_ + 1) * HD], AF.Square,
                            bias=0.0, scale=1.0,
                            accum_out=nrmt[:, hh_:hh_ + 1]))(qstage, hh),
                            waits=tuple(w), inc="sq" if hh == 3 else None)
                    p.emit("vector", (lambda m_, hg_: lambda e: e.tensor_max(
                        m_[:, hg_ * 4:(hg_ + 1) * 4], m_[:, hg_ * 4:(hg_ + 1) * 4],
                        nrmt[:, :]))(mx_dst, hg),
                        waits=(("sq", p.cur("sq")),), inc="nrm")
                    def mk_pend(tg_=tg, qs_=qstage, ropd_=ropd, hg_=hg):
                        def flush():
                            for i in range(8):
                                w = []
                                if i == 0:
                                    w = [("ropb", ropd_), ("tpe", p.cur("tpe"))]
                                p.emit("tensor", (lambda i_: lambda e: e.transpose(
                                    pbf[:, i_ * 128:(i_ + 1) * 128],
                                    qs_[:, i_ * 128:(i_ + 1) * 128],
                                    ident_b[:, :]))(i),
                                    waits=tuple(w), inc="tps" if i == 7 else None)
                            tpsv = p.cur("tps")
                            for i in range(8):
                                w = [("tps", tpsv)] if i == 0 else []
                                if i == 0 and tg_ % 4 == 0:
                                    w.append(("qst", p.cur("qst")))
                                p.emit("vector", (lambda i_, t4_: lambda e:
                                       e.tensor_copy(
                                           qt_blk[:, i_ * 512 + t4_ * 128:
                                                  i_ * 512 + t4_ * 128 + 128],
                                           pbf[:, i_ * 128:(i_ + 1) * 128]))(
                                               i, tg_ % 4),
                                       waits=tuple(w),
                                       inc="tpe" if i == 7 else None)
                            tpev = p.cur("tpe")
                            if tg_ % 4 == 3:
                                qc = tg_ // 4
                                for i in range(8):
                                    w = [("tpe", tpev)] if i == 0 else []
                                    p.dma("sync", (lambda i_, qc_: lambda e:
                                          e.dma_start(
                                              out_scr[qc_, hg_ * 1024 + i_ * 128:
                                                      hg_ * 1024 + (i_ + 1) * 128, :],
                                              qt_blk[:, i_ * 512:(i_ + 1) * 512]))(
                                                  i, qc),
                                          waits=tuple(w), inc="qst")
                        return flush
                    pend_tr[0] = mk_pend()
                else:
                    for hl in range(4):
                        w = [("bkf", qevv)] if hl == 0 else []
                        p.dma("sync", (lambda hl_, tg_, hg_, qs_: lambda e: e.dma_start(
                            vt_h.ap()[hg_ * 4 + hl_, tg_ * 128:(tg_ + 1) * 128, :],
                            qs_[:, hl_ * HD:(hl_ + 1) * HD]))(hl, tg, hg, qstage),
                            waits=tuple(w), inc="qst")
                    vqst.append(p.cur("qst"))
        if pend_tr[0] is not None:
            pend_tr[0]()
            pend_tr[0] = None
        p.barrier(pre={"sync": [("qst", p.cur("qst"))],
                       "vector": [("nrm", p.cur("nrm")), ("sev", p.cur("sev"))],
                       "scalar": [("bkf", p.cur("bkf"))]})

    for hg in range(2):
        emit_proj("k", hg, xbf_v, 64, sink_h.ap(), cosk_h.ap(), negm_row, r_cols,
                  kt_h.ap(), kmax2)
    for hg in range(2):
        emit_proj("v", hg, xbf_v, 64, None, None, negm_row, r_cols, None, None)
    for hg in range(2):
        emit_proj("q", hg, xqbf_v, 32, sinq_h.ap(), cosq_h.ap(), negmq_row, rq_cols,
                  qt_h.ap(), qmax2)

    p.emit("vector", lambda e: e.tensor_add(qmax2[:, :], qmax2[:, :], kmax2[:, :]),
           inc="sev")
    p.emit("tensor", lambda e: e.transpose(
        pmisc[0:NHC, 0:128], qmax2[:, :], ident_f[:, :]),
        waits=(("sev", p.cur("sev")),), inc="smm")
    p.emit("vector", lambda e: e.reduce_max(
        qmax2[0:NHC, 0:1], pmisc[0:NHC, 0:128], axis=AX.X),
        waits=(("smm", p.cur("smm")),))
    p.emit("vector", lambda e: e.tensor_scalar_mul(
        qmax2[0:NHC, 0:1], qmax2[0:NHC, 0:1], -1.0 / 32.0), inc="sev")
    p.emit("tensor", lambda e: e.transpose(
        pmisc[0:1, 0:NHC], qmax2[0:NHC, 0:1], ident_f[0:NHC, 0:NHC]),
        waits=(("sev", p.cur("sev")),), inc="smm")
    p.emit("vector", lambda e: e.tensor_copy(negc_row[0:1, :], pmisc[0:1, 0:NHC]),
           waits=(("smm", p.cur("smm")),), inc="sev")
    sevn2 = p.cur("sev")
    for h in range(NHC):
        p.emit("tensor", (lambda h_: lambda e: e.matmul(
            pmisc[:, h_:h_ + 1], onesrow_f[0:1, :], negc_row[0:1, h_:h_ + 1],
            start=True, stop=True))(h),
            waits=(("sev", sevn2),) if h == 0 else (),
            inc="smm" if h == NHC - 1 else None)
    p.emit("vector", lambda e: e.tensor_copy(negcb[:, :], pmisc[:, 0:NHC]),
           waits=(("smm", p.cur("smm")),))
    p.emit("vector", lambda e: e.tensor_scalar_add(
        negcy[:, :], negcb[:, :], yflag_sb[:, 0:1]), inc="sev")
    p.barrier(pre={"vector": [("sev", p.cur("sev"))]})
    st2.close()

    st3 = ExitStack()
    kt_sb = st3.enter_context(nc.sbuf_tensor("kt_sb", [128, 2 * S], BF16))
    vt_sb = st3.enter_context(nc.sbuf_tensor("vt_sb", [128, 64 * HD], BF16))
    qch = [st3.enter_context(nc.sbuf_tensor(f"qch{i}", [128, 2 * 512], BF16))
           for i in range(2)]
    psb = [st3.enter_context(nc.sbuf_tensor(f"psb{i}", [128, 512], BF16))
           for i in range(2)]
    srow = st3.enter_context(nc.sbuf_tensor("srow", [1, 512], F32))
    rbf = st3.enter_context(nc.sbuf_tensor("rbf", [128, 512], F32))
    ostage = st3.enter_context(nc.sbuf_tensor("ostage", [128, 1024], BF16))

    sregs = {}

    def sync_regs(e):
        if "tmp" not in sregs:
            sregs["tmp"] = e.alloc_register("shtmp")
        return sregs
    sidx = [0]

    gi = 0
    for h in range(NHC):
        for dh in range(2):
            w = []
            if h > 0 and dh == 0:
                w = [("tsc", p.cur("tsc")), ("tpv", p.cur("tpv"))]
            p.dma("sync", (lambda h_, dh_: lambda e: e.dma_start(
                kt_sb[:, dh_ * S:(dh_ + 1) * S],
                bass.AP(kt_h, (h_ * HD + dh_ * 128) * 512,
                        [[512, 128], [2048 * 512, 16], [1, 512]])))(h, dh),
                waits=tuple(w), inc="ktld")
        for sp in range(8):
            p.dma("sync", (lambda h_, sp_: lambda e: e.dma_start(
                vt_sb[:, sp_ * 8 * HD:(sp_ + 1) * 8 * HD],
                bass.AP(vt_h, h_ * S * HD + sp_ * 8 * 128 * HD,
                        [[HD, 128], [128 * HD, 8], [1, HD]])))(h, sp),
                inc="ktld")
        ktl = p.cur("ktld")
        for qc in range(8):
            qb = qc % 2
            for dh in range(2):
                w = []
                if dh == 0:
                    w.append(("tsc", p.cur("tsc")))
                    if qc == 0:
                        w.append(("ktld", ktl))
                p.dma("sync", (lambda h_, dh_, qc_, qb_: lambda e: e.dma_start(
                    qch[qb_][:, dh_ * 512:(dh_ + 1) * 512],
                    bass.AP(qt_h, qc_ * 2048 * 512 + (h_ * HD + dh_ * 128) * 512,
                            [[512, 128], [1, 512]])))(h, dh, qc, qb),
                    waits=tuple(w), inc="qcld")
            qcl = p.cur("qcld")
            nj = 4 * qc + 36
            pend = None
            for j in range(nj):
                sbank = pp[gi % 2]
                pb = gi % 2
                if 4 * qc <= j < 4 * qc + 4:
                    cls, d = "d1", j - 4 * qc
                elif 4 * qc + 32 <= j:
                    cls, d = "d2", j - 4 * qc - 32
                elif j >= 4 * qc + 4:
                    cls, d = "y", 0
                else:
                    cls, d = "x", 0
                w = []
                if j == 0:
                    w.append(("qcld", qcl))
                if gi >= 2:
                    w.append(("sexp", gi - 1))
                for dh in range(2):
                    p.emit("tensor", (lambda j_, dh_, qb_, st_, sp_, b_:
                           lambda e: e.matmul(
                               b_[:, :],
                               kt_sb[:, dh_ * S + j_ * 128: dh_ * S + j_ * 128 + 128],
                               qch[qb_][:, dh_ * 512:(dh_ + 1) * 512],
                               start=st_, stop=sp_))(j, dh, qb, dh == 0, dh == 1,
                                                     sbank),
                           waits=tuple(w) if dh == 0 else (),
                           inc="tsc" if dh == 1 else None)
                tscv = p.cur("tsc")
                if cls in ("d1", "d2"):
                    msb = maskd1_sb if cls == "d1" else maskd2_sb
                    p.emit("vector", (lambda d_, b_, m_: lambda e: e.tensor_add(
                        b_[:, :], b_[:, :], m_[:, d_ * 512:(d_ + 1) * 512]))(
                            d, sbank, msb), waits=(("tsc", tscv),), inc="vmsk")
                w = [("vmsk", p.cur("vmsk"))] if cls in ("d1", "d2") else                     [("tsc", tscv)]
                if gi >= 2:
                    w.append(("tpv", gi - 1))
                bias_t = negcy if cls in ("y", "d2") else negcb
                p.emit("scalar", (lambda h_, b_, pb_, bt_: lambda e: e.activation(
                    psb[pb_][:, :], b_[:, :], AF.Exp,
                    bias=bt_[:, h_:h_ + 1], scale=1.0 / 16.0))(h, sbank, pb, bias_t),
                    waits=tuple(w), inc="sexp")
                sev_ = p.cur("sexp")

                def emit_pv(j_, pb_, sexp_tgt):
                    for dh in range(2):
                        w2 = []
                        if dh == 0:
                            w2.append(("sexp", sexp_tgt))
                            if j_ == 0:
                                w2.append(("voev", p.cur("voev")))
                        p.emit("tensor",
                               (lambda j2, dh_, pb2, st_, sp_: lambda e: e.matmul(
                                   pp[2 + dh_][:, :],
                                   vt_sb[:, j2 * HD + dh_ * 128:
                                         j2 * HD + dh_ * 128 + 128],
                                   psb[pb2][:, :], start=st_, stop=sp_))(
                                       j_, dh, pb_, j_ == 0, j_ == nj - 1),
                               waits=tuple(w2))
                    w2 = []
                    if j_ == 0:
                        w2.append(("vrs", p.cur("vrs")))
                    p.emit("tensor", (lambda pb2, st_, sp_: lambda e: e.matmul(
                        ps_a[0:1, :], ones_b[:, :], psb[pb2][:, :],
                        start=st_, stop=sp_))(pb_, j_ == 0, j_ == nj - 1),
                        waits=tuple(w2), inc="tpv")

                if pend is not None:
                    emit_pv(*pend)
                pend = (j, pb, sev_)
                if j == nj - 1:
                    emit_pv(*pend)
                    pend = None
                gi += 1
            tpvv = p.cur("tpv")
            p.emit("vector", lambda e: e.tensor_copy(srow[0:1, :], ps_a[0:1, :]),
                   waits=(("tpv", tpvv),))
            p.emit("vector", lambda e: e.reciprocal(srow[0:1, :], srow[0:1, :]),
                   inc="vrs")
            p.emit("tensor", lambda e: e.matmul(
                pmisc[:, :], onesrow_f[0:1, :], srow[0:1, :], start=True, stop=True),
                waits=(("vrs", p.cur("vrs")), ("voev", p.cur("voev"))), inc="trb")
            p.emit("vector", lambda e: e.tensor_copy(rbf[:, :], pmisc[:, :]),
                   waits=(("trb", p.cur("trb")), ("ost", p.cur("ost"))))
            for dh in range(2):
                p.emit("vector", (lambda dh_: lambda e: e.tensor_mul(
                    ostage[:, dh_ * 512:(dh_ + 1) * 512], pp[2 + dh_][:, :],
                    rbf[:, :]))(dh), inc="voev" if dh == 1 else None)
            vov = p.cur("voev")
            for dh in range(2):
                cst = (h * HD + dh * 128) * QW + qc * 512
                p.dma("sync", (lambda dh_, cst_: lambda e: e.dma_start(
                    bass.AP(attn_loc, cst_, [[QW, 128], [1, 512]]),
                    ostage[:, dh_ * 512:(dh_ + 1) * 512]))(dh, cst),
                    waits=(("voev", vov),) if dh == 0 else (), inc="ost")

    ostv = p.cur("ost")
    for k in range(4):
        p.dma("gpsimd", (lambda k_: lambda e: (
            sync_regs(e),
            e.reg_load(sregs["tmp"], offtab_sb[0:1, k_:k_ + 1]),
            e.dma_start(bass.AP(attnsh, sregs["tmp"], [[2097152, 1], [1, 2097152]]),
                        bass.AP(attn_loc, k_ * 2097152,
                                [[2097152, 1], [1, 2097152]])))[-1])(k),
            waits=(("ost", ostv),) if k == 0 else (), inc="xpo")

    def xch_fn(e):
        foff = e.alloc_register("foff")
        poff = e.alloc_register("poff")
        pv = e.alloc_register("pv")
        nv = e.alloc_register("nv")
        nf = e.alloc_register("nf")
        e.reg_load(foff, offtab_sb[0:1, 144:145])
        e.reg_load(nv, nonce_sb[0:1, 0:1].bitcast(mybir.dt.int32))
        e.reg_load(poff, offtab_sb[0:1, 145:146])
        e.dma_start(bass.AP(flagsh, foff, [[64, 1], [1, 64]]),
                    nonce_sb[0:1, :]).then_inc(sems["xpo"], 16)
        e.wait_ge(sems["xpo"], 16 * 5)
        e.reg_mov(nf, 1)
        with e.While(nf):
            e.dma_start(flag_sb[0:1, :],
                        bass.AP(flagsh, poff, [[64, 1], [1, 64]])).then_inc(
                            sems["xpo"], 16)
            e.nop(cycle_cnt=40000)
            e.reg_load(pv, flag_sb[0:1, 0:1].bitcast(mybir.dt.int32))
            e.reg_alu(nf, pv, nv, OP.not_equal)
        return None
    p.emit("gpsimd", xch_fn, waits=(("xpo", p.cur("xpo")), ("wcns", 16)))
    p.emit("gpsimd", lambda e: e.nop(), inc="xch")
    xchv = p.cur("xch")
    st3.close()
    stm.close()

    stB = ExitStack()
    abuf = stB.enter_context(nc.sbuf_tensor("abuf", [128, 32 * 512], BF16))
    wbuf2 = [stB.enter_context(nc.sbuf_tensor(f"wbufB{i}", [128, 16 * 512], BF16))
             for i in range(2)]
    x2g = stB.enter_context(nc.sbuf_tensor("x2g", [128, 4 * D], F32))
    x2bf = stB.enter_context(nc.sbuf_tensor("x2bf", [128, D], BF16))
    x2T = stB.enter_context(nc.sbuf_tensor("x2T", [128, 32 * 128], BF16))
    sqs = stB.enter_context(nc.sbuf_tensor("sqs", [128, D], BF16))

    woT_v = woT_h.ap().rearrange("(c p) d -> p c d", p=128)

    def eread_fn(sp, g):
        idx_ = 128 + g * 2 + sp

        def fn(e):
            r = sync_regs(e)
            e.reg_load(r["tmp"], offtab_sb[0:1, idx_:idx_ + 1])
            return e.dma_start(
                abuf[:, sp * 16 * 512:(sp + 1) * 16 * 512],
                bass.AP(attnsh, r["tmp"],
                        [[QW, 128], [128 * QW, 16], [1, 512]]))
        return fn

    for g in range(4):
        t0 = g * 512
        for sp in range(2):
            w = [("womm", p.cur("womm"))] if sp == 0 else []
            if g == 0 and sp == 0:
                w.append(("xch", xchv))
            p.dma("gpsimd", eread_fn(sp, g), waits=tuple(w), inc="ald")
        aldv = p.cur("ald")
        for ti in range(4):
            w = ([(f"x2wr{(g - 1) % 2}", 64 * ((g - 1) // 2 + 1)),
                  ("sq", p.cur("sq")), ("x2ev", p.cur("x2ev")),
                  ("rms", p.cur("rms"))]
                 if (ti == 0 and g > 0) else [])
            p.dma("sync", (lambda ti_, t0_: lambda e: e.dma_start(
                x2g[:, ti_ * D:(ti_ + 1) * D],
                xres_h.ap()[t0_ + ti_ * 128: t0_ + ti_ * 128 + 128, :]))(ti, t0),
                waits=tuple(w), inc="xrd")
        xrdv = p.cur("xrd")
        for dc in range(8):
            wldns = []
            for sp in range(2):
                wb = wbuf2[sp]
                w = []
                if dc > 0 or g > 0:
                    if sp == 0:
                        w.append(("womh", p.cur("womh")))
                    else:
                        w.append(("womm", p.cur("womm")))
                p.dma("gpsimd", (lambda sp_, dc_, wb_: lambda e: e.dma_start(
                    wb_[:, :],
                    woT_v[:, sp_ * 16:(sp_ + 1) * 16, dc_ * 512:(dc_ + 1) * 512]))(
                        sp, dc, wb), waits=tuple(w), inc=f"wld{sp}")
                wldns.append((f"wld{sp}", p.cur(f"wld{sp}")))
            for ti in range(4):
                bank = pp[ti % 2]
                for c in range(32):
                    w = []
                    if c == 0:
                        w = [wldns[0], ("ald", aldv),
                             ("x2ev", max(0, p.cur("x2ev") - 1))]
                    elif c == 16:
                        w = [wldns[1]]
                    p.emit("tensor",
                           (lambda c_, ti_, st_, sp_, b_: lambda e: e.matmul(
                               b_[:, :],
                               abuf[:, c_ * 512 + ti_ * 128:
                                    c_ * 512 + ti_ * 128 + 128],
                               wbuf2[c_ // 16][:, (c_ % 16) * 512:
                                               (c_ % 16 + 1) * 512],
                               start=st_, stop=sp_))(c, ti, c == 0, c == 31,
                                                     bank),
                           waits=tuple(w),
                           inc=("womm" if c == 31 else
                                ("womh" if c == 15 else None)))
                wommv = p.cur("womm")
                w = [("womm", wommv)]
                if dc == 0:
                    w.append(("xrd", xrdv))
                p.emit("vector", (lambda ti_, dc_, b_: lambda e: e.tensor_add(
                    x2g[:, ti_ * D + dc_ * 512: ti_ * D + (dc_ + 1) * 512],
                    b_[:, :],
                    x2g[:, ti_ * D + dc_ * 512: ti_ * D + (dc_ + 1) * 512]))(
                        ti, dc, bank), waits=tuple(w), inc="x2ev")
        x2evv = p.cur("x2ev")
        for ti in range(4):
            tg = g * 4 + ti
            p.dma("gpsimd", (lambda ti_, t0_: lambda e: e.dma_start(
                x2_h.ap()[t0_ + ti_ * 128: t0_ + ti_ * 128 + 128, :],
                x2g[:, ti_ * D:(ti_ + 1) * D]))(ti, t0),
                waits=(("x2ev", x2evv),) if ti == 0 else (), inc=f"x2wr{g % 2}")
            p.emit("vector", (lambda ti_, tg_: lambda e: e.reduce_sum(
                m16[:, tg_:tg_ + 1], x2g[:, ti_ * D:(ti_ + 1) * D], axis=AX.X))(ti, tg),
                waits=(("x2ev", x2evv),))
            p.emit("vector", (lambda tg_: lambda e: e.tensor_scalar_mul(
                m16[:, tg_:tg_ + 1], m16[:, tg_:tg_ + 1], 1.0 / D))(tg))
            p.emit("scalar", (lambda ti_, tg_: lambda e: e.activation(
                sqs[:, :], x2g[:, ti_ * D:(ti_ + 1) * D], AF.Square,
                accum_out=rs16[:, tg_:tg_ + 1]))(ti, tg),
                waits=(("x2ev", x2evv),), inc="sq")
            p.emit("vector", (lambda tg_: lambda e: e.tensor_scalar_mul(
                rs16[:, tg_:tg_ + 1], rs16[:, tg_:tg_ + 1], 1.0 / D))(tg),
                waits=(("sq", p.cur("sq")),))
            p.emit("vector", (lambda tg_: lambda e: e.tensor_mul(
                sc1[:, 0:1], m16[:, tg_:tg_ + 1], m16[:, tg_:tg_ + 1]))(tg))
            p.emit("vector", (lambda tg_: lambda e: e.tensor_sub(
                rs16[:, tg_:tg_ + 1], rs16[:, tg_:tg_ + 1], sc1[:, 0:1]))(tg),
                inc="rms")
            p.emit("scalar", (lambda tg_: lambda e: e.activation(
                rs16[:, tg_:tg_ + 1], rs16[:, tg_:tg_ + 1], AF.Sqrt,
                bias=epsl[:, 0:1], scale=1.0))(tg),
                waits=(("rms", p.cur("rms")),), inc="sq")
            p.emit("vector", (lambda tg_: lambda e: e.reciprocal(
                rs16[:, tg_:tg_ + 1], rs16[:, tg_:tg_ + 1]))(tg),
                waits=(("sq", p.cur("sq")),))
            p.emit("vector", (lambda ti_, tg_: lambda e: e.tensor_scalar(
                x2bf[:, :], x2g[:, ti_ * D:(ti_ + 1) * D],
                m16[:, tg_:tg_ + 1], rs16[:, tg_:tg_ + 1],
                OP.subtract, OP.mult))(ti, tg),
                waits=(("lgT", p.cur("lgT")),), inc="x2ev")
            cstv = p.cur("x2ev")
            p.emit("scalar", (lambda tg_: lambda e: e.activation(
                sqs[:, :], x2bf[:, :], AF.Square,
                accum_out=r2c[:, tg_:tg_ + 1]))(tg),
                waits=(("x2ev", cstv),), inc="sq")
            p.emit("scalar", (lambda tg_: lambda e: e.activation(
                r2c[:, tg_:tg_ + 1], r2c[:, tg_:tg_ + 1], AF.Sqrt,
                bias=epsb[:, 0:1], scale=1.0 / D))(tg),
                waits=(("sq", p.cur("sq")),), inc="sq")
            p.emit("vector", (lambda tg_: lambda e: e.reciprocal(
                r2c[:, tg_:tg_ + 1], r2c[:, tg_:tg_ + 1]))(tg),
                waits=(("sq", p.cur("sq")),), inc="rms")
            for rnd in range(4):
                for i in range(8):
                    w = []
                    if i == 0:
                        w = [("x2ev", cstv), ("lgTe", p.cur("lgTe"))]
                    p.emit("tensor", (lambda rnd_, i_: lambda e: e.transpose(
                        pbf[:, i_ * 128:(i_ + 1) * 128],
                        x2bf[:, rnd_ * 1024 + i_ * 128: rnd_ * 1024 + (i_ + 1) * 128],
                        ident_b[:, :]))(rnd, i),
                        waits=tuple(w), inc="lgT" if i == 7 else None)
                tvv = p.cur("lgT")
                for i in range(8):
                    p.emit("vector", (lambda rnd_, i_: lambda e: e.tensor_copy(
                        x2T[:, (rnd_ * 8 + i_) * 128:(rnd_ * 8 + i_ + 1) * 128],
                        pbf[:, i_ * 128:(i_ + 1) * 128]))(rnd, i),
                        waits=(("lgT", tvv),) if i == 0 else (),
                        inc="lgTe" if i == 7 else None)
            tev = p.cur("lgTe")
            for c in range(32):
                w = []
                if c == 0:
                    w = [("lgTe", tev), ("lgev", p.cur("lgev")), ("wcns", 16)]
                p.emit("tensor", (lambda c_, st_, sp_: lambda e: e.matmul(
                    pmisc[:, 0:24], x2T[:, c_ * 128:(c_ + 1) * 128],
                    wall_b[:, c_ * 24:(c_ + 1) * 24], start=st_, stop=sp_))(
                        c, c == 0, c == 31),
                    waits=tuple(w), inc="lgmm" if c == 31 else None)
            p.emit("scalar", (lambda tg_: lambda e: e.activation(
                lga[:, tg_ * 24:(tg_ + 1) * 24], pmisc[:, 0:24], AF.Copy,
                bias=0.0, scale=r2c[:, tg_:tg_ + 1]))(tg),
                waits=(("lgmm", p.cur("lgmm")), ("rms", p.cur("rms"))), inc="sq")
            p.emit("vector", (lambda tg_: lambda e: e.tensor_add(
                lga[:, tg_ * 24:(tg_ + 1) * 24], lga[:, tg_ * 24:(tg_ + 1) * 24],
                basesb[:, :]))(tg),
                waits=(("sq", p.cur("sq")),), inc="lgev")
    p.barrier(pre={"gpsimd": [("x2wr0", p.cur("x2wr0")),
                              ("x2wr1", p.cur("x2wr1"))],
                   "vector": [("lgev", p.cur("lgev"))]})
    stB.close()

    lview = lga[:, :].rearrange("p (t c) -> p t c", c=24)
    Zv = Z[:, :].rearrange("p (t i j) -> p t i j", i=4, j=4)
    tv_ = tmp[:, :].rearrange("p (t i j) -> p t i j", i=4, j=4)
    lnv_v = lnv[:, :].rearrange("p (t j) -> p t j", j=4)
    lnu_v = lnu[:, :].rearrange("p (t i) -> p t i", i=4)

    p.emit("vector", lambda e: e.tensor_scalar_mul(
        Zv, lview[:, :, 0:16].rearrange("p t (i j) -> p t i j", j=4), 1.0 / TAU))
    p.emit("vector", lambda e: e.memset(lnv[:, :], 0.0), inc="snk")
    for it in range(SINK_IT):
        for i in range(4):
            p.emit("vector", (lambda i_: lambda e: e.tensor_sub(
                tv_[:, :, i_, :], Zv[:, :, i_, :], lnv_v))(i),
                waits=(("snk", p.cur("snk")),) if i == 0 else (),
                inc="snk" if i == 3 else None)
        p.emit("scalar", lambda e: e.activation(tmpe[:, :], tmp[:, :], AF.Exp),
               waits=(("snk", p.cur("snk")),), inc="sq")
        p.emit("vector", lambda e: e.tensor_reduce(
            ssum[:, :], tmpe[:, :].rearrange("p (ti j) -> p ti j", j=4),
            axis=AX.X, op=OP.add),
            waits=(("sq", p.cur("sq")),), inc="snk")
        p.emit("scalar", lambda e: e.activation(lnu[:, :], ssum[:, :], AF.Ln),
               waits=(("snk", p.cur("snk")),), inc="sq")
        sqv = p.cur("sq")
        for j in range(4):
            p.emit("vector", (lambda j_: lambda e: e.tensor_sub(
                tv_[:, :, j_, :], Zv[:, :, :, j_], lnu_v))(j),
                waits=(("sq", sqv),) if j == 0 else (),
                inc="snk" if j == 3 else None)
        p.emit("scalar", lambda e: e.activation(tmpe[:, :], tmp[:, :], AF.Exp),
               waits=(("snk", p.cur("snk")),), inc="sq")
        p.emit("vector", lambda e: e.tensor_reduce(
            ssum[:, :], tmpe[:, :].rearrange("p (tj i) -> p tj i", i=4),
            axis=AX.X, op=OP.add),
            waits=(("sq", p.cur("sq")),), inc="snk")
        p.emit("scalar", lambda e: e.activation(lnv[:, :], ssum[:, :], AF.Ln),
               waits=(("snk", p.cur("snk")),), inc="sq")
        p.emit("vector", lambda e: e.tensor_copy(sm[:, 0:1], lnv[:, 0:1]),
               waits=(("sq", p.cur("sq")),), inc="snk")
    u4v = u4[:, :].rearrange("p (t i j) -> p t i j", i=4, j=4)
    for j in range(4):
        p.emit("vector", (lambda j_: lambda e: e.tensor_copy(
            u4v[:, :, :, j_], lnu_v))(j))
    p.emit("vector", lambda e: e.tensor_sub(Hm[:, :], Z[:, :], u4[:, :]))
    Hv = Hm[:, :].rearrange("p (t i j) -> p t i j", i=4, j=4)
    for i in range(4):
        p.emit("vector", (lambda i_: lambda e: e.tensor_sub(
            tv_[:, :, i_, :], Hv[:, :, i_, :], lnv_v))(i),
            inc="snk" if i == 3 else None)
    p.emit("scalar", lambda e: e.activation(Hm[:, :], tmp[:, :], AF.Exp),
           waits=(("snk", p.cur("snk")),), inc="sq")
    p.emit("vector", lambda e: e.tensor_scalar_mul(Hm[:, :], Hm[:, :], gb[:, 0:1]),
           waits=(("sq", p.cur("sq")),))
    diagv = Hm[:, :].rearrange("p (t d) -> p t d", d=16)[:, :, 0:16:5]
    p.emit("vector", lambda e: e.tensor_scalar_add(diagv, diagv, omgb[:, 0:1]),
           inc="snk")
    for (src0, dst) in ((16, hp), (20, hq)):
        pv = lview[:, :, src0:src0 + 4]
        dv = dst[:, :].rearrange("p (t n) -> p t n", n=4)
        s4 = sm4[:, :].rearrange("p (t n) -> p t n", n=4)
        p.emit("vector", (lambda pv_: lambda e: e.reduce_max(
            sm[:, :], pv_, axis=AX.X))(pv))
        for n in range(4):
            p.emit("vector", (lambda n_, s4_: lambda e: e.tensor_copy(
                s4_[:, :, n_], sm[:, :]))(n, s4))
        p.emit("vector", (lambda pv_, dv_, s4_: lambda e: e.tensor_sub(
            dv_, pv_, s4_))(pv, dv, s4), inc="snk")
        p.emit("scalar", (lambda d_: lambda e: e.activation(
            d_[:, :], d_[:, :], AF.Exp))(dst),
            waits=(("snk", p.cur("snk")),), inc="sq")
        p.emit("vector", (lambda dv_: lambda e: e.tensor_reduce(
            sm[:, :], dv_, axis=AX.X, op=OP.add))(dv),
            waits=(("sq", p.cur("sq")),))
        p.emit("vector", lambda e: e.reciprocal(sm[:, :], sm[:, :]))
        for n in range(4):
            p.emit("vector", (lambda n_, s4_: lambda e: e.tensor_copy(
                s4_[:, :, n_], sm[:, :]))(n, s4))
        p.emit("vector", (lambda d_: lambda e: e.tensor_mul(
            d_[:, :], d_[:, :], sm4[:, :]))(dst), inc="snk")
    p.barrier(pre={"vector": [("snk", p.cur("snk"))]})

    stC = ExitStack()
    wbr_b = stC.enter_context(nc.sbuf_tensor("wbr_b", [128, 8 * 1024], BF16))
    x2m = [stC.enter_context(nc.sbuf_tensor(f"x2m{i}", [128, D], F32))
           for i in range(2)]
    xpre = stC.enter_context(nc.sbuf_tensor("xpre", [128, 1024], F32))
    ln2m = [stC.enter_context(nc.sbuf_tensor(f"ln2m{i}", [128, D], F32))
            for i in range(2)]
    xpre_b = stC.enter_context(nc.sbuf_tensor("xpre_b", [128, 1024], BF16))
    xpT = stC.enter_context(nc.sbuf_tensor("xpT", [128, 8 * 128], BF16))
    ybuf = stC.enter_context(nc.sbuf_tensor("ybuf", [128, 1024], F32))
    obuf = [stC.enter_context(nc.sbuf_tensor(f"obuf{i}", [128, D], F32))
            for i in range(2)]

    p.dma("gpsimd", lambda e: e.dma_start(
        wbr_b[:, :], wbr_h.ap().rearrange("(c p) d -> p c d", p=128)), inc="wbld")
    wbl = p.cur("wbld")
    for tg in range(16):
        bi = tg % 2
        w = [("mix", max(0, p.cur("mix") - 1))] if tg >= 2 else []
        p.dma("sync", (lambda tg_, bi_: lambda e: e.dma_start(
            x2m[bi_][:, :], x2_h.ap()[tg_ * 128:(tg_ + 1) * 128, :]))(tg, bi),
            waits=tuple(w), inc=f"x2rd{bi}")
        x2rv = p.cur(f"x2rd{bi}")
        w = [(f"x2rd{bi}", x2rv)]
        if tg == 0:
            w.append(("wbld", wbl))
        p.emit("vector", (lambda bi_, tg_: lambda e: e.tensor_scalar(
            ln2m[bi_][:, :], x2m[bi_][:, :], m16[:, tg_:tg_ + 1],
            rs16[:, tg_:tg_ + 1], OP.subtract, OP.mult))(bi, tg), waits=tuple(w))
        p.emit("vector", (lambda bi_, tg_: lambda e: e.tensor_scalar_mul(
            xpre[:, :], ln2m[bi_][:, 0:1024], hp[:, 4 * tg_:4 * tg_ + 1]))(bi, tg))
        for n in range(1, 4):
            p.emit("vector", (lambda bi_, tg_, n_: lambda e: e.scalar_tensor_tensor(
                xpre[:, :], ln2m[bi_][:, n_ * 1024:(n_ + 1) * 1024],
                hp[:, 4 * tg_ + n_:4 * tg_ + n_ + 1], xpre[:, :],
                OP.mult, OP.add))(bi, tg, n))
        p.emit("vector", lambda e: e.tensor_copy(xpre_b[:, :], xpre[:, :]),
               waits=(("lgT", p.cur("lgT")),), inc="mix")
        mxv = p.cur("mix")
        for i in range(8):
            w = []
            if i == 0:
                w = [("mix", mxv), ("lgTe", p.cur("lgTe")), ("yev", p.cur("yev"))]
            p.emit("tensor", (lambda i_: lambda e: e.transpose(
                pbf[:, i_ * 128:(i_ + 1) * 128], xpre_b[:, i_ * 128:(i_ + 1) * 128],
                ident_b[:, :]))(i), waits=tuple(w), inc="lgT" if i == 7 else None)
        tvv = p.cur("lgT")
        for i in range(8):
            p.emit("vector", (lambda i_: lambda e: e.tensor_copy(
                xpT[:, i_ * 128:(i_ + 1) * 128], pbf[:, i_ * 128:(i_ + 1) * 128]))(i),
                waits=(("lgT", tvv),) if i == 0 else (),
                inc="lgTe" if i == 7 else None)
        tev = p.cur("lgTe")
        for n in range(2):
            for c in range(8):
                w = []
                if n == 0 and c == 0:
                    w = [("lgTe", tev), ("yev", p.cur("yev"))]
                p.emit("tensor", (lambda n_, c_, st_, sp_: lambda e: e.matmul(
                    pp[n_][:, :], xpT[:, c_ * 128:(c_ + 1) * 128],
                    wbr_b[:, c_ * 1024 + n_ * 512: c_ * 1024 + (n_ + 1) * 512],
                    start=st_, stop=sp_))(n, c, c == 0, c == 7),
                    waits=tuple(w), inc="ymm" if (n == 1 and c == 7) else None)
        ymv = p.cur("ymm")
        for n in range(2):
            p.emit("vector", (lambda n_: lambda e: e.tensor_copy(
                ybuf[:, n_ * 512:(n_ + 1) * 512], pp[n_][:, :]))(n),
                waits=(("ymm", ymv),) if n == 0 else (),
                inc="yev" if n == 1 else None)
        yev = p.cur("yev")
        for i in range(4):
            w = []
            if i == 0:
                w = [("yev", yev), (f"ost{bi}", p.cur(f"ost{bi}"))]
            p.emit("vector", (lambda bi_, tg_, i_: lambda e: e.scalar_tensor_tensor(
                obuf[bi_][:, i_ * 1024:(i_ + 1) * 1024], ln2m[bi_][:, 0:1024],
                Hm[:, 16 * tg_ + 4 * i_: 16 * tg_ + 4 * i_ + 1],
                x2m[bi_][:, i_ * 1024:(i_ + 1) * 1024],
                OP.mult, OP.add))(bi, tg, i),
                waits=tuple(w))
            for jj in range(1, 4):
                p.emit("vector",
                       (lambda bi_, tg_, i_, j_: lambda e: e.scalar_tensor_tensor(
                           obuf[bi_][:, i_ * 1024:(i_ + 1) * 1024],
                           ln2m[bi_][:, j_ * 1024:(j_ + 1) * 1024],
                           Hm[:, 16 * tg_ + 4 * i_ + j_: 16 * tg_ + 4 * i_ + j_ + 1],
                           obuf[bi_][:, i_ * 1024:(i_ + 1) * 1024],
                           OP.mult, OP.add))(bi, tg, i, jj))
            p.emit("vector", (lambda bi_, tg_, i_: lambda e: e.scalar_tensor_tensor(
                obuf[bi_][:, i_ * 1024:(i_ + 1) * 1024], ybuf[:, :],
                hq[:, 4 * tg_ + i_:4 * tg_ + i_ + 1],
                obuf[bi_][:, i_ * 1024:(i_ + 1) * 1024],
                OP.mult, OP.add))(bi, tg, i),
                inc="mix" if i == 3 else None)
        mxo = p.cur("mix")
        p.dma("sync", (lambda bi_, tg_: lambda e: e.dma_start(
            out_h.ap()[tg_ * 128:(tg_ + 1) * 128, :], obuf[bi_][:, :]))(bi, tg),
            waits=(("mix", mxo),), inc=f"ost{bi}")

    p.emit("gpsimd", None, waits=(("ost0", p.cur("ost0")), ("ost1", p.cur("ost1"))))
    p.emit("sync", None, waits=(("ost0", p.cur("ost0")), ("ost1", p.cur("ost1"))))

    _block_emit(nc, p)
    stC.close()
    stack.close()
    return nc


def _fused_masks():
    base = np.zeros((4, 128, 512), np.float32)
    kk = np.arange(128)[:, None]
    qq = np.arange(512)[None, :]
    for di, d in enumerate((0, 128, 256, 384)):
        base[di] = np.where(kk + d > qq, np.float32(-1e9), np.float32(0.0))
    diag = np.ascontiguousarray(base.reshape(512, 512))
    zero = np.zeros((512, 512), np.float32)
    return diag, zero


def _fused_rope_tables():
    i = np.arange(RD // 2, dtype=np.float64)
    inv_freq = 1.0 / (10000.0 ** (2.0 * i / RD))
    t = np.arange(S, dtype=np.float64)
    ang = np.outer(t, inv_freq)
    sin8 = np.tile(np.sin(ang).astype(np.float32), (1, 8))
    cos8 = np.tile(np.cos(ang).astype(np.float32), (1, 8))
    return np.ascontiguousarray(sin8), np.ascontiguousarray(cos8)


def kernel_fused(inputs):
    x = np.asarray(inputs["L0_attn_hidden_states"])
    wq = np.asarray(inputs["L0_attn_wq"])
    wk = np.asarray(inputs["L0_attn_wk"])
    wv = np.asarray(inputs["L0_attn_wv"])
    wo = np.asarray(inputs["L0_attn_wo"])

    sin8, cos8 = _fused_rope_tables()
    diagm, zerom = _fused_masks()
    ident = np.eye(128, dtype=np.float32)
    onesrow = np.ones((1, 128), np.float32)
    nonce = np.float32(np.random.default_rng().integers(1, 2 ** 30))
    noncev = np.full((1, 64), nonce, np.float32)

    xT = [np.ascontiguousarray(x[b].T) for b in range(B)]
    wT = {"q": wq, "k": wk, "v": wv}
    DQC = 2048
    wTs = {pj: [np.ascontiguousarray(wT[pj][r * DQC:(r + 1) * DQC, :].T)
                for r in range(2)] for pj in wT}
    woT = np.ascontiguousarray(wo.T)
    wallT = np.ascontiguousarray(np.concatenate(
        [np.asarray(inputs["L0_ffn_W_res"]), np.asarray(inputs["L0_ffn_W_pre"]),
         np.asarray(inputs["L0_ffn_W_post"])], axis=0).T)
    bases = np.concatenate(
        [np.asarray(inputs["L0_ffn_H_res_base"]).reshape(-1),
         np.asarray(inputs["L0_ffn_H_pre_base"]).reshape(-1),
         np.asarray(inputs["L0_ffn_H_post_base"]).reshape(-1)]
    ).reshape(1, 24).astype(np.float32)
    gate2 = np.asarray(inputs["L0_ffn_gate"]).reshape(1, 1).astype(np.float32)
    wbr = np.asarray(inputs["L0_ffn_W_branch"])

    nc = build_fused()
    in_maps = []
    for c in range(NCORE):
        r = c % 2
        pp_ = c // 2
        b = pp_ // 2
        h2 = pp_ % 2
        q0 = h2 * 4096
        tok0 = q0 + r * TPC
        in_maps.append(dict(
            xT=xT[b],
            xq=np.ascontiguousarray(xT[b][:, q0:q0 + 4096]),
            wqT=wTs["q"][r], wkT=wTs["k"][r], wvT=wTs["v"][r],
            sinq=np.ascontiguousarray(sin8[q0:q0 + 4096]),
            cosq=np.ascontiguousarray(cos8[q0:q0 + 4096]),
            sink=sin8, cosk=cos8,
            maskd1=(diagm if h2 == 0 else zerom),
            maskd2=(zerom if h2 == 0 else diagm),
            yflag=np.full((128, 1), -1e9 if h2 == 0 else 0.0, np.float32),
            roff=np.array([[r * TPC, r * DQC * 4096, (1 - r) * 64 and r * 0 or r * 64,
                            0]], dtype=np.int32),
            noncev=noncev,
            ident=ident, onesrow=onesrow,
            xres=np.ascontiguousarray(x[b, tok0:tok0 + TPC, :]),
            woT=woT, wallT=wallT, bases=bases, gate=gate2, wbr=wbr))
        in_maps[-1]["roff"] = np.array(
            [[r * TPC, r * DQC * 4096, r * 64, 0]], dtype=np.int32)
        ot = np.zeros((1, 256), np.int32)
        for k in range(4):
            ot[0, k] = r * DQC * 4096 + k * 2097152
        for gg in range(4):
            for spp in range(2):
                ot[0, 128 + gg * 2 + spp] = (r * TPC + gg * 512
                                             + spp * 16 * 128 * 4096)
        ot[0, 144] = r * 64
        ot[0, 145] = (1 - r) * 64
        in_maps[-1]["offtab"] = ot
    res = run_bass_kernel_spmd(nc, in_maps, core_ids=list(range(NCORE)))
    out = np.empty((B, S, D), np.float32)
    for c in range(NCORE):
        r = c % 2
        pp_ = c // 2
        b = pp_ // 2
        tok0 = (pp_ % 2) * 4096 + r * TPC
        out[b, tok0:tok0 + TPC, :] = res.results[c]["out"]
    return out
